# version 07
# speedup vs baseline: 1.5959x; 1.0009x over previous
import contextlib
import math
import numpy as np
import concourse.bass as bass
import concourse.mybir as mybir
from concourse.bass_utils import run_bass_kernel_spmd

F32 = mybir.dt.float32
BF16 = mybir.dt.bfloat16
I32 = mybir.dt.int32
AF = mybir.ActivationFunctionType
ALU = mybir.AluOpType
AX = mybir.AxisListType

NSEQ = 2
T = 2048
D = 1024
DCH = D // 128
TT = 512
NTT = T // TT

D_INNER = 2048
D_STATE = 128
NHEADS = 32
HEADDIM = 64
CONV_DIM = D_INNER + 2 * D_STATE
D_IN_PROJ = 2 * D_INNER + 2 * D_STATE + NHEADS
TC = 128
H_FFN = 4096
N_EXP = 8
H_MOE = 2048

LN_EPS = 1e-5


class Sem:
    def __init__(self, h, name):
        self.h = h
        self.name = name
        self.n = 0

    def bump(self, k):
        self.n += k
        return self.n


class Prog:
    ENGINES = ("tensor", "vector", "scalar", "gpsimd", "sync")
    AUTO = ("vector", "scalar", "gpsimd")

    def __init__(self, nc):
        self.nc = nc
        self.stack = contextlib.ExitStack()
        self.block = self.stack.enter_context(nc.Block())
        self.sbuf_bytes = 0
        self.pending = {e: [] for e in self.ENGINES}
        self.ev = self.sem("auto_v")
        self.ea = self.sem("auto_a")
        self.eg = self.sem("auto_g")
        self._auto = {"vector": self.ev, "scalar": self.ea, "gpsimd": self.eg}

    def flush(self):
        for e in self.ENGINES:
            fns = self.pending[e]
            if not fns:
                continue
            self.pending[e] = []
            getattr(self.block, e)(lambda eng, fns=fns: [f(eng) for f in fns])

    def sem(self, name):
        return Sem(self.stack.enter_context(self.nc.semaphore(name)), name)

    def sbuf(self, name, shape, dtype):
        self.sbuf_bytes += shape[-1] * mybir.dt.size(dtype)
        assert self.sbuf_bytes < 205 * 1024, f"SBUF over budget at {name}: {self.sbuf_bytes}"
        st = self._scope if self._scope is not None else self.stack
        return st.enter_context(self.nc.sbuf_tensor(name, shape, dtype))

    def psum(self, name, shape, dtype=F32):
        st = self._scope if self._scope is not None else self.stack
        return st.enter_context(self.nc.psum_tensor(name, shape, dtype))

    _scope = None

    @contextlib.contextmanager
    def scope(self):
        prev, prev_bytes = self._scope, self.sbuf_bytes
        self._scope = contextlib.ExitStack()
        try:
            yield
        finally:
            self.flush()
            self._scope.close()
            self._scope = prev
            self.sbuf_bytes = prev_bytes

    def op(self, eng, fn, waits=(), inc=None):
        if eng in self.AUTO:
            assert inc is None, "use auto sem counts for cross-engine signaling"
            es = self._auto[eng]
            waits = list(waits) + [(es, es.n)]
            inc = (es, 1)
        if inc is not None:
            inc[0].bump(inc[1])

        def run(e, fn=fn, waits=tuple(waits), inc=inc):
            for (s, v) in waits:
                if v > 0:
                    e.wait_ge(s.h, v)
            r = fn(e)
            if inc is not None:
                r.then_inc(inc[0].h, inc[1])
        self.pending[eng].append(run)

    def sdma(self, fn, waits=(), inc=None):
        if inc is not None:
            inc[0].bump(inc[1])

        def run(e, fn=fn, waits=tuple(waits), inc=inc):
            for (sm, v) in waits:
                if v > 0:
                    e.wait_ge(sm.h, v)
            r = fn(e)
            if inc is not None:
                r.then_inc(inc[0].h, inc[1])
        self.pending["scalar"].append(run)

    def gdma(self, fn, waits=(), inc=None):
        if inc is not None:
            inc[0].bump(inc[1])

        def run(e, fn=fn, waits=tuple(waits), inc=inc):
            for (s, v) in waits:
                if v > 0:
                    e.wait_ge(s.h, v)
            r = fn(e)
            if inc is not None:
                r.then_inc(inc[0].h, inc[1])
        self.pending["gpsimd"].append(run)

    def t(self, fn, **kw): self.op("tensor", fn, **kw)
    def v(self, fn, **kw): self.op("vector", fn, **kw)
    def s(self, fn, **kw): self.op("scalar", fn, **kw)
    def g(self, fn, **kw): self.op("gpsimd", fn, **kw)
    def y(self, fn, **kw): self.op("sync", fn, **kw)

    def barrier(self, waits):
        full = list(waits) + [(self.ev, self.ev.n), (self.ea, self.ea.n),
                              (self.eg, self.eg.n)]
        for e in self.ENGINES:
            if e in self.AUTO:
                self.op(e, lambda x: x.nop(),
                        waits=[(s, v) for (s, v) in full if s is not self._auto[e]])
            else:
                self.op(e, lambda x: x.nop(), waits=full)

    def finalize(self):
        self.flush()
        self.stack.close()


def make_consts(p):
    c = {}
    c["ident_f32"] = p.sbuf("ident_f32", [128, 128], F32)
    c["ident_bf16"] = p.sbuf("ident_bf16", [128, 128], BF16)
    iota_tmp = p.sbuf("iota_tmp", [128, 128], I32)
    c["ones_f32_col"] = p.sbuf("ones_f32_col", [128, 1], F32)
    c["ones_f32_row"] = p.sbuf("ones_f32_row", [1, 128], F32)
    c["ones_bf16_col"] = p.sbuf("ones_bf16_col", [128, 1], BF16)
    c["eps_col"] = p.sbuf("eps_col", [128, 1], F32)
    c["eps6_col"] = p.sbuf("eps6_col", [128, 1], F32)
    c["epsf_col"] = p.sbuf("epsf_col", [128, 1], F32)
    p.g(lambda e: e.memset(c["eps_col"][:, :], LN_EPS))
    p.g(lambda e: e.memset(c["eps6_col"][:, :], 1e-6))
    p.g(lambda e: e.memset(c["epsf_col"][:, :], float(np.finfo(np.float32).eps)))
    p.g(lambda e: e.iota(iota_tmp[:, :], [[1, 128]], base=0, channel_multiplier=-1))
    iota_n = p.eg.n
    p.g(lambda e: e.memset(c["ones_f32_col"][:, :], 1.0))
    p.g(lambda e: e.memset(c["ones_f32_row"][:, :], 1.0))
    p.g(lambda e: e.memset(c["ones_bf16_col"][:, :], 1.0))
    p.v(lambda e: e.tensor_single_scalar(c["ident_f32"][:, :], iota_tmp[:, :], 0,
                                         ALU.is_equal), waits=[(p.eg, iota_n)])
    p.v(lambda e: e.tensor_single_scalar(c["ident_bf16"][:, :], iota_tmp[:, :], 0,
                                         ALU.is_equal))
    c["cmt"] = p.sbuf("cmt", [128, 128], F32)
    p.v(lambda e: e.tensor_single_scalar(c["cmt"][:, :], iota_tmp[:, :], 0,
                                         ALU.is_lt), waits=[(p.eg, iota_n)])
    p.v(lambda e: e.tensor_scalar(c["cmt"][:, :], c["cmt"][:, :], 30000.0, None,
                                  ALU.mult))
    c["waits"] = [(p.eg, p.eg.n), (p.ev, p.ev.n)]
    return c


def build_entry(p, c, x_in, X):
    xt = [p.sbuf(f"en_xt{i}", [128, D], F32) for i in range(2)]
    pt = [p.psum(f"en_ps{i}", [128, 1024], F32) for i in range(2)]
    stg = [p.sbuf(f"en_stg{i}", [128, DCH * TT], F32) for i in range(2)]
    s_ld = [p.sem(f"en_ld{i}") for i in range(2)]
    s_tp = p.sem("en_tp")
    s_st = [p.sem(f"en_st{i}") for i in range(2)]
    tp0 = s_tp.n
    evac_n = {}
    ntile = NSEQ * T // 128
    gpt = TT // 128
    for i in range(ntile):
        s, ti = divmod(i, T // 128)
        g = i // gpt
        b = i % 2
        gb = g % 2
        p.y(lambda e, s=s, ti=ti, b=b: e.dma_start(
                out=xt[b][:, :], in_=x_in[s, ti * 128:(ti + 1) * 128, :]),
            waits=[(s_tp, tp0 + i - 1)],
            inc=(s_ld[b], 16))
        for j in range(DCH):
            p.t(lambda e, b=b, j=j: e.transpose(
                    pt[b][:, j * 128:(j + 1) * 128],
                    xt[b][:, j * 128:(j + 1) * 128], c["ident_f32"][:, :]),
                waits=(c["waits"] + [(s_ld[b], 16 * (i // 2 + 1)),
                                     (p.ev, evac_n.get(i - 2, 0))]
                       if j == 0 else []),
                inc=((s_tp, 1) if j == DCH - 1 else None))
        p.v(lambda e, b=b, gb=gb, off=(i % gpt) * 128: e.tensor_copy(
                stg[gb].ap().rearrange("p (dk tt) -> p dk tt", dk=DCH)[:, :, off:off + 128],
                pt[b].ap().rearrange("p (dk cc) -> p dk cc", dk=DCH)[:, :, :]),
            waits=[(s_tp, tp0 + i + 1), (s_st[gb], 8 * 16 * (g // 2))])
        evac_n[i] = p.ev.n
        if i % gpt == gpt - 1:
            tg = (i % (T // 128)) // gpt
            for dk in range(DCH):
                p.y(lambda e, s=s, dk=dk, tg=tg, gb=gb: e.dma_start(
                        out=X[s, dk, :, tg * TT:(tg + 1) * TT],
                        in_=stg[gb][:, dk * TT:(dk + 1) * TT]),
                    waits=([(p.ev, evac_n[i])] if dk == 0 else []),
                    inc=(s_st[gb], 16))
    return [(s_st[0], s_st[0].n), (s_st[1], s_st[1].n)]


def build_exit(p, c, Xf, y_out, pre_waits=()):
    stg = [p.sbuf(f"ex_stg{i}", [128, DCH * TT], F32) for i in range(2)]
    pt = [p.psum(f"ex_ps{i}", [128, 1024], F32) for i in range(2)]
    ot = [p.sbuf(f"ex_ot{i}", [128, D], F32) for i in range(2)]
    s_ld = [p.sem(f"ex_ld{i}") for i in range(2)]
    s_tp = p.sem("ex_tp")
    s_st = [p.sem(f"ex_st{i}") for i in range(2)]
    tp0 = s_tp.n
    evac_n = {}
    ngrp = NSEQ * T // TT
    for g in range(ngrp):
        s, tg = divmod(g, NTT)
        gb = g % 2
        for dk in range(DCH):
            p.y(lambda e, s=s, tg=tg, dk=dk, gb=gb: e.dma_start(
                    out=stg[gb][:, dk * TT:(dk + 1) * TT],
                    in_=Xf[s, dk, :, tg * TT:(tg + 1) * TT]),
                waits=(list(pre_waits) + [(s_tp, tp0 + 4 * (g - 1))]
                       if dk == 0 else []),
                inc=(s_ld[gb], 16))
        for q in range(TT // 128):
            i = g * 4 + q
            b = i % 2
            for dk in range(DCH):
                p.t(lambda e, gb=gb, dk=dk, q=q, b=b: e.transpose(
                        pt[b][:, dk * 128:(dk + 1) * 128],
                        stg[gb][:, dk * TT + q * 128:dk * TT + q * 128 + 128],
                        c["ident_f32"][:, :]),
                    waits=(c["waits"] + [(s_ld[gb], 16 * 8 * (g // 2 + 1)),
                                         (p.ev, evac_n.get(i - 2, 0))]
                           if dk == 0 else []),
                    inc=((s_tp, 1) if dk == DCH - 1 else None))
            p.v(lambda e, b=b: e.tensor_copy(ot[b][:, :], pt[b][:, :]),
                waits=[(s_tp, tp0 + i + 1), (s_st[b], 16 * (i // 2))])
            evac_n[i] = p.ev.n
            p.y(lambda e, s=s, tg=tg, q=q, b=b: e.dma_start(
                    out=y_out[s, tg * TT + q * 128:tg * TT + (q + 1) * 128, :],
                    in_=ot[b][:, :]),
                waits=[(p.ev, evac_n[i])],
                inc=(s_st[b], 16))
    return [(s_st[0], s_st[0].n), (s_st[1], s_st[1].n)]



LNR = dict(mu=slice(0, TT), mu2=slice(TT, 2 * TT), var=slice(TT, 2 * TT),
           sd=slice(TT, 2 * TT), rs=slice(2 * TT, 3 * TT),
           nmr=slice(3 * TT, 4 * TT))


def make_lnctx(p, c, prefix, st1_ps, st2_ps, rs_ps, nm_ps, s_t):
    return dict(st1=st1_ps, st2=st2_ps, rs=rs_ps, nm=nm_ps, s_t=s_t,
                rowbuf=p.sbuf(prefix + "_rows", [1, 4 * TT], F32),
                rsbc=p.sbuf(prefix + "_rsbc", [128, TT], F32),
                nmrbc=p.sbuf(prefix + "_nmrbc", [128, TT], F32))


def ln_rows_and_bc(p, c, L, n_feat, wait_pairs):
    rowbuf, s_t = L["rowbuf"], L["s_t"]
    p.v(lambda e: e.tensor_scalar(rowbuf[0:1, LNR["mu"]], L["st1"][0:1, :],
                                  1.0 / n_feat, None, ALU.mult),
        waits=wait_pairs)
    p.v(lambda e: e.tensor_mul(rowbuf[0:1, LNR["mu2"]], rowbuf[0:1, LNR["mu"]],
                               rowbuf[0:1, LNR["mu"]]))
    p.v(lambda e: e.scalar_tensor_tensor(
            rowbuf[0:1, LNR["var"]], L["st2"][0:1, :], 1.0 / n_feat,
            rowbuf[0:1, LNR["mu2"]], ALU.mult, ALU.subtract))
    vn = p.ev.n
    p.s(lambda e: e.activation(rowbuf[0:1, LNR["sd"]], rowbuf[0:1, LNR["var"]],
                               AF.Sqrt, bias=c["eps_col"][0:1, :]),
        waits=[(p.ev, vn)])
    an = p.ea.n
    p.v(lambda e: e.reciprocal(rowbuf[0:1, LNR["rs"]], rowbuf[0:1, LNR["sd"]]),
        waits=[(p.ea, an)])
    p.v(lambda e: e.scalar_tensor_tensor(
            rowbuf[0:1, LNR["nmr"]], rowbuf[0:1, LNR["mu"]], -1.0,
            rowbuf[0:1, LNR["rs"]], ALU.mult, ALU.mult))
    vn = p.ev.n
    p.t(lambda e: e.matmul(L["rs"][:, :], c["ones_f32_row"][:, :],
                           rowbuf[0:1, LNR["rs"]], start=True, stop=True),
        waits=[(p.ev, vn)], inc=(s_t, 1))
    p.t(lambda e: e.matmul(L["nm"][:, :], c["ones_f32_row"][:, :],
                           rowbuf[0:1, LNR["nmr"]], start=True, stop=True),
        inc=(s_t, 1))
    tn = s_t.n
    p.v(lambda e: e.tensor_copy(L["rsbc"][:, :], L["rs"][:, :]),
        waits=[(s_t, tn), (p.eg, p.eg.n)])
    p.v(lambda e: e.tensor_copy(L["nmrbc"][:, :], L["nm"][:, :]))
    return p.ev.n


def emit_boundary(p, c, L, xbuf, stg_a, stg_b, t_scr, t_scr_g, Xout, s, tsl,
                  ready, s_xs):
    s_t = L["s_t"]
    st_mm_n = {}
    for dk in range(DCH):
        p.g(lambda e, dk=dk: e.tensor_copy(stg_a[dk % 2][:, :],
                                           xbuf[:, dk * TT:(dk + 1) * TT]),
            waits=(list(ready) if dk == 0 else []) +
                  [(s_t, st_mm_n.get(dk - 2, s_t.n if dk == 0 else 0))])
        p.g(lambda e, dk=dk: e.tensor_mul(stg_b[dk % 2][:, :],
                                          xbuf[:, dk * TT:(dk + 1) * TT],
                                          xbuf[:, dk * TT:(dk + 1) * TT]))
        gn = p.eg.n
        p.t(lambda e, dk=dk: e.matmul(
                L["st1"][0:1, :], c["ones_bf16_col"][:, :], stg_a[dk % 2][:, :],
                start=(dk == 0), stop=(dk == DCH - 1)),
            waits=[(p.eg, gn)] + ([(p.ev, p.ev.n)] if dk == 0 else []),
            inc=(s_t, 1))
        p.t(lambda e, dk=dk: e.matmul(
                L["st2"][0:1, :], c["ones_bf16_col"][:, :], stg_b[dk % 2][:, :],
                start=(dk == 0), stop=(dk == DCH - 1)),
            inc=(s_t, 1))
        st_mm_n[dk] = s_t.n
    ln_rows_and_bc(p, c, L, D, [(s_t, s_t.n)])
    for dk in range(DCH):
        if dk % 2 == 0:
            p.v(lambda e, dk=dk: e.tensor_mul(t_scr[:, :],
                                              xbuf[:, dk * TT:(dk + 1) * TT],
                                              L["rsbc"][:, :]))
            p.v(lambda e, dk=dk: e.tensor_add(xbuf[:, dk * TT:(dk + 1) * TT],
                                              t_scr[:, :], L["nmrbc"][:, :]))
            en, sem = p.ev.n, p.ev
        else:
            p.g(lambda e, dk=dk: e.tensor_mul(t_scr_g[:, :],
                                              xbuf[:, dk * TT:(dk + 1) * TT],
                                              L["rsbc"][:, :]),
                waits=[(p.ev, p.ev.n)])
            p.g(lambda e, dk=dk: e.tensor_add(xbuf[:, dk * TT:(dk + 1) * TT],
                                              t_scr_g[:, :], L["nmrbc"][:, :]))
            en, sem = p.eg.n, p.eg
        p.y(lambda e, s=s, dk=dk, tsl=tsl: e.dma_start(
                out=Xout[s, dk, :, tsl], in_=xbuf[:, dk * TT:(dk + 1) * TT]),
            waits=[(sem, en)], inc=(s_xs, 16))



def build_ffn_prologue(p, c, W_fc, W_proj, ln_w, ln_b, scr_fcT, scr_projT,
                       persist, pre_waits=()):
    s_ld = p.sem("fp_ld")
    s_t = p.sem("fp_t")
    s_st = p.sem("fp_st")
    with p.scope():
        wt = p.sbuf("fp_wt", [128, 4096], F32)
        s1 = p.sbuf("fp_s1", [128, 1024], F32)
        s2 = p.sbuf("fp_s2", [128, 1024], F32)
        qt = p.sbuf("fp_qt", [128, 4096], BF16)
        tstage = p.sbuf("fp_tst", [128, 4096], BF16)
        qraw = p.sbuf("fp_qraw", [128, 128], F32)
        abscols = p.sbuf("fp_abs", [128, 40], F32)
        gcols = p.sbuf("fp_gcols", [128, 32], F32)
        bcols = p.sbuf("fp_bcols", [128, 32], F32)
        lnv = p.sbuf("fp_lnv", [32, 128], F32)
        rows = p.sbuf("fp_rows", [1, 8], F32)
        ccol = p.sbuf("fp_ccol", [128, 2], F32)
        nccol = p.sbuf("fp_nccol", [128, 2], F32)
        colsum = p.sbuf("fp_colsum", [128, 2], F32)
        tp_bf = p.psum("fp_tpbf", [128, 512], BF16)
        tp_f = p.psum("fp_tpf", [128, 128], F32)
        mi_ps = p.psum("fp_mi", [128, 512], F32)
        st_ps = p.psum("fp_stp", [128, 512], F32)
        cb_ps = p.psum("fp_cb", [128, 512], F32)

        CW = [(W_fc, 32, 1024), (W_proj, 8, 4096)]

        for w, (src, dst) in enumerate([(ln_w, gcols), (ln_b, bcols)]):
            p.y(lambda e, src=src: e.dma_start(
                    out=lnv[:, :], in_=src.rearrange("(a b) -> a b", a=32)),
                waits=list(pre_waits) + [(s_t, s_t.n)], inc=(s_ld, 16))
            ldn = s_ld.n
            p.t(lambda e: e.transpose(tp_f[:, :32], lnv[:32, :],
                                      c["ident_f32"][:32, :32]),
                waits=c["waits"] + [(s_ld, ldn), (p.ev, p.ev.n)], inc=(s_t, 1))
            p.v(lambda e, dst=dst: e.tensor_copy(dst[:, :32], tp_f[:, :32]),
                waits=[(s_t, s_t.n)])

        coli = 0
        for w, (W, ntiles, tw) in enumerate(CW):
            lo = coli
            for i in range(ntiles):
                p.y(lambda e, W=W, i=i, tw=tw: e.dma_start(
                        out=wt[:, :tw], in_=W[i * 128:(i + 1) * 128, :]),
                    waits=[(p.ev, p.ev.n)], inc=(s_ld, 16))
                ldn = s_ld.n
                p.v(lambda e, ci=coli, tw=tw: e.reduce_sum(
                        abscols[:, ci:ci + 1], wt[:, :tw], AX.X,
                        apply_absolute_value=True),
                    waits=[(s_ld, ldn)])
                coli += 1
            p.v(lambda e, w=w, lo=lo, hi=coli: e.reduce_sum(
                    colsum[:, w:w + 1], abscols[:, lo:hi], AX.X))
            p.t(lambda e, w=w: e.matmul(mi_ps[0:1, w:w + 1], colsum[:, w:w + 1],
                                        c["ones_f32_col"][:, :], start=True, stop=True),
                waits=[(p.ev, p.ev.n)], inc=(s_t, 1))
            p.v(lambda e, w=w: e.tensor_scalar(rows[0:1, 2 + w:3 + w],
                                               mi_ps[0:1, w:w + 1],
                                               1.0 / 4194304.0, 1e-6,
                                               ALU.mult, ALU.add),
                waits=[(s_t, s_t.n)])
            p.v(lambda e, w=w: e.tensor_scalar(rows[0:1, 4 + w:5 + w],
                                               rows[0:1, 2 + w:3 + w],
                                               0.5, None, ALU.mult))
            p.t(lambda e, w=w: e.matmul(mi_ps[:, 4 + w:5 + w], c["ones_f32_row"][:, :],
                                        rows[0:1, 4 + w:5 + w], start=True, stop=True),
                waits=[(p.ev, p.ev.n)], inc=(s_t, 1))
            p.v(lambda e, w=w: e.tensor_copy(ccol[:, w:w + 1], mi_ps[:, 4 + w:5 + w]),
                waits=[(s_t, s_t.n)])
            p.v(lambda e, w=w: e.tensor_scalar(nccol[:, w:w + 1], ccol[:, w:w + 1],
                                               -1.0, None, ALU.mult))

        for w, (W, ntiles, tw) in enumerate(CW):
            for i in range(ntiles):
                p.y(lambda e, W=W, i=i, tw=tw: e.dma_start(
                        out=wt[:, :tw], in_=W[i * 128:(i + 1) * 128, :]),
                    waits=[(p.ea, p.ea.n), (p.ev, p.ev.n)], inc=(s_ld, 16))
                ldn = s_ld.n
                for st in range(tw // 1024):
                    sl = slice(st * 1024, (st + 1) * 1024)
                    p.s(lambda e, sl=sl, w=w: e.sign(s1[:, :], wt[:, sl],
                                                     bias=nccol[:, w:w + 1]),
                        waits=[(s_ld, ldn), (p.ev, p.ev.n)])
                    p.s(lambda e, sl=sl, w=w: e.sign(s2[:, :], wt[:, sl],
                                                     bias=ccol[:, w:w + 1]))
                    p.v(lambda e: e.tensor_add(s1[:, :], s1[:, :], s2[:, :]),
                        waits=[(p.ea, p.ea.n)])
                    p.s(lambda e, sl=sl: e.activation(qt[:, sl], s1[:, :], AF.Copy,
                                                      scale=0.5),
                        waits=[(p.ev, p.ev.n), (s_t, s_t.n)])
                an = p.ea.n
                nsub = tw // 128
                for j in range(nsub):
                    q = j % 4
                    p.t(lambda e, j=j, q=q: e.transpose(
                            tp_bf[:, q * 128:(q + 1) * 128],
                            qt[:, j * 128:(j + 1) * 128], c["ident_bf16"][:, :]),
                        waits=[(p.ea, an), (p.ev, p.ev.n)], inc=(s_t, 1))
                    tn = s_t.n
                    if w == 0:
                        p.v(lambda e, j=j, q=q: e.tensor_copy(
                                tstage[:, j * 128:(j + 1) * 128],
                                tp_bf[:, q * 128:(q + 1) * 128]),
                            waits=[(s_t, tn), (s_st, s_st.n)])
                        vn = p.ev.n
                        p.y(lambda e, j=j, i=i: e.dma_start(
                                out=scr_fcT[j, :, i * 128:(i + 1) * 128],
                                in_=tstage[:, j * 128:(j + 1) * 128]),
                            waits=[(p.ev, vn)], inc=(s_st, 16))
                    else:
                        p.v(lambda e, q=q: e.tensor_copy(
                                qraw[:, :], tp_bf[:, q * 128:(q + 1) * 128]),
                            waits=[(s_t, tn)])
                        vn = p.ev.n
                        p.t(lambda e, i=i, j=j: e.matmul(
                                cb_ps[:, i:i + 1], qraw[:, :], bcols[:, j:j + 1],
                                start=(j == 0), stop=(j == nsub - 1)),
                            waits=[(p.ev, vn)], inc=(s_t, 1))
                        p.v(lambda e, j=j, q=q: e.tensor_scalar(
                                tstage[:, j * 128:(j + 1) * 128],
                                tp_bf[:, q * 128:(q + 1) * 128],
                                gcols[:, j:j + 1], None, ALU.mult),
                            waits=[(s_t, s_t.n), (s_st, s_st.n)])
                        vn = p.ev.n
                        p.t(lambda e, i=i, j=j: e.matmul(
                                st_ps[:, i:i + 1],
                                tstage[:, j * 128:(j + 1) * 128],
                                c["ones_bf16_col"][:, :],
                                start=(j == 0), stop=(j == nsub - 1)),
                            waits=[(p.ev, vn)], inc=(s_t, 1))
                        p.y(lambda e, j=j, i=i: e.dma_start(
                                out=scr_projT[j, :, i * 128:(i + 1) * 128],
                                in_=tstage[:, j * 128:(j + 1) * 128]),
                            waits=[(s_t, s_t.n)], inc=(s_st, 16))
        p.v(lambda e: e.tensor_copy(persist["st_col"][:, :], st_ps[:, 0:8]),
            waits=[(s_t, s_t.n)])
        p.v(lambda e: e.tensor_copy(persist["cb_col"][:, :], cb_ps[:, 0:8]))
        done = [(p.ev, p.ev.n), (s_st, s_st.n), (p.ea, p.ea.n), (s_t, s_t.n),
                (s_ld, s_ld.n)]
        p.barrier(done)
    return done


def build_ffn(p, c, Xin, Xout, scr_fcT, scr_projT, persist, pre_waits=()):
    NHK = H_FFN // 128
    s_w = p.sem("ff_w")
    s_wp = [p.sem(f"ff_wp{i}") for i in range(2)]
    s_xl = p.sem("ff_xl")
    s_xs = p.sem("ff_xs")
    s_t = p.sem("ff_t")
    wfc = [p.sbuf(f"ff_wfc{dk}", [128, H_FFN], BF16) for dk in range(DCH)]
    wprs = [p.sbuf(f"ff_wprs{i}", [128, D], BF16) for i in range(2)]
    xbuf = p.sbuf("ff_x", [128, DCH * TT], F32)
    xb = p.sbuf("ff_xb", [128, DCH * TT], BF16)
    h_sb = p.sbuf("ff_h", [128, (H_FFN // 128) * TT], BF16)
    stg_a = [p.sbuf(f"ff_sa{i}", [128, TT], BF16) for i in range(2)]
    stg_b = [p.sbuf(f"ff_sb{i}", [128, TT], BF16) for i in range(2)]
    t_scr = p.sbuf("ff_t0", [128, TT], F32)
    t_scr_g = p.sbuf("ff_t0g", [128, TT], F32)
    rowbuf = p.sbuf("ff_rows", [1, 4 * TT], F32)
    rsbc = p.sbuf("ff_rsbc", [128, TT], F32)
    nmrbc = p.sbuf("ff_nmrbc", [128, TT], F32)
    pb = [p.psum(f"ff_pb{i}", [128, TT], F32) for i in range(8)]
    h_ps = [pb[0], pb[1]]
    st1_ps, st2_ps = pb[2], pb[3]
    rs_ps, nm_ps = pb[4], pb[5]
    R = dict(mu=slice(0, TT), mu2=slice(TT, 2 * TT), var=slice(TT, 2 * TT),
             sd=slice(TT, 2 * TT), rs=slice(2 * TT, 3 * TT),
             nmr=slice(3 * TT, 4 * TT))

    for dk in range(DCH):
        p.y(lambda e, dk=dk: e.dma_start(out=wfc[dk][:, :], in_=scr_fcT[dk, :, :]),
            waits=(list(pre_waits) if dk == 0 else []), inc=(s_w, 16))
    wn = s_w.n

    gelu_cnt = {}
    t_after_P_prev = 0

    def rows_and_bc(n_feat, wait_pairs):
        p.v(lambda e: e.tensor_scalar(rowbuf[0:1, R["mu"]], st1_ps[0:1, :],
                                      1.0 / n_feat, None, ALU.mult),
            waits=wait_pairs)
        vn = p.ev.n
        p.s(lambda e: e.activation(rowbuf[0:1, R["mu2"]], rowbuf[0:1, R["mu"]],
                                   AF.Square), waits=[(p.ev, vn)])
        an = p.ea.n
        p.v(lambda e: e.scalar_tensor_tensor(
                rowbuf[0:1, R["var"]], st2_ps[0:1, :], 1.0 / n_feat,
                rowbuf[0:1, R["mu2"]], ALU.mult, ALU.subtract),
            waits=[(p.ea, an)])
        vn = p.ev.n
        p.s(lambda e: e.activation(rowbuf[0:1, R["sd"]], rowbuf[0:1, R["var"]],
                                   AF.Sqrt, bias=c["eps_col"][0:1, :]),
            waits=[(p.ev, vn)])
        an = p.ea.n
        p.v(lambda e: e.reciprocal(rowbuf[0:1, R["rs"]], rowbuf[0:1, R["sd"]]),
            waits=[(p.ea, an)])
        p.v(lambda e: e.scalar_tensor_tensor(
                rowbuf[0:1, R["nmr"]], rowbuf[0:1, R["mu"]], -1.0,
                rowbuf[0:1, R["rs"]], ALU.mult, ALU.mult))
        vn = p.ev.n
        p.t(lambda e: e.matmul(rs_ps[:, :], c["ones_f32_row"][:, :],
                               rowbuf[0:1, R["rs"]], start=True, stop=True),
            waits=[(p.ev, vn)], inc=(s_t, 1))
        p.t(lambda e: e.matmul(nm_ps[:, :], c["ones_f32_row"][:, :],
                               rowbuf[0:1, R["nmr"]], start=True, stop=True),
            inc=(s_t, 1))
        tn = s_t.n
        p.v(lambda e: e.tensor_copy(rsbc[:, :], rs_ps[:, :]),
            waits=[(s_t, tn), (p.eg, p.eg.n)])
        p.v(lambda e: e.tensor_copy(nmrbc[:, :], nm_ps[:, :]))
        return p.ev.n

    NHK = H_FFN // 128
    for tt in range(NSEQ * NTT):
        s, tg = divmod(tt, NTT)
        tsl = slice(tg * TT, (tg + 1) * TT)
        for dk in range(DCH):
            p.y(lambda e, s=s, dk=dk, tsl=tsl: e.dma_start(
                    out=xbuf[:, dk * TT:(dk + 1) * TT], in_=Xin[s, dk, :, tsl]),
                waits=([(s_xs, s_xs.n), (p.ev, p.ev.n), (p.eg, p.eg.n)]
                       if dk == 0 else []),
                inc=(s_xl, 16))
        xln = s_xl.n
        for dk in range(DCH):
            p.v(lambda e, dk=dk: e.tensor_copy(xb[:, dk * TT:(dk + 1) * TT],
                                               xbuf[:, dk * TT:(dk + 1) * TT]),
                waits=([(s_xl, xln), (s_t, t_after_P_prev)] if dk == 0 else []))
        cast_vn = p.ev.n
        for hk in range(NHK):
            hsl = slice(hk * TT, (hk + 1) * TT)
            for dk in range(DCH):
                p.t(lambda e, hk=hk, dk=dk: e.matmul(
                        h_ps[hk % 2][:, :],
                        wfc[dk][:, hk * 128:(hk + 1) * 128],
                        xb[:, dk * TT:(dk + 1) * TT],
                        start=(dk == 0), stop=(dk == DCH - 1)),
                    waits=([(s_w, wn), (p.ev, cast_vn),
                            (p.ea, gelu_cnt.get((tt, hk - 2),
                                  gelu_cnt.get((tt - 1, NHK - 2 + hk), 0)))]
                           if dk == 0 else []),
                    inc=((s_t, 1) if dk == DCH - 1 else None))
            tn = s_t.n
            p.s(lambda e, hk=hk, hsl=hsl: e.activation(
                    h_sb[:, hsl], h_ps[hk % 2][:, :], AF.Gelu),
                waits=[(s_t, tn)])
            gelu_cnt[(tt, hk)] = p.ea.n
            p.v(lambda e, hk=hk, hsl=hsl: e.tensor_mul(
                    stg_b[hk % 2][:, :], h_sb[:, hsl], h_sb[:, hsl]),
                waits=[(s_t, s_t.n), (p.ea, gelu_cnt[(tt, hk)])])
            sq_vn = p.ev.n
            p.t(lambda e, hk=hk, hsl=hsl: e.matmul(
                    st1_ps[0:1, :], c["ones_bf16_col"][:, :], h_sb[:, hsl],
                    start=(hk == 0), stop=(hk == NHK - 1)),
                waits=[(p.ea, gelu_cnt[(tt, hk)])] +
                      ([(p.ev, p.ev.n)] if hk == 0 else []),
                inc=(s_t, 1))
            p.t(lambda e, hk=hk: e.matmul(
                    st2_ps[0:1, :], c["ones_bf16_col"][:, :], stg_b[hk % 2][:, :],
                    start=(hk == 0), stop=(hk == NHK - 1)),
                waits=[(p.ev, sq_vn)], inc=(s_t, 1))
        all_gelu = p.ea.n
        bc_vn = rows_and_bc(H_FFN, [(s_t, s_t.n)])
        wpr_mm_n = {}
        for hk in range(NHK):
            b = hk % 2
            p.y(lambda e, hk=hk, b=b: e.dma_start(out=wprs[b][:, :],
                                                  in_=scr_projT[hk, :, :]),
                waits=[(s_t, wpr_mm_n.get(hk - 2, 0))], inc=(s_wp[b], 16))
            wpn = s_wp[b].n
            for dk in range(DCH):
                p.t(lambda e, hk=hk, dk=dk, b=b: e.matmul(
                        pb[dk][:, :],
                        wprs[b][:, dk * 128:(dk + 1) * 128],
                        h_sb[:, hk * TT:(hk + 1) * TT],
                        start=(hk == 0), stop=(hk == NHK - 1)),
                    waits=([(p.ea, all_gelu), (s_wp[b], wpn), (p.ev, bc_vn)]
                           if dk == 0 else []),
                    inc=((s_t, 1) if dk == DCH - 1 else None))
            wpr_mm_n[hk] = s_t.n
        tn_P = s_t.n
        for dk in range(DCH):
            p.v(lambda e, dk=dk: e.tensor_mul(t_scr[:, :], pb[dk][:, :],
                                              rsbc[:, :]),
                waits=[(s_t, tn_P)])
            p.v(lambda e, dk=dk: e.scalar_tensor_tensor(
                    t_scr[:, :], nmrbc[:, :], persist["st_col"][:, dk:dk + 1],
                    t_scr[:, :], ALU.mult, ALU.add))
            p.v(lambda e, dk=dk: e.scalar_tensor_tensor(
                    xbuf[:, dk * TT:(dk + 1) * TT], t_scr[:, :],
                    persist["cb_col"][:, dk:dk + 1],
                    xbuf[:, dk * TT:(dk + 1) * TT], ALU.add, ALU.add))
        epi_vn = p.ev.n
        for dk in range(DCH):
            p.g(lambda e, dk=dk: e.tensor_copy(stg_a[dk % 2][:, :],
                                               xbuf[:, dk * TT:(dk + 1) * TT]),
                waits=[(p.ev, epi_vn), (s_t, s_t.n)])
            p.g(lambda e, dk=dk: e.tensor_mul(stg_b[dk % 2][:, :],
                                              xbuf[:, dk * TT:(dk + 1) * TT],
                                              xbuf[:, dk * TT:(dk + 1) * TT]))
            gn = p.eg.n
            p.t(lambda e, dk=dk: e.matmul(
                    st1_ps[0:1, :], c["ones_bf16_col"][:, :], stg_a[dk % 2][:, :],
                    start=(dk == 0), stop=(dk == DCH - 1)),
                waits=[(p.eg, gn)] + ([(p.ev, p.ev.n)] if dk == 0 else []),
                inc=(s_t, 1))
            p.t(lambda e, dk=dk: e.matmul(
                    st2_ps[0:1, :], c["ones_bf16_col"][:, :], stg_b[dk % 2][:, :],
                    start=(dk == 0), stop=(dk == DCH - 1)),
                inc=(s_t, 1))
        rows_and_bc(D, [(s_t, s_t.n)])
        for dk in range(DCH):
            if dk % 2 == 0:
                p.v(lambda e, dk=dk: e.tensor_mul(t_scr[:, :],
                                                  xbuf[:, dk * TT:(dk + 1) * TT],
                                                  rsbc[:, :]))
                p.v(lambda e, dk=dk: e.tensor_add(xbuf[:, dk * TT:(dk + 1) * TT],
                                                  t_scr[:, :], nmrbc[:, :]))
                en, sem = p.ev.n, p.ev
            else:
                p.g(lambda e, dk=dk: e.tensor_mul(t_scr_g[:, :],
                                                  xbuf[:, dk * TT:(dk + 1) * TT],
                                                  rsbc[:, :]),
                    waits=[(p.ev, p.ev.n)])
                p.g(lambda e, dk=dk: e.tensor_add(xbuf[:, dk * TT:(dk + 1) * TT],
                                                  t_scr_g[:, :], nmrbc[:, :]))
                en, sem = p.eg.n, p.eg
            p.y(lambda e, s=s, dk=dk, tsl=tsl: e.dma_start(
                    out=Xout[s, dk, :, tsl], in_=xbuf[:, dk * TT:(dk + 1) * TT]),
                waits=[(sem, en)], inc=(s_xs, 16))
        t_after_P_prev = s_t.n
    return [(s_xs, s_xs.n)]



MAMBA_PLIM = None


def build_mamba_prologue(p, c, in_proj_w, out_proj_w, scr_inT, scr_outT,
                         pre_waits=()):
    s_ld = [p.sem(f"mp_ld{i}") for i in range(2)]
    s_t = p.sem("mp_t")
    s_st4 = [[p.sem(f"mp_st{i}{j}") for j in range(2)] for i in range(2)]
    with p.scope():
        qt = [p.sbuf(f"mp_qt{i}", [128, 2048], BF16) for i in range(2)]
        tstage = [p.sbuf(f"mp_tst{i}", [128, 2048], BF16) for i in range(2)]
        tp_bf = [p.psum(f"mp_tpbf{i}", [128, 1024], BF16) for i in range(2)]
        CW = [(in_proj_w, 35, 1024, scr_inT, DCH),
              (out_proj_w, 8, 2048, scr_outT, 16)]
        if MAMBA_PLIM is not None:
            CW = [(W, min(nt, MAMBA_PLIM[wi]), tw, scr, ns)
                  for wi, (W, nt, tw, scr, ns) in enumerate(CW)]
        tile_mm = {}
        bank_ev = {0: 0, 1: 0}
        st_by_slot = {0: None, 1: None}
        ti = 0
        for w, (W, ntiles, tw, scr, nsub) in enumerate(CW):
            for i in range(ntiles):
                rows = 128 if (w == 1 or i < 34) else 32
                b = ti % 2
                p.gdma(lambda e, W=W, i=i, tw=tw, rows=rows, b=b: e.dma_start(
                        out=qt[b][:rows, :tw], in_=W[i * 128:i * 128 + rows, :]),
                    waits=list(pre_waits) + [(s_t, tile_mm.get(ti - 2, 0))],
                    inc=(s_ld[b], 16))
                ldn = s_ld[b].n
                nround = nsub // 8
                for r in range(nround):
                    bank = (ti * nround + r) % 2
                    for jj in range(8):
                        j = r * 8 + jj
                        p.t(lambda e, j=j, jj=jj, b=b, bank=bank, rows=rows:
                            e.transpose(
                                tp_bf[bank][:, jj * 128:jj * 128 + rows],
                                qt[b][:rows, j * 128:(j + 1) * 128],
                                c["ident_bf16"][:rows, :rows]),
                            waits=(c["waits"] + [(s_ld[b], ldn),
                                                 (p.ev, bank_ev[bank])]
                                   if jj == 0 else []),
                            inc=((s_t, 1) if jj == 7 else None))
                    tn = s_t.n
                    p.v(lambda e, r=r, b=b, bank=bank, rows=rows: e.tensor_copy(
                            tstage[b].ap().rearrange("pp (a q) -> pp a q", q=128)[:, r * 8:(r + 1) * 8, 0:rows],
                            tp_bf[bank].ap().rearrange("pp (a q) -> pp a q", q=128)[:, :, 0:rows]),
                        waits=[(s_t, tn)] + (st_by_slot[b] or []))
                    bank_ev[bank] = p.ev.n
                vn = p.ev.n
                tile_mm[ti] = s_t.n
                hs = nsub if nsub <= 8 else (nsub + 1) // 2
                for sh in range(2 if nsub > 8 else 1):
                    lo = sh * hs
                    hi = min(nsub, lo + hs)
                    sem_st = s_st4[b][sh]
                    p.y(lambda e, scr=scr, i=i, rows=rows, lo=lo, hi=hi, b=b: e.dma_start(
                            out=scr.rearrange("a pp ee -> pp a ee")[:, lo:hi, i * 128:i * 128 + rows],
                            in_=tstage[b].ap().rearrange("pp (a q) -> pp a q", q=128)[:, lo:hi, 0:rows]),
                        waits=[(p.ev, vn)], inc=(sem_st, 16))
                st_by_slot[b] = [(s_st4[b][0], s_st4[b][0].n),
                                 (s_st4[b][1], s_st4[b][1].n)]
                ti += 1
        done = ([(p.ev, p.ev.n), (s_t, s_t.n),
                 (s_ld[0], s_ld[0].n), (s_ld[1], s_ld[1].n)] +
                [(s_st4[i][j], s_st4[i][j].n) for i in range(2) for j in range(2)])
        p.barrier(done)
    return done


MAMBA_LIMIT = None


def build_mamba(p, c, Xin, Xout, scr_inT, scr_outT, conv_w, conv_b, dt_bias,
                A_log, Dp, norm_w, ln_w, ln_b, pre_waits=(), dbg=None):
    lim = MAMBA_LIMIT or (NSEQ, NTT, TT // TC, NHEADS)

    def dump(name, tile):
        if dbg is None or name not in dbg:
            return
        p.barrier([])
        p.y(lambda e: e.dma_start(out=dbg[name], in_=tile), inc=(s_xl, 16))
        p.barrier([(s_xl, s_xl.n)])
    """x1 = LN(x + MambaBlock(x)). MambaBlock = postLN(gamma,beta) of
    (out_proj(gated-RMS(SSD(conv(in_proj x)))) + x)."""
    NCC = CONV_DIM // 128
    NZC = D_INNER // 128
    NEC = D_IN_PROJ // 128
    s_w = p.sem("mm_w")
    s_wa = [p.sem(f"mm_wa{i}") for i in range(2)]
    s_xl = p.sem("mm_xl")
    s_xs = p.sem("mm_xs")
    s_t = p.sem("mm_t")
    s_fl = p.sem("mm_fl")
    s_rb = p.sem("mm_rb")
    s_rb2 = [p.sem(f"mm_rb2{i}") for i in range(2)]
    lam_flat_dram = p.nc.dram_tensor("lam_flat_dram", [1, 4 * NHEADS * TC], F32,
                                     kind="Internal").ap()
    rbc_cons = {0: 0, 1: 0}
    wA = [p.sbuf(f"mm_wA{i}", [128, DCH * 128], BF16) for i in range(2)]
    z_sil = p.sbuf("mm_zsil", [128, NZC * TT], BF16)
    xbc = p.sbuf("mm_xbc", [128, NCC * (TT + 3)], BF16)
    xssm = p.sbuf("mm_xssm", [128, NZC * TT], BF16)
    bcb = p.sbuf("mm_bcb", [128, 2 * TT], BF16)
    y_fm = p.sbuf("mm_yfm", [128, NZC * TT], BF16)
    xe = p.sbuf("mm_xe", [128, DCH * TT], F32)
    xb1 = p.sbuf("mm_xb1", [128, DCH * TT], F32)
    X_tm = p.sbuf("mm_xtm", [128, D_INNER], BF16)
    Y_tm = p.sbuf("mm_ytm", [128, D_INNER], BF16)
    xt_all = p.sbuf("mm_xtall", [128, D_INNER], BF16)
    H_f = p.sbuf("mm_hf", [128, D_INNER], F32)
    H_b = p.sbuf("mm_hb", [128, D_INNER], BF16)
    B_tm = p.sbuf("mm_btm", [128, 128], BF16)
    mt_half = p.sbuf("mm_mthalf", [128, 2048], BF16)
    lam_rbc = [p.sbuf(f"mm_lamrbc{i}", [128, 2048], F32) for i in range(2)]
    ytmp4 = p.sbuf("mm_ytmp4", [128, 256], F32)
    dt_tm_bf = p.sbuf("mm_dttmb", [128, 128], BF16)
    dtr = p.sbuf("mm_dtr", [32, TT], F32)
    dt_fm = p.sbuf("mm_dtfm", [32, TT], F32)
    r_scr = p.sbuf("mm_rscr", [32, TT], F32)
    a_scr = p.sbuf("mm_ascr", [32, TT], F32)
    lam_fm = p.sbuf("mm_lam", [32, TT], F32)
    lam_loc_all = p.sbuf("mm_lamloc", [32, TT], F32)
    zeros32 = p.sbuf("mm_z32", [32, TT], F32)
    carry = p.sbuf("mm_carry", [32, 2], F32)
    lam_tm_all = p.sbuf("mm_lamtm", [128, 128], F32)
    dt_tm_all = p.sbuf("mm_dttm", [128, 128], F32)
    expl_all = p.sbuf("mm_expltm", [128, 128], F32)
    dlbc_all = p.sbuf("mm_dlbc", [128, 128], F32)
    decs_all = p.sbuf("mm_decs", [128, 128], F32)
    w_all4 = p.sbuf("mm_wall", [128, 128], F32)
    wdt_all4 = p.sbuf("mm_wdt", [128, 128], BF16)
    dlrow4 = p.sbuf("mm_dlr", [1, 128], F32)
    dl_cols = p.sbuf("mm_dlcols", [32, 4], F32)
    cwcols = p.sbuf("mm_cw", [128, NCC * 4], F32)
    cbcols = p.sbuf("mm_cb", [128, NCC], F32)
    dtb_col = p.sbuf("mm_dtb", [32, 1], F32)
    nega_col = p.sbuf("mm_negA", [32, 1], F32)
    d_bc = p.sbuf("mm_dbc", [128, 32], BF16)
    nw_cols = p.sbuf("mm_nw", [128, NZC], F32)
    lnw_cols = p.sbuf("mm_lnw", [128, DCH], F32)
    lnb_cols = p.sbuf("mm_lnb", [128, DCH], F32)
    rn_bc = p.sbuf("mm_rnbc", [128, TT], BF16)
    stg_a = [p.sbuf(f"mm_sa{i}", [128, TT], BF16) for i in range(2)]
    stg_b = [p.sbuf(f"mm_sb{i}", [128, TT], BF16) for i in range(2)]
    t_scr = p.sbuf("mm_t0", [128, TT], F32)
    t_scr_g = p.sbuf("mm_t0g", [128, TT], F32)
    tmp_g = p.sbuf("mm_tmpg", [128, TT], F32)
    xbb = p.sbuf("mm_xbb", [128, DCH * TT], BF16)
    pbS = [p.psum(f"mm_pb{i}", [128, TT], F32) for i in range(7)]
    G_ps = pbS[0]
    R_ps = [pbS[1], pbS[2]]
    A_ps = [pbS[3], pbS[5]]
    B_ps = [pbS[4], pbS[6]]
    TP_ps = pbS[5]
    tp_bf = p.psum("mm_tpb", [128, 1024], BF16)
    L = make_lnctx(p, c, "mm", pbS[2], pbS[3], pbS[4], pbS[5], s_t)

    first_w = list(pre_waits)
    for a in range(NCC):
        p.y(lambda e, a=a: e.dma_start(out=cwcols[:, a * 4:(a + 1) * 4],
                                       in_=conv_w[a * 128:(a + 1) * 128, :]),
            waits=(first_w if a == 0 else []), inc=(s_w, 16))
        p.y(lambda e, a=a: e.dma_start(
                out=cbcols[:, a:a + 1],
                in_=conv_b[a * 128:(a + 1) * 128].rearrange("(x b) -> x b", b=1)),
            inc=(s_w, 16))
    p.y(lambda e: e.dma_start(out=dtb_col[:, :],
                              in_=dt_bias.rearrange("(x b) -> x b", b=1)),
        inc=(s_w, 16))
    p.y(lambda e: e.dma_start(out=zeros32[0:1, 0:32],
                              in_=A_log.rearrange("(x b) -> b x", b=1)),
        inc=(s_w, 16))
    p.y(lambda e: e.dma_start(out=zeros32[0:1, 32:64],
                              in_=Dp.rearrange("(x b) -> b x", b=1)),
        inc=(s_w, 16))
    for src, dst, n in [(norm_w, nw_cols, NZC), (ln_w, lnw_cols, DCH),
                        (ln_b, lnb_cols, DCH)]:
        p.y(lambda e, src=src, n=n: e.dma_start(
                out=zeros32[0:n, 64:192],
                in_=src.rearrange("(a b) -> a b", a=n)),
            waits=[(s_t, s_t.n)], inc=(s_w, 16))
        wn0 = s_w.n
        p.t(lambda e, n=n: e.transpose(TP_ps[:, 0:n], zeros32[0:n, 64:192],
                                       c["ident_f32"][:n, :n]),
            waits=c["waits"] + [(s_w, wn0), (p.ev, p.ev.n)], inc=(s_t, 1))
        p.v(lambda e, dst=dst, n=n: e.tensor_copy(dst[:, 0:n], TP_ps[:, 0:n]),
            waits=[(s_t, s_t.n)])
    wn0 = s_w.n
    p.s(lambda e: e.activation(zeros32[0:1, 192:224], zeros32[0:1, 0:32], AF.Exp),
        waits=[(s_w, wn0)])
    an = p.ea.n
    p.t(lambda e: e.transpose(TP_ps[0:32, 0:1], zeros32[0:1, 192:224],
                              c["ident_f32"][:1, :1]),
        waits=[(p.ea, an), (p.ev, p.ev.n)], inc=(s_t, 1))
    p.v(lambda e: e.tensor_scalar(nega_col[:, :], TP_ps[0:32, 0:1], -1.0, None,
                                  ALU.mult),
        waits=[(s_t, s_t.n)])
    p.t(lambda e: e.matmul(TP_ps[:, 64:96], c["ones_f32_row"][:, :],
                           zeros32[0:1, 32:64], start=True, stop=True),
        waits=[(s_w, wn0), (p.ev, p.ev.n)], inc=(s_t, 1))
    p.v(lambda e: e.tensor_copy(d_bc[:, :], TP_ps[:, 64:96]),
        waits=[(s_t, s_t.n)])
    p.v(lambda e: e.memset(zeros32[:, :], 0.0))
    zero_vn = p.ev.n

    for s in range(lim[0]):
        p.g(lambda e: e.memset(H_f[:, :], 0.0),
            waits=[(s_t, s_t.n), (p.ev, p.ev.n), (p.ea, p.ea.n)])
        p.g(lambda e: e.memset(H_b[:, :], 0.0))
        p.g(lambda e: e.memset(carry[:, :], 0.0))
        p.g(lambda e: e.memset(
                xbc.ap().rearrange("p (a w) -> p a w", a=NCC)[:, :, 0:3], 0.0))
        init_gn = p.eg.n
        for tt in range(lim[1]):
            tsl = slice(tt * TT, (tt + 1) * TT)
            for dk in range(DCH):
                p.y(lambda e, s=s, dk=dk, tsl=tsl: e.dma_start(
                        out=xe[:, dk * TT:(dk + 1) * TT], in_=Xin[s, dk, :, tsl]),
                    waits=([(s_xs, s_xs.n), (p.ev, p.ev.n), (p.eg, p.eg.n),
                            (p.ea, p.ea.n), (s_t, s_t.n)] if dk == 0 else []),
                    inc=(s_xl, 16))
            xln = s_xl.n
            for dk in range(DCH):
                p.v(lambda e, dk=dk: e.tensor_copy(xbb[:, dk * TT:(dk + 1) * TT],
                                                   xe[:, dk * TT:(dk + 1) * TT]),
                    waits=([(s_xl, xln), (s_t, s_t.n), (p.eg, init_gn)]
                           if dk == 0 else []))
            cast_vn = p.ev.n
            evac_cnt = {}
            mm_cnt = {}
            for ech in range(NEC + 1):
                ecols = 128 if ech < NEC else D_IN_PROJ - NEC * 128
                b = ech % 2
                p.y(lambda e, ech=ech, ecols=ecols, b=b: e.dma_start(
                        out=wA[b].ap().rearrange("p (a ee) -> p a ee", a=DCH)[:, :, 0:ecols],
                        in_=scr_inT.rearrange("a pp ee -> pp a ee")[:, :, ech * 128:ech * 128 + ecols]),
                    waits=[(s_t, mm_cnt.get(ech - 2, 0))], inc=(s_wa[b], 16))
                wan = s_wa[b].n
                for dk in range(DCH):
                    p.t(lambda e, ech=ech, ecols=ecols, b=b, dk=dk: e.matmul(
                            pbS[b][0:ecols, :],
                            wA[b][:, dk * 128:dk * 128 + ecols],
                            xbb[:, dk * TT:(dk + 1) * TT],
                            start=(dk == 0), stop=(dk == DCH - 1)),
                        waits=([(s_wa[b], wan), (p.ev, cast_vn)] +
                               ([evac_cnt[ech - 2]] if ech - 2 in evac_cnt else [])
                               if dk == 0 else []),
                        inc=((s_t, 1) if dk == DCH - 1 else None))
                tn = s_t.n
                mm_cnt[ech] = tn
                if ech < NZC:
                    p.s(lambda e, ech=ech: e.activation(
                            z_sil[:, ech * TT:(ech + 1) * TT], pbS[ech % 2][:, :],
                            AF.Silu),
                        waits=[(s_t, tn)])
                    evac_cnt[ech] = (p.ea, p.ea.n)
                elif ech < NZC + NCC:
                    a = ech - NZC
                    p.s(lambda e, a=a, ech=ech: e.activation(
                            xbc[:, a * (TT + 3) + 3:(a + 1) * (TT + 3)],
                            pbS[ech % 2][:, :], AF.Copy),
                        waits=[(s_t, tn)])
                    evac_cnt[ech] = (p.ea, p.ea.n)
                else:
                    p.v(lambda e, ech=ech: e.tensor_scalar(
                            dtr[:, :], pbS[ech % 2][0:32, :], dtb_col[:, 0:1],
                            None, ALU.add),
                        waits=[(s_t, tn)])
                    evac_cnt[ech] = (p.ev, p.ev.n)
            vn0 = p.ev.n
            p.s(lambda e: e.activation(r_scr[:, :], dtr[:, :], AF.Relu),
                waits=[(p.ev, vn0)])
            p.s(lambda e: e.activation(a_scr[:, :], dtr[:, :], AF.Abs))
            p.s(lambda e: e.activation(a_scr[:, :], a_scr[:, :], AF.Exp,
                                       scale=-1.0))
            p.s(lambda e: e.activation(a_scr[:, :], a_scr[:, :], AF.Ln,
                                       bias=c["ones_f32_col"][0:32, 0:1]))
            an = p.ea.n
            p.v(lambda e: e.tensor_add(dt_fm[:, :], r_scr[:, :], a_scr[:, :]),
                waits=[(p.ea, an)])
            p.v(lambda e: e.tensor_scalar(r_scr[:, :], dt_fm[:, :],
                                          nega_col[:, 0:1], None, ALU.mult))
            p.v(lambda e: e.tensor_copy(carry[:, 1:2], carry[:, 0:1]))
            p.v(lambda e: e.tensor_tensor_scan(lam_fm[:, :], r_scr[:, :],
                                               zeros32[:, :], carry[:, 0:1],
                                               ALU.add, ALU.add),
                waits=[(p.ev, zero_vn)])
            p.v(lambda e: e.tensor_copy(carry[:, 0:1], lam_fm[:, TT - 1:TT]))
            dt_vn = p.ev.n
            conv_first = True
            conv_sil = {}
            for a in range(NCC):
                base = a * (TT + 3)
                use_v = True
                acc = t_scr if (a % 2 == 0) else t_scr_g
                em = p.v
                if conv_first:
                    w0 = [(s_w, wn0), (s_t, s_t.n), (p.ea, p.ea.n)]
                    conv_first = False
                else:
                    w0 = [(p.ea, conv_sil.get(a - 2, 0))]
                em(lambda e, base=base, a=a, acc=acc: e.tensor_scalar(
                        acc[:, :], xbc[:, base:base + TT],
                        cwcols[:, 4 * a:4 * a + 1], None, ALU.mult),
                   waits=w0 + [evac_cnt[NZC + a]])
                for k2 in range(1, 4):
                    if use_v:
                        em(lambda e, base=base, a=a, k2=k2, acc=acc:
                           e.scalar_tensor_tensor(
                                acc[:, :], xbc[:, base + k2:base + k2 + TT],
                                cwcols[:, 4 * a + k2:4 * a + k2 + 1], acc[:, :],
                                ALU.mult, ALU.add))
                    else:
                        em(lambda e, base=base, a=a, k2=k2: e.tensor_scalar(
                                tmp_g[:, :], xbc[:, base + k2:base + k2 + TT],
                                cwcols[:, 4 * a + k2:4 * a + k2 + 1], None,
                                ALU.mult))
                        em(lambda e, acc=acc: e.tensor_add(acc[:, :], acc[:, :],
                                                           tmp_g[:, :]))
                ecnt = (p.ev, p.ev.n) if use_v else (p.eg, p.eg.n)
                if a < NZC:
                    p.s(lambda e, a=a, acc=acc: e.activation(
                            xssm[:, a * TT:(a + 1) * TT], acc[:, :], AF.Silu,
                            bias=cbcols[:, a:a + 1]),
                        waits=[ecnt, (s_t, s_t.n)])
                else:
                    p.s(lambda e, a=a, acc=acc: e.activation(
                            bcb[:, (a - NZC) * TT:(a - NZC + 1) * TT], acc[:, :],
                            AF.Silu, bias=cbcols[:, a:a + 1]),
                        waits=[ecnt, (s_t, s_t.n)])
                conv_sil[a] = p.ea.n
            conv_an = p.ea.n
            p.v(lambda e: e.tensor_copy(
                    xbc.ap().rearrange("p (a w) -> p a w", a=NCC)[:, :, 0:3],
                    xbc.ap().rearrange("p (a w) -> p a w", a=NCC)[:, :, TT:TT + 3]),
                waits=[(p.ea, conv_an), (p.eg, p.eg.n)])
            if s == 0 and tt == 0:
                dump("z_sil", z_sil[:, :])
                dump("xssm", xssm[:, :])
                dump("bcb", bcb[:, :])
                dump("dt_fm", dt_fm[:, :])
                dump("lam_fm", lam_fm[:, :])
            for q in range(4):
                if q == 0:
                    p.v(lambda e: e.tensor_scalar(
                            lam_loc_all[:, 0:TC], lam_fm[:, 0:TC],
                            carry[:, 1:2], None, ALU.subtract),
                        waits=[(s_t, s_t.n)])
                else:
                    p.v(lambda e, q=q: e.tensor_scalar(
                            lam_loc_all[:, q * TC:(q + 1) * TC],
                            lam_fm[:, q * TC:(q + 1) * TC],
                            lam_fm[:, q * TC - 1:q * TC], None, ALU.subtract))
                p.v(lambda e, q=q: e.tensor_copy(dl_cols[:, q:q + 1],
                                                 lam_loc_all[:, (q + 1) * TC - 1:(q + 1) * TC]))
            lamall_vn = p.ev.n
            for q in range(4):
                p.y(lambda e, q=q: e.dma_start(
                        out=lam_flat_dram.rearrange("o (qq a b) -> o qq a b", qq=4, a=32)[:, q, :, :],
                        in_=lam_loc_all[:, q * TC:(q + 1) * TC]),
                    waits=([(p.ev, lamall_vn), (s_rb2[0], s_rb2[0].n),
                            (s_rb2[1], s_rb2[1].n)] if q == 0 else []),
                    inc=(s_fl, 16))
            fln = s_fl.n
            for q in range(4):
                p.t(lambda e, q=q: e.transpose(pbS[5][:, q * 32:(q + 1) * 32],
                                               lam_loc_all[:, q * TC:(q + 1) * TC],
                                               c["ident_f32"][:32, :32]),
                    waits=([(p.ev, lamall_vn)] if q == 0 else []),
                    inc=((s_t, 1) if q == 3 else None))
            tn = s_t.n
            p.v(lambda e: e.tensor_copy(lam_tm_all[:, :], pbS[5][:, 0:128]),
                waits=[(s_t, tn)])
            for q in range(4):
                p.t(lambda e, q=q: e.transpose(pbS[5][:, 128 + q * 32:128 + (q + 1) * 32],
                                               dt_fm[:, q * TC:(q + 1) * TC],
                                               c["ident_f32"][:32, :32]),
                    waits=([(p.ev, p.ev.n)] if q == 0 else []),
                    inc=((s_t, 1) if q == 3 else None))
            tn = s_t.n
            p.v(lambda e: e.tensor_copy(dt_tm_all[:, :], pbS[5][:, 128:256]),
                waits=[(s_t, tn)])
            p.v(lambda e: e.tensor_copy(dt_tm_bf[:, :], dt_tm_all[:, :]))
            for q in range(4):
                p.t(lambda e, q=q: e.transpose(pbS[5][0:1, 256 + q * 32:256 + (q + 1) * 32],
                                               dl_cols[:, q:q + 1],
                                               c["ident_f32"][:32, :32]),
                    waits=([(p.ev, p.ev.n)] if q == 0 else []),
                    inc=((s_t, 1) if q == 3 else None))
            tn = s_t.n
            p.v(lambda e: e.tensor_copy(dlrow4[0:1, :], pbS[5][0:1, 256:384]),
                waits=[(s_t, tn)])
            vn = p.ev.n
            for q in range(4):
                p.t(lambda e, q=q: e.matmul(
                        pbS[5][:, (q % 2) * 32:(q % 2) * 32 + 32],
                        c["ones_f32_row"][:, :], dlrow4[0:1, q * 32:(q + 1) * 32],
                        start=True, stop=True),
                    waits=[(p.ev, vn if q == 0 else p.ev.n)],
                    inc=(s_t, 1))
                tnq = s_t.n
                p.v(lambda e, q=q: e.tensor_copy(dlbc_all[:, q * 32:(q + 1) * 32],
                                                 pbS[5][:, (q % 2) * 32:(q % 2) * 32 + 32]),
                    waits=[(s_t, tnq)])
            vn = p.ev.n
            p.s(lambda e: e.activation(expl_all[:, :], lam_tm_all[:, :], AF.Exp),
                waits=[(p.ev, vn)])
            p.s(lambda e: e.activation(decs_all[:, :], dlbc_all[:, :], AF.Exp))
            p.v(lambda e: e.tensor_sub(w_all4[:, :], dlbc_all[:, :],
                                       lam_tm_all[:, :]),
                waits=[(p.ea, p.ea.n)])
            vn = p.ev.n
            p.s(lambda e: e.activation(w_all4[:, :], w_all4[:, :], AF.Exp),
                waits=[(p.ev, vn)])
            an = p.ea.n
            p.v(lambda e: e.tensor_mul(wdt_all4[:, :], w_all4[:, :],
                                       dt_tm_all[:, :]),
                waits=[(p.ea, an)])
            prep_vn = p.ev.n
            prep_an = p.ea.n
            rb_n = {}
            for q in range(lim[2]):
                qof = q * NHEADS * TC
                ch = slice(q * TC, (q + 1) * TC)
                for r in range(2):
                    for j in range(8):
                        k = r * 8 + j
                        p.t(lambda e, k=k, j=j, q=q: e.transpose(
                                tp_bf[:, j * 128:(j + 1) * 128],
                                xssm[:, k * TT + q * TC:k * TT + (q + 1) * TC],
                                c["ident_bf16"][:, :]),
                            waits=([(p.ev, p.ev.n), (p.ea, p.ea.n)] if j == 0 else []),
                            inc=((s_t, 1) if j == 7 else None))
                    tn = s_t.n
                    p.v(lambda e, r=r: e.tensor_copy(
                            X_tm[:, r * 1024:(r + 1) * 1024], tp_bf[:, :]),
                        waits=[(s_t, tn)])
                p.t(lambda e, q=q: e.transpose(
                        tp_bf[:, 0:128], bcb[:, q * TC:(q + 1) * TC],
                        c["ident_bf16"][:, :]),
                    waits=[(p.ev, p.ev.n)], inc=(s_t, 1))
                tn = s_t.n
                p.v(lambda e: e.tensor_copy(B_tm[:, :], tp_bf[:, 0:128]),
                    waits=[(s_t, tn)])
                xtm_vn = p.ev.n
                p.t(lambda e, q=q: e.matmul(
                        G_ps[:, 0:128], bcb[:, q * TC:(q + 1) * TC],
                        bcb[:, TT + q * TC:TT + (q + 1) * TC],
                        start=True, stop=True),
                    waits=[(p.ev, xtm_vn)], inc=(s_t, 1))
                g_tn = s_t.n
                p.v(lambda e, q=q: e.tensor_tensor(
                        xt_all[:, :], X_tm[:, :],
                        bass.AP(dt_tm_bf.ap().tensor, q * 32,
                                [[128, 128], [1, 32], [0, 64]]),
                        ALU.mult))
                xt3_vn = p.ev.n
                ev4 = {}
                for half in range(2):
                    hb2 = (2 * q + half) % 2
                    if (q, half) not in rb_n:
                        for ah in range(2 * q + half, min(8, 2 * q + half + 2)):
                            aq, ahf = divmod(ah, 2)
                            ab = ah % 2
                            p.y(lambda e, aq=aq, ahf=ahf, ab=ab: e.dma_start(
                                    out=lam_rbc[ab][:, :],
                                    in_=bass.AP(lam_flat_dram.tensor,
                                                aq * NHEADS * TC + ahf * 2048,
                                                [[0, 128], [1, 2048]])),
                                waits=[(s_fl, fln),
                                       (p.ev, rbc_cons[ab])],
                                inc=(s_rb2[ab], 16))
                            rb_n[(aq, ahf)] = s_rb2[ab].n
                    import os as _os
                    _dqh = _os.environ.get("MAMBA_DQH")
                    _hit = (_dqh is not None and dbg is not None and s == 0
                            and tt == 0 and (q, half) ==
                            tuple(int(z) for z in _dqh.split(",")))
                    if _hit:
                        p.barrier([(s_rb2[hb2], rb_n[(q, half)])])
                        p.y(lambda e, hb2=hb2: e.dma_start(
                                out=dbg["rbc_raw"], in_=lam_rbc[hb2][:, :]),
                            inc=(s_xl, 16))
                        p.barrier([(s_xl, s_xl.n)])
                    p.v(lambda e, q=q, half=half, hb2=hb2: e.tensor_tensor(
                            lam_rbc[hb2][:, :], lam_rbc[hb2][:, :],
                            bass.AP(lam_tm_all.ap().tensor, q * 32 + half * 16,
                                    [[128, 128], [1, 16], [0, 128]]),
                            ALU.subtract),
                        waits=[(s_rb2[hb2], rb_n[(q, half)])])
                    if _hit:
                        dump("rbc_t1", lam_rbc[hb2][:, :])
                    p.v(lambda e, hb2=hb2: e.tensor_tensor(
                            lam_rbc[hb2][:, :], lam_rbc[hb2][:, :],
                            bass.AP(c["cmt"].ap().tensor, 0,
                                    [[128, 128], [0, 16], [1, 128]]),
                            ALU.subtract))
                    vn2 = p.ev.n
                    if _hit:
                        dump("rbc_t2", lam_rbc[hb2][:, :])
                        dump("lam_tm_all", lam_tm_all[:, :])
                        dump("lam_loc_all", lam_loc_all[:, :])
                    p.s(lambda e, hb2=hb2: e.activation(
                            lam_rbc[hb2][:, :], lam_rbc[hb2][:, :], AF.Exp),
                        waits=[(p.ev, vn2)])
                    an = p.ea.n
                    p.v(lambda e, hb2=hb2: e.tensor_tensor(
                            mt_half[:, :], lam_rbc[hb2][:, :],
                            bass.AP(G_ps.ap().tensor, 0,
                                    [[TT, 128], [0, 16], [1, 128]]),
                            ALU.mult),
                        waits=[(p.ea, an), (s_t, s_t.n)])
                    rbc_cons[hb2] = p.ev.n
                    mt_vn = p.ev.n
                    for grp in range(4):
                        gb2 = grp % 2
                        for hh in range(4):
                            h = half * 16 + grp * 4 + hh
                            hl = grp * 4 + hh
                            p.t(lambda e, h=h, hl=hl, gb2=gb2, hh=hh: e.matmul(
                                    A_ps[gb2][:, hh * 64:(hh + 1) * 64],
                                    mt_half[:, hl * 128:(hl + 1) * 128],
                                    xt_all[:, h * 64:(h + 1) * 64],
                                    start=True, stop=True),
                                waits=([(p.ev, mt_vn), (s_t, s_t.n)]
                                       if hh == 0 and grp < 2 else
                                       ([(p.ev, ev4.get((half, grp - 2), 0))]
                                        if hh == 0 else [])),
                                inc=None)
                            p.t(lambda e, h=h, gb2=gb2, hh=hh, q=q: e.matmul(
                                    B_ps[gb2][:, hh * 64:(hh + 1) * 64],
                                    bcb[:, TT + q * TC:TT + (q + 1) * TC],
                                    H_b[:, h * 64:(h + 1) * 64],
                                    start=True, stop=True),
                                inc=((s_t, 1) if hh == 3 else None))
                        tn = s_t.n
                        h0 = half * 16 + grp * 4
                        p.v(lambda e, h0=h0, gb2=gb2, q=q: e.tensor_tensor(
                                ytmp4[:, :], B_ps[gb2][:, 0:256],
                                bass.AP(expl_all.ap().tensor, q * 32 + h0 - (h0 // 32) * 32 if False else q * 32 + (h0 % 32),
                                        [[128, 128], [1, 4], [0, 64]]),
                                ALU.mult),
                            waits=[(s_t, tn)])
                        p.v(lambda e, h0=h0, gb2=gb2: e.tensor_add(
                                Y_tm[:, h0 * 64:(h0 + 4) * 64],
                                A_ps[gb2][:, 0:256], ytmp4[:, :]))
                        ev4[(half, grp)] = p.ev.n
                skip_add_gn = {}
                for part in range(4):
                    dview = bass.AP(d_bc.ap().tensor, part * 8,
                                    [[32, 128], [1, 8], [0, 64]])
                    p.v(lambda e, part=part, dview=dview: e.tensor_tensor(
                            stg_b[1][:, :TT] if part % 2 == 0 else stg_a[1][:, :TT],
                            X_tm[:, part * 512:part * 512 + 512],
                            dview, ALU.mult),
                        waits=[(p.eg, skip_add_gn.get(part - 2, p.eg.n if part < 2 else 0))])
                    vnp = p.ev.n
                    p.g(lambda e, part=part: e.tensor_add(
                            Y_tm[:, part * 512:part * 512 + 512],
                            Y_tm[:, part * 512:part * 512 + 512],
                            stg_b[1][:, :TT] if part % 2 == 0 else stg_a[1][:, :TT]),
                        waits=[(p.ev, vnp)])
                    skip_add_gn[part] = p.eg.n
                skip_gn = p.eg.n
                p.v(lambda e, q=q: e.tensor_tensor(
                        xt_all[:, :], X_tm[:, :],
                        bass.AP(wdt_all4.ap().tensor, q * 32,
                                [[128, 128], [1, 32], [0, 64]]),
                        ALU.mult),
                    waits=[(s_t, s_t.n)])
                xt_vn = p.ev.n
                for part in range(4):
                    pb = A_ps[part % 2]
                    p.t(lambda e, part=part, pb=pb: e.matmul(
                            pb[:, :], B_tm[:, :],
                            xt_all[:, part * 512:(part + 1) * 512],
                            start=True, stop=True),
                        waits=[(p.ev, xt_vn if part < 2 else p.ev.n)],
                        inc=(s_t, 1))
                    tn = s_t.n
                    dcv = bass.AP(decs_all.ap().tensor, q * 32 + part * 8,
                                  [[128, 128], [1, 8], [0, 64]])
                    p.v(lambda e, part=part, dcv=dcv: e.tensor_tensor(
                            t_scr[:, :], H_f[:, part * 512:(part + 1) * 512],
                            dcv, ALU.mult),
                        waits=[(s_t, tn)])
                    p.v(lambda e, part=part, pb=pb: e.tensor_add(
                            H_f[:, part * 512:(part + 1) * 512], t_scr[:, :],
                            pb[:, :]))
                p.v(lambda e: e.tensor_copy(H_b[:, :], H_f[:, :]))
                for r in range(2):
                    vn = p.ev.n
                    for j in range(8):
                        k = r * 8 + j
                        p.t(lambda e, k=k, j=j: e.transpose(
                                tp_bf[:, j * 128:(j + 1) * 128],
                                Y_tm[:, k * 128:(k + 1) * 128],
                                c["ident_bf16"][:, :]),
                            waits=([(p.ev, vn), (p.eg, skip_gn)] if j == 0 else []),
                            inc=((s_t, 1) if j == 7 else None))
                    tn = s_t.n
                    p.v(lambda e, r=r, q=q: e.tensor_copy(
                            y_fm.ap().rearrange("p (k t) -> p k t", k=NZC)
                                [:, r * 8:(r + 1) * 8, q * TC:(q + 1) * TC],
                            tp_bf.ap().rearrange("p (j t) -> p j t", j=8)[:, :, :]),
                        waits=[(s_t, tn)])
            gst_vn = p.ev.n
            for k in range(NZC):
                p.v(lambda e, k=k: e.tensor_mul(
                        y_fm[:, k * TT:(k + 1) * TT], y_fm[:, k * TT:(k + 1) * TT],
                        z_sil[:, k * TT:(k + 1) * TT]),
                    waits=([(p.ea, p.ea.n)] if k == 0 else []))
            for k in range(NZC):
                p.s(lambda e, k=k: e.activation(stg_b[k % 2][:, :],
                                                y_fm[:, k * TT:(k + 1) * TT],
                                                AF.Square),
                    waits=[(p.ev, p.ev.n), (s_t, s_t.n)] if k == 0 else
                          [(s_t, s_t.n)])
                an = p.ea.n
                p.t(lambda e, k=k: e.matmul(
                        pbS[6][0:1, :], c["ones_bf16_col"][:, :],
                        stg_b[k % 2][:, :],
                        start=(k == 0), stop=(k == NZC - 1)),
                    waits=[(p.ea, an)] + ([(p.ev, p.ev.n)] if k == 0 else []),
                    inc=(s_t, 1))
            tn = s_t.n
            p.v(lambda e: e.tensor_scalar(L["rowbuf"][0:1, LNR["mu"]],
                                          pbS[6][0:1, :], 1.0 / D_INNER, None,
                                          ALU.mult),
                waits=[(s_t, tn)])
            vn = p.ev.n
            p.s(lambda e: e.activation(L["rowbuf"][0:1, LNR["sd"]],
                                       L["rowbuf"][0:1, LNR["mu"]], AF.Sqrt,
                                       bias=c["eps_col"][0:1, :]),
                waits=[(p.ev, vn)])
            an = p.ea.n
            p.v(lambda e: e.reciprocal(L["rowbuf"][0:1, LNR["rs"]],
                                       L["rowbuf"][0:1, LNR["sd"]]),
                waits=[(p.ea, an)])
            vn = p.ev.n
            p.t(lambda e: e.matmul(pbS[4][:, :], c["ones_f32_row"][:, :],
                                   L["rowbuf"][0:1, LNR["rs"]],
                                   start=True, stop=True),
                waits=[(p.ev, vn)], inc=(s_t, 1))
            tn = s_t.n
            p.v(lambda e: e.tensor_copy(rn_bc[:, :], pbS[4][:, :]),
                waits=[(s_t, tn)])
            for k in range(NZC):
                p.v(lambda e, k=k: e.scalar_tensor_tensor(
                        y_fm[:, k * TT:(k + 1) * TT],
                        y_fm[:, k * TT:(k + 1) * TT], nw_cols[:, k:k + 1],
                        rn_bc[:, :], ALU.mult, ALU.mult))
            gfin_vn = p.ev.n
            if s == 0 and tt == 0:
                dump("y_fm", y_fm[:, :])
            wo_mm = {}
            pass_ready_vn = gfin_vn
            for pa in range(2):
                for hk in range(NZC):
                    b = hk % 2
                    p.y(lambda e, hk=hk, b=b: e.dma_start(
                            out=wA[b][:, :],
                            in_=scr_outT[hk, :, :]),
                        waits=[(s_t, wo_mm.get((pa, hk - 2),
                                wo_mm.get((pa - 1, NZC - 2 + hk), mm_cnt.get(NEC, 0))))],
                        inc=(s_wa[b], 16))
                    wan = s_wa[b].n
                    for dd in range(4):
                        dk = pa * 4 + dd
                        p.t(lambda e, hk=hk, b=b, dd=dd, dk=dk: e.matmul(
                                pbS[dd][:, :],
                                wA[b][:, dk * 128:(dk + 1) * 128],
                                y_fm[:, hk * TT:(hk + 1) * TT],
                                start=(hk == 0), stop=(hk == NZC - 1)),
                            waits=([(s_wa[b], wan), (p.ev, pass_ready_vn)]
                                   if dd == 0 and hk == 0 else
                                   ([(s_wa[b], wan)] if dd == 0 else [])),
                            inc=((s_t, 1) if dd == 3 else None))
                    wo_mm[(pa, hk)] = s_t.n
                tn = s_t.n
                for dd in range(4):
                    dk = pa * 4 + dd
                    p.v(lambda e, dd=dd, dk=dk: e.tensor_add(
                            xb1[:, dk * TT:(dk + 1) * TT], pbS[dd][:, :],
                            xe[:, dk * TT:(dk + 1) * TT]),
                        waits=([(s_t, tn), (s_xl, xln), (p.eg, p.eg.n),
                                (s_xs, s_xs.n)] if dd == 0 else []))
                pass_ready_vn = p.ev.n
            pre1_vn = p.ev.n
            st_mm_n = {}
            for dk in range(DCH):
                p.g(lambda e, dk=dk: e.tensor_copy(stg_a[dk % 2][:, :],
                                                   xb1[:, dk * TT:(dk + 1) * TT]),
                    waits=([(p.ev, pre1_vn)] if dk == 0 else []) +
                          [(s_t, st_mm_n.get(dk - 2, s_t.n if dk == 0 else 0))])
                p.g(lambda e, dk=dk: e.tensor_mul(stg_b[dk % 2][:, :],
                                                  xb1[:, dk * TT:(dk + 1) * TT],
                                                  xb1[:, dk * TT:(dk + 1) * TT]))
                gn = p.eg.n
                p.t(lambda e, dk=dk: e.matmul(
                        L["st1"][0:1, :], c["ones_bf16_col"][:, :],
                        stg_a[dk % 2][:, :],
                        start=(dk == 0), stop=(dk == DCH - 1)),
                    waits=[(p.eg, gn)] + ([(p.ev, p.ev.n)] if dk == 0 else []),
                    inc=(s_t, 1))
                p.t(lambda e, dk=dk: e.matmul(
                        L["st2"][0:1, :], c["ones_bf16_col"][:, :],
                        stg_b[dk % 2][:, :],
                        start=(dk == 0), stop=(dk == DCH - 1)),
                    inc=(s_t, 1))
                st_mm_n[dk] = s_t.n
            ln_rows_and_bc(p, c, L, D, [(s_t, s_t.n)])
            for dk in range(DCH):
                p.v(lambda e, dk=dk: e.tensor_mul(t_scr[:, :],
                                                  xb1[:, dk * TT:(dk + 1) * TT],
                                                  L["rsbc"][:, :]))
                p.v(lambda e, dk=dk: e.tensor_add(t_scr[:, :], t_scr[:, :],
                                                  L["nmrbc"][:, :]))
                lnbv = bass.AP(lnb_cols.ap().tensor, dk, [[DCH, 128], [0, TT]])
                p.v(lambda e, dk=dk, lnbv=lnbv: e.scalar_tensor_tensor(
                        t_scr[:, :], t_scr[:, :], lnw_cols[:, dk:dk + 1],
                        lnbv, ALU.mult, ALU.add))
                p.v(lambda e, dk=dk: e.tensor_add(
                        xb1[:, dk * TT:(dk + 1) * TT], t_scr[:, :],
                        xe[:, dk * TT:(dk + 1) * TT]))
            pre2_vn = p.ev.n
            emit_boundary(p, c, L, xb1, stg_a, stg_b, t_scr, t_scr_g, Xout, s,
                          tsl, [(p.ev, pre2_vn)], s_xs)
    return [(s_xs, s_xs.n)]



def build_attn(p, c, Xin, Xout, Wqk, Wv, Wo, tau, pre_waits=()):
    s_wA = p.sem("at_wA")
    s_wv = p.sem("at_wv")
    s_xl = p.sem("at_xl")
    s_xs = p.sem("at_xs")
    s_t = p.sem("at_t")
    wA = p.sbuf("at_wA", [128, DCH * 1024], BF16)
    wv = p.sbuf("at_wv", [128, DCH * 1024], BF16)
    xw = p.sbuf("at_xw", [128, DCH * T], BF16)
    qk_tm = p.sbuf("at_qk", [128, (T // 128) * 1024], BF16)
    v_fm = p.sbuf("at_v", [128, DCH * T], BF16)
    xe = p.sbuf("at_xe", [128, DCH * TT], F32)
    stg_q = [p.sbuf(f"at_sq{i}", [128, 1024], BF16) for i in range(2)]
    P_all = p.sbuf("at_P", [128, 512], BF16)
    lg_sb = p.sbuf("at_lg", [128, 512], F32)
    q2row = p.sbuf("at_q2r", [1, 1024], F32)
    q2rbc = p.sbuf("at_q2rbc", [128, 1024], F32)
    q2cols = p.sbuf("at_q2c", [128, 8], F32)
    tauS = p.sbuf("at_tau", [128, 16], F32)
    tau_row = p.sbuf("at_taur", [1, 16], F32)
    gnm_row = p.sbuf("at_gnm", [1, 16], F32)
    mxc = p.sbuf("at_mxc", [128, 1], F32)
    ngm_cols = p.sbuf("at_ngmc", [128, 16], F32)
    s_colsb = p.sbuf("at_scols", [128, 8], F32)
    r_cols = p.sbuf("at_rcols", [128, 8], F32)
    stg_a = [p.sbuf(f"at_sa{i}", [128, TT], BF16) for i in range(2)]
    stg_b = [p.sbuf(f"at_sb{i}", [128, TT], BF16) for i in range(2)]
    t_scr = p.sbuf("at_t0", [128, TT], F32)
    t_scr_g = p.sbuf("at_t0g", [128, TT], F32)
    pbA = [p.psum(f"at_pb{i}", [128, TT], F32) for i in range(8)]
    L = make_lnctx(p, c, "at", pbA[2], pbA[3], pbA[4], pbA[5], s_t)

    p.y(lambda e: e.dma_start(out=tau_row[:, :],
                              in_=tau.rearrange("a b c -> b (a c)")),
        waits=list(pre_waits), inc=(s_xl, 16))
    xln0 = s_xl.n
    p.v(lambda e: e.tensor_scalar(tau_row[:, :], tau_row[:, :],
                                  1.0 / math.sqrt(float(T)), None, ALU.mult),
        waits=[(s_xl, xln0)])
    vn = p.ev.n
    p.t(lambda e: e.matmul(pbA[6][:, 0:16], c["ones_f32_row"][:, :],
                           tau_row[0:1, :], start=True, stop=True),
        waits=c["waits"] + [(p.ev, vn)], inc=(s_t, 1))
    p.v(lambda e: e.tensor_copy(tauS[:, :], pbA[6][:, 0:16]),
        waits=[(s_t, s_t.n)])

    for s in range(NSEQ):
        wA_gen = s * 2
        for dk in range(DCH):
            p.gdma(lambda e, dk=dk: e.dma_start(
                    out=wA[:, dk * 1024:(dk + 1) * 1024],
                    in_=Wqk[dk * 128:(dk + 1) * 128, :]),
                waits=([(s_t, s_t.n)] if dk == 0 else []), inc=(s_wA, 16))
        wqk_n = s_wA.n
        for dk in range(DCH):
            p.gdma(lambda e, dk=dk: e.dma_start(
                    out=wv[:, dk * 1024:(dk + 1) * 1024],
                    in_=Wv[dk * 128:(dk + 1) * 128, :]),
                waits=([(s_t, s_t.n)] if dk == 0 else []), inc=(s_wv, 16))
        wv_n = s_wv.n
        for tt in range(NTT):
            for dk in range(DCH):
                p.y(lambda e, s=s, dk=dk, tt=tt: e.dma_start(
                        out=xe[:, dk * TT:(dk + 1) * TT],
                        in_=Xin[s, dk, :, tt * TT:(tt + 1) * TT]),
                    waits=([(s_xs, s_xs.n), (p.ev, p.ev.n), (p.eg, p.eg.n),
                            (s_t, s_t.n)] if dk == 0 else []),
                    inc=(s_xl, 16))
            xln = s_xl.n
            for dk in range(DCH):
                p.v(lambda e, dk=dk, tt=tt: e.tensor_copy(
                        xw[:, dk * T + tt * TT:dk * T + (tt + 1) * TT],
                        xe[:, dk * TT:(dk + 1) * TT]),
                    waits=([(s_xl, xln), (s_t, s_t.n)] if dk == 0 else []))
        cast_vn = p.ev.n
        evac_an = {}
        for i in range(T // 128):
            for eh in range(2):
                k = 2 * i + eh
                for dk in range(DCH):
                    p.t(lambda e, i=i, eh=eh, dk=dk, k=k: e.matmul(
                            pbA[k % 2][:, :],
                            xw[:, dk * T + i * 128:dk * T + (i + 1) * 128],
                            wA[:, dk * 1024 + eh * 512:dk * 1024 + (eh + 1) * 512],
                            start=(dk == 0), stop=(dk == DCH - 1)),
                        waits=([(s_wA, wqk_n), (p.ev, cast_vn),
                                (p.ea, evac_an.get(k - 2, 0))]
                               if dk == 0 else []),
                        inc=((s_t, 1) if dk == DCH - 1 else None))
                tn = s_t.n
                p.s(lambda e, i=i, eh=eh, k=k: e.activation(
                        qk_tm[:, i * 1024 + eh * 512:i * 1024 + (eh + 1) * 512],
                        pbA[k % 2][:, :], AF.Copy),
                    waits=[(s_t, tn)])
                evac_an[k] = p.ea.n
        qk_an = p.ea.n
        qk_tn = s_t.n
        for dk in range(DCH):
            p.gdma(lambda e, dk=dk: e.dma_start(
                    out=wA[:, dk * 1024:(dk + 1) * 1024],
                    in_=Wo[dk * 128:(dk + 1) * 128, :]),
                waits=([(s_t, qk_tn)] if dk == 0 else []), inc=(s_wA, 16))
        wo_n = s_wA.n
        evac_cn = {-2: qk_an, -1: qk_an}
        for tt in range(NTT):
            for ek in range(DCH):
                k = tt * DCH + ek
                for dk in range(DCH):
                    p.t(lambda e, tt=tt, ek=ek, dk=dk, k=k: e.matmul(
                            pbA[k % 2][:, :],
                            wv[:, dk * 1024 + ek * 128:dk * 1024 + (ek + 1) * 128],
                            xw[:, dk * T + tt * TT:dk * T + (tt + 1) * TT],
                            start=(dk == 0), stop=(dk == DCH - 1)),
                        waits=([(s_wv, wv_n), (p.ea, evac_cn[k - 2])]
                               if dk == 0 else []),
                        inc=((s_t, 1) if dk == DCH - 1 else None))
                tn = s_t.n
                p.s(lambda e, tt=tt, ek=ek, k=k: e.activation(
                        v_fm[:, ek * T + tt * TT:ek * T + (tt + 1) * TT],
                        pbA[k % 2][:, :], AF.Copy),
                    waits=[(s_t, tn)])
                evac_cn[k] = p.ea.n
        v_an = p.ea.n
        for i in range(T // 128):
            p.v(lambda e, i=i: e.tensor_mul(stg_q[i % 2][:, :],
                                            qk_tm[:, i * 1024:(i + 1) * 1024],
                                            qk_tm[:, i * 1024:(i + 1) * 1024]),
                waits=[(s_t, s_t.n), (p.ea, p.ea.n)])
            an = p.ev.n
            p.t(lambda e, i=i: e.matmul(
                    pbA[2][0:1, :], c["ones_bf16_col"][:, :],
                    stg_q[i % 2][:, 0:512],
                    start=(i == 0), stop=(i == T // 128 - 1)),
                waits=[(p.ev, an)],
                inc=(s_t, 1))
            p.t(lambda e, i=i: e.matmul(
                    pbA[3][0:1, :], c["ones_bf16_col"][:, :],
                    stg_q[i % 2][:, 512:1024],
                    start=(i == 0), stop=(i == T // 128 - 1)),
                inc=(s_t, 1))
        tn = s_t.n
        p.v(lambda e: e.tensor_copy(q2row[0:1, 0:512], pbA[2][0:1, :]),
            waits=[(s_t, tn)])
        p.v(lambda e: e.tensor_copy(q2row[0:1, 512:1024], pbA[3][0:1, :]))
        vn = p.ev.n
        for hh in range(2):
            p.t(lambda e, hh=hh: e.matmul(pbA[6][:, :], c["ones_f32_row"][:, :],
                                          q2row[0:1, hh * 512:(hh + 1) * 512],
                                          start=True, stop=True),
                waits=[(p.ev, p.ev.n)], inc=(s_t, 1))
            tn = s_t.n
            p.v(lambda e, hh=hh: e.tensor_copy(q2rbc[:, hh * 512:(hh + 1) * 512],
                                               pbA[6][:, :]),
                waits=[(s_t, tn)])
        for j in range(DCH):
            p.t(lambda e, j=j: e.transpose(pbA[7][0:128, j:j + 1],
                                           q2row[0:1, j * 128:(j + 1) * 128],
                                           c["ident_f32"][:1, :1]),
                waits=([(p.ev, p.ev.n)] if j == 0 else []), inc=(s_t, 1))
        tn = s_t.n
        p.v(lambda e: e.tensor_copy(q2cols[:, :], pbA[7][0:128, 0:8]),
            waits=[(s_t, tn)])
        for h in range(16):
            r0 = 64 * (h % 2)
            cp = h // 2
            for i in range(T // 128):
                p.t(lambda e, h=h, i=i, r0=r0, cp=cp: e.matmul(
                        pbA[2][r0:r0 + 64, cp * 64:(cp + 1) * 64],
                        qk_tm[:, i * 1024 + h * 64:i * 1024 + (h + 1) * 64],
                        qk_tm[:, i * 1024 + h * 64:i * 1024 + (h + 1) * 64],
                        start=(i == 0), stop=(i == T // 128 - 1)),
                    waits=([(p.ev, p.ev.n), (p.ea, v_an)]
                           if h == 0 and i == 0 else []),
                    inc=((s_t, 1) if i == T // 128 - 1 else None))
        dot_tn = s_t.n
        for h in range(16):
            r0 = 64 * (h % 2)
            cp = h // 2
            sl = (slice(r0, r0 + 64), slice(cp * 64, (cp + 1) * 64))
            p.v(lambda e, sl=sl: e.tensor_scalar(
                    lg_sb[sl[0], sl[1]], pbA[2][sl[0], sl[1]],
                    2.0, None, ALU.mult),
                waits=[(s_t, dot_tn)])
            p.v(lambda e, sl=sl, r0=r0, cp=cp: e.tensor_scalar(
                    lg_sb[sl[0], sl[1]], lg_sb[sl[0], sl[1]],
                    q2cols[r0:r0 + 64, cp:cp + 1], None, ALU.subtract))
            p.v(lambda e, sl=sl, h=h, r0=r0: e.tensor_sub(
                    lg_sb[sl[0], sl[1]], lg_sb[sl[0], sl[1]],
                    q2rbc[r0:r0 + 64, h * 64:(h + 1) * 64]))
            p.v(lambda e, sl=sl, r0=r0, h=h: e.tensor_scalar(
                    lg_sb[sl[0], sl[1]], lg_sb[sl[0], sl[1]],
                    tauS[r0:r0 + 64, h:h + 1], None, ALU.mult))
            p.v(lambda e, sl=sl, r0=r0: e.reduce_max(
                    mxc[r0:r0 + 64, 0:1], lg_sb[sl[0], sl[1]], AX.X))
            vn = p.ev.n
            p.t(lambda e, r0=r0: e.transpose(pbA[6][0:1, 0:64],
                                             mxc[r0:r0 + 64, 0:1],
                                             c["ident_f32"][r0:r0 + 64, r0:r0 + 64]),
                waits=[(p.ev, vn)], inc=(s_t, 1))
            tn = s_t.n
            p.v(lambda e, h=h: e.reduce_max(gnm_row[0:1, h:h + 1],
                                            pbA[6][0:1, 0:64], AX.X, negate=True),
                waits=[(s_t, tn)])
            vn = p.ev.n
            p.t(lambda e, h=h: e.matmul(
                    pbA[7][:, 0:1], c["ones_f32_row"][:, :],
                    gnm_row[0:1, h:h + 1], start=True, stop=True),
                waits=[(p.ev, vn)], inc=(s_t, 1))
            tn = s_t.n
            p.v(lambda e, h=h: e.tensor_copy(
                    ngm_cols[:, h:h + 1], pbA[7][:, 0:1]),
                waits=[(s_t, tn)])
            vn = p.ev.n
            p.s(lambda e, sl=sl, r0=r0, h=h: e.activation(
                    P_all[sl[0], sl[1]], lg_sb[sl[0], sl[1]], AF.Exp,
                    bias=ngm_cols[r0:r0 + 64, h:h + 1]),
                waits=[(p.ev, vn)])
            an = p.ea.n
            p.v(lambda e, sl=sl, r0=r0, cp=cp: e.reduce_sum(
                    s_colsb[r0:r0 + 64, cp:cp + 1], P_all[sl[0], sl[1]], AX.X),
                waits=[(p.ea, an)])
            p.v(lambda e, r0=r0, cp=cp: e.reciprocal(
                    r_cols[r0:r0 + 64, cp:cp + 1], s_colsb[r0:r0 + 64, cp:cp + 1]))
        soft_vn = p.ev.n
        soft_an = p.ea.n
        for tt in range(NTT):
            for pr in range(8):
                b = (tt * 8 + pr) % 2
                for hh in range(2):
                    h = 2 * pr + hh
                    r0 = 64 * hh
                    p.t(lambda e, tt=tt, pr=pr, hh=hh, h=h, r0=r0, b=b: e.matmul(
                            pbA[5 + b][r0:r0 + 64, :],
                            P_all[r0:r0 + 64, pr * 64:(pr + 1) * 64],
                            v_fm[r0:r0 + 64, pr * T + tt * TT:pr * T + (tt + 1) * TT],
                            start=True, stop=True),
                        waits=([(p.ea, soft_an), (p.ev, soft_vn)]
                               if tt == 0 and pr == 0 and hh == 0 else
                               ([(p.ev, p.ev.n)] if hh == 0 else [])),
                        inc=((s_t, 1) if hh == 1 else None))
                tn = s_t.n
                p.v(lambda e, tt=tt, pr=pr, b=b: e.tensor_scalar(
                        xw[:, pr * T + tt * TT:pr * T + (tt + 1) * TT],
                        pbA[5 + b][:, :], r_cols[:, pr:pr + 1], None, ALU.mult),
                    waits=[(s_t, tn)])
        w_vn = p.ev.n
        for tt in range(NTT):
            tsl = slice(tt * TT, (tt + 1) * TT)
            for dk in range(DCH):
                p.y(lambda e, s=s, dk=dk, tsl=tsl: e.dma_start(
                        out=xe[:, dk * TT:(dk + 1) * TT], in_=Xin[s, dk, :, tsl]),
                    waits=([(s_xs, s_xs.n), (p.ev, p.ev.n), (p.eg, p.eg.n)]
                           if dk == 0 else []),
                    inc=(s_xl, 16))
            xln = s_xl.n
            add_vn = {}
            for ek in range(DCH):
                b = ek % 2
                for dk in range(DCH):
                    p.t(lambda e, tt=tt, ek=ek, dk=dk, b=b: e.matmul(
                            pbA[b][:, :],
                            wA[:, dk * 1024 + ek * 128:dk * 1024 + (ek + 1) * 128],
                            xw[:, dk * T + tt * TT:dk * T + (tt + 1) * TT],
                            start=(dk == 0), stop=(dk == DCH - 1)),
                        waits=(([(s_wA, wo_n)] if ek == 0 else []) +
                               [(p.ev, (w_vn if tt == 0 else p.ev.n) if ek < 2
                                 else add_vn[ek - 2])]
                               if dk == 0 else []),
                        inc=((s_t, 1) if dk == DCH - 1 else None))
                tn = s_t.n
                p.v(lambda e, ek=ek, b=b: e.tensor_add(
                        xe[:, ek * TT:(ek + 1) * TT], pbA[b][:, :],
                        xe[:, ek * TT:(ek + 1) * TT]),
                    waits=[(s_t, tn), (s_xl, xln)])
                add_vn[ek] = p.ev.n
            epi_vn = p.ev.n
            emit_boundary(p, c, L, xe, stg_a, stg_b, t_scr, t_scr_g, Xout, s,
                          tsl, [(p.ev, epi_vn)], s_xs)
    return [(s_xs, s_xs.n)]



def build_moe(p, c, Xin, Xout, gate_w, gate_b, rms_w, W1, B1, W2, B2, Yacc,
              pre_waits=()):
    NHK2 = H_MOE // 128
    s_w1 = p.sem("mo_w1")
    s_w2 = p.sem("mo_w2")
    s_xl = p.sem("mo_xl")
    s_xs = p.sem("mo_xs")
    s_t = p.sem("mo_t")
    s_acc = [p.sem(f"mo_acc{i}") for i in range(NTT)]
    s_sm = p.sem("mo_sm")
    xb = p.sbuf("mo_xb", [128, DCH * T], BF16)
    w1b = p.sbuf("mo_w1b", [128, DCH * H_MOE], BF16)
    w2b = p.sbuf("mo_w2b", [128, NHK2 * D], BF16)
    h_e = p.sbuf("mo_h", [128, NHK2 * TT], BF16)
    yt = [p.sbuf(f"mo_yt{i}", [128, DCH * TT], F32) for i in range(2)]
    xe = p.sbuf("mo_xe", [128, DCH * TT], F32)
    ngb = p.sbuf("mo_ngb", [128, NTT * TT], F32)
    gwT = p.sbuf("mo_gwT", [128, DCH * N_EXP], BF16)
    gb_col = p.sbuf("mo_gb", [N_EXP, 1], F32)
    rms2bc = p.sbuf("mo_rms2", [128, N_EXP], F32)
    b1cols = p.sbuf("mo_b1c", [128, NHK2 * N_EXP], F32)
    b2s = p.sbuf("mo_b2s", [128, DCH], F32)
    smv = p.sbuf("mo_smv", [N_EXP, H_MOE], F32)
    lgsb = p.sbuf("mo_lgsb", [N_EXP, TT], F32)
    ltm = p.sbuf("mo_ltm", [128, 32], F32)
    grow = p.sbuf("mo_grow", [1, 8], F32)
    gcolsb = p.sbuf("mo_gcol", [128, 16], F32)
    ngrow = p.sbuf("mo_ngrow", [1, TT], F32)
    stg_a = [p.sbuf(f"mo_sa{i}", [128, TT], BF16) for i in range(2)]
    stg_b = [p.sbuf(f"mo_sb{i}", [128, TT], BF16) for i in range(2)]
    t_scr = p.sbuf("mo_t0", [128, TT], F32)
    t_scr_g = p.sbuf("mo_t0g", [128, TT], F32)
    pbM = [p.psum(f"mo_pb{i}", [128, TT], F32) for i in range(8)]
    L = make_lnctx(p, c, "mo", pbM[4], pbM[5], pbM[6], pbM[7], s_t)

    p.y(lambda e: e.dma_start(out=smv[:N_EXP, :D], in_=gate_w[:, :]),
        waits=list(pre_waits), inc=(s_sm, 16))
    smn = s_sm.n
    for dk in range(DCH):
        p.t(lambda e, dk=dk: e.transpose(
                pbM[0][:, dk * 8:(dk + 1) * 8],
                smv[:N_EXP, dk * 128:(dk + 1) * 128], c["ident_f32"][:8, :8]),
            waits=(c["waits"] + [(s_sm, smn)] if dk == 0 else []),
            inc=((s_t, 1) if dk == DCH - 1 else None))
    p.v(lambda e: e.tensor_copy(gwT[:, :], pbM[0][:, 0:DCH * N_EXP]),
        waits=[(s_t, s_t.n)])
    p.y(lambda e: e.dma_start(out=smv[0:1, 0:8],
                              in_=gate_b.rearrange("(a b) -> a b", a=1)),
        waits=[(p.ev, p.ev.n)], inc=(s_sm, 16))
    smn = s_sm.n
    p.t(lambda e: e.transpose(pbM[1][0:8, 0:1], smv[0:1, 0:8],
                              c["ident_f32"][:1, :1]),
        waits=[(s_sm, smn)], inc=(s_t, 1))
    p.v(lambda e: e.tensor_copy(gb_col[:, :], pbM[1][0:8, 0:1]),
        waits=[(s_t, s_t.n)])
    p.y(lambda e: e.dma_start(out=smv[0:1, 8:16],
                              in_=rms_w.rearrange("(a b) -> a b", a=1)),
        waits=[(p.ev, p.ev.n)], inc=(s_sm, 16))
    smn = s_sm.n
    p.t(lambda e: e.matmul(pbM[2][:, 0:8], c["ones_f32_row"][:, :],
                           smv[0:1, 8:16], start=True, stop=True),
        waits=[(s_sm, smn)], inc=(s_t, 1))
    p.v(lambda e: e.tensor_scalar(rms2bc[:, :], pbM[2][:, 0:8], 2.0, None,
                                  ALU.mult),
        waits=[(s_t, s_t.n)])
    p.y(lambda e: e.dma_start(out=smv[:N_EXP, :H_MOE], in_=B1[:, :]),
        waits=[(s_t, s_t.n), (p.ev, p.ev.n)], inc=(s_sm, 16))
    smn = s_sm.n
    for hk in range(NHK2):
        p.t(lambda e, hk=hk: e.transpose(
                pbM[3][:, hk * 8:(hk + 1) * 8],
                smv[:N_EXP, hk * 128:(hk + 1) * 128], c["ident_f32"][:8, :8]),
            waits=([(s_sm, smn)] if hk == 0 else []),
            inc=((s_t, 1) if hk == NHK2 - 1 else None))
    p.v(lambda e: e.tensor_copy(b1cols[:, :], pbM[3][:, 0:NHK2 * N_EXP]),
        waits=[(s_t, s_t.n)])
    p.y(lambda e: e.dma_start(out=smv[:N_EXP, :D], in_=B2[:, :]),
        waits=[(s_t, s_t.n), (p.ev, p.ev.n)], inc=(s_sm, 16))
    smn = s_sm.n
    for dk in range(DCH):
        p.t(lambda e, dk=dk: e.transpose(
                pbM[0][:, 64 + dk * 8:64 + (dk + 1) * 8],
                smv[:N_EXP, dk * 128:(dk + 1) * 128], c["ident_f32"][:8, :8]),
            waits=([(s_sm, smn), (p.ev, p.ev.n)] if dk == 0 else []),
            inc=((s_t, 1) if dk == DCH - 1 else None))
    p.v(lambda e: e.reduce_sum(
            b2s.ap().rearrange("p (a b) -> p a b", a=DCH)[:, :, :],
            pbM[0].ap().rearrange("p (cc a b) -> p cc a b", cc=8, a=DCH)[:, 1, :, :],
            AX.X),
        waits=[(s_t, s_t.n)])

    for s in range(NSEQ):
        seq_start_tn = s_t.n
        for tt in range(NTT):
            tsl = slice(tt * TT, (tt + 1) * TT)
            for dk in range(DCH):
                p.y(lambda e, s=s, dk=dk, tsl=tsl: e.dma_start(
                        out=xe[:, dk * TT:(dk + 1) * TT], in_=Xin[s, dk, :, tsl]),
                    waits=([(s_xs, s_xs.n), (p.ev, p.ev.n), (p.eg, p.eg.n),
                            (s_t, s_t.n)] if dk == 0 else []),
                    inc=(s_xl, 16))
            xln = s_xl.n
            for dk in range(DCH):
                p.v(lambda e, dk=dk, tt=tt: e.tensor_copy(
                        xb[:, dk * T + tt * TT:dk * T + (tt + 1) * TT],
                        xe[:, dk * TT:(dk + 1) * TT]),
                    waits=([(s_xl, xln), (s_t, s_t.n)] if dk == 0 else []))
            cast_vn = p.ev.n
            for dk in range(DCH):
                p.t(lambda e, dk=dk, tt=tt: e.matmul(
                        pbM[2][0:N_EXP, :], gwT[:, dk * 8:(dk + 1) * 8],
                        xb[:, dk * T + tt * TT:dk * T + (tt + 1) * TT],
                        start=(dk == 0), stop=(dk == DCH - 1)),
                    waits=([(p.ev, cast_vn)] if dk == 0 else []),
                    inc=((s_t, 1) if dk == DCH - 1 else None))
            tn = s_t.n
            p.v(lambda e: e.tensor_scalar(lgsb[:, :], pbM[2][0:N_EXP, :],
                                          gb_col[:, 0:1], None, ALU.add),
                waits=[(s_t, tn)])
            vn = p.ev.n
            for q in range(4):
                p.t(lambda e, q=q: e.transpose(
                        pbM[3][:, q * 8:(q + 1) * 8],
                        lgsb[:N_EXP, q * 128:(q + 1) * 128],
                        c["ident_f32"][:8, :8]),
                    waits=([(p.ev, vn)] if q == 0 else []),
                    inc=((s_t, 1) if q == 3 else None))
            p.v(lambda e: e.tensor_copy(ltm[:, :], pbM[3][:, 0:32]),
                waits=[(s_t, s_t.n)])
            p.v(lambda e: e.tensor_mul(t_scr[:, 0:32], ltm[:, :], ltm[:, :]))
            p.v(lambda e: e.reduce_sum(
                    gcolsb[:, 0:4],
                    t_scr.ap().rearrange("p (cc a b) -> p cc a b", cc=16, a=4)[:, 0, :, :],
                    AX.X))
            p.s(lambda e: e.activation(gcolsb[:, 4:8], gcolsb[:, 0:4], AF.Sqrt,
                                       scale=1.0 / 8.0, bias=c["epsf_col"][:, 0:1]),
                waits=[(p.ev, p.ev.n)])
            p.v(lambda e: e.reciprocal(gcolsb[:, 8:12], gcolsb[:, 4:8]),
                waits=[(p.ea, p.ea.n)])
            for q in range(4):
                p.v(lambda e, q=q: e.tensor_scalar(
                        ltm[:, q * 8:(q + 1) * 8], ltm[:, q * 8:(q + 1) * 8],
                        gcolsb[:, 8 + q:9 + q], None, ALU.mult))
                p.v(lambda e, q=q: e.tensor_mul(
                        ltm[:, q * 8:(q + 1) * 8], ltm[:, q * 8:(q + 1) * 8],
                        rms2bc[:, :]))
            p.v(lambda e: e.reduce_max(
                    gcolsb.ap().rearrange("p (a b) -> p a b", a=4)[:, 0:4, 0:1],
                    ltm.ap().rearrange("p (a b) -> p a b", a=4)[:, :, :],
                    AX.X, negate=True))
            vn = p.ev.n
            for q in range(4):
                p.s(lambda e, q=q: e.activation(
                        ltm[:, q * 8:(q + 1) * 8], ltm[:, q * 8:(q + 1) * 8],
                        AF.Exp, bias=gcolsb[:, q:q + 1]),
                    waits=([(p.ev, vn)] if q == 0 else []))
            an = p.ea.n
            p.v(lambda e: e.reduce_sum(
                    gcolsb[:, 4:8],
                    ltm.ap().rearrange("p (a b) -> p a b", a=4)[:, :, :], AX.X),
                waits=[(p.ea, an)])
            p.v(lambda e: e.reduce_max(
                    gcolsb[:, 8:12],
                    ltm.ap().rearrange("p (a b) -> p a b", a=4)[:, :, :], AX.X))
            p.v(lambda e: e.reciprocal(gcolsb[:, 12:16], gcolsb[:, 4:8]))
            p.v(lambda e: e.tensor_mul(gcolsb[:, 8:12], gcolsb[:, 8:12],
                                       gcolsb[:, 12:16]))
            p.v(lambda e: e.tensor_scalar(gcolsb[:, 12:16], gcolsb[:, 8:12],
                                          1e-6, None, ALU.add))
            p.v(lambda e: e.reciprocal(gcolsb[:, 12:16], gcolsb[:, 12:16]))
            p.v(lambda e: e.tensor_mul(gcolsb[:, 8:12], gcolsb[:, 8:12],
                                       gcolsb[:, 12:16]))
            vn = p.ev.n
            for q in range(4):
                p.t(lambda e, q=q: e.transpose(
                        pbM[3][0:1, q * 128:(q + 1) * 128],
                        gcolsb[:, 8 + q:9 + q], c["ident_f32"][:, :]),
                    waits=([(p.ev, vn)] if q == 0 else []),
                    inc=((s_t, 1) if q == 3 else None))
            p.v(lambda e: e.tensor_copy(ngrow[0:1, :], pbM[3][0:1, 0:TT]),
                waits=[(s_t, s_t.n)])
            vn = p.ev.n
            p.t(lambda e: e.matmul(pbM[3][:, :], c["ones_f32_row"][:, :],
                                   ngrow[0:1, :], start=True, stop=True),
                waits=[(p.ev, vn)], inc=(s_t, 1))
            p.v(lambda e, tt=tt: e.tensor_copy(ngb[:, tt * TT:(tt + 1) * TT],
                                               pbM[3][:, :]),
                waits=[(s_t, s_t.n)])
        gate_done_vn = p.ev.n
        gate_done_tn = s_t.n
        gelu_an = {}
        for ex in range(N_EXP):
            w1h = []
            for hf in range(2):
                p.gdma(lambda e, ex=ex, hf=hf: e.dma_start(
                        out=w1b.ap().rearrange("p (a h) -> p a h", a=DCH)[:, hf * 4:(hf + 1) * 4, :],
                        in_=W1[ex, :, :].rearrange("(a pp) h -> pp a h", pp=128)[:, hf * 4:(hf + 1) * 4, :]),
                    waits=([(s_t, seq_start_tn if ex == 0 else s_t.n)]
                           if hf == 0 else []),
                    inc=(s_w1, 16))
                w1h.append(s_w1.n)
            w1n = s_w1.n
            w2h = []
            for hf in range(2):
                p.gdma(lambda e, ex=ex, hf=hf: e.dma_start(
                        out=w2b.ap().rearrange("p (a dd) -> p a dd", a=NHK2)[:, hf * 8:(hf + 1) * 8, :],
                        in_=W2[ex, :, :].rearrange("(a pp) dd -> pp a dd", pp=128)[:, hf * 8:(hf + 1) * 8, :]),
                    waits=([(s_t, s_t.n)] if hf == 0 else []),
                    inc=(s_w2, 16))
                w2h.append(s_w2.n)
            w2n = s_w2.n
            for tt in range(NTT):
                for hk in range(NHK2):
                    for dk in range(DCH):
                        p.t(lambda e, hk=hk, dk=dk, tt=tt: e.matmul(
                                pbM[hk % 2][:, :],
                                w1b[:, dk * H_MOE + hk * 128:dk * H_MOE + (hk + 1) * 128],
                                xb[:, dk * T + tt * TT:dk * T + (tt + 1) * TT],
                                start=(dk == 0), stop=(dk == DCH - 1)),
                            waits=(([(s_w1, w1h[0]), (p.ev, gate_done_vn),
                                     (p.ea, gelu_an.get(hk - 2,
                                           gelu_an.get(NHK2 + hk - 2, 0)))]
                                    if dk == 0 else
                                    ([(s_w1, w1h[1])] if dk == 4 else []))
                                   if tt == 0 else
                                   ([(p.ea, gelu_an.get(hk - 2,
                                         gelu_an.get(NHK2 + hk - 2, 0)))]
                                    if dk == 0 else [])),
                            inc=((s_t, 1) if dk == DCH - 1 else None))
                    tn = s_t.n
                    p.s(lambda e, hk=hk, ex=ex: e.activation(
                            h_e[:, hk * TT:(hk + 1) * TT], pbM[hk % 2][:, :],
                            AF.Gelu, bias=b1cols[:, hk * 8 + ex:hk * 8 + ex + 1]),
                        waits=[(s_t, tn)])
                    gelu_an[hk] = p.ea.n
                all_g = p.ea.n
                for dk in range(DCH):
                    for hk in range(NHK2):
                        p.t(lambda e, dk=dk, hk=hk: e.matmul(
                                pbM[2 + dk % 2][:, :],
                                w2b[:, hk * D + dk * 128:hk * D + (dk + 1) * 128],
                                h_e[:, hk * TT:(hk + 1) * TT],
                                start=(hk == 0), stop=(hk == NHK2 - 1)),
                            waits=(([(s_w2, w2h[0]), (p.ea, all_g), (p.ev, p.ev.n)]
                                    if hk == 0 else
                                    ([(s_w2, w2h[1])] if hk == 8 and tt == 0 else []))
                                   if dk == 0 else
                                   ([(p.ev, p.ev.n)] if hk == 0 else [])),
                            inc=((s_t, 1) if hk == NHK2 - 1 else None))
                    tn = s_t.n
                    p.v(lambda e, dk=dk, tt=tt: e.tensor_copy(
                            yt[tt % 2][:, dk * TT:(dk + 1) * TT],
                            pbM[2 + dk % 2][:, :]),
                        waits=[(s_t, tn),
                               (s_acc[tt], 16 * (N_EXP * s + ex))] +
                              (([(s_acc[tt - 2], 16 * (N_EXP * s + ex + 1))]
                                if tt >= 2 else
                                [(s_acc[tt + 2], 16 * (N_EXP * s + ex))])
                               if dk == 0 else []))
                ev_y = p.ev.n
                p.gdma(lambda e, s=s, tt=tt, ex=ex: e.dma_start(
                        out=Yacc[s, tt, :, :], in_=yt[tt % 2][:, :],
                        accum_op=(ALU.bypass if ex == 0 else ALU.add),
                        max_dma_last_dim=2048),
                    waits=[(p.ev, ev_y)], inc=(s_acc[tt], 16))
        for tt in range(NTT):
            tsl = slice(tt * TT, (tt + 1) * TT)
            p.y(lambda e, s=s, tt=tt: e.dma_start(
                    out=yt[tt % 2][:, :], in_=Yacc[s, tt, :, :]),
                waits=[(s_acc[tt], 16 * N_EXP * (s + 1)), (p.ev, p.ev.n)] +
                      ([(s_acc[tt + 2], 16 * N_EXP * (s + 1))] if tt < 2 else []),
                inc=(s_xl, 16))
            for dk in range(DCH):
                p.y(lambda e, s=s, dk=dk, tsl=tsl: e.dma_start(
                        out=xe[:, dk * TT:(dk + 1) * TT], in_=Xin[s, dk, :, tsl]),
                    waits=([(s_xs, s_xs.n), (p.eg, p.eg.n), (p.ev, p.ev.n)]
                           if dk == 0 else []),
                    inc=(s_xl, 16))
            xln = s_xl.n
            for dk in range(DCH):
                p.v(lambda e, dk=dk, tt=tt: e.scalar_tensor_tensor(
                        t_scr[:, :], yt[tt % 2][:, dk * TT:(dk + 1) * TT],
                        b2s[:, dk:dk + 1], ngb[:, tt * TT:(tt + 1) * TT],
                        ALU.add, ALU.mult),
                    waits=([(s_xl, xln)] if dk == 0 else []))
                p.v(lambda e, dk=dk: e.tensor_add(
                        xe[:, dk * TT:(dk + 1) * TT], t_scr[:, :],
                        xe[:, dk * TT:(dk + 1) * TT]))
            epi_vn = p.ev.n
            emit_boundary(p, c, L, xe, stg_a, stg_b, t_scr, t_scr_g, Xout, s,
                          tsl, [(p.ev, epi_vn)], s_xs)
    return [(s_xs, s_xs.n)]


def _run_mamba():
    nc = bass.Bass("TRN2", target_bir_lowering=False)
    Xin = nc.dram_tensor("Xin", [NSEQ, DCH, 128, T], F32, kind="ExternalInput").ap()
    in_proj_w = nc.dram_tensor("in_proj_w", [D_IN_PROJ, D], F32, kind="ExternalInput").ap()
    conv_w = nc.dram_tensor("conv_w", [CONV_DIM, 4], F32, kind="ExternalInput").ap()
    conv_b = nc.dram_tensor("conv_b", [CONV_DIM], F32, kind="ExternalInput").ap()
    dt_bias = nc.dram_tensor("dt_bias", [NHEADS], F32, kind="ExternalInput").ap()
    A_log = nc.dram_tensor("A_log", [NHEADS], F32, kind="ExternalInput").ap()
    Dp = nc.dram_tensor("Dp", [NHEADS], F32, kind="ExternalInput").ap()
    norm_w = nc.dram_tensor("norm_w", [D_INNER], F32, kind="ExternalInput").ap()
    out_proj_w = nc.dram_tensor("out_proj_w", [D, D_INNER], F32, kind="ExternalInput").ap()
    ln_w = nc.dram_tensor("ln_w", [D], F32, kind="ExternalInput").ap()
    ln_b = nc.dram_tensor("ln_b", [D], F32, kind="ExternalInput").ap()
    Xout = nc.dram_tensor("Xout", [NSEQ, DCH, 128, T], F32, kind="ExternalOutput").ap()
    scr_inT = nc.dram_tensor("scr_inT", [DCH, 128, D_IN_PROJ], BF16, kind="Internal").ap()
    scr_outT = nc.dram_tensor("scr_outT", [D_INNER // 128, 128, D], BF16, kind="Internal").ap()
    import os
    dbg = None
    if os.environ.get("MAMBA_DBG"):
        shapes = {"rbc_raw": [128, 2048], "rbc_t1": [128, 2048],
                  "rbc_t2": [128, 2048], "lam_tm_all": [128, 128],
                  "lam_loc_all": [32, TT],
                  "z_sil": [128, 16 * TT], "xssm": [128, 16 * TT],
                  "bcb": [128, 2 * TT], "dt_fm": [32, TT], "lam_fm": [32, TT],
                  "X_tm": [128, D_INNER], "Y_tm": [128, D_INNER],
                  "H_f": [128, D_INNER], "expl_tm": [128, 32],
                  "wdt_all": [128, 32], "decs_bc": [128, 32],
                  "lam_tm": [128, 32], "e_sb": [128, 128],
                  "y_fm": [128, 16 * TT]}
        dts = {"vread": BF16, "z_sil": BF16, "xssm": BF16, "bcb": BF16, "X_tm": BF16,
               "Y_tm": BF16, "wdt_all": BF16, "y_fm": BF16}
        dbg = {k: nc.dram_tensor("dbg_" + k, v, dts.get(k, F32),
                                 kind="ExternalOutput").ap()
               for k, v in shapes.items()}
    p = Prog(nc)
    c = make_consts(p)
    dw = build_mamba_prologue(p, c, in_proj_w, out_proj_w, scr_inT, scr_outT)
    done = build_mamba(p, c, Xin, Xout, scr_inT, scr_outT, conv_w, conv_b,
                       dt_bias, A_log, Dp, norm_w, ln_w, ln_b, pre_waits=dw,
                       dbg=dbg)
    p.y(lambda e: e.nop(), waits=done)
    p.finalize()
    return nc


def _run_moe():
    nc = bass.Bass("TRN2", target_bir_lowering=False)
    Xin = nc.dram_tensor("Xin", [NSEQ, DCH, 128, T], F32, kind="ExternalInput").ap()
    gate_w = nc.dram_tensor("gate_w", [N_EXP, D], F32, kind="ExternalInput").ap()
    gate_b = nc.dram_tensor("gate_b", [N_EXP], F32, kind="ExternalInput").ap()
    rms_w = nc.dram_tensor("rms_w", [N_EXP], F32, kind="ExternalInput").ap()
    W1 = nc.dram_tensor("w1", [N_EXP, D, H_MOE], F32, kind="ExternalInput").ap()
    B1 = nc.dram_tensor("b1", [N_EXP, H_MOE], F32, kind="ExternalInput").ap()
    W2 = nc.dram_tensor("w2", [N_EXP, H_MOE, D], F32, kind="ExternalInput").ap()
    B2 = nc.dram_tensor("b2", [N_EXP, D], F32, kind="ExternalInput").ap()
    Xout = nc.dram_tensor("Xout", [NSEQ, DCH, 128, T], F32, kind="ExternalOutput").ap()
    Yacc = nc.dram_tensor("Yacc", [NSEQ, NTT, 128, DCH * TT], F32, kind="Internal").ap()
    p = Prog(nc)
    c = make_consts(p)
    done = build_moe(p, c, Xin, Xout, gate_w, gate_b, rms_w, W1, B1, W2, B2, Yacc)
    p.y(lambda e: e.nop(), waits=done)
    p.finalize()
    return nc



def _run_identity(x8=None):
    nc = bass.Bass("TRN2", target_bir_lowering=False)
    x_in = nc.dram_tensor("x_in", [NSEQ, T, D], F32, kind="ExternalInput").ap()
    y_out = nc.dram_tensor("y_out", [NSEQ, T, D], F32, kind="ExternalOutput").ap()
    X = nc.dram_tensor("Xs", [NSEQ, DCH, 128, T], F32, kind="Internal").ap()
    p = Prog(nc)
    c = make_consts(p)
    done = build_entry(p, c, x_in, X)
    done2 = build_exit(p, c, X, y_out, pre_waits=done)
    p.y(lambda e: e.nop(), waits=done2)
    p.finalize()
    return nc


def _run_ffn():
    nc = bass.Bass("TRN2", target_bir_lowering=False)
    Xin = nc.dram_tensor("Xin", [NSEQ, DCH, 128, T], F32, kind="ExternalInput").ap()
    W_fc = nc.dram_tensor("W_fc", [H_FFN, D], F32, kind="ExternalInput").ap()
    ln_w = nc.dram_tensor("ln_w", [H_FFN], F32, kind="ExternalInput").ap()
    ln_b = nc.dram_tensor("ln_b", [H_FFN], F32, kind="ExternalInput").ap()
    W_proj = nc.dram_tensor("W_proj", [D, H_FFN], F32, kind="ExternalInput").ap()
    Xout = nc.dram_tensor("Xout", [NSEQ, DCH, 128, T], F32, kind="ExternalOutput").ap()
    scr_fcT = nc.dram_tensor("scr_fcT", [DCH, 128, H_FFN], BF16, kind="Internal").ap()
    scr_projT = nc.dram_tensor("scr_projT", [H_FFN // 128, 128, D], BF16,
                               kind="Internal").ap()
    p = Prog(nc)
    c = make_consts(p)
    persist = {"st_col": p.sbuf("ff_stcol", [128, DCH], F32),
               "cb_col": p.sbuf("ff_cbcol", [128, DCH], F32)}
    dw = build_ffn_prologue(p, c, W_fc, W_proj, ln_w, ln_b, scr_fcT, scr_projT,
                            persist)
    done = build_ffn(p, c, Xin, Xout, scr_fcT, scr_projT, persist, pre_waits=dw)
    p.y(lambda e: e.nop(), waits=done)
    p.finalize()
    return nc


def _run_attn():
    nc = bass.Bass("TRN2", target_bir_lowering=False)
    Xin = nc.dram_tensor("Xin", [NSEQ, DCH, 128, T], F32, kind="ExternalInput").ap()
    Wqk = nc.dram_tensor("Wqk", [D, D], F32, kind="ExternalInput").ap()
    Wv = nc.dram_tensor("Wv", [D, D], F32, kind="ExternalInput").ap()
    Wo = nc.dram_tensor("Wo", [D, D], F32, kind="ExternalInput").ap()
    tau = nc.dram_tensor("tau", [16, 1, 1], F32, kind="ExternalInput").ap()
    Xout = nc.dram_tensor("Xout", [NSEQ, DCH, 128, T], F32, kind="ExternalOutput").ap()
    p = Prog(nc)
    c = make_consts(p)
    done = build_attn(p, c, Xin, Xout, Wqk, Wv, Wo, tau)
    p.y(lambda e: e.nop(), waits=done)
    p.finalize()
    return nc


def _build_launch1():
    nc = bass.Bass("TRN2", target_bir_lowering=False)
    x_in = nc.dram_tensor("x_in", [NSEQ, T, D], F32, kind="ExternalInput").ap()
    in_proj_w = nc.dram_tensor("in_proj_w", [D_IN_PROJ, D], F32, kind="ExternalInput").ap()
    conv_w = nc.dram_tensor("conv_w", [CONV_DIM, 4], F32, kind="ExternalInput").ap()
    conv_b = nc.dram_tensor("conv_b", [CONV_DIM], F32, kind="ExternalInput").ap()
    dt_bias = nc.dram_tensor("dt_bias", [NHEADS], F32, kind="ExternalInput").ap()
    A_log = nc.dram_tensor("A_log", [NHEADS], F32, kind="ExternalInput").ap()
    Dp = nc.dram_tensor("Dp", [NHEADS], F32, kind="ExternalInput").ap()
    norm_w = nc.dram_tensor("norm_w", [D_INNER], F32, kind="ExternalInput").ap()
    out_proj_w = nc.dram_tensor("out_proj_w", [D, D_INNER], F32, kind="ExternalInput").ap()
    ln_w = nc.dram_tensor("ln_w", [D], F32, kind="ExternalInput").ap()
    ln_b = nc.dram_tensor("ln_b", [D], F32, kind="ExternalInput").ap()
    X1 = nc.dram_tensor("X1", [NSEQ, DCH, 128, T], F32, kind="ExternalOutput").ap()
    X0 = nc.dram_tensor("X0", [NSEQ, DCH, 128, T], F32, kind="Internal").ap()
    scr_inT = nc.dram_tensor("scr_inT", [DCH, 128, D_IN_PROJ], BF16, kind="Internal").ap()
    scr_outT = nc.dram_tensor("scr_outT", [D_INNER // 128, 128, D], BF16, kind="Internal").ap()
    p = Prog(nc)
    c = make_consts(p)
    dw = build_mamba_prologue(p, c, in_proj_w, out_proj_w, scr_inT, scr_outT)
    with p.scope():
        done = build_entry(p, c, x_in, X0)
        p.barrier(done)
    done = build_mamba(p, c, X0, X1, scr_inT, scr_outT, conv_w, conv_b,
                       dt_bias, A_log, Dp, norm_w, ln_w, ln_b, pre_waits=dw)
    p.y(lambda e: e.nop(), waits=done)
    p.finalize()
    return nc


def _build_launch2():
    nc = bass.Bass("TRN2", target_bir_lowering=False)
    Xin = nc.dram_tensor("Xin", [NSEQ, DCH, 128, T], F32, kind="ExternalInput").ap()
    W_fc = nc.dram_tensor("W_fc", [H_FFN, D], F32, kind="ExternalInput").ap()
    ln_w = nc.dram_tensor("ln_w", [H_FFN], F32, kind="ExternalInput").ap()
    ln_b = nc.dram_tensor("ln_b", [H_FFN], F32, kind="ExternalInput").ap()
    W_proj = nc.dram_tensor("W_proj", [D, H_FFN], F32, kind="ExternalInput").ap()
    Xout = nc.dram_tensor("Xout", [NSEQ, DCH, 128, T], F32, kind="ExternalOutput").ap()
    scr_fcT = nc.dram_tensor("scr_fcT", [DCH, 128, H_FFN], BF16, kind="Internal").ap()
    scr_projT = nc.dram_tensor("scr_projT", [H_FFN // 128, 128, D], BF16,
                               kind="Internal").ap()
    p = Prog(nc)
    c = make_consts(p)
    persist = {"st_col": p.sbuf("ff_stcol", [128, DCH], F32),
               "cb_col": p.sbuf("ff_cbcol", [128, DCH], F32)}
    dw = build_ffn_prologue(p, c, W_fc, W_proj, ln_w, ln_b, scr_fcT, scr_projT,
                            persist)
    done = build_ffn(p, c, Xin, Xout, scr_fcT, scr_projT, persist, pre_waits=dw)
    p.y(lambda e: e.nop(), waits=done)
    p.finalize()
    return nc


def _build_launch3():
    nc = bass.Bass("TRN2", target_bir_lowering=False)
    Xin = nc.dram_tensor("Xin", [NSEQ, DCH, 128, T], F32, kind="ExternalInput").ap()
    Wqk = nc.dram_tensor("Wqk", [D, D], F32, kind="ExternalInput").ap()
    Wv = nc.dram_tensor("Wv", [D, D], F32, kind="ExternalInput").ap()
    Wo = nc.dram_tensor("Wo", [D, D], F32, kind="ExternalInput").ap()
    tau = nc.dram_tensor("tau", [16, 1, 1], F32, kind="ExternalInput").ap()
    Xout = nc.dram_tensor("Xout", [NSEQ, DCH, 128, T], F32, kind="ExternalOutput").ap()
    p = Prog(nc)
    c = make_consts(p)
    done = build_attn(p, c, Xin, Xout, Wqk, Wv, Wo, tau)
    p.y(lambda e: e.nop(), waits=done)
    p.finalize()
    return nc


def _build_launch4():
    nc = bass.Bass("TRN2", target_bir_lowering=False)
    Xin = nc.dram_tensor("Xin", [NSEQ, DCH, 128, T], F32, kind="ExternalInput").ap()
    gate_w = nc.dram_tensor("gate_w", [N_EXP, D], F32, kind="ExternalInput").ap()
    gate_b = nc.dram_tensor("gate_b", [N_EXP], F32, kind="ExternalInput").ap()
    rms_w = nc.dram_tensor("rms_w", [N_EXP], F32, kind="ExternalInput").ap()
    W1 = nc.dram_tensor("w1", [N_EXP, D, H_MOE], F32, kind="ExternalInput").ap()
    B1 = nc.dram_tensor("b1", [N_EXP, H_MOE], F32, kind="ExternalInput").ap()
    W2 = nc.dram_tensor("w2", [N_EXP, H_MOE, D], F32, kind="ExternalInput").ap()
    B2 = nc.dram_tensor("b2", [N_EXP, D], F32, kind="ExternalInput").ap()
    y_out = nc.dram_tensor("y_out", [NSEQ, T, D], F32, kind="ExternalOutput").ap()
    X4 = nc.dram_tensor("X4", [NSEQ, DCH, 128, T], F32, kind="Internal").ap()
    Yacc = nc.dram_tensor("Yacc", [NSEQ, NTT, 128, DCH * TT], F32, kind="Internal").ap()
    p = Prog(nc)
    c = make_consts(p)
    with p.scope():
        done = build_moe(p, c, Xin, X4, gate_w, gate_b, rms_w, W1, B1, W2, B2,
                         Yacc)
        p.barrier(done)
    done = build_exit(p, c, X4, y_out)
    p.y(lambda e: e.nop(), waits=done)
    p.finalize()
    return nc


def _build_fused():
    nc = bass.Bass("TRN2", target_bir_lowering=False)
    EI, EO, IN = "ExternalInput", "ExternalOutput", "Internal"
    x_in = nc.dram_tensor("x_in", [NSEQ, T, D], F32, kind=EI).ap()
    in_proj_w = nc.dram_tensor("in_proj_w", [D_IN_PROJ, D], F32, kind=EI).ap()
    conv_w = nc.dram_tensor("conv_w", [CONV_DIM, 4], F32, kind=EI).ap()
    conv_b = nc.dram_tensor("conv_b", [CONV_DIM], F32, kind=EI).ap()
    dt_bias = nc.dram_tensor("dt_bias", [NHEADS], F32, kind=EI).ap()
    A_log = nc.dram_tensor("A_log", [NHEADS], F32, kind=EI).ap()
    Dp = nc.dram_tensor("Dp", [NHEADS], F32, kind=EI).ap()
    norm_w = nc.dram_tensor("norm_w", [D_INNER], F32, kind=EI).ap()
    out_proj_w = nc.dram_tensor("out_proj_w", [D, D_INNER], F32, kind=EI).ap()
    mln_w = nc.dram_tensor("mln_w", [D], F32, kind=EI).ap()
    mln_b = nc.dram_tensor("mln_b", [D], F32, kind=EI).ap()
    W_fc = nc.dram_tensor("W_fc", [H_FFN, D], F32, kind=EI).ap()
    fln_w = nc.dram_tensor("fln_w", [H_FFN], F32, kind=EI).ap()
    fln_b = nc.dram_tensor("fln_b", [H_FFN], F32, kind=EI).ap()
    W_proj = nc.dram_tensor("W_proj", [D, H_FFN], F32, kind=EI).ap()
    Wqk = nc.dram_tensor("Wqk", [D, D], F32, kind=EI).ap()
    Wv = nc.dram_tensor("Wv", [D, D], F32, kind=EI).ap()
    Wo = nc.dram_tensor("Wo", [D, D], F32, kind=EI).ap()
    tau = nc.dram_tensor("tau", [16, 1, 1], F32, kind=EI).ap()
    gate_w = nc.dram_tensor("gate_w", [N_EXP, D], F32, kind=EI).ap()
    gate_b = nc.dram_tensor("gate_b", [N_EXP], F32, kind=EI).ap()
    rms_w = nc.dram_tensor("rms_w", [N_EXP], F32, kind=EI).ap()
    W1 = nc.dram_tensor("w1", [N_EXP, D, H_MOE], F32, kind=EI).ap()
    B1 = nc.dram_tensor("b1", [N_EXP, H_MOE], F32, kind=EI).ap()
    W2 = nc.dram_tensor("w2", [N_EXP, H_MOE, D], F32, kind=EI).ap()
    B2 = nc.dram_tensor("b2", [N_EXP, D], F32, kind=EI).ap()
    y_out = nc.dram_tensor("y_out", [NSEQ, T, D], F32, kind=EO).ap()
    X0 = nc.dram_tensor("X0", [NSEQ, DCH, 128, T], F32, kind=IN).ap()
    X1 = nc.dram_tensor("X1", [NSEQ, DCH, 128, T], F32, kind=IN).ap()
    X2 = nc.dram_tensor("X2", [NSEQ, DCH, 128, T], F32, kind=IN).ap()
    X3 = nc.dram_tensor("X3", [NSEQ, DCH, 128, T], F32, kind=IN).ap()
    X4 = nc.dram_tensor("X4", [NSEQ, DCH, 128, T], F32, kind=IN).ap()
    scr_inT = nc.dram_tensor("scr_inT", [DCH, 128, D_IN_PROJ], BF16, kind=IN).ap()
    scr_outT = nc.dram_tensor("scr_outT", [D_INNER // 128, 128, D], BF16, kind=IN).ap()
    scr_fcT = nc.dram_tensor("scr_fcT", [DCH, 128, H_FFN], BF16, kind=IN).ap()
    scr_projT = nc.dram_tensor("scr_projT", [H_FFN // 128, 128, D], BF16, kind=IN).ap()
    Yacc = nc.dram_tensor("Yacc", [NSEQ, NTT, 128, DCH * TT], F32, kind=IN).ap()

    p = Prog(nc)
    c = make_consts(p)
    persist = {"st_col": p.sbuf("ff_stcol", [128, DCH], F32),
               "cb_col": p.sbuf("ff_cbcol", [128, DCH], F32)}
    dw_m = build_mamba_prologue(p, c, in_proj_w, out_proj_w, scr_inT, scr_outT)
    dw_f = build_ffn_prologue(p, c, W_fc, W_proj, fln_w, fln_b, scr_fcT,
                              scr_projT, persist, pre_waits=dw_m)
    with p.scope():
        done = build_entry(p, c, x_in, X0)
        p.barrier(done)
    with p.scope():
        done = build_mamba(p, c, X0, X1, scr_inT, scr_outT, conv_w, conv_b,
                           dt_bias, A_log, Dp, norm_w, mln_w, mln_b,
                           pre_waits=dw_f)
        p.barrier(done)
    with p.scope():
        done = build_ffn(p, c, X1, X2, scr_fcT, scr_projT, persist)
        p.barrier(done)
    with p.scope():
        done = build_attn(p, c, X2, X3, Wqk, Wv, Wo, tau)
        p.barrier(done)
    with p.scope():
        done = build_moe(p, c, X3, X4, gate_w, gate_b, rms_w, W1, B1, W2, B2,
                         Yacc)
        p.barrier(done)
    done = build_exit(p, c, X4, y_out)
    p.y(lambda e: e.nop(), waits=done)
    p.finalize()
    return nc


def _kernel_in_map(inputs, shard):
    return {"x_in": shard,
            "in_proj_w": inputs["L0_attn_in_proj_w"],
            "conv_w": inputs["L0_attn_conv_w"],
            "conv_b": inputs["L0_attn_conv_b"],
            "dt_bias": inputs["L0_attn_dt_bias"],
            "A_log": inputs["L0_attn_A_log"],
            "Dp": inputs["L0_attn_D"],
            "norm_w": inputs["L0_attn_norm_w"],
            "out_proj_w": inputs["L0_attn_out_proj_w"],
            "mln_w": inputs["L0_attn_ln_w"],
            "mln_b": inputs["L0_attn_ln_b"],
            "W_fc": inputs["L0_ffn_W_fc"],
            "fln_w": inputs["L0_ffn_ln_w"],
            "fln_b": inputs["L0_ffn_ln_b"],
            "W_proj": inputs["L0_ffn_W_proj"],
            "Wqk": inputs["L1_attn_Wqk"],
            "Wv": inputs["L1_attn_Wv"],
            "Wo": inputs["L1_attn_Wo"],
            "tau": inputs["L1_attn_tau"],
            "gate_w": inputs["L1_ffn_gate_w"],
            "gate_b": inputs["L1_ffn_gate_b"],
            "rms_w": inputs["L1_ffn_rms_w"],
            "w1": inputs["L1_ffn_w1"],
            "b1": inputs["L1_ffn_b1"],
            "w2": inputs["L1_ffn_w2"],
            "b2": inputs["L1_ffn_b2"]}


_NC_CACHE = {}


def kernel_fused(**inputs):
    x = np.ascontiguousarray(inputs["L0_attn_x"])
    if "fused" not in _NC_CACHE:
        _NC_CACHE["fused"] = _build_fused()
    cores = list(range(8))
    maps = [_kernel_in_map(inputs, np.ascontiguousarray(x[2 * i:2 * i + 2]))
            for i in range(8)]
    res = run_bass_kernel_spmd(_NC_CACHE["fused"], maps, core_ids=cores)
    return np.concatenate([res.results[ci]["y_out"] for ci in cores], axis=0)


def kernel(**inputs):
    return kernel_fused(**inputs)


def _kernel_4launch(**inputs):
    x = np.ascontiguousarray(inputs["L0_attn_x"])
    ncs = _NC_CACHE
    if not ncs:
        ncs[1] = _build_launch1()
        ncs[2] = _build_launch2()
        ncs[3] = _build_launch3()
        ncs[4] = _build_launch4()
    shards = [np.ascontiguousarray(x[2 * i:2 * i + 2]) for i in range(8)]
    cores = list(range(8))
    m1 = [{"x_in": shards[ci],
           "in_proj_w": inputs["L0_attn_in_proj_w"],
           "conv_w": inputs["L0_attn_conv_w"],
           "conv_b": inputs["L0_attn_conv_b"],
           "dt_bias": inputs["L0_attn_dt_bias"],
           "A_log": inputs["L0_attn_A_log"],
           "Dp": inputs["L0_attn_D"],
           "norm_w": inputs["L0_attn_norm_w"],
           "out_proj_w": inputs["L0_attn_out_proj_w"],
           "ln_w": inputs["L0_attn_ln_w"],
           "ln_b": inputs["L0_attn_ln_b"]} for ci in cores]
    r1 = run_bass_kernel_spmd(ncs[1], m1, core_ids=cores)
    m2 = [{"Xin": r1.results[ci]["X1"],
           "W_fc": inputs["L0_ffn_W_fc"],
           "ln_w": inputs["L0_ffn_ln_w"],
           "ln_b": inputs["L0_ffn_ln_b"],
           "W_proj": inputs["L0_ffn_W_proj"]} for ci in cores]
    r2 = run_bass_kernel_spmd(ncs[2], m2, core_ids=cores)
    m3 = [{"Xin": r2.results[ci]["Xout"],
           "Wqk": inputs["L1_attn_Wqk"],
           "Wv": inputs["L1_attn_Wv"],
           "Wo": inputs["L1_attn_Wo"],
           "tau": inputs["L1_attn_tau"]} for ci in cores]
    r3 = run_bass_kernel_spmd(ncs[3], m3, core_ids=cores)
    m4 = [{"Xin": r3.results[ci]["Xout"],
           "gate_w": inputs["L1_ffn_gate_w"],
           "gate_b": inputs["L1_ffn_gate_b"],
           "rms_w": inputs["L1_ffn_rms_w"],
           "w1": inputs["L1_ffn_w1"],
           "b1": inputs["L1_ffn_b1"],
           "w2": inputs["L1_ffn_w2"],
           "b2": inputs["L1_ffn_b2"]} for ci in cores]
    r4 = run_bass_kernel_spmd(ncs[4], m4, core_ids=cores)
    return np.concatenate([r4.results[ci]["y_out"] for ci in cores], axis=0)


# version 09
# speedup vs baseline: 1.6184x; 1.0141x over previous
import contextlib
import math
import numpy as np
import concourse.bass as bass
import concourse.mybir as mybir
from concourse.bass_utils import run_bass_kernel_spmd

F32 = mybir.dt.float32
BF16 = mybir.dt.bfloat16
I32 = mybir.dt.int32
AF = mybir.ActivationFunctionType
ALU = mybir.AluOpType
AX = mybir.AxisListType

NSEQ = 2
T = 2048
D = 1024
DCH = D // 128
TT = 512
NTT = T // TT

D_INNER = 2048
D_STATE = 128
NHEADS = 32
HEADDIM = 64
CONV_DIM = D_INNER + 2 * D_STATE
D_IN_PROJ = 2 * D_INNER + 2 * D_STATE + NHEADS
TC = 128
H_FFN = 4096
N_EXP = 8
H_MOE = 2048

LN_EPS = 1e-5


class Sem:
    def __init__(self, h, name):
        self.h = h
        self.name = name
        self.n = 0

    def bump(self, k):
        self.n += k
        return self.n


class Prog:
    ENGINES = ("tensor", "vector", "scalar", "gpsimd", "sync")
    AUTO = ("vector", "scalar", "gpsimd")

    def __init__(self, nc):
        self.nc = nc
        self.stack = contextlib.ExitStack()
        self.block = self.stack.enter_context(nc.Block())
        self.sbuf_bytes = 0
        self.pending = {e: [] for e in self.ENGINES}
        self.ev = self.sem("auto_v")
        self.ea = self.sem("auto_a")
        self.eg = self.sem("auto_g")
        self._auto = {"vector": self.ev, "scalar": self.ea, "gpsimd": self.eg}

    def flush(self):
        for e in self.ENGINES:
            fns = self.pending[e]
            if not fns:
                continue
            self.pending[e] = []
            getattr(self.block, e)(lambda eng, fns=fns: [f(eng) for f in fns])

    def sem(self, name):
        return Sem(self.stack.enter_context(self.nc.semaphore(name)), name)

    def sbuf(self, name, shape, dtype):
        self.sbuf_bytes += shape[-1] * mybir.dt.size(dtype)
        assert self.sbuf_bytes < 205 * 1024, f"SBUF over budget at {name}: {self.sbuf_bytes}"
        st = self._scope if self._scope is not None else self.stack
        return st.enter_context(self.nc.sbuf_tensor(name, shape, dtype))

    def psum(self, name, shape, dtype=F32):
        st = self._scope if self._scope is not None else self.stack
        return st.enter_context(self.nc.psum_tensor(name, shape, dtype))

    _scope = None

    @contextlib.contextmanager
    def scope(self):
        prev, prev_bytes = self._scope, self.sbuf_bytes
        self._scope = contextlib.ExitStack()
        try:
            yield
        finally:
            self.flush()
            self._scope.close()
            self._scope = prev
            self.sbuf_bytes = prev_bytes

    def op(self, eng, fn, waits=(), inc=None):
        if eng in self.AUTO:
            assert inc is None, "use auto sem counts for cross-engine signaling"
            es = self._auto[eng]
            waits = list(waits) + [(es, es.n)]
            inc = (es, 1)
        if inc is not None:
            inc[0].bump(inc[1])

        def run(e, fn=fn, waits=tuple(waits), inc=inc):
            for (s, v) in waits:
                if v > 0:
                    e.wait_ge(s.h, v)
            r = fn(e)
            if inc is not None:
                r.then_inc(inc[0].h, inc[1])
        self.pending[eng].append(run)

    def sdma(self, fn, waits=(), inc=None):
        if inc is not None:
            inc[0].bump(inc[1])

        def run(e, fn=fn, waits=tuple(waits), inc=inc):
            for (sm, v) in waits:
                if v > 0:
                    e.wait_ge(sm.h, v)
            r = fn(e)
            if inc is not None:
                r.then_inc(inc[0].h, inc[1])
        self.pending["scalar"].append(run)

    def gdma(self, fn, waits=(), inc=None):
        if inc is not None:
            inc[0].bump(inc[1])

        def run(e, fn=fn, waits=tuple(waits), inc=inc):
            for (s, v) in waits:
                if v > 0:
                    e.wait_ge(s.h, v)
            r = fn(e)
            if inc is not None:
                r.then_inc(inc[0].h, inc[1])
        self.pending["gpsimd"].append(run)

    def t(self, fn, **kw): self.op("tensor", fn, **kw)
    def v(self, fn, **kw): self.op("vector", fn, **kw)
    def s(self, fn, **kw): self.op("scalar", fn, **kw)
    def g(self, fn, **kw): self.op("gpsimd", fn, **kw)
    def y(self, fn, **kw): self.op("sync", fn, **kw)

    def barrier(self, waits):
        full = list(waits) + [(self.ev, self.ev.n), (self.ea, self.ea.n),
                              (self.eg, self.eg.n)]
        for e in self.ENGINES:
            if e in self.AUTO:
                self.op(e, lambda x: x.nop(),
                        waits=[(s, v) for (s, v) in full if s is not self._auto[e]])
            else:
                self.op(e, lambda x: x.nop(), waits=full)

    def finalize(self):
        self.flush()
        self.stack.close()


def make_consts(p):
    c = {}
    c["ident_f32"] = p.sbuf("ident_f32", [128, 128], F32)
    c["ident_bf16"] = p.sbuf("ident_bf16", [128, 128], BF16)
    iota_tmp = p.sbuf("iota_tmp", [128, 128], I32)
    c["ones_f32_col"] = p.sbuf("ones_f32_col", [128, 1], F32)
    c["ones_f32_row"] = p.sbuf("ones_f32_row", [1, 128], F32)
    c["ones_bf16_col"] = p.sbuf("ones_bf16_col", [128, 1], BF16)
    c["eps_col"] = p.sbuf("eps_col", [128, 1], F32)
    c["eps6_col"] = p.sbuf("eps6_col", [128, 1], F32)
    c["epsf_col"] = p.sbuf("epsf_col", [128, 1], F32)
    p.g(lambda e: e.memset(c["eps_col"][:, :], LN_EPS))
    p.g(lambda e: e.memset(c["eps6_col"][:, :], 1e-6))
    p.g(lambda e: e.memset(c["epsf_col"][:, :], float(np.finfo(np.float32).eps)))
    p.g(lambda e: e.iota(iota_tmp[:, :], [[1, 128]], base=0, channel_multiplier=-1))
    iota_n = p.eg.n
    p.g(lambda e: e.memset(c["ones_f32_col"][:, :], 1.0))
    p.g(lambda e: e.memset(c["ones_f32_row"][:, :], 1.0))
    p.g(lambda e: e.memset(c["ones_bf16_col"][:, :], 1.0))
    p.v(lambda e: e.tensor_single_scalar(c["ident_f32"][:, :], iota_tmp[:, :], 0,
                                         ALU.is_equal), waits=[(p.eg, iota_n)])
    p.v(lambda e: e.tensor_single_scalar(c["ident_bf16"][:, :], iota_tmp[:, :], 0,
                                         ALU.is_equal))
    c["cmt"] = p.sbuf("cmt", [128, 128], F32)
    p.v(lambda e: e.tensor_single_scalar(c["cmt"][:, :], iota_tmp[:, :], 0,
                                         ALU.is_lt), waits=[(p.eg, iota_n)])
    p.v(lambda e: e.tensor_scalar(c["cmt"][:, :], c["cmt"][:, :], 30000.0, None,
                                  ALU.mult))
    c["waits"] = [(p.eg, p.eg.n), (p.ev, p.ev.n)]
    return c


def build_entry(p, c, x_in, X):
    xt = [p.sbuf(f"en_xt{i}", [128, D], F32) for i in range(2)]
    pt = [p.psum(f"en_ps{i}", [128, 1024], F32) for i in range(2)]
    stg = [p.sbuf(f"en_stg{i}", [128, DCH * TT], F32) for i in range(2)]
    s_ld = [p.sem(f"en_ld{i}") for i in range(2)]
    s_tp = p.sem("en_tp")
    s_st = [p.sem(f"en_st{i}") for i in range(2)]
    tp0 = s_tp.n
    evac_n = {}
    ntile = NSEQ * T // 128
    gpt = TT // 128
    for i in range(ntile):
        s, ti = divmod(i, T // 128)
        g = i // gpt
        b = i % 2
        gb = g % 2
        p.y(lambda e, s=s, ti=ti, b=b: e.dma_start(
                out=xt[b][:, :], in_=x_in[s, ti * 128:(ti + 1) * 128, :]),
            waits=[(s_tp, tp0 + i - 1)],
            inc=(s_ld[b], 16))
        for j in range(DCH):
            p.t(lambda e, b=b, j=j: e.transpose(
                    pt[b][:, j * 128:(j + 1) * 128],
                    xt[b][:, j * 128:(j + 1) * 128], c["ident_f32"][:, :]),
                waits=(c["waits"] + [(s_ld[b], 16 * (i // 2 + 1)),
                                     (p.ev, evac_n.get(i - 2, 0))]
                       if j == 0 else []),
                inc=((s_tp, 1) if j == DCH - 1 else None))
        p.v(lambda e, b=b, gb=gb, off=(i % gpt) * 128: e.tensor_copy(
                stg[gb].ap().rearrange("p (dk tt) -> p dk tt", dk=DCH)[:, :, off:off + 128],
                pt[b].ap().rearrange("p (dk cc) -> p dk cc", dk=DCH)[:, :, :]),
            waits=[(s_tp, tp0 + i + 1), (s_st[gb], 8 * 16 * (g // 2))])
        evac_n[i] = p.ev.n
        if i % gpt == gpt - 1:
            tg = (i % (T // 128)) // gpt
            for dk in range(DCH):
                p.y(lambda e, s=s, dk=dk, tg=tg, gb=gb: e.dma_start(
                        out=X[s, dk, :, tg * TT:(tg + 1) * TT],
                        in_=stg[gb][:, dk * TT:(dk + 1) * TT]),
                    waits=([(p.ev, evac_n[i])] if dk == 0 else []),
                    inc=(s_st[gb], 16))
    return [(s_st[0], s_st[0].n), (s_st[1], s_st[1].n)]


def build_exit(p, c, Xf, y_out, pre_waits=()):
    stg = [p.sbuf(f"ex_stg{i}", [128, DCH * TT], F32) for i in range(2)]
    pt = [p.psum(f"ex_ps{i}", [128, 1024], F32) for i in range(2)]
    ot = [p.sbuf(f"ex_ot{i}", [128, D], F32) for i in range(2)]
    s_ld = [p.sem(f"ex_ld{i}") for i in range(2)]
    s_tp = p.sem("ex_tp")
    s_st = [p.sem(f"ex_st{i}") for i in range(2)]
    tp0 = s_tp.n
    evac_n = {}
    ngrp = NSEQ * T // TT
    for g in range(ngrp):
        s, tg = divmod(g, NTT)
        gb = g % 2
        for dk in range(DCH):
            p.y(lambda e, s=s, tg=tg, dk=dk, gb=gb: e.dma_start(
                    out=stg[gb][:, dk * TT:(dk + 1) * TT],
                    in_=Xf[s, dk, :, tg * TT:(tg + 1) * TT]),
                waits=(list(pre_waits) + [(s_tp, tp0 + 4 * (g - 1))]
                       if dk == 0 else []),
                inc=(s_ld[gb], 16))
        for q in range(TT // 128):
            i = g * 4 + q
            b = i % 2
            for dk in range(DCH):
                p.t(lambda e, gb=gb, dk=dk, q=q, b=b: e.transpose(
                        pt[b][:, dk * 128:(dk + 1) * 128],
                        stg[gb][:, dk * TT + q * 128:dk * TT + q * 128 + 128],
                        c["ident_f32"][:, :]),
                    waits=(c["waits"] + [(s_ld[gb], 16 * 8 * (g // 2 + 1)),
                                         (p.ev, evac_n.get(i - 2, 0))]
                           if dk == 0 else []),
                    inc=((s_tp, 1) if dk == DCH - 1 else None))
            p.v(lambda e, b=b: e.tensor_copy(ot[b][:, :], pt[b][:, :]),
                waits=[(s_tp, tp0 + i + 1), (s_st[b], 16 * (i // 2))])
            evac_n[i] = p.ev.n
            p.y(lambda e, s=s, tg=tg, q=q, b=b: e.dma_start(
                    out=y_out[s, tg * TT + q * 128:tg * TT + (q + 1) * 128, :],
                    in_=ot[b][:, :]),
                waits=[(p.ev, evac_n[i])],
                inc=(s_st[b], 16))
    return [(s_st[0], s_st[0].n), (s_st[1], s_st[1].n)]



LNR = dict(mu=slice(0, TT), mu2=slice(TT, 2 * TT), var=slice(TT, 2 * TT),
           sd=slice(TT, 2 * TT), rs=slice(2 * TT, 3 * TT),
           nmr=slice(3 * TT, 4 * TT))


def make_lnctx(p, c, prefix, st1_ps, st2_ps, rs_ps, nm_ps, s_t):
    return dict(st1=st1_ps, st2=st2_ps, rs=rs_ps, nm=nm_ps, s_t=s_t,
                rowbuf=p.sbuf(prefix + "_rows", [1, 4 * TT], F32),
                rsbc=p.sbuf(prefix + "_rsbc", [128, TT], F32),
                nmrbc=p.sbuf(prefix + "_nmrbc", [128, TT], F32))


def ln_rows_and_bc(p, c, L, n_feat, wait_pairs):
    rowbuf, s_t = L["rowbuf"], L["s_t"]
    p.v(lambda e: e.tensor_scalar(rowbuf[0:1, LNR["mu"]], L["st1"][0:1, :],
                                  1.0 / n_feat, None, ALU.mult),
        waits=wait_pairs)
    p.v(lambda e: e.tensor_mul(rowbuf[0:1, LNR["mu2"]], rowbuf[0:1, LNR["mu"]],
                               rowbuf[0:1, LNR["mu"]]))
    p.v(lambda e: e.scalar_tensor_tensor(
            rowbuf[0:1, LNR["var"]], L["st2"][0:1, :], 1.0 / n_feat,
            rowbuf[0:1, LNR["mu2"]], ALU.mult, ALU.subtract))
    vn = p.ev.n
    p.s(lambda e: e.activation(rowbuf[0:1, LNR["sd"]], rowbuf[0:1, LNR["var"]],
                               AF.Sqrt, bias=c["eps_col"][0:1, :]),
        waits=[(p.ev, vn)])
    an = p.ea.n
    p.v(lambda e: e.reciprocal(rowbuf[0:1, LNR["rs"]], rowbuf[0:1, LNR["sd"]]),
        waits=[(p.ea, an)])
    p.v(lambda e: e.scalar_tensor_tensor(
            rowbuf[0:1, LNR["nmr"]], rowbuf[0:1, LNR["mu"]], -1.0,
            rowbuf[0:1, LNR["rs"]], ALU.mult, ALU.mult))
    vn = p.ev.n
    p.t(lambda e: e.matmul(L["rs"][:, :], c["ones_f32_row"][:, :],
                           rowbuf[0:1, LNR["rs"]], start=True, stop=True),
        waits=[(p.ev, vn)], inc=(s_t, 1))
    p.t(lambda e: e.matmul(L["nm"][:, :], c["ones_f32_row"][:, :],
                           rowbuf[0:1, LNR["nmr"]], start=True, stop=True),
        inc=(s_t, 1))
    tn = s_t.n
    p.v(lambda e: e.tensor_copy(L["rsbc"][:, :], L["rs"][:, :]),
        waits=[(s_t, tn), (p.eg, p.eg.n)])
    p.v(lambda e: e.tensor_copy(L["nmrbc"][:, :], L["nm"][:, :]))
    return p.ev.n


def emit_boundary(p, c, L, xbuf, stg_a, stg_b, t_scr, t_scr_g, Xout, s, tsl,
                  ready, s_xs):
    s_t = L["s_t"]
    st_mm_n = {}
    for dk in range(DCH):
        p.g(lambda e, dk=dk: e.tensor_copy(stg_a[dk % 2][:, :],
                                           xbuf[:, dk * TT:(dk + 1) * TT]),
            waits=(list(ready) if dk == 0 else []) +
                  [(s_t, st_mm_n.get(dk - 2, s_t.n if dk == 0 else 0))])
        p.g(lambda e, dk=dk: e.tensor_mul(stg_b[dk % 2][:, :],
                                          xbuf[:, dk * TT:(dk + 1) * TT],
                                          xbuf[:, dk * TT:(dk + 1) * TT]))
        gn = p.eg.n
        p.t(lambda e, dk=dk: e.matmul(
                L["st1"][0:1, :], c["ones_bf16_col"][:, :], stg_a[dk % 2][:, :],
                start=(dk == 0), stop=(dk == DCH - 1)),
            waits=[(p.eg, gn)] + ([(p.ev, p.ev.n)] if dk == 0 else []),
            inc=(s_t, 1))
        p.t(lambda e, dk=dk: e.matmul(
                L["st2"][0:1, :], c["ones_bf16_col"][:, :], stg_b[dk % 2][:, :],
                start=(dk == 0), stop=(dk == DCH - 1)),
            inc=(s_t, 1))
        st_mm_n[dk] = s_t.n
    ln_rows_and_bc(p, c, L, D, [(s_t, s_t.n)])
    for dk in range(DCH):
        if dk % 2 == 0:
            p.v(lambda e, dk=dk: e.tensor_mul(t_scr[:, :],
                                              xbuf[:, dk * TT:(dk + 1) * TT],
                                              L["rsbc"][:, :]))
            p.v(lambda e, dk=dk: e.tensor_add(xbuf[:, dk * TT:(dk + 1) * TT],
                                              t_scr[:, :], L["nmrbc"][:, :]))
            en, sem = p.ev.n, p.ev
        else:
            p.g(lambda e, dk=dk: e.tensor_mul(t_scr_g[:, :],
                                              xbuf[:, dk * TT:(dk + 1) * TT],
                                              L["rsbc"][:, :]),
                waits=[(p.ev, p.ev.n)])
            p.g(lambda e, dk=dk: e.tensor_add(xbuf[:, dk * TT:(dk + 1) * TT],
                                              t_scr_g[:, :], L["nmrbc"][:, :]))
            en, sem = p.eg.n, p.eg
        p.y(lambda e, s=s, dk=dk, tsl=tsl: e.dma_start(
                out=Xout[s, dk, :, tsl], in_=xbuf[:, dk * TT:(dk + 1) * TT]),
            waits=[(sem, en)], inc=(s_xs, 16))



def build_ffn_prologue(p, c, W_fc, W_proj, ln_w, ln_b, scr_fcT, scr_projT,
                       persist, pre_waits=()):
    s_ld = p.sem("fp_ld")
    s_la = [p.sem(f"fp_la{i}") for i in range(2)]
    s_lb = [p.sem(f"fp_lb{i}") for i in range(2)]
    s_t = p.sem("fp_t")
    s_st = p.sem("fp_st")
    with p.scope():
        wt = [p.sbuf(f"fp_wt{i}", [128, 4096], F32) for i in range(2)]
        s1 = [p.sbuf(f"fp_s1{i}", [128, 1024], F32) for i in range(2)]
        s2 = [p.sbuf(f"fp_s2{i}", [128, 1024], F32) for i in range(2)]
        qt = [p.sbuf(f"fp_qt{i}", [128, 4096], BF16) for i in range(2)]
        tstage = p.sbuf("fp_tst", [128, 4096], BF16)
        qraw = [p.sbuf(f"fp_qraw{i}", [128, 128], F32) for i in range(2)]
        abscols = p.sbuf("fp_abs", [128, 40], F32)
        gcols = p.sbuf("fp_gcols", [128, 32], F32)
        bcols = p.sbuf("fp_bcols", [128, 32], F32)
        lnv = p.sbuf("fp_lnv", [32, 128], F32)
        rows = p.sbuf("fp_rows", [1, 8], F32)
        ccol = p.sbuf("fp_ccol", [128, 2], F32)
        nccol = p.sbuf("fp_nccol", [128, 2], F32)
        colsum = p.sbuf("fp_colsum", [128, 2], F32)
        tp_bf = p.psum("fp_tpbf", [128, 512], BF16)
        tp_f = p.psum("fp_tpf", [128, 128], F32)
        mi_ps = p.psum("fp_mi", [128, 512], F32)
        st_ps = p.psum("fp_stp", [128, 512], F32)
        cb_ps = p.psum("fp_cb", [128, 512], F32)

        CW = [(W_fc, 32, 1024), (W_proj, 8, 4096)]

        for w, (src, dst) in enumerate([(ln_w, gcols), (ln_b, bcols)]):
            p.y(lambda e, src=src: e.dma_start(
                    out=lnv[:, :], in_=src.rearrange("(a b) -> a b", a=32)),
                waits=list(pre_waits) + [(s_t, s_t.n)], inc=(s_ld, 16))
            ldn = s_ld.n
            p.t(lambda e: e.transpose(tp_f[:, :32], lnv[:32, :],
                                      c["ident_f32"][:32, :32]),
                waits=c["waits"] + [(s_ld, ldn), (p.ev, p.ev.n)], inc=(s_t, 1))
            p.v(lambda e, dst=dst: e.tensor_copy(dst[:, :32], tp_f[:, :32]),
                waits=[(s_t, s_t.n)])

        coli = 0
        red_vn = {}
        la_cnt = [0, 0]
        gi = 0
        for w, (W, ntiles, tw) in enumerate(CW):
            lo = coli
            for i in range(ntiles):
                b = gi % 2
                la_cnt[b] += 1
                p.y(lambda e, W=W, i=i, tw=tw, b=b: e.dma_start(
                        out=wt[b][:, :tw], in_=W[i * 128:(i + 1) * 128, :]),
                    waits=[(p.ev, red_vn.get(gi - 2, 0))], inc=(s_la[b], 16))
                lan = 16 * la_cnt[b]
                p.v(lambda e, ci=coli, tw=tw, b=b: e.reduce_sum(
                        abscols[:, ci:ci + 1], wt[b][:, :tw], AX.X,
                        apply_absolute_value=True),
                    waits=[(s_la[b], lan)])
                red_vn[gi] = p.ev.n
                coli += 1
                gi += 1
            p.v(lambda e, w=w, lo=lo, hi=coli: e.reduce_sum(
                    colsum[:, w:w + 1], abscols[:, lo:hi], AX.X))
            p.t(lambda e, w=w: e.matmul(mi_ps[0:1, w:w + 1], colsum[:, w:w + 1],
                                        c["ones_f32_col"][:, :], start=True, stop=True),
                waits=[(p.ev, p.ev.n)], inc=(s_t, 1))
            p.v(lambda e, w=w: e.tensor_scalar(rows[0:1, 2 + w:3 + w],
                                               mi_ps[0:1, w:w + 1],
                                               1.0 / 4194304.0, 1e-6,
                                               ALU.mult, ALU.add),
                waits=[(s_t, s_t.n)])
            p.v(lambda e, w=w: e.tensor_scalar(rows[0:1, 4 + w:5 + w],
                                               rows[0:1, 2 + w:3 + w],
                                               0.5, None, ALU.mult))
            p.t(lambda e, w=w: e.matmul(mi_ps[:, 4 + w:5 + w], c["ones_f32_row"][:, :],
                                        rows[0:1, 4 + w:5 + w], start=True, stop=True),
                waits=[(p.ev, p.ev.n)], inc=(s_t, 1))
            p.v(lambda e, w=w: e.tensor_copy(ccol[:, w:w + 1], mi_ps[:, 4 + w:5 + w]),
                waits=[(s_t, s_t.n)])
            p.v(lambda e, w=w: e.tensor_scalar(nccol[:, w:w + 1], ccol[:, w:w + 1],
                                               -1.0, None, ALU.mult))

        vnA = p.ev.n
        lb_cnt = [0, 0]
        sign_an = {}
        add_vn = {}
        tp_done = {}
        q_free = {}
        gi = 0
        ck = 0
        for w, (W, ntiles, tw) in enumerate(CW):
            nsub = tw // 128
            for i in range(ntiles):
                b = gi % 2
                lb_cnt[b] += 1
                p.y(lambda e, W=W, i=i, tw=tw, b=b: e.dma_start(
                        out=wt[b][:, :tw], in_=W[i * 128:(i + 1) * 128, :]),
                    waits=[(p.ea, sign_an.get(gi - 2, 0)),
                           (p.ev, vnA if gi < 2 else 0)],
                    inc=(s_lb[b], 16))
                lbn = 16 * lb_cnt[b]
                for st in range(tw // 1024):
                    sl = slice(st * 1024, (st + 1) * 1024)
                    cb = ck % 2
                    p.s(lambda e, sl=sl, w=w, b=b, cb=cb: e.sign(
                            s1[cb][:, :], wt[b][:, sl], bias=nccol[:, w:w + 1]),
                        waits=[(s_lb[b], lbn),
                               (p.ev, add_vn.get(ck - 2, vnA if ck < 2 else 0))])
                    p.s(lambda e, sl=sl, w=w, b=b, cb=cb: e.sign(
                            s2[cb][:, :], wt[b][:, sl], bias=ccol[:, w:w + 1]))
                    an = p.ea.n
                    p.v(lambda e, cb=cb: e.tensor_add(
                            s1[cb][:, :], s1[cb][:, :], s2[cb][:, :]),
                        waits=[(p.ea, an)])
                    p.v(lambda e, sl=sl, b=b, cb=cb: e.tensor_scalar(
                            qt[b][:, sl], s1[cb][:, :], 0.5, None, ALU.mult),
                        waits=([(s_t, tp_done.get(gi - 2, 0))] if st == 0 else []))
                    add_vn[ck] = p.ev.n
                    ck += 1
                sign_an[gi] = p.ea.n
                qt_vn = p.ev.n
                for j in range(nsub):
                    q = j % 4
                    p.t(lambda e, j=j, q=q, b=b: e.transpose(
                            tp_bf[:, q * 128:(q + 1) * 128],
                            qt[b][:, j * 128:(j + 1) * 128], c["ident_bf16"][:, :]),
                        waits=[(p.ev, p.ev.n)], inc=(s_t, 1))
                    tn = s_t.n
                    if w == 0:
                        p.v(lambda e, j=j, q=q: e.tensor_copy(
                                tstage[:, j * 128:(j + 1) * 128],
                                tp_bf[:, q * 128:(q + 1) * 128]),
                            waits=[(s_t, tn), (s_st, s_st.n)])
                        vn = p.ev.n
                        p.y(lambda e, j=j, i=i: e.dma_start(
                                out=scr_fcT[j, :, i * 128:(i + 1) * 128],
                                in_=tstage[:, j * 128:(j + 1) * 128]),
                            waits=[(p.ev, vn)], inc=(s_st, 16))
                    else:
                        p.v(lambda e, q=q: e.tensor_copy(
                                qraw[0][:, :], tp_bf[:, q * 128:(q + 1) * 128]),
                            waits=[(s_t, tn)])
                        vn = p.ev.n
                        p.t(lambda e, i=i, j=j: e.matmul(
                                cb_ps[:, i:i + 1], qraw[0][:, :], bcols[:, j:j + 1],
                                start=(j == 0), stop=(j == nsub - 1)),
                            waits=[(p.ev, vn)], inc=(s_t, 1))
                        p.v(lambda e, j=j, q=q: e.tensor_scalar(
                                tstage[:, j * 128:(j + 1) * 128],
                                tp_bf[:, q * 128:(q + 1) * 128],
                                gcols[:, j:j + 1], None, ALU.mult),
                            waits=[(s_t, s_t.n), (s_st, s_st.n)])
                        vn = p.ev.n
                        p.t(lambda e, i=i, j=j: e.matmul(
                                st_ps[:, i:i + 1],
                                tstage[:, j * 128:(j + 1) * 128],
                                c["ones_bf16_col"][:, :],
                                start=(j == 0), stop=(j == nsub - 1)),
                            waits=[(p.ev, vn)], inc=(s_t, 1))
                        p.y(lambda e, j=j, i=i: e.dma_start(
                                out=scr_projT[j, :, i * 128:(i + 1) * 128],
                                in_=tstage[:, j * 128:(j + 1) * 128]),
                            waits=[(s_t, s_t.n)], inc=(s_st, 16))
                tp_done[gi] = s_t.n
                gi += 1
        p.v(lambda e: e.tensor_copy(persist["st_col"][:, :], st_ps[:, 0:8]),
            waits=[(s_t, s_t.n)])
        p.v(lambda e: e.tensor_copy(persist["cb_col"][:, :], cb_ps[:, 0:8]))
        done = [(p.ev, p.ev.n), (s_st, s_st.n), (p.ea, p.ea.n), (s_t, s_t.n),
                (s_ld, s_ld.n)]
        p.barrier(done)
    return done


def build_ffn(p, c, Xin, Xout, scr_fcT, scr_projT, persist, pre_waits=()):
    NHK = H_FFN // 128
    s_w = p.sem("ff_w")
    s_wp = [p.sem(f"ff_wp{i}") for i in range(2)]
    s_xl = p.sem("ff_xl")
    s_xs = p.sem("ff_xs")
    s_t = p.sem("ff_t")
    wfc = [p.sbuf(f"ff_wfc{dk}", [128, H_FFN], BF16) for dk in range(DCH)]
    wprs = [p.sbuf(f"ff_wprs{i}", [128, D], BF16) for i in range(2)]
    xbuf = p.sbuf("ff_x", [128, DCH * TT], F32)
    xb = p.sbuf("ff_xb", [128, DCH * TT], BF16)
    h_sb = p.sbuf("ff_h", [128, (H_FFN // 128) * TT], BF16)
    stg_a = [p.sbuf(f"ff_sa{i}", [128, TT], BF16) for i in range(2)]
    stg_b = [p.sbuf(f"ff_sb{i}", [128, TT], BF16) for i in range(2)]
    t_scr = p.sbuf("ff_t0", [128, TT], F32)
    t_scr_g = p.sbuf("ff_t0g", [128, TT], F32)
    rowbuf = p.sbuf("ff_rows", [1, 4 * TT], F32)
    rsbc = p.sbuf("ff_rsbc", [128, TT], F32)
    nmrbc = p.sbuf("ff_nmrbc", [128, TT], F32)
    pb = [p.psum(f"ff_pb{i}", [128, TT], F32) for i in range(8)]
    h_ps = [pb[0], pb[1]]
    st1_ps, st2_ps = pb[2], pb[3]
    rs_ps, nm_ps = pb[4], pb[5]
    R = dict(mu=slice(0, TT), mu2=slice(TT, 2 * TT), var=slice(TT, 2 * TT),
             sd=slice(TT, 2 * TT), rs=slice(2 * TT, 3 * TT),
             nmr=slice(3 * TT, 4 * TT))

    for dk in range(DCH):
        p.y(lambda e, dk=dk: e.dma_start(out=wfc[dk][:, :], in_=scr_fcT[dk, :, :]),
            waits=(list(pre_waits) if dk == 0 else []), inc=(s_w, 16))
    wn = s_w.n

    gelu_cnt = {}
    t_after_P_prev = 0

    def rows_and_bc(n_feat, wait_pairs):
        p.v(lambda e: e.tensor_scalar(rowbuf[0:1, R["mu"]], st1_ps[0:1, :],
                                      1.0 / n_feat, None, ALU.mult),
            waits=wait_pairs)
        vn = p.ev.n
        p.s(lambda e: e.activation(rowbuf[0:1, R["mu2"]], rowbuf[0:1, R["mu"]],
                                   AF.Square), waits=[(p.ev, vn)])
        an = p.ea.n
        p.v(lambda e: e.scalar_tensor_tensor(
                rowbuf[0:1, R["var"]], st2_ps[0:1, :], 1.0 / n_feat,
                rowbuf[0:1, R["mu2"]], ALU.mult, ALU.subtract),
            waits=[(p.ea, an)])
        vn = p.ev.n
        p.s(lambda e: e.activation(rowbuf[0:1, R["sd"]], rowbuf[0:1, R["var"]],
                                   AF.Sqrt, bias=c["eps_col"][0:1, :]),
            waits=[(p.ev, vn)])
        an = p.ea.n
        p.v(lambda e: e.reciprocal(rowbuf[0:1, R["rs"]], rowbuf[0:1, R["sd"]]),
            waits=[(p.ea, an)])
        p.v(lambda e: e.scalar_tensor_tensor(
                rowbuf[0:1, R["nmr"]], rowbuf[0:1, R["mu"]], -1.0,
                rowbuf[0:1, R["rs"]], ALU.mult, ALU.mult))
        vn = p.ev.n
        p.t(lambda e: e.matmul(rs_ps[:, :], c["ones_f32_row"][:, :],
                               rowbuf[0:1, R["rs"]], start=True, stop=True),
            waits=[(p.ev, vn)], inc=(s_t, 1))
        p.t(lambda e: e.matmul(nm_ps[:, :], c["ones_f32_row"][:, :],
                               rowbuf[0:1, R["nmr"]], start=True, stop=True),
            inc=(s_t, 1))
        tn = s_t.n
        p.v(lambda e: e.tensor_copy(rsbc[:, :], rs_ps[:, :]),
            waits=[(s_t, tn), (p.eg, p.eg.n)])
        p.v(lambda e: e.tensor_copy(nmrbc[:, :], nm_ps[:, :]))
        return p.ev.n

    NHK = H_FFN // 128
    for tt in range(NSEQ * NTT):
        s, tg = divmod(tt, NTT)
        tsl = slice(tg * TT, (tg + 1) * TT)
        for dk in range(DCH):
            p.y(lambda e, s=s, dk=dk, tsl=tsl: e.dma_start(
                    out=xbuf[:, dk * TT:(dk + 1) * TT], in_=Xin[s, dk, :, tsl]),
                waits=([(s_xs, s_xs.n), (p.ev, p.ev.n), (p.eg, p.eg.n)]
                       if dk == 0 else []),
                inc=(s_xl, 16))
        xln = s_xl.n
        for dk in range(DCH):
            p.v(lambda e, dk=dk: e.tensor_copy(xb[:, dk * TT:(dk + 1) * TT],
                                               xbuf[:, dk * TT:(dk + 1) * TT]),
                waits=([(s_xl, xln), (s_t, t_after_P_prev)] if dk == 0 else []))
        cast_vn = p.ev.n
        for hk in range(NHK):
            hsl = slice(hk * TT, (hk + 1) * TT)
            for dk in range(DCH):
                p.t(lambda e, hk=hk, dk=dk: e.matmul(
                        h_ps[hk % 2][:, :],
                        wfc[dk][:, hk * 128:(hk + 1) * 128],
                        xb[:, dk * TT:(dk + 1) * TT],
                        start=(dk == 0), stop=(dk == DCH - 1)),
                    waits=([(s_w, wn), (p.ev, cast_vn),
                            (p.ea, gelu_cnt.get((tt, hk - 2),
                                  gelu_cnt.get((tt - 1, NHK - 2 + hk), 0)))]
                           if dk == 0 else []),
                    inc=((s_t, 1) if dk == DCH - 1 else None))
            tn = s_t.n
            p.s(lambda e, hk=hk, hsl=hsl: e.activation(
                    h_sb[:, hsl], h_ps[hk % 2][:, :], AF.Gelu),
                waits=[(s_t, tn)])
            gelu_cnt[(tt, hk)] = p.ea.n
            p.v(lambda e, hk=hk, hsl=hsl: e.tensor_mul(
                    stg_b[hk % 2][:, :], h_sb[:, hsl], h_sb[:, hsl]),
                waits=[(s_t, s_t.n), (p.ea, gelu_cnt[(tt, hk)])])
            sq_vn = p.ev.n
            p.t(lambda e, hk=hk, hsl=hsl: e.matmul(
                    st1_ps[0:1, :], c["ones_bf16_col"][:, :], h_sb[:, hsl],
                    start=(hk == 0), stop=(hk == NHK - 1)),
                waits=[(p.ea, gelu_cnt[(tt, hk)])] +
                      ([(p.ev, p.ev.n)] if hk == 0 else []),
                inc=(s_t, 1))
            p.t(lambda e, hk=hk: e.matmul(
                    st2_ps[0:1, :], c["ones_bf16_col"][:, :], stg_b[hk % 2][:, :],
                    start=(hk == 0), stop=(hk == NHK - 1)),
                waits=[(p.ev, sq_vn)], inc=(s_t, 1))
        all_gelu = p.ea.n
        bc_vn = rows_and_bc(H_FFN, [(s_t, s_t.n)])
        wpr_mm_n = {}
        for hk in range(NHK):
            b = hk % 2
            p.y(lambda e, hk=hk, b=b: e.dma_start(out=wprs[b][:, :],
                                                  in_=scr_projT[hk, :, :]),
                waits=[(s_t, wpr_mm_n.get(hk - 2, 0))], inc=(s_wp[b], 16))
            wpn = s_wp[b].n
            for dk in range(DCH):
                p.t(lambda e, hk=hk, dk=dk, b=b: e.matmul(
                        pb[dk][:, :],
                        wprs[b][:, dk * 128:(dk + 1) * 128],
                        h_sb[:, hk * TT:(hk + 1) * TT],
                        start=(hk == 0), stop=(hk == NHK - 1)),
                    waits=([(p.ea, all_gelu), (s_wp[b], wpn), (p.ev, bc_vn)]
                           if dk == 0 else []),
                    inc=((s_t, 1) if dk == DCH - 1 else None))
            wpr_mm_n[hk] = s_t.n
        tn_P = s_t.n
        for dk in range(DCH):
            p.v(lambda e, dk=dk: e.tensor_mul(t_scr[:, :], pb[dk][:, :],
                                              rsbc[:, :]),
                waits=[(s_t, tn_P)])
            p.v(lambda e, dk=dk: e.scalar_tensor_tensor(
                    t_scr[:, :], nmrbc[:, :], persist["st_col"][:, dk:dk + 1],
                    t_scr[:, :], ALU.mult, ALU.add))
            p.v(lambda e, dk=dk: e.scalar_tensor_tensor(
                    xbuf[:, dk * TT:(dk + 1) * TT], t_scr[:, :],
                    persist["cb_col"][:, dk:dk + 1],
                    xbuf[:, dk * TT:(dk + 1) * TT], ALU.add, ALU.add))
        epi_vn = p.ev.n
        for dk in range(DCH):
            p.g(lambda e, dk=dk: e.tensor_copy(stg_a[dk % 2][:, :],
                                               xbuf[:, dk * TT:(dk + 1) * TT]),
                waits=[(p.ev, epi_vn), (s_t, s_t.n)])
            p.g(lambda e, dk=dk: e.tensor_mul(stg_b[dk % 2][:, :],
                                              xbuf[:, dk * TT:(dk + 1) * TT],
                                              xbuf[:, dk * TT:(dk + 1) * TT]))
            gn = p.eg.n
            p.t(lambda e, dk=dk: e.matmul(
                    st1_ps[0:1, :], c["ones_bf16_col"][:, :], stg_a[dk % 2][:, :],
                    start=(dk == 0), stop=(dk == DCH - 1)),
                waits=[(p.eg, gn)] + ([(p.ev, p.ev.n)] if dk == 0 else []),
                inc=(s_t, 1))
            p.t(lambda e, dk=dk: e.matmul(
                    st2_ps[0:1, :], c["ones_bf16_col"][:, :], stg_b[dk % 2][:, :],
                    start=(dk == 0), stop=(dk == DCH - 1)),
                inc=(s_t, 1))
        rows_and_bc(D, [(s_t, s_t.n)])
        for dk in range(DCH):
            if dk % 2 == 0:
                p.v(lambda e, dk=dk: e.tensor_mul(t_scr[:, :],
                                                  xbuf[:, dk * TT:(dk + 1) * TT],
                                                  rsbc[:, :]))
                p.v(lambda e, dk=dk: e.tensor_add(xbuf[:, dk * TT:(dk + 1) * TT],
                                                  t_scr[:, :], nmrbc[:, :]))
                en, sem = p.ev.n, p.ev
            else:
                p.g(lambda e, dk=dk: e.tensor_mul(t_scr_g[:, :],
                                                  xbuf[:, dk * TT:(dk + 1) * TT],
                                                  rsbc[:, :]),
                    waits=[(p.ev, p.ev.n)])
                p.g(lambda e, dk=dk: e.tensor_add(xbuf[:, dk * TT:(dk + 1) * TT],
                                                  t_scr_g[:, :], nmrbc[:, :]))
                en, sem = p.eg.n, p.eg
            p.y(lambda e, s=s, dk=dk, tsl=tsl: e.dma_start(
                    out=Xout[s, dk, :, tsl], in_=xbuf[:, dk * TT:(dk + 1) * TT]),
                waits=[(sem, en)], inc=(s_xs, 16))
        t_after_P_prev = s_t.n
    return [(s_xs, s_xs.n)]



MAMBA_PLIM = None


def build_mamba_prologue(p, c, in_proj_w, out_proj_w, scr_inT, scr_outT,
                         pre_waits=()):
    s_ld = [p.sem(f"mp_ld{i}") for i in range(2)]
    s_t = p.sem("mp_t")
    s_st4 = [[p.sem(f"mp_st{i}{j}") for j in range(2)] for i in range(2)]
    with p.scope():
        qt = [p.sbuf(f"mp_qt{i}", [128, 2048], BF16) for i in range(2)]
        tstage = [p.sbuf(f"mp_tst{i}", [128, 2048], BF16) for i in range(2)]
        tp_bf = [p.psum(f"mp_tpbf{i}", [128, 1024], BF16) for i in range(2)]
        CW = [(in_proj_w, 35, 1024, scr_inT, DCH),
              (out_proj_w, 8, 2048, scr_outT, 16)]
        if MAMBA_PLIM is not None:
            CW = [(W, min(nt, MAMBA_PLIM[wi]), tw, scr, ns)
                  for wi, (W, nt, tw, scr, ns) in enumerate(CW)]
        tile_mm = {}
        bank_ev = {0: 0, 1: 0}
        st_by_slot = {0: None, 1: None}
        ti = 0
        for w, (W, ntiles, tw, scr, nsub) in enumerate(CW):
            for i in range(ntiles):
                rows = 128 if (w == 1 or i < 34) else 32
                b = ti % 2
                p.gdma(lambda e, W=W, i=i, tw=tw, rows=rows, b=b: e.dma_start(
                        out=qt[b][:rows, :tw], in_=W[i * 128:i * 128 + rows, :]),
                    waits=list(pre_waits) + [(s_t, tile_mm.get(ti - 2, 0))],
                    inc=(s_ld[b], 16))
                ldn = s_ld[b].n
                nround = nsub // 8
                for r in range(nround):
                    bank = (ti * nround + r) % 2
                    for jj in range(8):
                        j = r * 8 + jj
                        p.t(lambda e, j=j, jj=jj, b=b, bank=bank, rows=rows:
                            e.transpose(
                                tp_bf[bank][:, jj * 128:jj * 128 + rows],
                                qt[b][:rows, j * 128:(j + 1) * 128],
                                c["ident_bf16"][:rows, :rows]),
                            waits=(c["waits"] + [(s_ld[b], ldn),
                                                 (p.ev, bank_ev[bank])]
                                   if jj == 0 else []),
                            inc=((s_t, 1) if jj == 7 else None))
                    tn = s_t.n
                    p.v(lambda e, r=r, b=b, bank=bank, rows=rows: e.tensor_copy(
                            tstage[b].ap().rearrange("pp (a q) -> pp a q", q=128)[:, r * 8:(r + 1) * 8, 0:rows],
                            tp_bf[bank].ap().rearrange("pp (a q) -> pp a q", q=128)[:, :, 0:rows]),
                        waits=[(s_t, tn)] + (st_by_slot[b] or []))
                    bank_ev[bank] = p.ev.n
                vn = p.ev.n
                tile_mm[ti] = s_t.n
                hs = nsub if nsub <= 8 else (nsub + 1) // 2
                for sh in range(2 if nsub > 8 else 1):
                    lo = sh * hs
                    hi = min(nsub, lo + hs)
                    sem_st = s_st4[b][sh]
                    p.y(lambda e, scr=scr, i=i, rows=rows, lo=lo, hi=hi, b=b: e.dma_start(
                            out=scr.rearrange("a pp ee -> pp a ee")[:, lo:hi, i * 128:i * 128 + rows],
                            in_=tstage[b].ap().rearrange("pp (a q) -> pp a q", q=128)[:, lo:hi, 0:rows]),
                        waits=[(p.ev, vn)], inc=(sem_st, 16))
                st_by_slot[b] = [(s_st4[b][0], s_st4[b][0].n),
                                 (s_st4[b][1], s_st4[b][1].n)]
                ti += 1
        done = ([(p.ev, p.ev.n), (s_t, s_t.n),
                 (s_ld[0], s_ld[0].n), (s_ld[1], s_ld[1].n)] +
                [(s_st4[i][j], s_st4[i][j].n) for i in range(2) for j in range(2)])
        p.barrier(done)
    return done


MAMBA_LIMIT = None


def build_mamba(p, c, Xin, Xout, scr_inT, scr_outT, conv_w, conv_b, dt_bias,
                A_log, Dp, norm_w, ln_w, ln_b, pre_waits=(), dbg=None):
    lim = MAMBA_LIMIT or (NSEQ, NTT, TT // TC, NHEADS)

    def dump(name, tile):
        if dbg is None or name not in dbg:
            return
        p.barrier([])
        p.y(lambda e: e.dma_start(out=dbg[name], in_=tile), inc=(s_xl, 16))
        p.barrier([(s_xl, s_xl.n)])
    """x1 = LN(x + MambaBlock(x)). MambaBlock = postLN(gamma,beta) of
    (out_proj(gated-RMS(SSD(conv(in_proj x)))) + x)."""
    NCC = CONV_DIM // 128
    NZC = D_INNER // 128
    NEC = D_IN_PROJ // 128
    s_w = p.sem("mm_w")
    s_wa = [p.sem(f"mm_wa{i}") for i in range(2)]
    s_xl = p.sem("mm_xl")
    s_xs = p.sem("mm_xs")
    s_t = p.sem("mm_t")
    s_fl = p.sem("mm_fl")
    s_rb = p.sem("mm_rb")
    s_rb2 = [p.sem(f"mm_rb2{i}") for i in range(2)]
    lam_flat_dram = p.nc.dram_tensor("lam_flat_dram", [1, 4 * NHEADS * TC], F32,
                                     kind="Internal").ap()
    rbc_cons = {0: 0, 1: 0}
    wA = [p.sbuf(f"mm_wA{i}", [128, DCH * 128], BF16) for i in range(2)]
    z_sil = p.sbuf("mm_zsil", [128, NZC * TT], BF16)
    xbc = p.sbuf("mm_xbc", [128, NCC * (TT + 3)], BF16)
    xssm = p.sbuf("mm_xssm", [128, NZC * TT], BF16)
    bcb = p.sbuf("mm_bcb", [128, 2 * TT], BF16)
    y_fm = p.sbuf("mm_yfm", [128, NZC * TT], BF16)
    xe = p.sbuf("mm_xe", [128, DCH * TT], F32)
    xb1 = p.sbuf("mm_xb1", [128, DCH * TT], F32)
    X_tm = p.sbuf("mm_xtm", [128, D_INNER], BF16)
    Y_tm = p.sbuf("mm_ytm", [128, D_INNER], BF16)
    xt_all = p.sbuf("mm_xtall", [128, D_INNER], BF16)
    H_f = p.sbuf("mm_hf", [128, D_INNER], F32)
    H_b = p.sbuf("mm_hb", [128, D_INNER], BF16)
    B_tm = p.sbuf("mm_btm", [128, 128], BF16)
    mt_half = p.sbuf("mm_mthalf", [128, 2048], BF16)
    lam_rbc = [p.sbuf(f"mm_lamrbc{i}", [128, 2048], F32) for i in range(2)]
    ytmp4 = p.sbuf("mm_ytmp4", [128, 256], F32)
    dt_tm_bf = p.sbuf("mm_dttmb", [128, 128], BF16)
    dtr = p.sbuf("mm_dtr", [32, TT], F32)
    dt_fm = p.sbuf("mm_dtfm", [32, TT], F32)
    r_scr = p.sbuf("mm_rscr", [32, TT], F32)
    a_scr = p.sbuf("mm_ascr", [32, TT], F32)
    lam_fm = p.sbuf("mm_lam", [32, TT], F32)
    lam_loc_all = p.sbuf("mm_lamloc", [32, TT], F32)
    zeros32 = p.sbuf("mm_z32", [32, TT], F32)
    carry = p.sbuf("mm_carry", [32, 2], F32)
    lam_tm_all = p.sbuf("mm_lamtm", [128, 128], F32)
    dt_tm_all = p.sbuf("mm_dttm", [128, 128], F32)
    expl_all = p.sbuf("mm_expltm", [128, 128], F32)
    dlbc_all = p.sbuf("mm_dlbc", [128, 128], F32)
    decs_all = p.sbuf("mm_decs", [128, 128], F32)
    w_all4 = p.sbuf("mm_wall", [128, 128], F32)
    wdt_all4 = p.sbuf("mm_wdt", [128, 128], BF16)
    dlrow4 = p.sbuf("mm_dlr", [1, 128], F32)
    dl_cols = p.sbuf("mm_dlcols", [32, 4], F32)
    cwcols = p.sbuf("mm_cw", [128, NCC * 4], F32)
    cbcols = p.sbuf("mm_cb", [128, NCC], F32)
    dtb_col = p.sbuf("mm_dtb", [32, 1], F32)
    nega_col = p.sbuf("mm_negA", [32, 1], F32)
    d_bc = p.sbuf("mm_dbc", [128, 32], BF16)
    nw_cols = p.sbuf("mm_nw", [128, NZC], F32)
    lnw_cols = p.sbuf("mm_lnw", [128, DCH], F32)
    lnb_cols = p.sbuf("mm_lnb", [128, DCH], F32)
    rn_bc = p.sbuf("mm_rnbc", [128, TT], BF16)
    stg_a = [p.sbuf(f"mm_sa{i}", [128, TT], BF16) for i in range(2)]
    stg_b = [p.sbuf(f"mm_sb{i}", [128, TT], BF16) for i in range(2)]
    t_scr = p.sbuf("mm_t0", [128, TT], F32)
    t_scr_g = p.sbuf("mm_t0g", [128, TT], F32)
    tmp_g = p.sbuf("mm_tmpg", [128, TT], F32)
    xbb = p.sbuf("mm_xbb", [128, DCH * TT], BF16)
    pbS = [p.psum(f"mm_pb{i}", [128, TT], F32) for i in range(7)]
    G_ps = pbS[0]
    R_ps = [pbS[1], pbS[2]]
    A_ps = [pbS[3], pbS[5]]
    B_ps = [pbS[4], pbS[6]]
    TP_ps = pbS[5]
    tp_bf = p.psum("mm_tpb", [128, 1024], BF16)
    L = make_lnctx(p, c, "mm", pbS[2], pbS[3], pbS[4], pbS[5], s_t)

    first_w = list(pre_waits)
    for a in range(NCC):
        p.y(lambda e, a=a: e.dma_start(out=cwcols[:, a * 4:(a + 1) * 4],
                                       in_=conv_w[a * 128:(a + 1) * 128, :]),
            waits=(first_w if a == 0 else []), inc=(s_w, 16))
        p.y(lambda e, a=a: e.dma_start(
                out=cbcols[:, a:a + 1],
                in_=conv_b[a * 128:(a + 1) * 128].rearrange("(x b) -> x b", b=1)),
            inc=(s_w, 16))
    p.y(lambda e: e.dma_start(out=dtb_col[:, :],
                              in_=dt_bias.rearrange("(x b) -> x b", b=1)),
        inc=(s_w, 16))
    p.y(lambda e: e.dma_start(out=zeros32[0:1, 0:32],
                              in_=A_log.rearrange("(x b) -> b x", b=1)),
        inc=(s_w, 16))
    p.y(lambda e: e.dma_start(out=zeros32[0:1, 32:64],
                              in_=Dp.rearrange("(x b) -> b x", b=1)),
        inc=(s_w, 16))
    for src, dst, n in [(norm_w, nw_cols, NZC), (ln_w, lnw_cols, DCH),
                        (ln_b, lnb_cols, DCH)]:
        p.y(lambda e, src=src, n=n: e.dma_start(
                out=zeros32[0:n, 64:192],
                in_=src.rearrange("(a b) -> a b", a=n)),
            waits=[(s_t, s_t.n)], inc=(s_w, 16))
        wn0 = s_w.n
        p.t(lambda e, n=n: e.transpose(TP_ps[:, 0:n], zeros32[0:n, 64:192],
                                       c["ident_f32"][:n, :n]),
            waits=c["waits"] + [(s_w, wn0), (p.ev, p.ev.n)], inc=(s_t, 1))
        p.v(lambda e, dst=dst, n=n: e.tensor_copy(dst[:, 0:n], TP_ps[:, 0:n]),
            waits=[(s_t, s_t.n)])
    wn0 = s_w.n
    p.s(lambda e: e.activation(zeros32[0:1, 192:224], zeros32[0:1, 0:32], AF.Exp),
        waits=[(s_w, wn0)])
    an = p.ea.n
    p.t(lambda e: e.transpose(TP_ps[0:32, 0:1], zeros32[0:1, 192:224],
                              c["ident_f32"][:1, :1]),
        waits=[(p.ea, an), (p.ev, p.ev.n)], inc=(s_t, 1))
    p.v(lambda e: e.tensor_scalar(nega_col[:, :], TP_ps[0:32, 0:1], -1.0, None,
                                  ALU.mult),
        waits=[(s_t, s_t.n)])
    p.t(lambda e: e.matmul(TP_ps[:, 64:96], c["ones_f32_row"][:, :],
                           zeros32[0:1, 32:64], start=True, stop=True),
        waits=[(s_w, wn0), (p.ev, p.ev.n)], inc=(s_t, 1))
    p.v(lambda e: e.tensor_copy(d_bc[:, :], TP_ps[:, 64:96]),
        waits=[(s_t, s_t.n)])
    p.v(lambda e: e.memset(zeros32[:, :], 0.0))
    zero_vn = p.ev.n

    for s in range(lim[0]):
        p.g(lambda e: e.memset(H_f[:, :], 0.0),
            waits=[(s_t, s_t.n), (p.ev, p.ev.n), (p.ea, p.ea.n)])
        p.g(lambda e: e.memset(H_b[:, :], 0.0))
        p.g(lambda e: e.memset(carry[:, :], 0.0))
        p.g(lambda e: e.memset(
                xbc.ap().rearrange("p (a w) -> p a w", a=NCC)[:, :, 0:3], 0.0))
        init_gn = p.eg.n
        for tt in range(lim[1]):
            tsl = slice(tt * TT, (tt + 1) * TT)
            for dk in range(DCH):
                p.y(lambda e, s=s, dk=dk, tsl=tsl: e.dma_start(
                        out=xe[:, dk * TT:(dk + 1) * TT], in_=Xin[s, dk, :, tsl]),
                    waits=([(s_xs, s_xs.n), (p.ev, p.ev.n), (p.eg, p.eg.n),
                            (p.ea, p.ea.n), (s_t, s_t.n)] if dk == 0 else []),
                    inc=(s_xl, 16))
            xln = s_xl.n
            for dk in range(DCH):
                p.v(lambda e, dk=dk: e.tensor_copy(xbb[:, dk * TT:(dk + 1) * TT],
                                                   xe[:, dk * TT:(dk + 1) * TT]),
                    waits=([(s_xl, xln), (s_t, s_t.n), (p.eg, init_gn)]
                           if dk == 0 else []))
            cast_vn = p.ev.n
            evac_cnt = {}
            mm_cnt = {}
            for ech in range(NEC + 1):
                ecols = 128 if ech < NEC else D_IN_PROJ - NEC * 128
                b = ech % 2
                p.y(lambda e, ech=ech, ecols=ecols, b=b: e.dma_start(
                        out=wA[b].ap().rearrange("p (a ee) -> p a ee", a=DCH)[:, :, 0:ecols],
                        in_=scr_inT.rearrange("a pp ee -> pp a ee")[:, :, ech * 128:ech * 128 + ecols]),
                    waits=[(s_t, mm_cnt.get(ech - 2, 0))], inc=(s_wa[b], 16))
                wan = s_wa[b].n
                for dk in range(DCH):
                    p.t(lambda e, ech=ech, ecols=ecols, b=b, dk=dk: e.matmul(
                            pbS[b][0:ecols, :],
                            wA[b][:, dk * 128:dk * 128 + ecols],
                            xbb[:, dk * TT:(dk + 1) * TT],
                            start=(dk == 0), stop=(dk == DCH - 1)),
                        waits=([(s_wa[b], wan), (p.ev, cast_vn)] +
                               ([evac_cnt[ech - 2]] if ech - 2 in evac_cnt else [])
                               if dk == 0 else []),
                        inc=((s_t, 1) if dk == DCH - 1 else None))
                tn = s_t.n
                mm_cnt[ech] = tn
                if ech < NZC:
                    p.s(lambda e, ech=ech: e.activation(
                            z_sil[:, ech * TT:(ech + 1) * TT], pbS[ech % 2][:, :],
                            AF.Silu),
                        waits=[(s_t, tn)])
                    evac_cnt[ech] = (p.ea, p.ea.n)
                elif ech < NZC + NCC:
                    a = ech - NZC
                    p.s(lambda e, a=a, ech=ech: e.activation(
                            xbc[:, a * (TT + 3) + 3:(a + 1) * (TT + 3)],
                            pbS[ech % 2][:, :], AF.Copy),
                        waits=[(s_t, tn)])
                    evac_cnt[ech] = (p.ea, p.ea.n)
                else:
                    p.v(lambda e, ech=ech: e.tensor_scalar(
                            dtr[:, :], pbS[ech % 2][0:32, :], dtb_col[:, 0:1],
                            None, ALU.add),
                        waits=[(s_t, tn)])
                    evac_cnt[ech] = (p.ev, p.ev.n)
            vn0 = p.ev.n
            p.s(lambda e: e.activation(r_scr[:, :], dtr[:, :], AF.Relu),
                waits=[(p.ev, vn0)])
            p.s(lambda e: e.activation(a_scr[:, :], dtr[:, :], AF.Abs))
            p.s(lambda e: e.activation(a_scr[:, :], a_scr[:, :], AF.Exp,
                                       scale=-1.0))
            p.s(lambda e: e.activation(a_scr[:, :], a_scr[:, :], AF.Ln,
                                       bias=c["ones_f32_col"][0:32, 0:1]))
            an = p.ea.n
            p.v(lambda e: e.tensor_add(dt_fm[:, :], r_scr[:, :], a_scr[:, :]),
                waits=[(p.ea, an)])
            p.v(lambda e: e.tensor_scalar(r_scr[:, :], dt_fm[:, :],
                                          nega_col[:, 0:1], None, ALU.mult))
            p.v(lambda e: e.tensor_copy(carry[:, 1:2], carry[:, 0:1]))
            p.v(lambda e: e.tensor_tensor_scan(lam_fm[:, :], r_scr[:, :],
                                               zeros32[:, :], carry[:, 0:1],
                                               ALU.add, ALU.add),
                waits=[(p.ev, zero_vn)])
            p.v(lambda e: e.tensor_copy(carry[:, 0:1], lam_fm[:, TT - 1:TT]))
            dt_vn = p.ev.n
            conv_first = True
            conv_sil = {}
            for a in range(NCC):
                base = a * (TT + 3)
                use_v = True
                acc = t_scr if (a % 2 == 0) else t_scr_g
                em = p.v
                if conv_first:
                    w0 = [(s_w, wn0), (s_t, s_t.n), (p.ea, p.ea.n)]
                    conv_first = False
                else:
                    w0 = [(p.ea, conv_sil.get(a - 2, 0))]
                em(lambda e, base=base, a=a, acc=acc: e.tensor_scalar(
                        acc[:, :], xbc[:, base:base + TT],
                        cwcols[:, 4 * a:4 * a + 1], None, ALU.mult),
                   waits=w0 + [evac_cnt[NZC + a]])
                for k2 in range(1, 4):
                    if use_v:
                        em(lambda e, base=base, a=a, k2=k2, acc=acc:
                           e.scalar_tensor_tensor(
                                acc[:, :], xbc[:, base + k2:base + k2 + TT],
                                cwcols[:, 4 * a + k2:4 * a + k2 + 1], acc[:, :],
                                ALU.mult, ALU.add))
                    else:
                        em(lambda e, base=base, a=a, k2=k2: e.tensor_scalar(
                                tmp_g[:, :], xbc[:, base + k2:base + k2 + TT],
                                cwcols[:, 4 * a + k2:4 * a + k2 + 1], None,
                                ALU.mult))
                        em(lambda e, acc=acc: e.tensor_add(acc[:, :], acc[:, :],
                                                           tmp_g[:, :]))
                ecnt = (p.ev, p.ev.n) if use_v else (p.eg, p.eg.n)
                if a < NZC:
                    p.s(lambda e, a=a, acc=acc: e.activation(
                            xssm[:, a * TT:(a + 1) * TT], acc[:, :], AF.Silu,
                            bias=cbcols[:, a:a + 1]),
                        waits=[ecnt, (s_t, s_t.n)])
                else:
                    p.s(lambda e, a=a, acc=acc: e.activation(
                            bcb[:, (a - NZC) * TT:(a - NZC + 1) * TT], acc[:, :],
                            AF.Silu, bias=cbcols[:, a:a + 1]),
                        waits=[ecnt, (s_t, s_t.n)])
                conv_sil[a] = p.ea.n
            conv_an = p.ea.n
            p.v(lambda e: e.tensor_copy(
                    xbc.ap().rearrange("p (a w) -> p a w", a=NCC)[:, :, 0:3],
                    xbc.ap().rearrange("p (a w) -> p a w", a=NCC)[:, :, TT:TT + 3]),
                waits=[(p.ea, conv_an), (p.eg, p.eg.n)])
            if s == 0 and tt == 0:
                dump("z_sil", z_sil[:, :])
                dump("xssm", xssm[:, :])
                dump("bcb", bcb[:, :])
                dump("dt_fm", dt_fm[:, :])
                dump("lam_fm", lam_fm[:, :])
            for q in range(4):
                if q == 0:
                    p.v(lambda e: e.tensor_scalar(
                            lam_loc_all[:, 0:TC], lam_fm[:, 0:TC],
                            carry[:, 1:2], None, ALU.subtract),
                        waits=[(s_t, s_t.n)])
                else:
                    p.v(lambda e, q=q: e.tensor_scalar(
                            lam_loc_all[:, q * TC:(q + 1) * TC],
                            lam_fm[:, q * TC:(q + 1) * TC],
                            lam_fm[:, q * TC - 1:q * TC], None, ALU.subtract))
                p.v(lambda e, q=q: e.tensor_copy(dl_cols[:, q:q + 1],
                                                 lam_loc_all[:, (q + 1) * TC - 1:(q + 1) * TC]))
            lamall_vn = p.ev.n
            for q in range(4):
                p.y(lambda e, q=q: e.dma_start(
                        out=lam_flat_dram.rearrange("o (qq a b) -> o qq a b", qq=4, a=32)[:, q, :, :],
                        in_=lam_loc_all[:, q * TC:(q + 1) * TC]),
                    waits=([(p.ev, lamall_vn), (s_rb2[0], s_rb2[0].n),
                            (s_rb2[1], s_rb2[1].n)] if q == 0 else []),
                    inc=(s_fl, 16))
            fln = s_fl.n
            for q in range(4):
                p.t(lambda e, q=q: e.transpose(pbS[5][:, q * 32:(q + 1) * 32],
                                               lam_loc_all[:, q * TC:(q + 1) * TC],
                                               c["ident_f32"][:32, :32]),
                    waits=([(p.ev, lamall_vn)] if q == 0 else []),
                    inc=((s_t, 1) if q == 3 else None))
            tn = s_t.n
            p.v(lambda e: e.tensor_copy(lam_tm_all[:, :], pbS[5][:, 0:128]),
                waits=[(s_t, tn)])
            for q in range(4):
                p.t(lambda e, q=q: e.transpose(pbS[5][:, 128 + q * 32:128 + (q + 1) * 32],
                                               dt_fm[:, q * TC:(q + 1) * TC],
                                               c["ident_f32"][:32, :32]),
                    waits=([(p.ev, p.ev.n)] if q == 0 else []),
                    inc=((s_t, 1) if q == 3 else None))
            tn = s_t.n
            p.v(lambda e: e.tensor_copy(dt_tm_all[:, :], pbS[5][:, 128:256]),
                waits=[(s_t, tn)])
            p.v(lambda e: e.tensor_copy(dt_tm_bf[:, :], dt_tm_all[:, :]))
            for q in range(4):
                p.t(lambda e, q=q: e.transpose(pbS[5][0:1, 256 + q * 32:256 + (q + 1) * 32],
                                               dl_cols[:, q:q + 1],
                                               c["ident_f32"][:32, :32]),
                    waits=([(p.ev, p.ev.n)] if q == 0 else []),
                    inc=((s_t, 1) if q == 3 else None))
            tn = s_t.n
            p.v(lambda e: e.tensor_copy(dlrow4[0:1, :], pbS[5][0:1, 256:384]),
                waits=[(s_t, tn)])
            vn = p.ev.n
            for q in range(4):
                p.t(lambda e, q=q: e.matmul(
                        pbS[5][:, (q % 2) * 32:(q % 2) * 32 + 32],
                        c["ones_f32_row"][:, :], dlrow4[0:1, q * 32:(q + 1) * 32],
                        start=True, stop=True),
                    waits=[(p.ev, vn if q == 0 else p.ev.n)],
                    inc=(s_t, 1))
                tnq = s_t.n
                p.v(lambda e, q=q: e.tensor_copy(dlbc_all[:, q * 32:(q + 1) * 32],
                                                 pbS[5][:, (q % 2) * 32:(q % 2) * 32 + 32]),
                    waits=[(s_t, tnq)])
            vn = p.ev.n
            p.s(lambda e: e.activation(expl_all[:, :], lam_tm_all[:, :], AF.Exp),
                waits=[(p.ev, vn)])
            p.s(lambda e: e.activation(decs_all[:, :], dlbc_all[:, :], AF.Exp))
            p.v(lambda e: e.tensor_sub(w_all4[:, :], dlbc_all[:, :],
                                       lam_tm_all[:, :]),
                waits=[(p.ea, p.ea.n)])
            vn = p.ev.n
            p.s(lambda e: e.activation(w_all4[:, :], w_all4[:, :], AF.Exp),
                waits=[(p.ev, vn)])
            an = p.ea.n
            p.v(lambda e: e.tensor_mul(wdt_all4[:, :], w_all4[:, :],
                                       dt_tm_all[:, :]),
                waits=[(p.ea, an)])
            prep_vn = p.ev.n
            prep_an = p.ea.n
            rb_n = {}
            for q in range(lim[2]):
                qof = q * NHEADS * TC
                ch = slice(q * TC, (q + 1) * TC)
                for r in range(2):
                    for j in range(8):
                        k = r * 8 + j
                        p.t(lambda e, k=k, j=j, q=q: e.transpose(
                                tp_bf[:, j * 128:(j + 1) * 128],
                                xssm[:, k * TT + q * TC:k * TT + (q + 1) * TC],
                                c["ident_bf16"][:, :]),
                            waits=([(p.ev, p.ev.n), (p.ea, p.ea.n)] if j == 0 else []),
                            inc=((s_t, 1) if j == 7 else None))
                    tn = s_t.n
                    p.v(lambda e, r=r: e.tensor_copy(
                            X_tm[:, r * 1024:(r + 1) * 1024], tp_bf[:, :]),
                        waits=[(s_t, tn)])
                p.t(lambda e, q=q: e.transpose(
                        tp_bf[:, 0:128], bcb[:, q * TC:(q + 1) * TC],
                        c["ident_bf16"][:, :]),
                    waits=[(p.ev, p.ev.n)], inc=(s_t, 1))
                tn = s_t.n
                p.v(lambda e: e.tensor_copy(B_tm[:, :], tp_bf[:, 0:128]),
                    waits=[(s_t, tn)])
                xtm_vn = p.ev.n
                p.t(lambda e, q=q: e.matmul(
                        G_ps[:, 0:128], bcb[:, q * TC:(q + 1) * TC],
                        bcb[:, TT + q * TC:TT + (q + 1) * TC],
                        start=True, stop=True),
                    waits=[(p.ev, xtm_vn)], inc=(s_t, 1))
                g_tn = s_t.n
                p.v(lambda e, q=q: e.tensor_tensor(
                        xt_all[:, :], X_tm[:, :],
                        bass.AP(dt_tm_bf.ap().tensor, q * 32,
                                [[128, 128], [1, 32], [0, 64]]),
                        ALU.mult))
                xt3_vn = p.ev.n
                ev4 = {}
                for half in range(2):
                    hb2 = (2 * q + half) % 2
                    if (q, half) not in rb_n:
                        for ah in range(2 * q + half, min(8, 2 * q + half + 2)):
                            aq, ahf = divmod(ah, 2)
                            ab = ah % 2
                            p.y(lambda e, aq=aq, ahf=ahf, ab=ab: e.dma_start(
                                    out=lam_rbc[ab][:, :],
                                    in_=bass.AP(lam_flat_dram.tensor,
                                                aq * NHEADS * TC + ahf * 2048,
                                                [[0, 128], [1, 2048]])),
                                waits=[(s_fl, fln),
                                       (p.ev, rbc_cons[ab])],
                                inc=(s_rb2[ab], 16))
                            rb_n[(aq, ahf)] = s_rb2[ab].n
                    import os as _os
                    _dqh = _os.environ.get("MAMBA_DQH")
                    _hit = (_dqh is not None and dbg is not None and s == 0
                            and tt == 0 and (q, half) ==
                            tuple(int(z) for z in _dqh.split(",")))
                    if _hit:
                        p.barrier([(s_rb2[hb2], rb_n[(q, half)])])
                        p.y(lambda e, hb2=hb2: e.dma_start(
                                out=dbg["rbc_raw"], in_=lam_rbc[hb2][:, :]),
                            inc=(s_xl, 16))
                        p.barrier([(s_xl, s_xl.n)])
                    p.v(lambda e, q=q, half=half, hb2=hb2: e.tensor_tensor(
                            lam_rbc[hb2][:, :], lam_rbc[hb2][:, :],
                            bass.AP(lam_tm_all.ap().tensor, q * 32 + half * 16,
                                    [[128, 128], [1, 16], [0, 128]]),
                            ALU.subtract),
                        waits=[(s_rb2[hb2], rb_n[(q, half)])])
                    if _hit:
                        dump("rbc_t1", lam_rbc[hb2][:, :])
                    p.v(lambda e, hb2=hb2: e.tensor_tensor(
                            lam_rbc[hb2][:, :], lam_rbc[hb2][:, :],
                            bass.AP(c["cmt"].ap().tensor, 0,
                                    [[128, 128], [0, 16], [1, 128]]),
                            ALU.subtract))
                    vn2 = p.ev.n
                    if _hit:
                        dump("rbc_t2", lam_rbc[hb2][:, :])
                        dump("lam_tm_all", lam_tm_all[:, :])
                        dump("lam_loc_all", lam_loc_all[:, :])
                    p.s(lambda e, hb2=hb2: e.activation(
                            lam_rbc[hb2][:, :], lam_rbc[hb2][:, :], AF.Exp),
                        waits=[(p.ev, vn2)])
                    an = p.ea.n
                    p.v(lambda e, hb2=hb2: e.tensor_tensor(
                            mt_half[:, :], lam_rbc[hb2][:, :],
                            bass.AP(G_ps.ap().tensor, 0,
                                    [[TT, 128], [0, 16], [1, 128]]),
                            ALU.mult),
                        waits=[(p.ea, an), (s_t, s_t.n)])
                    rbc_cons[hb2] = p.ev.n
                    mt_vn = p.ev.n
                    for grp in range(4):
                        gb2 = grp % 2
                        for hh in range(4):
                            h = half * 16 + grp * 4 + hh
                            hl = grp * 4 + hh
                            p.t(lambda e, h=h, hl=hl, gb2=gb2, hh=hh: e.matmul(
                                    A_ps[gb2][:, hh * 64:(hh + 1) * 64],
                                    mt_half[:, hl * 128:(hl + 1) * 128],
                                    xt_all[:, h * 64:(h + 1) * 64],
                                    start=True, stop=True),
                                waits=([(p.ev, mt_vn), (s_t, s_t.n)]
                                       if hh == 0 and grp < 2 else
                                       ([(p.ev, ev4.get((half, grp - 2), 0))]
                                        if hh == 0 else [])),
                                inc=None)
                            p.t(lambda e, h=h, gb2=gb2, hh=hh, q=q: e.matmul(
                                    B_ps[gb2][:, hh * 64:(hh + 1) * 64],
                                    bcb[:, TT + q * TC:TT + (q + 1) * TC],
                                    H_b[:, h * 64:(h + 1) * 64],
                                    start=True, stop=True),
                                inc=((s_t, 1) if hh == 3 else None))
                        tn = s_t.n
                        h0 = half * 16 + grp * 4
                        p.v(lambda e, h0=h0, gb2=gb2, q=q: e.tensor_tensor(
                                ytmp4[:, :], B_ps[gb2][:, 0:256],
                                bass.AP(expl_all.ap().tensor, q * 32 + h0 - (h0 // 32) * 32 if False else q * 32 + (h0 % 32),
                                        [[128, 128], [1, 4], [0, 64]]),
                                ALU.mult),
                            waits=[(s_t, tn)])
                        p.v(lambda e, h0=h0, gb2=gb2: e.tensor_add(
                                Y_tm[:, h0 * 64:(h0 + 4) * 64],
                                A_ps[gb2][:, 0:256], ytmp4[:, :]))
                        ev4[(half, grp)] = p.ev.n
                skip_add_gn = {}
                for part in range(4):
                    dview = bass.AP(d_bc.ap().tensor, part * 8,
                                    [[32, 128], [1, 8], [0, 64]])
                    p.v(lambda e, part=part, dview=dview: e.tensor_tensor(
                            stg_b[1][:, :TT] if part % 2 == 0 else stg_a[1][:, :TT],
                            X_tm[:, part * 512:part * 512 + 512],
                            dview, ALU.mult),
                        waits=[(p.eg, skip_add_gn.get(part - 2, p.eg.n if part < 2 else 0))])
                    vnp = p.ev.n
                    p.g(lambda e, part=part: e.tensor_add(
                            Y_tm[:, part * 512:part * 512 + 512],
                            Y_tm[:, part * 512:part * 512 + 512],
                            stg_b[1][:, :TT] if part % 2 == 0 else stg_a[1][:, :TT]),
                        waits=[(p.ev, vnp)])
                    skip_add_gn[part] = p.eg.n
                skip_gn = p.eg.n
                p.v(lambda e, q=q: e.tensor_tensor(
                        xt_all[:, :], X_tm[:, :],
                        bass.AP(wdt_all4.ap().tensor, q * 32,
                                [[128, 128], [1, 32], [0, 64]]),
                        ALU.mult),
                    waits=[(s_t, s_t.n)])
                xt_vn = p.ev.n
                for part in range(4):
                    pb = A_ps[part % 2]
                    p.t(lambda e, part=part, pb=pb: e.matmul(
                            pb[:, :], B_tm[:, :],
                            xt_all[:, part * 512:(part + 1) * 512],
                            start=True, stop=True),
                        waits=[(p.ev, xt_vn if part < 2 else p.ev.n)],
                        inc=(s_t, 1))
                    tn = s_t.n
                    dcv = bass.AP(decs_all.ap().tensor, q * 32 + part * 8,
                                  [[128, 128], [1, 8], [0, 64]])
                    p.v(lambda e, part=part, dcv=dcv: e.tensor_tensor(
                            t_scr[:, :], H_f[:, part * 512:(part + 1) * 512],
                            dcv, ALU.mult),
                        waits=[(s_t, tn)])
                    p.v(lambda e, part=part, pb=pb: e.tensor_add(
                            H_f[:, part * 512:(part + 1) * 512], t_scr[:, :],
                            pb[:, :]))
                p.v(lambda e: e.tensor_copy(H_b[:, :], H_f[:, :]))
                for r in range(2):
                    vn = p.ev.n
                    for j in range(8):
                        k = r * 8 + j
                        p.t(lambda e, k=k, j=j: e.transpose(
                                tp_bf[:, j * 128:(j + 1) * 128],
                                Y_tm[:, k * 128:(k + 1) * 128],
                                c["ident_bf16"][:, :]),
                            waits=([(p.ev, vn), (p.eg, skip_gn)] if j == 0 else []),
                            inc=((s_t, 1) if j == 7 else None))
                    tn = s_t.n
                    p.v(lambda e, r=r, q=q: e.tensor_copy(
                            y_fm.ap().rearrange("p (k t) -> p k t", k=NZC)
                                [:, r * 8:(r + 1) * 8, q * TC:(q + 1) * TC],
                            tp_bf.ap().rearrange("p (j t) -> p j t", j=8)[:, :, :]),
                        waits=[(s_t, tn)])
            gst_vn = p.ev.n
            for k in range(NZC):
                p.v(lambda e, k=k: e.tensor_mul(
                        y_fm[:, k * TT:(k + 1) * TT], y_fm[:, k * TT:(k + 1) * TT],
                        z_sil[:, k * TT:(k + 1) * TT]),
                    waits=([(p.ea, p.ea.n)] if k == 0 else []))
            for k in range(NZC):
                p.s(lambda e, k=k: e.activation(stg_b[k % 2][:, :],
                                                y_fm[:, k * TT:(k + 1) * TT],
                                                AF.Square),
                    waits=[(p.ev, p.ev.n), (s_t, s_t.n)] if k == 0 else
                          [(s_t, s_t.n)])
                an = p.ea.n
                p.t(lambda e, k=k: e.matmul(
                        pbS[6][0:1, :], c["ones_bf16_col"][:, :],
                        stg_b[k % 2][:, :],
                        start=(k == 0), stop=(k == NZC - 1)),
                    waits=[(p.ea, an)] + ([(p.ev, p.ev.n)] if k == 0 else []),
                    inc=(s_t, 1))
            tn = s_t.n
            p.v(lambda e: e.tensor_scalar(L["rowbuf"][0:1, LNR["mu"]],
                                          pbS[6][0:1, :], 1.0 / D_INNER, None,
                                          ALU.mult),
                waits=[(s_t, tn)])
            vn = p.ev.n
            p.s(lambda e: e.activation(L["rowbuf"][0:1, LNR["sd"]],
                                       L["rowbuf"][0:1, LNR["mu"]], AF.Sqrt,
                                       bias=c["eps_col"][0:1, :]),
                waits=[(p.ev, vn)])
            an = p.ea.n
            p.v(lambda e: e.reciprocal(L["rowbuf"][0:1, LNR["rs"]],
                                       L["rowbuf"][0:1, LNR["sd"]]),
                waits=[(p.ea, an)])
            vn = p.ev.n
            p.t(lambda e: e.matmul(pbS[4][:, :], c["ones_f32_row"][:, :],
                                   L["rowbuf"][0:1, LNR["rs"]],
                                   start=True, stop=True),
                waits=[(p.ev, vn)], inc=(s_t, 1))
            tn = s_t.n
            p.v(lambda e: e.tensor_copy(rn_bc[:, :], pbS[4][:, :]),
                waits=[(s_t, tn)])
            for k in range(NZC):
                p.v(lambda e, k=k: e.scalar_tensor_tensor(
                        y_fm[:, k * TT:(k + 1) * TT],
                        y_fm[:, k * TT:(k + 1) * TT], nw_cols[:, k:k + 1],
                        rn_bc[:, :], ALU.mult, ALU.mult))
            gfin_vn = p.ev.n
            if s == 0 and tt == 0:
                dump("y_fm", y_fm[:, :])
            wo_mm = {}
            pass_ready_vn = gfin_vn
            for pa in range(2):
                for hk in range(NZC):
                    b = hk % 2
                    p.y(lambda e, hk=hk, b=b: e.dma_start(
                            out=wA[b][:, :],
                            in_=scr_outT[hk, :, :]),
                        waits=[(s_t, wo_mm.get((pa, hk - 2),
                                wo_mm.get((pa - 1, NZC - 2 + hk), mm_cnt.get(NEC, 0))))],
                        inc=(s_wa[b], 16))
                    wan = s_wa[b].n
                    for dd in range(4):
                        dk = pa * 4 + dd
                        p.t(lambda e, hk=hk, b=b, dd=dd, dk=dk: e.matmul(
                                pbS[dd][:, :],
                                wA[b][:, dk * 128:(dk + 1) * 128],
                                y_fm[:, hk * TT:(hk + 1) * TT],
                                start=(hk == 0), stop=(hk == NZC - 1)),
                            waits=([(s_wa[b], wan), (p.ev, pass_ready_vn)]
                                   if dd == 0 and hk == 0 else
                                   ([(s_wa[b], wan)] if dd == 0 else [])),
                            inc=((s_t, 1) if dd == 3 else None))
                    wo_mm[(pa, hk)] = s_t.n
                tn = s_t.n
                for dd in range(4):
                    dk = pa * 4 + dd
                    p.v(lambda e, dd=dd, dk=dk: e.tensor_add(
                            xb1[:, dk * TT:(dk + 1) * TT], pbS[dd][:, :],
                            xe[:, dk * TT:(dk + 1) * TT]),
                        waits=([(s_t, tn), (s_xl, xln), (p.eg, p.eg.n),
                                (s_xs, s_xs.n)] if dd == 0 else []))
                pass_ready_vn = p.ev.n
            pre1_vn = p.ev.n
            st_mm_n = {}
            for dk in range(DCH):
                p.g(lambda e, dk=dk: e.tensor_copy(stg_a[dk % 2][:, :],
                                                   xb1[:, dk * TT:(dk + 1) * TT]),
                    waits=([(p.ev, pre1_vn)] if dk == 0 else []) +
                          [(s_t, st_mm_n.get(dk - 2, s_t.n if dk == 0 else 0))])
                p.g(lambda e, dk=dk: e.tensor_mul(stg_b[dk % 2][:, :],
                                                  xb1[:, dk * TT:(dk + 1) * TT],
                                                  xb1[:, dk * TT:(dk + 1) * TT]))
                gn = p.eg.n
                p.t(lambda e, dk=dk: e.matmul(
                        L["st1"][0:1, :], c["ones_bf16_col"][:, :],
                        stg_a[dk % 2][:, :],
                        start=(dk == 0), stop=(dk == DCH - 1)),
                    waits=[(p.eg, gn)] + ([(p.ev, p.ev.n)] if dk == 0 else []),
                    inc=(s_t, 1))
                p.t(lambda e, dk=dk: e.matmul(
                        L["st2"][0:1, :], c["ones_bf16_col"][:, :],
                        stg_b[dk % 2][:, :],
                        start=(dk == 0), stop=(dk == DCH - 1)),
                    inc=(s_t, 1))
                st_mm_n[dk] = s_t.n
            ln_rows_and_bc(p, c, L, D, [(s_t, s_t.n)])
            for dk in range(DCH):
                p.v(lambda e, dk=dk: e.tensor_mul(t_scr[:, :],
                                                  xb1[:, dk * TT:(dk + 1) * TT],
                                                  L["rsbc"][:, :]))
                p.v(lambda e, dk=dk: e.tensor_add(t_scr[:, :], t_scr[:, :],
                                                  L["nmrbc"][:, :]))
                lnbv = bass.AP(lnb_cols.ap().tensor, dk, [[DCH, 128], [0, TT]])
                p.v(lambda e, dk=dk, lnbv=lnbv: e.scalar_tensor_tensor(
                        t_scr[:, :], t_scr[:, :], lnw_cols[:, dk:dk + 1],
                        lnbv, ALU.mult, ALU.add))
                p.v(lambda e, dk=dk: e.tensor_add(
                        xb1[:, dk * TT:(dk + 1) * TT], t_scr[:, :],
                        xe[:, dk * TT:(dk + 1) * TT]))
            pre2_vn = p.ev.n
            emit_boundary(p, c, L, xb1, stg_a, stg_b, t_scr, t_scr_g, Xout, s,
                          tsl, [(p.ev, pre2_vn)], s_xs)
    return [(s_xs, s_xs.n)]



def build_attn(p, c, Xin, Xout, Wqk, Wv, Wo, tau, pre_waits=()):
    s_wA = p.sem("at_wA")
    s_wv = p.sem("at_wv")
    s_xl = p.sem("at_xl")
    s_xs = p.sem("at_xs")
    s_t = p.sem("at_t")
    wA = p.sbuf("at_wA", [128, DCH * 1024], BF16)
    wv = p.sbuf("at_wv", [128, DCH * 1024], BF16)
    xw = p.sbuf("at_xw", [128, DCH * T], BF16)
    qk_tm = p.sbuf("at_qk", [128, (T // 128) * 1024], BF16)
    v_fm = p.sbuf("at_v", [128, DCH * T], BF16)
    xe = p.sbuf("at_xe", [128, DCH * TT], F32)
    stg_q = [p.sbuf(f"at_sq{i}", [128, 1024], BF16) for i in range(2)]
    P_all = p.sbuf("at_P", [128, 512], BF16)
    lg_sb = p.sbuf("at_lg", [128, 512], F32)
    q2row = p.sbuf("at_q2r", [1, 1024], F32)
    q2rbc = p.sbuf("at_q2rbc", [128, 1024], F32)
    q2cols = p.sbuf("at_q2c", [128, 8], F32)
    tauS = p.sbuf("at_tau", [128, 16], F32)
    tau_row = p.sbuf("at_taur", [1, 16], F32)
    gnm_row = p.sbuf("at_gnm", [1, 16], F32)
    mxc = p.sbuf("at_mxc", [128, 1], F32)
    ngm_cols = p.sbuf("at_ngmc", [128, 16], F32)
    s_colsb = p.sbuf("at_scols", [128, 8], F32)
    r_cols = p.sbuf("at_rcols", [128, 8], F32)
    stg_a = [p.sbuf(f"at_sa{i}", [128, TT], BF16) for i in range(2)]
    stg_b = [p.sbuf(f"at_sb{i}", [128, TT], BF16) for i in range(2)]
    t_scr = p.sbuf("at_t0", [128, TT], F32)
    t_scr_g = p.sbuf("at_t0g", [128, TT], F32)
    pbA = [p.psum(f"at_pb{i}", [128, TT], F32) for i in range(8)]
    L = make_lnctx(p, c, "at", pbA[2], pbA[3], pbA[4], pbA[5], s_t)

    p.y(lambda e: e.dma_start(out=tau_row[:, :],
                              in_=tau.rearrange("a b c -> b (a c)")),
        waits=list(pre_waits), inc=(s_xl, 16))
    xln0 = s_xl.n
    p.v(lambda e: e.tensor_scalar(tau_row[:, :], tau_row[:, :],
                                  1.0 / math.sqrt(float(T)), None, ALU.mult),
        waits=[(s_xl, xln0)])
    vn = p.ev.n
    p.t(lambda e: e.matmul(pbA[6][:, 0:16], c["ones_f32_row"][:, :],
                           tau_row[0:1, :], start=True, stop=True),
        waits=c["waits"] + [(p.ev, vn)], inc=(s_t, 1))
    p.v(lambda e: e.tensor_copy(tauS[:, :], pbA[6][:, 0:16]),
        waits=[(s_t, s_t.n)])

    for s in range(NSEQ):
        wA_gen = s * 2
        for dk in range(DCH):
            p.gdma(lambda e, dk=dk: e.dma_start(
                    out=wA[:, dk * 1024:(dk + 1) * 1024],
                    in_=Wqk[dk * 128:(dk + 1) * 128, :]),
                waits=([(s_t, s_t.n)] if dk == 0 else []), inc=(s_wA, 16))
        wqk_n = s_wA.n
        for dk in range(DCH):
            p.gdma(lambda e, dk=dk: e.dma_start(
                    out=wv[:, dk * 1024:(dk + 1) * 1024],
                    in_=Wv[dk * 128:(dk + 1) * 128, :]),
                waits=([(s_t, s_t.n)] if dk == 0 else []), inc=(s_wv, 16))
        wv_n = s_wv.n
        for tt in range(NTT):
            for dk in range(DCH):
                p.y(lambda e, s=s, dk=dk, tt=tt: e.dma_start(
                        out=xe[:, dk * TT:(dk + 1) * TT],
                        in_=Xin[s, dk, :, tt * TT:(tt + 1) * TT]),
                    waits=([(s_xs, s_xs.n), (p.ev, p.ev.n), (p.eg, p.eg.n),
                            (s_t, s_t.n)] if dk == 0 else []),
                    inc=(s_xl, 16))
            xln = s_xl.n
            for dk in range(DCH):
                p.v(lambda e, dk=dk, tt=tt: e.tensor_copy(
                        xw[:, dk * T + tt * TT:dk * T + (tt + 1) * TT],
                        xe[:, dk * TT:(dk + 1) * TT]),
                    waits=([(s_xl, xln), (s_t, s_t.n)] if dk == 0 else []))
        cast_vn = p.ev.n
        evac_an = {}
        for i in range(T // 128):
            for eh in range(2):
                k = 2 * i + eh
                for dk in range(DCH):
                    p.t(lambda e, i=i, eh=eh, dk=dk, k=k: e.matmul(
                            pbA[k % 2][:, :],
                            xw[:, dk * T + i * 128:dk * T + (i + 1) * 128],
                            wA[:, dk * 1024 + eh * 512:dk * 1024 + (eh + 1) * 512],
                            start=(dk == 0), stop=(dk == DCH - 1)),
                        waits=([(s_wA, wqk_n), (p.ev, cast_vn),
                                (p.ea, evac_an.get(k - 2, 0))]
                               if dk == 0 else []),
                        inc=((s_t, 1) if dk == DCH - 1 else None))
                tn = s_t.n
                p.s(lambda e, i=i, eh=eh, k=k: e.activation(
                        qk_tm[:, i * 1024 + eh * 512:i * 1024 + (eh + 1) * 512],
                        pbA[k % 2][:, :], AF.Copy),
                    waits=[(s_t, tn)])
                evac_an[k] = p.ea.n
        qk_an = p.ea.n
        qk_tn = s_t.n
        for dk in range(DCH):
            p.gdma(lambda e, dk=dk: e.dma_start(
                    out=wA[:, dk * 1024:(dk + 1) * 1024],
                    in_=Wo[dk * 128:(dk + 1) * 128, :]),
                waits=([(s_t, qk_tn)] if dk == 0 else []), inc=(s_wA, 16))
        wo_n = s_wA.n
        evac_cn = {-2: qk_an, -1: qk_an}
        for tt in range(NTT):
            for ek in range(DCH):
                k = tt * DCH + ek
                for dk in range(DCH):
                    p.t(lambda e, tt=tt, ek=ek, dk=dk, k=k: e.matmul(
                            pbA[k % 2][:, :],
                            wv[:, dk * 1024 + ek * 128:dk * 1024 + (ek + 1) * 128],
                            xw[:, dk * T + tt * TT:dk * T + (tt + 1) * TT],
                            start=(dk == 0), stop=(dk == DCH - 1)),
                        waits=([(s_wv, wv_n), (p.ea, evac_cn[k - 2])]
                               if dk == 0 else []),
                        inc=((s_t, 1) if dk == DCH - 1 else None))
                tn = s_t.n
                p.s(lambda e, tt=tt, ek=ek, k=k: e.activation(
                        v_fm[:, ek * T + tt * TT:ek * T + (tt + 1) * TT],
                        pbA[k % 2][:, :], AF.Copy),
                    waits=[(s_t, tn)])
                evac_cn[k] = p.ea.n
        v_an = p.ea.n
        for i in range(T // 128):
            p.v(lambda e, i=i: e.tensor_mul(stg_q[i % 2][:, :],
                                            qk_tm[:, i * 1024:(i + 1) * 1024],
                                            qk_tm[:, i * 1024:(i + 1) * 1024]),
                waits=[(s_t, s_t.n), (p.ea, p.ea.n)])
            an = p.ev.n
            p.t(lambda e, i=i: e.matmul(
                    pbA[2][0:1, :], c["ones_bf16_col"][:, :],
                    stg_q[i % 2][:, 0:512],
                    start=(i == 0), stop=(i == T // 128 - 1)),
                waits=[(p.ev, an)],
                inc=(s_t, 1))
            p.t(lambda e, i=i: e.matmul(
                    pbA[3][0:1, :], c["ones_bf16_col"][:, :],
                    stg_q[i % 2][:, 512:1024],
                    start=(i == 0), stop=(i == T // 128 - 1)),
                inc=(s_t, 1))
        tn = s_t.n
        p.v(lambda e: e.tensor_copy(q2row[0:1, 0:512], pbA[2][0:1, :]),
            waits=[(s_t, tn)])
        p.v(lambda e: e.tensor_copy(q2row[0:1, 512:1024], pbA[3][0:1, :]))
        vn = p.ev.n
        for hh in range(2):
            p.t(lambda e, hh=hh: e.matmul(pbA[6][:, :], c["ones_f32_row"][:, :],
                                          q2row[0:1, hh * 512:(hh + 1) * 512],
                                          start=True, stop=True),
                waits=[(p.ev, p.ev.n)], inc=(s_t, 1))
            tn = s_t.n
            p.v(lambda e, hh=hh: e.tensor_copy(q2rbc[:, hh * 512:(hh + 1) * 512],
                                               pbA[6][:, :]),
                waits=[(s_t, tn)])
        for j in range(DCH):
            p.t(lambda e, j=j: e.transpose(pbA[7][0:128, j:j + 1],
                                           q2row[0:1, j * 128:(j + 1) * 128],
                                           c["ident_f32"][:1, :1]),
                waits=([(p.ev, p.ev.n)] if j == 0 else []), inc=(s_t, 1))
        tn = s_t.n
        p.v(lambda e: e.tensor_copy(q2cols[:, :], pbA[7][0:128, 0:8]),
            waits=[(s_t, tn)])
        for h in range(16):
            r0 = 64 * (h % 2)
            cp = h // 2
            for i in range(T // 128):
                p.t(lambda e, h=h, i=i, r0=r0, cp=cp: e.matmul(
                        pbA[2][r0:r0 + 64, cp * 64:(cp + 1) * 64],
                        qk_tm[:, i * 1024 + h * 64:i * 1024 + (h + 1) * 64],
                        qk_tm[:, i * 1024 + h * 64:i * 1024 + (h + 1) * 64],
                        start=(i == 0), stop=(i == T // 128 - 1)),
                    waits=([(p.ev, p.ev.n), (p.ea, v_an)]
                           if h == 0 and i == 0 else []),
                    inc=((s_t, 1) if i == T // 128 - 1 else None))
        dot_tn = s_t.n
        for h in range(16):
            r0 = 64 * (h % 2)
            cp = h // 2
            sl = (slice(r0, r0 + 64), slice(cp * 64, (cp + 1) * 64))
            p.v(lambda e, sl=sl: e.tensor_scalar(
                    lg_sb[sl[0], sl[1]], pbA[2][sl[0], sl[1]],
                    2.0, None, ALU.mult),
                waits=[(s_t, dot_tn)])
            p.v(lambda e, sl=sl, r0=r0, cp=cp: e.tensor_scalar(
                    lg_sb[sl[0], sl[1]], lg_sb[sl[0], sl[1]],
                    q2cols[r0:r0 + 64, cp:cp + 1], None, ALU.subtract))
            p.v(lambda e, sl=sl, h=h, r0=r0: e.tensor_sub(
                    lg_sb[sl[0], sl[1]], lg_sb[sl[0], sl[1]],
                    q2rbc[r0:r0 + 64, h * 64:(h + 1) * 64]))
            p.v(lambda e, sl=sl, r0=r0, h=h: e.tensor_scalar(
                    lg_sb[sl[0], sl[1]], lg_sb[sl[0], sl[1]],
                    tauS[r0:r0 + 64, h:h + 1], None, ALU.mult))
            p.v(lambda e, sl=sl, r0=r0: e.reduce_max(
                    mxc[r0:r0 + 64, 0:1], lg_sb[sl[0], sl[1]], AX.X))
            vn = p.ev.n
            p.t(lambda e, r0=r0: e.transpose(pbA[6][0:1, 0:64],
                                             mxc[r0:r0 + 64, 0:1],
                                             c["ident_f32"][r0:r0 + 64, r0:r0 + 64]),
                waits=[(p.ev, vn)], inc=(s_t, 1))
            tn = s_t.n
            p.v(lambda e, h=h: e.reduce_max(gnm_row[0:1, h:h + 1],
                                            pbA[6][0:1, 0:64], AX.X, negate=True),
                waits=[(s_t, tn)])
            vn = p.ev.n
            p.t(lambda e, h=h: e.matmul(
                    pbA[7][:, 0:1], c["ones_f32_row"][:, :],
                    gnm_row[0:1, h:h + 1], start=True, stop=True),
                waits=[(p.ev, vn)], inc=(s_t, 1))
            tn = s_t.n
            p.v(lambda e, h=h: e.tensor_copy(
                    ngm_cols[:, h:h + 1], pbA[7][:, 0:1]),
                waits=[(s_t, tn)])
            vn = p.ev.n
            p.s(lambda e, sl=sl, r0=r0, h=h: e.activation(
                    P_all[sl[0], sl[1]], lg_sb[sl[0], sl[1]], AF.Exp,
                    bias=ngm_cols[r0:r0 + 64, h:h + 1]),
                waits=[(p.ev, vn)])
            an = p.ea.n
            p.v(lambda e, sl=sl, r0=r0, cp=cp: e.reduce_sum(
                    s_colsb[r0:r0 + 64, cp:cp + 1], P_all[sl[0], sl[1]], AX.X),
                waits=[(p.ea, an)])
            p.v(lambda e, r0=r0, cp=cp: e.reciprocal(
                    r_cols[r0:r0 + 64, cp:cp + 1], s_colsb[r0:r0 + 64, cp:cp + 1]))
        soft_vn = p.ev.n
        soft_an = p.ea.n
        for tt in range(NTT):
            for pr in range(8):
                b = (tt * 8 + pr) % 2
                for hh in range(2):
                    h = 2 * pr + hh
                    r0 = 64 * hh
                    p.t(lambda e, tt=tt, pr=pr, hh=hh, h=h, r0=r0, b=b: e.matmul(
                            pbA[5 + b][r0:r0 + 64, :],
                            P_all[r0:r0 + 64, pr * 64:(pr + 1) * 64],
                            v_fm[r0:r0 + 64, pr * T + tt * TT:pr * T + (tt + 1) * TT],
                            start=True, stop=True),
                        waits=([(p.ea, soft_an), (p.ev, soft_vn)]
                               if tt == 0 and pr == 0 and hh == 0 else
                               ([(p.ev, p.ev.n)] if hh == 0 else [])),
                        inc=((s_t, 1) if hh == 1 else None))
                tn = s_t.n
                p.v(lambda e, tt=tt, pr=pr, b=b: e.tensor_scalar(
                        xw[:, pr * T + tt * TT:pr * T + (tt + 1) * TT],
                        pbA[5 + b][:, :], r_cols[:, pr:pr + 1], None, ALU.mult),
                    waits=[(s_t, tn)])
        w_vn = p.ev.n
        for tt in range(NTT):
            tsl = slice(tt * TT, (tt + 1) * TT)
            for dk in range(DCH):
                p.y(lambda e, s=s, dk=dk, tsl=tsl: e.dma_start(
                        out=xe[:, dk * TT:(dk + 1) * TT], in_=Xin[s, dk, :, tsl]),
                    waits=([(s_xs, s_xs.n), (p.ev, p.ev.n), (p.eg, p.eg.n)]
                           if dk == 0 else []),
                    inc=(s_xl, 16))
            xln = s_xl.n
            add_vn = {}
            for ek in range(DCH):
                b = ek % 2
                for dk in range(DCH):
                    p.t(lambda e, tt=tt, ek=ek, dk=dk, b=b: e.matmul(
                            pbA[b][:, :],
                            wA[:, dk * 1024 + ek * 128:dk * 1024 + (ek + 1) * 128],
                            xw[:, dk * T + tt * TT:dk * T + (tt + 1) * TT],
                            start=(dk == 0), stop=(dk == DCH - 1)),
                        waits=(([(s_wA, wo_n)] if ek == 0 else []) +
                               [(p.ev, (w_vn if tt == 0 else p.ev.n) if ek < 2
                                 else add_vn[ek - 2])]
                               if dk == 0 else []),
                        inc=((s_t, 1) if dk == DCH - 1 else None))
                tn = s_t.n
                p.v(lambda e, ek=ek, b=b: e.tensor_add(
                        xe[:, ek * TT:(ek + 1) * TT], pbA[b][:, :],
                        xe[:, ek * TT:(ek + 1) * TT]),
                    waits=[(s_t, tn), (s_xl, xln)])
                add_vn[ek] = p.ev.n
            epi_vn = p.ev.n
            emit_boundary(p, c, L, xe, stg_a, stg_b, t_scr, t_scr_g, Xout, s,
                          tsl, [(p.ev, epi_vn)], s_xs)
    return [(s_xs, s_xs.n)]



def build_moe(p, c, Xin, Xout, gate_w, gate_b, rms_w, W1, B1, W2, B2, Yacc,
              pre_waits=()):
    NHK2 = H_MOE // 128
    s_w1 = p.sem("mo_w1")
    s_w2 = p.sem("mo_w2")
    s_xl = p.sem("mo_xl")
    s_xs = p.sem("mo_xs")
    s_t = p.sem("mo_t")
    s_acc = [p.sem(f"mo_acc{i}") for i in range(NTT)]
    s_sm = p.sem("mo_sm")
    xb = p.sbuf("mo_xb", [128, DCH * T], BF16)
    w1b = p.sbuf("mo_w1b", [128, DCH * H_MOE], BF16)
    w2b = p.sbuf("mo_w2b", [128, NHK2 * D], BF16)
    h_e = p.sbuf("mo_h", [128, NHK2 * TT], BF16)
    yt = [p.sbuf(f"mo_yt{i}", [128, DCH * TT], F32) for i in range(2)]
    xe = p.sbuf("mo_xe", [128, DCH * TT], F32)
    ngb = p.sbuf("mo_ngb", [128, NTT * TT], F32)
    gwT = p.sbuf("mo_gwT", [128, DCH * N_EXP], BF16)
    gb_col = p.sbuf("mo_gb", [N_EXP, 1], F32)
    rms2bc = p.sbuf("mo_rms2", [128, N_EXP], F32)
    b1cols = p.sbuf("mo_b1c", [128, NHK2 * N_EXP], F32)
    b2s = p.sbuf("mo_b2s", [128, DCH], F32)
    smv = p.sbuf("mo_smv", [N_EXP, H_MOE], F32)
    lgsb = p.sbuf("mo_lgsb", [N_EXP, TT], F32)
    ltm = p.sbuf("mo_ltm", [128, 32], F32)
    grow = p.sbuf("mo_grow", [1, 8], F32)
    gcolsb = p.sbuf("mo_gcol", [128, 16], F32)
    ngrow = p.sbuf("mo_ngrow", [1, TT], F32)
    stg_a = [p.sbuf(f"mo_sa{i}", [128, TT], BF16) for i in range(2)]
    stg_b = [p.sbuf(f"mo_sb{i}", [128, TT], BF16) for i in range(2)]
    t_scr = p.sbuf("mo_t0", [128, TT], F32)
    t_scr_g = p.sbuf("mo_t0g", [128, TT], F32)
    pbM = [p.psum(f"mo_pb{i}", [128, TT], F32) for i in range(8)]
    L = make_lnctx(p, c, "mo", pbM[4], pbM[5], pbM[6], pbM[7], s_t)

    p.y(lambda e: e.dma_start(out=smv[:N_EXP, :D], in_=gate_w[:, :]),
        waits=list(pre_waits), inc=(s_sm, 16))
    smn = s_sm.n
    for dk in range(DCH):
        p.t(lambda e, dk=dk: e.transpose(
                pbM[0][:, dk * 8:(dk + 1) * 8],
                smv[:N_EXP, dk * 128:(dk + 1) * 128], c["ident_f32"][:8, :8]),
            waits=(c["waits"] + [(s_sm, smn)] if dk == 0 else []),
            inc=((s_t, 1) if dk == DCH - 1 else None))
    p.v(lambda e: e.tensor_copy(gwT[:, :], pbM[0][:, 0:DCH * N_EXP]),
        waits=[(s_t, s_t.n)])
    p.y(lambda e: e.dma_start(out=smv[0:1, 0:8],
                              in_=gate_b.rearrange("(a b) -> a b", a=1)),
        waits=[(p.ev, p.ev.n)], inc=(s_sm, 16))
    smn = s_sm.n
    p.t(lambda e: e.transpose(pbM[1][0:8, 0:1], smv[0:1, 0:8],
                              c["ident_f32"][:1, :1]),
        waits=[(s_sm, smn)], inc=(s_t, 1))
    p.v(lambda e: e.tensor_copy(gb_col[:, :], pbM[1][0:8, 0:1]),
        waits=[(s_t, s_t.n)])
    p.y(lambda e: e.dma_start(out=smv[0:1, 8:16],
                              in_=rms_w.rearrange("(a b) -> a b", a=1)),
        waits=[(p.ev, p.ev.n)], inc=(s_sm, 16))
    smn = s_sm.n
    p.t(lambda e: e.matmul(pbM[2][:, 0:8], c["ones_f32_row"][:, :],
                           smv[0:1, 8:16], start=True, stop=True),
        waits=[(s_sm, smn)], inc=(s_t, 1))
    p.v(lambda e: e.tensor_scalar(rms2bc[:, :], pbM[2][:, 0:8], 2.0, None,
                                  ALU.mult),
        waits=[(s_t, s_t.n)])
    p.y(lambda e: e.dma_start(out=smv[:N_EXP, :H_MOE], in_=B1[:, :]),
        waits=[(s_t, s_t.n), (p.ev, p.ev.n)], inc=(s_sm, 16))
    smn = s_sm.n
    for hk in range(NHK2):
        p.t(lambda e, hk=hk: e.transpose(
                pbM[3][:, hk * 8:(hk + 1) * 8],
                smv[:N_EXP, hk * 128:(hk + 1) * 128], c["ident_f32"][:8, :8]),
            waits=([(s_sm, smn)] if hk == 0 else []),
            inc=((s_t, 1) if hk == NHK2 - 1 else None))
    p.v(lambda e: e.tensor_copy(b1cols[:, :], pbM[3][:, 0:NHK2 * N_EXP]),
        waits=[(s_t, s_t.n)])
    p.y(lambda e: e.dma_start(out=smv[:N_EXP, :D], in_=B2[:, :]),
        waits=[(s_t, s_t.n), (p.ev, p.ev.n)], inc=(s_sm, 16))
    smn = s_sm.n
    for dk in range(DCH):
        p.t(lambda e, dk=dk: e.transpose(
                pbM[0][:, 64 + dk * 8:64 + (dk + 1) * 8],
                smv[:N_EXP, dk * 128:(dk + 1) * 128], c["ident_f32"][:8, :8]),
            waits=([(s_sm, smn), (p.ev, p.ev.n)] if dk == 0 else []),
            inc=((s_t, 1) if dk == DCH - 1 else None))
    p.v(lambda e: e.reduce_sum(
            b2s.ap().rearrange("p (a b) -> p a b", a=DCH)[:, :, :],
            pbM[0].ap().rearrange("p (cc a b) -> p cc a b", cc=8, a=DCH)[:, 1, :, :],
            AX.X),
        waits=[(s_t, s_t.n)])

    for s in range(NSEQ):
        seq_start_tn = s_t.n
        for tt in range(NTT):
            tsl = slice(tt * TT, (tt + 1) * TT)
            for dk in range(DCH):
                p.y(lambda e, s=s, dk=dk, tsl=tsl: e.dma_start(
                        out=xe[:, dk * TT:(dk + 1) * TT], in_=Xin[s, dk, :, tsl]),
                    waits=([(s_xs, s_xs.n), (p.ev, p.ev.n), (p.eg, p.eg.n),
                            (s_t, s_t.n)] if dk == 0 else []),
                    inc=(s_xl, 16))
            xln = s_xl.n
            for dk in range(DCH):
                p.v(lambda e, dk=dk, tt=tt: e.tensor_copy(
                        xb[:, dk * T + tt * TT:dk * T + (tt + 1) * TT],
                        xe[:, dk * TT:(dk + 1) * TT]),
                    waits=([(s_xl, xln), (s_t, s_t.n)] if dk == 0 else []))
            cast_vn = p.ev.n
            for dk in range(DCH):
                p.t(lambda e, dk=dk, tt=tt: e.matmul(
                        pbM[2][0:N_EXP, :], gwT[:, dk * 8:(dk + 1) * 8],
                        xb[:, dk * T + tt * TT:dk * T + (tt + 1) * TT],
                        start=(dk == 0), stop=(dk == DCH - 1)),
                    waits=([(p.ev, cast_vn)] if dk == 0 else []),
                    inc=((s_t, 1) if dk == DCH - 1 else None))
            tn = s_t.n
            p.v(lambda e: e.tensor_scalar(lgsb[:, :], pbM[2][0:N_EXP, :],
                                          gb_col[:, 0:1], None, ALU.add),
                waits=[(s_t, tn)])
            vn = p.ev.n
            for q in range(4):
                p.t(lambda e, q=q: e.transpose(
                        pbM[3][:, q * 8:(q + 1) * 8],
                        lgsb[:N_EXP, q * 128:(q + 1) * 128],
                        c["ident_f32"][:8, :8]),
                    waits=([(p.ev, vn)] if q == 0 else []),
                    inc=((s_t, 1) if q == 3 else None))
            p.v(lambda e: e.tensor_copy(ltm[:, :], pbM[3][:, 0:32]),
                waits=[(s_t, s_t.n)])
            p.v(lambda e: e.tensor_mul(t_scr[:, 0:32], ltm[:, :], ltm[:, :]))
            p.v(lambda e: e.reduce_sum(
                    gcolsb[:, 0:4],
                    t_scr.ap().rearrange("p (cc a b) -> p cc a b", cc=16, a=4)[:, 0, :, :],
                    AX.X))
            p.s(lambda e: e.activation(gcolsb[:, 4:8], gcolsb[:, 0:4], AF.Sqrt,
                                       scale=1.0 / 8.0, bias=c["epsf_col"][:, 0:1]),
                waits=[(p.ev, p.ev.n)])
            p.v(lambda e: e.reciprocal(gcolsb[:, 8:12], gcolsb[:, 4:8]),
                waits=[(p.ea, p.ea.n)])
            for q in range(4):
                p.v(lambda e, q=q: e.tensor_scalar(
                        ltm[:, q * 8:(q + 1) * 8], ltm[:, q * 8:(q + 1) * 8],
                        gcolsb[:, 8 + q:9 + q], None, ALU.mult))
                p.v(lambda e, q=q: e.tensor_mul(
                        ltm[:, q * 8:(q + 1) * 8], ltm[:, q * 8:(q + 1) * 8],
                        rms2bc[:, :]))
            p.v(lambda e: e.reduce_max(
                    gcolsb.ap().rearrange("p (a b) -> p a b", a=4)[:, 0:4, 0:1],
                    ltm.ap().rearrange("p (a b) -> p a b", a=4)[:, :, :],
                    AX.X, negate=True))
            vn = p.ev.n
            for q in range(4):
                p.s(lambda e, q=q: e.activation(
                        ltm[:, q * 8:(q + 1) * 8], ltm[:, q * 8:(q + 1) * 8],
                        AF.Exp, bias=gcolsb[:, q:q + 1]),
                    waits=([(p.ev, vn)] if q == 0 else []))
            an = p.ea.n
            p.v(lambda e: e.reduce_sum(
                    gcolsb[:, 4:8],
                    ltm.ap().rearrange("p (a b) -> p a b", a=4)[:, :, :], AX.X),
                waits=[(p.ea, an)])
            p.v(lambda e: e.reduce_max(
                    gcolsb[:, 8:12],
                    ltm.ap().rearrange("p (a b) -> p a b", a=4)[:, :, :], AX.X))
            p.v(lambda e: e.reciprocal(gcolsb[:, 12:16], gcolsb[:, 4:8]))
            p.v(lambda e: e.tensor_mul(gcolsb[:, 8:12], gcolsb[:, 8:12],
                                       gcolsb[:, 12:16]))
            p.v(lambda e: e.tensor_scalar(gcolsb[:, 12:16], gcolsb[:, 8:12],
                                          1e-6, None, ALU.add))
            p.v(lambda e: e.reciprocal(gcolsb[:, 12:16], gcolsb[:, 12:16]))
            p.v(lambda e: e.tensor_mul(gcolsb[:, 8:12], gcolsb[:, 8:12],
                                       gcolsb[:, 12:16]))
            vn = p.ev.n
            for q in range(4):
                p.t(lambda e, q=q: e.transpose(
                        pbM[3][0:1, q * 128:(q + 1) * 128],
                        gcolsb[:, 8 + q:9 + q], c["ident_f32"][:, :]),
                    waits=([(p.ev, vn)] if q == 0 else []),
                    inc=((s_t, 1) if q == 3 else None))
            p.v(lambda e: e.tensor_copy(ngrow[0:1, :], pbM[3][0:1, 0:TT]),
                waits=[(s_t, s_t.n)])
            vn = p.ev.n
            p.t(lambda e: e.matmul(pbM[3][:, :], c["ones_f32_row"][:, :],
                                   ngrow[0:1, :], start=True, stop=True),
                waits=[(p.ev, vn)], inc=(s_t, 1))
            p.v(lambda e, tt=tt: e.tensor_copy(ngb[:, tt * TT:(tt + 1) * TT],
                                               pbM[3][:, :]),
                waits=[(s_t, s_t.n)])
        gate_done_vn = p.ev.n
        gate_done_tn = s_t.n
        gelu_an = {}
        for ex in range(N_EXP):
            w1h = []
            for hf in range(2):
                p.gdma(lambda e, ex=ex, hf=hf: e.dma_start(
                        out=w1b.ap().rearrange("p (a h) -> p a h", a=DCH)[:, hf * 4:(hf + 1) * 4, :],
                        in_=W1[ex, :, :].rearrange("(a pp) h -> pp a h", pp=128)[:, hf * 4:(hf + 1) * 4, :]),
                    waits=([(s_t, seq_start_tn if ex == 0 else s_t.n)]
                           if hf == 0 else []),
                    inc=(s_w1, 16))
                w1h.append(s_w1.n)
            w1n = s_w1.n
            w2h = []
            for hf in range(2):
                p.gdma(lambda e, ex=ex, hf=hf: e.dma_start(
                        out=w2b.ap().rearrange("p (a dd) -> p a dd", a=NHK2)[:, hf * 8:(hf + 1) * 8, :],
                        in_=W2[ex, :, :].rearrange("(a pp) dd -> pp a dd", pp=128)[:, hf * 8:(hf + 1) * 8, :]),
                    waits=([(s_t, s_t.n)] if hf == 0 else []),
                    inc=(s_w2, 16))
                w2h.append(s_w2.n)
            w2n = s_w2.n
            for tt in range(NTT):
                for hk in range(NHK2):
                    for dk in range(DCH):
                        p.t(lambda e, hk=hk, dk=dk, tt=tt: e.matmul(
                                pbM[hk % 2][:, :],
                                w1b[:, dk * H_MOE + hk * 128:dk * H_MOE + (hk + 1) * 128],
                                xb[:, dk * T + tt * TT:dk * T + (tt + 1) * TT],
                                start=(dk == 0), stop=(dk == DCH - 1)),
                            waits=(([(s_w1, w1h[0]), (p.ev, gate_done_vn),
                                     (p.ea, gelu_an.get(hk - 2,
                                           gelu_an.get(NHK2 + hk - 2, 0)))]
                                    if dk == 0 else
                                    ([(s_w1, w1h[1])] if dk == 4 else []))
                                   if tt == 0 else
                                   ([(p.ea, gelu_an.get(hk - 2,
                                         gelu_an.get(NHK2 + hk - 2, 0)))]
                                    if dk == 0 else [])),
                            inc=((s_t, 1) if dk == DCH - 1 else None))
                    tn = s_t.n
                    p.s(lambda e, hk=hk, ex=ex: e.activation(
                            h_e[:, hk * TT:(hk + 1) * TT], pbM[hk % 2][:, :],
                            AF.Gelu, bias=b1cols[:, hk * 8 + ex:hk * 8 + ex + 1]),
                        waits=[(s_t, tn)])
                    gelu_an[hk] = p.ea.n
                all_g = p.ea.n
                for dk in range(DCH):
                    for hk in range(NHK2):
                        p.t(lambda e, dk=dk, hk=hk: e.matmul(
                                pbM[2 + dk % 2][:, :],
                                w2b[:, hk * D + dk * 128:hk * D + (dk + 1) * 128],
                                h_e[:, hk * TT:(hk + 1) * TT],
                                start=(hk == 0), stop=(hk == NHK2 - 1)),
                            waits=(([(s_w2, w2h[0]), (p.ea, all_g), (p.ev, p.ev.n)]
                                    if hk == 0 else
                                    ([(s_w2, w2h[1])] if hk == 8 and tt == 0 else []))
                                   if dk == 0 else
                                   ([(p.ev, p.ev.n)] if hk == 0 else [])),
                            inc=((s_t, 1) if hk == NHK2 - 1 else None))
                    tn = s_t.n
                    p.v(lambda e, dk=dk, tt=tt: e.tensor_copy(
                            yt[tt % 2][:, dk * TT:(dk + 1) * TT],
                            pbM[2 + dk % 2][:, :]),
                        waits=[(s_t, tn),
                               (s_acc[tt], 16 * (N_EXP * s + ex))] +
                              (([(s_acc[tt - 2], 16 * (N_EXP * s + ex + 1))]
                                if tt >= 2 else
                                [(s_acc[tt + 2], 16 * (N_EXP * s + ex))])
                               if dk == 0 else []))
                ev_y = p.ev.n
                p.gdma(lambda e, s=s, tt=tt, ex=ex: e.dma_start(
                        out=Yacc[s, tt, :, :], in_=yt[tt % 2][:, :],
                        accum_op=(ALU.bypass if ex == 0 else ALU.add),
                        max_dma_last_dim=2048),
                    waits=[(p.ev, ev_y)], inc=(s_acc[tt], 16))
        for tt in range(NTT):
            tsl = slice(tt * TT, (tt + 1) * TT)
            p.y(lambda e, s=s, tt=tt: e.dma_start(
                    out=yt[tt % 2][:, :], in_=Yacc[s, tt, :, :]),
                waits=[(s_acc[tt], 16 * N_EXP * (s + 1)), (p.ev, p.ev.n)] +
                      ([(s_acc[tt + 2], 16 * N_EXP * (s + 1))] if tt < 2 else []),
                inc=(s_xl, 16))
            for dk in range(DCH):
                p.y(lambda e, s=s, dk=dk, tsl=tsl: e.dma_start(
                        out=xe[:, dk * TT:(dk + 1) * TT], in_=Xin[s, dk, :, tsl]),
                    waits=([(s_xs, s_xs.n), (p.eg, p.eg.n), (p.ev, p.ev.n)]
                           if dk == 0 else []),
                    inc=(s_xl, 16))
            xln = s_xl.n
            for dk in range(DCH):
                p.v(lambda e, dk=dk, tt=tt: e.scalar_tensor_tensor(
                        t_scr[:, :], yt[tt % 2][:, dk * TT:(dk + 1) * TT],
                        b2s[:, dk:dk + 1], ngb[:, tt * TT:(tt + 1) * TT],
                        ALU.add, ALU.mult),
                    waits=([(s_xl, xln)] if dk == 0 else []))
                p.v(lambda e, dk=dk: e.tensor_add(
                        xe[:, dk * TT:(dk + 1) * TT], t_scr[:, :],
                        xe[:, dk * TT:(dk + 1) * TT]))
            epi_vn = p.ev.n
            emit_boundary(p, c, L, xe, stg_a, stg_b, t_scr, t_scr_g, Xout, s,
                          tsl, [(p.ev, epi_vn)], s_xs)
    return [(s_xs, s_xs.n)]


def _run_mamba():
    nc = bass.Bass("TRN2", target_bir_lowering=False)
    Xin = nc.dram_tensor("Xin", [NSEQ, DCH, 128, T], F32, kind="ExternalInput").ap()
    in_proj_w = nc.dram_tensor("in_proj_w", [D_IN_PROJ, D], F32, kind="ExternalInput").ap()
    conv_w = nc.dram_tensor("conv_w", [CONV_DIM, 4], F32, kind="ExternalInput").ap()
    conv_b = nc.dram_tensor("conv_b", [CONV_DIM], F32, kind="ExternalInput").ap()
    dt_bias = nc.dram_tensor("dt_bias", [NHEADS], F32, kind="ExternalInput").ap()
    A_log = nc.dram_tensor("A_log", [NHEADS], F32, kind="ExternalInput").ap()
    Dp = nc.dram_tensor("Dp", [NHEADS], F32, kind="ExternalInput").ap()
    norm_w = nc.dram_tensor("norm_w", [D_INNER], F32, kind="ExternalInput").ap()
    out_proj_w = nc.dram_tensor("out_proj_w", [D, D_INNER], F32, kind="ExternalInput").ap()
    ln_w = nc.dram_tensor("ln_w", [D], F32, kind="ExternalInput").ap()
    ln_b = nc.dram_tensor("ln_b", [D], F32, kind="ExternalInput").ap()
    Xout = nc.dram_tensor("Xout", [NSEQ, DCH, 128, T], F32, kind="ExternalOutput").ap()
    scr_inT = nc.dram_tensor("scr_inT", [DCH, 128, D_IN_PROJ], BF16, kind="Internal").ap()
    scr_outT = nc.dram_tensor("scr_outT", [D_INNER // 128, 128, D], BF16, kind="Internal").ap()
    import os
    dbg = None
    if os.environ.get("MAMBA_DBG"):
        shapes = {"rbc_raw": [128, 2048], "rbc_t1": [128, 2048],
                  "rbc_t2": [128, 2048], "lam_tm_all": [128, 128],
                  "lam_loc_all": [32, TT],
                  "z_sil": [128, 16 * TT], "xssm": [128, 16 * TT],
                  "bcb": [128, 2 * TT], "dt_fm": [32, TT], "lam_fm": [32, TT],
                  "X_tm": [128, D_INNER], "Y_tm": [128, D_INNER],
                  "H_f": [128, D_INNER], "expl_tm": [128, 32],
                  "wdt_all": [128, 32], "decs_bc": [128, 32],
                  "lam_tm": [128, 32], "e_sb": [128, 128],
                  "y_fm": [128, 16 * TT]}
        dts = {"vread": BF16, "z_sil": BF16, "xssm": BF16, "bcb": BF16, "X_tm": BF16,
               "Y_tm": BF16, "wdt_all": BF16, "y_fm": BF16}
        dbg = {k: nc.dram_tensor("dbg_" + k, v, dts.get(k, F32),
                                 kind="ExternalOutput").ap()
               for k, v in shapes.items()}
    p = Prog(nc)
    c = make_consts(p)
    dw = build_mamba_prologue(p, c, in_proj_w, out_proj_w, scr_inT, scr_outT)
    done = build_mamba(p, c, Xin, Xout, scr_inT, scr_outT, conv_w, conv_b,
                       dt_bias, A_log, Dp, norm_w, ln_w, ln_b, pre_waits=dw,
                       dbg=dbg)
    p.y(lambda e: e.nop(), waits=done)
    p.finalize()
    return nc


def _run_moe():
    nc = bass.Bass("TRN2", target_bir_lowering=False)
    Xin = nc.dram_tensor("Xin", [NSEQ, DCH, 128, T], F32, kind="ExternalInput").ap()
    gate_w = nc.dram_tensor("gate_w", [N_EXP, D], F32, kind="ExternalInput").ap()
    gate_b = nc.dram_tensor("gate_b", [N_EXP], F32, kind="ExternalInput").ap()
    rms_w = nc.dram_tensor("rms_w", [N_EXP], F32, kind="ExternalInput").ap()
    W1 = nc.dram_tensor("w1", [N_EXP, D, H_MOE], F32, kind="ExternalInput").ap()
    B1 = nc.dram_tensor("b1", [N_EXP, H_MOE], F32, kind="ExternalInput").ap()
    W2 = nc.dram_tensor("w2", [N_EXP, H_MOE, D], F32, kind="ExternalInput").ap()
    B2 = nc.dram_tensor("b2", [N_EXP, D], F32, kind="ExternalInput").ap()
    Xout = nc.dram_tensor("Xout", [NSEQ, DCH, 128, T], F32, kind="ExternalOutput").ap()
    Yacc = nc.dram_tensor("Yacc", [NSEQ, NTT, 128, DCH * TT], F32, kind="Internal").ap()
    p = Prog(nc)
    c = make_consts(p)
    done = build_moe(p, c, Xin, Xout, gate_w, gate_b, rms_w, W1, B1, W2, B2, Yacc)
    p.y(lambda e: e.nop(), waits=done)
    p.finalize()
    return nc



def _run_identity(x8=None):
    nc = bass.Bass("TRN2", target_bir_lowering=False)
    x_in = nc.dram_tensor("x_in", [NSEQ, T, D], F32, kind="ExternalInput").ap()
    y_out = nc.dram_tensor("y_out", [NSEQ, T, D], F32, kind="ExternalOutput").ap()
    X = nc.dram_tensor("Xs", [NSEQ, DCH, 128, T], F32, kind="Internal").ap()
    p = Prog(nc)
    c = make_consts(p)
    done = build_entry(p, c, x_in, X)
    done2 = build_exit(p, c, X, y_out, pre_waits=done)
    p.y(lambda e: e.nop(), waits=done2)
    p.finalize()
    return nc


def _run_ffn():
    nc = bass.Bass("TRN2", target_bir_lowering=False)
    Xin = nc.dram_tensor("Xin", [NSEQ, DCH, 128, T], F32, kind="ExternalInput").ap()
    W_fc = nc.dram_tensor("W_fc", [H_FFN, D], F32, kind="ExternalInput").ap()
    ln_w = nc.dram_tensor("ln_w", [H_FFN], F32, kind="ExternalInput").ap()
    ln_b = nc.dram_tensor("ln_b", [H_FFN], F32, kind="ExternalInput").ap()
    W_proj = nc.dram_tensor("W_proj", [D, H_FFN], F32, kind="ExternalInput").ap()
    Xout = nc.dram_tensor("Xout", [NSEQ, DCH, 128, T], F32, kind="ExternalOutput").ap()
    scr_fcT = nc.dram_tensor("scr_fcT", [DCH, 128, H_FFN], BF16, kind="Internal").ap()
    scr_projT = nc.dram_tensor("scr_projT", [H_FFN // 128, 128, D], BF16,
                               kind="Internal").ap()
    p = Prog(nc)
    c = make_consts(p)
    persist = {"st_col": p.sbuf("ff_stcol", [128, DCH], F32),
               "cb_col": p.sbuf("ff_cbcol", [128, DCH], F32)}
    dw = build_ffn_prologue(p, c, W_fc, W_proj, ln_w, ln_b, scr_fcT, scr_projT,
                            persist)
    done = build_ffn(p, c, Xin, Xout, scr_fcT, scr_projT, persist, pre_waits=dw)
    p.y(lambda e: e.nop(), waits=done)
    p.finalize()
    return nc


def _run_attn():
    nc = bass.Bass("TRN2", target_bir_lowering=False)
    Xin = nc.dram_tensor("Xin", [NSEQ, DCH, 128, T], F32, kind="ExternalInput").ap()
    Wqk = nc.dram_tensor("Wqk", [D, D], F32, kind="ExternalInput").ap()
    Wv = nc.dram_tensor("Wv", [D, D], F32, kind="ExternalInput").ap()
    Wo = nc.dram_tensor("Wo", [D, D], F32, kind="ExternalInput").ap()
    tau = nc.dram_tensor("tau", [16, 1, 1], F32, kind="ExternalInput").ap()
    Xout = nc.dram_tensor("Xout", [NSEQ, DCH, 128, T], F32, kind="ExternalOutput").ap()
    p = Prog(nc)
    c = make_consts(p)
    done = build_attn(p, c, Xin, Xout, Wqk, Wv, Wo, tau)
    p.y(lambda e: e.nop(), waits=done)
    p.finalize()
    return nc


def _build_launch1():
    nc = bass.Bass("TRN2", target_bir_lowering=False)
    x_in = nc.dram_tensor("x_in", [NSEQ, T, D], F32, kind="ExternalInput").ap()
    in_proj_w = nc.dram_tensor("in_proj_w", [D_IN_PROJ, D], F32, kind="ExternalInput").ap()
    conv_w = nc.dram_tensor("conv_w", [CONV_DIM, 4], F32, kind="ExternalInput").ap()
    conv_b = nc.dram_tensor("conv_b", [CONV_DIM], F32, kind="ExternalInput").ap()
    dt_bias = nc.dram_tensor("dt_bias", [NHEADS], F32, kind="ExternalInput").ap()
    A_log = nc.dram_tensor("A_log", [NHEADS], F32, kind="ExternalInput").ap()
    Dp = nc.dram_tensor("Dp", [NHEADS], F32, kind="ExternalInput").ap()
    norm_w = nc.dram_tensor("norm_w", [D_INNER], F32, kind="ExternalInput").ap()
    out_proj_w = nc.dram_tensor("out_proj_w", [D, D_INNER], F32, kind="ExternalInput").ap()
    ln_w = nc.dram_tensor("ln_w", [D], F32, kind="ExternalInput").ap()
    ln_b = nc.dram_tensor("ln_b", [D], F32, kind="ExternalInput").ap()
    X1 = nc.dram_tensor("X1", [NSEQ, DCH, 128, T], F32, kind="ExternalOutput").ap()
    X0 = nc.dram_tensor("X0", [NSEQ, DCH, 128, T], F32, kind="Internal").ap()
    scr_inT = nc.dram_tensor("scr_inT", [DCH, 128, D_IN_PROJ], BF16, kind="Internal").ap()
    scr_outT = nc.dram_tensor("scr_outT", [D_INNER // 128, 128, D], BF16, kind="Internal").ap()
    p = Prog(nc)
    c = make_consts(p)
    dw = build_mamba_prologue(p, c, in_proj_w, out_proj_w, scr_inT, scr_outT)
    with p.scope():
        done = build_entry(p, c, x_in, X0)
        p.barrier(done)
    done = build_mamba(p, c, X0, X1, scr_inT, scr_outT, conv_w, conv_b,
                       dt_bias, A_log, Dp, norm_w, ln_w, ln_b, pre_waits=dw)
    p.y(lambda e: e.nop(), waits=done)
    p.finalize()
    return nc


def _build_launch2():
    nc = bass.Bass("TRN2", target_bir_lowering=False)
    Xin = nc.dram_tensor("Xin", [NSEQ, DCH, 128, T], F32, kind="ExternalInput").ap()
    W_fc = nc.dram_tensor("W_fc", [H_FFN, D], F32, kind="ExternalInput").ap()
    ln_w = nc.dram_tensor("ln_w", [H_FFN], F32, kind="ExternalInput").ap()
    ln_b = nc.dram_tensor("ln_b", [H_FFN], F32, kind="ExternalInput").ap()
    W_proj = nc.dram_tensor("W_proj", [D, H_FFN], F32, kind="ExternalInput").ap()
    Xout = nc.dram_tensor("Xout", [NSEQ, DCH, 128, T], F32, kind="ExternalOutput").ap()
    scr_fcT = nc.dram_tensor("scr_fcT", [DCH, 128, H_FFN], BF16, kind="Internal").ap()
    scr_projT = nc.dram_tensor("scr_projT", [H_FFN // 128, 128, D], BF16,
                               kind="Internal").ap()
    p = Prog(nc)
    c = make_consts(p)
    persist = {"st_col": p.sbuf("ff_stcol", [128, DCH], F32),
               "cb_col": p.sbuf("ff_cbcol", [128, DCH], F32)}
    dw = build_ffn_prologue(p, c, W_fc, W_proj, ln_w, ln_b, scr_fcT, scr_projT,
                            persist)
    done = build_ffn(p, c, Xin, Xout, scr_fcT, scr_projT, persist, pre_waits=dw)
    p.y(lambda e: e.nop(), waits=done)
    p.finalize()
    return nc


def _build_launch3():
    nc = bass.Bass("TRN2", target_bir_lowering=False)
    Xin = nc.dram_tensor("Xin", [NSEQ, DCH, 128, T], F32, kind="ExternalInput").ap()
    Wqk = nc.dram_tensor("Wqk", [D, D], F32, kind="ExternalInput").ap()
    Wv = nc.dram_tensor("Wv", [D, D], F32, kind="ExternalInput").ap()
    Wo = nc.dram_tensor("Wo", [D, D], F32, kind="ExternalInput").ap()
    tau = nc.dram_tensor("tau", [16, 1, 1], F32, kind="ExternalInput").ap()
    Xout = nc.dram_tensor("Xout", [NSEQ, DCH, 128, T], F32, kind="ExternalOutput").ap()
    p = Prog(nc)
    c = make_consts(p)
    done = build_attn(p, c, Xin, Xout, Wqk, Wv, Wo, tau)
    p.y(lambda e: e.nop(), waits=done)
    p.finalize()
    return nc


def _build_launch4():
    nc = bass.Bass("TRN2", target_bir_lowering=False)
    Xin = nc.dram_tensor("Xin", [NSEQ, DCH, 128, T], F32, kind="ExternalInput").ap()
    gate_w = nc.dram_tensor("gate_w", [N_EXP, D], F32, kind="ExternalInput").ap()
    gate_b = nc.dram_tensor("gate_b", [N_EXP], F32, kind="ExternalInput").ap()
    rms_w = nc.dram_tensor("rms_w", [N_EXP], F32, kind="ExternalInput").ap()
    W1 = nc.dram_tensor("w1", [N_EXP, D, H_MOE], F32, kind="ExternalInput").ap()
    B1 = nc.dram_tensor("b1", [N_EXP, H_MOE], F32, kind="ExternalInput").ap()
    W2 = nc.dram_tensor("w2", [N_EXP, H_MOE, D], F32, kind="ExternalInput").ap()
    B2 = nc.dram_tensor("b2", [N_EXP, D], F32, kind="ExternalInput").ap()
    y_out = nc.dram_tensor("y_out", [NSEQ, T, D], F32, kind="ExternalOutput").ap()
    X4 = nc.dram_tensor("X4", [NSEQ, DCH, 128, T], F32, kind="Internal").ap()
    Yacc = nc.dram_tensor("Yacc", [NSEQ, NTT, 128, DCH * TT], F32, kind="Internal").ap()
    p = Prog(nc)
    c = make_consts(p)
    with p.scope():
        done = build_moe(p, c, Xin, X4, gate_w, gate_b, rms_w, W1, B1, W2, B2,
                         Yacc)
        p.barrier(done)
    done = build_exit(p, c, X4, y_out)
    p.y(lambda e: e.nop(), waits=done)
    p.finalize()
    return nc


def _build_fused():
    nc = bass.Bass("TRN2", target_bir_lowering=False)
    EI, EO, IN = "ExternalInput", "ExternalOutput", "Internal"
    x_in = nc.dram_tensor("x_in", [NSEQ, T, D], F32, kind=EI).ap()
    in_proj_w = nc.dram_tensor("in_proj_w", [D_IN_PROJ, D], F32, kind=EI).ap()
    conv_w = nc.dram_tensor("conv_w", [CONV_DIM, 4], F32, kind=EI).ap()
    conv_b = nc.dram_tensor("conv_b", [CONV_DIM], F32, kind=EI).ap()
    dt_bias = nc.dram_tensor("dt_bias", [NHEADS], F32, kind=EI).ap()
    A_log = nc.dram_tensor("A_log", [NHEADS], F32, kind=EI).ap()
    Dp = nc.dram_tensor("Dp", [NHEADS], F32, kind=EI).ap()
    norm_w = nc.dram_tensor("norm_w", [D_INNER], F32, kind=EI).ap()
    out_proj_w = nc.dram_tensor("out_proj_w", [D, D_INNER], F32, kind=EI).ap()
    mln_w = nc.dram_tensor("mln_w", [D], F32, kind=EI).ap()
    mln_b = nc.dram_tensor("mln_b", [D], F32, kind=EI).ap()
    W_fc = nc.dram_tensor("W_fc", [H_FFN, D], F32, kind=EI).ap()
    fln_w = nc.dram_tensor("fln_w", [H_FFN], F32, kind=EI).ap()
    fln_b = nc.dram_tensor("fln_b", [H_FFN], F32, kind=EI).ap()
    W_proj = nc.dram_tensor("W_proj", [D, H_FFN], F32, kind=EI).ap()
    Wqk = nc.dram_tensor("Wqk", [D, D], F32, kind=EI).ap()
    Wv = nc.dram_tensor("Wv", [D, D], F32, kind=EI).ap()
    Wo = nc.dram_tensor("Wo", [D, D], F32, kind=EI).ap()
    tau = nc.dram_tensor("tau", [16, 1, 1], F32, kind=EI).ap()
    gate_w = nc.dram_tensor("gate_w", [N_EXP, D], F32, kind=EI).ap()
    gate_b = nc.dram_tensor("gate_b", [N_EXP], F32, kind=EI).ap()
    rms_w = nc.dram_tensor("rms_w", [N_EXP], F32, kind=EI).ap()
    W1 = nc.dram_tensor("w1", [N_EXP, D, H_MOE], F32, kind=EI).ap()
    B1 = nc.dram_tensor("b1", [N_EXP, H_MOE], F32, kind=EI).ap()
    W2 = nc.dram_tensor("w2", [N_EXP, H_MOE, D], F32, kind=EI).ap()
    B2 = nc.dram_tensor("b2", [N_EXP, D], F32, kind=EI).ap()
    y_out = nc.dram_tensor("y_out", [NSEQ, T, D], F32, kind=EO).ap()
    X0 = nc.dram_tensor("X0", [NSEQ, DCH, 128, T], F32, kind=IN).ap()
    X1 = nc.dram_tensor("X1", [NSEQ, DCH, 128, T], F32, kind=IN).ap()
    X2 = nc.dram_tensor("X2", [NSEQ, DCH, 128, T], F32, kind=IN).ap()
    X3 = nc.dram_tensor("X3", [NSEQ, DCH, 128, T], F32, kind=IN).ap()
    X4 = nc.dram_tensor("X4", [NSEQ, DCH, 128, T], F32, kind=IN).ap()
    scr_inT = nc.dram_tensor("scr_inT", [DCH, 128, D_IN_PROJ], BF16, kind=IN).ap()
    scr_outT = nc.dram_tensor("scr_outT", [D_INNER // 128, 128, D], BF16, kind=IN).ap()
    scr_fcT = nc.dram_tensor("scr_fcT", [DCH, 128, H_FFN], BF16, kind=IN).ap()
    scr_projT = nc.dram_tensor("scr_projT", [H_FFN // 128, 128, D], BF16, kind=IN).ap()
    Yacc = nc.dram_tensor("Yacc", [NSEQ, NTT, 128, DCH * TT], F32, kind=IN).ap()

    p = Prog(nc)
    c = make_consts(p)
    persist = {"st_col": p.sbuf("ff_stcol", [128, DCH], F32),
               "cb_col": p.sbuf("ff_cbcol", [128, DCH], F32)}
    dw_m = build_mamba_prologue(p, c, in_proj_w, out_proj_w, scr_inT, scr_outT)
    dw_f = build_ffn_prologue(p, c, W_fc, W_proj, fln_w, fln_b, scr_fcT,
                              scr_projT, persist, pre_waits=dw_m)
    with p.scope():
        done = build_entry(p, c, x_in, X0)
        p.barrier(done)
    with p.scope():
        done = build_mamba(p, c, X0, X1, scr_inT, scr_outT, conv_w, conv_b,
                           dt_bias, A_log, Dp, norm_w, mln_w, mln_b,
                           pre_waits=dw_f)
        p.barrier(done)
    with p.scope():
        done = build_ffn(p, c, X1, X2, scr_fcT, scr_projT, persist)
        p.barrier(done)
    with p.scope():
        done = build_attn(p, c, X2, X3, Wqk, Wv, Wo, tau)
        p.barrier(done)
    with p.scope():
        done = build_moe(p, c, X3, X4, gate_w, gate_b, rms_w, W1, B1, W2, B2,
                         Yacc)
        p.barrier(done)
    done = build_exit(p, c, X4, y_out)
    p.y(lambda e: e.nop(), waits=done)
    p.finalize()
    return nc


def _kernel_in_map(inputs, shard):
    return {"x_in": shard,
            "in_proj_w": inputs["L0_attn_in_proj_w"],
            "conv_w": inputs["L0_attn_conv_w"],
            "conv_b": inputs["L0_attn_conv_b"],
            "dt_bias": inputs["L0_attn_dt_bias"],
            "A_log": inputs["L0_attn_A_log"],
            "Dp": inputs["L0_attn_D"],
            "norm_w": inputs["L0_attn_norm_w"],
            "out_proj_w": inputs["L0_attn_out_proj_w"],
            "mln_w": inputs["L0_attn_ln_w"],
            "mln_b": inputs["L0_attn_ln_b"],
            "W_fc": inputs["L0_ffn_W_fc"],
            "fln_w": inputs["L0_ffn_ln_w"],
            "fln_b": inputs["L0_ffn_ln_b"],
            "W_proj": inputs["L0_ffn_W_proj"],
            "Wqk": inputs["L1_attn_Wqk"],
            "Wv": inputs["L1_attn_Wv"],
            "Wo": inputs["L1_attn_Wo"],
            "tau": inputs["L1_attn_tau"],
            "gate_w": inputs["L1_ffn_gate_w"],
            "gate_b": inputs["L1_ffn_gate_b"],
            "rms_w": inputs["L1_ffn_rms_w"],
            "w1": inputs["L1_ffn_w1"],
            "b1": inputs["L1_ffn_b1"],
            "w2": inputs["L1_ffn_w2"],
            "b2": inputs["L1_ffn_b2"]}


_NC_CACHE = {}


def kernel_fused(**inputs):
    x = np.ascontiguousarray(inputs["L0_attn_x"])
    if "fused" not in _NC_CACHE:
        _NC_CACHE["fused"] = _build_fused()
    cores = list(range(8))
    maps = [_kernel_in_map(inputs, np.ascontiguousarray(x[2 * i:2 * i + 2]))
            for i in range(8)]
    res = run_bass_kernel_spmd(_NC_CACHE["fused"], maps, core_ids=cores)
    return np.concatenate([res.results[ci]["y_out"] for ci in cores], axis=0)


def kernel(**inputs):
    return kernel_fused(**inputs)


def _kernel_4launch(**inputs):
    x = np.ascontiguousarray(inputs["L0_attn_x"])
    ncs = _NC_CACHE
    if not ncs:
        ncs[1] = _build_launch1()
        ncs[2] = _build_launch2()
        ncs[3] = _build_launch3()
        ncs[4] = _build_launch4()
    shards = [np.ascontiguousarray(x[2 * i:2 * i + 2]) for i in range(8)]
    cores = list(range(8))
    m1 = [{"x_in": shards[ci],
           "in_proj_w": inputs["L0_attn_in_proj_w"],
           "conv_w": inputs["L0_attn_conv_w"],
           "conv_b": inputs["L0_attn_conv_b"],
           "dt_bias": inputs["L0_attn_dt_bias"],
           "A_log": inputs["L0_attn_A_log"],
           "Dp": inputs["L0_attn_D"],
           "norm_w": inputs["L0_attn_norm_w"],
           "out_proj_w": inputs["L0_attn_out_proj_w"],
           "ln_w": inputs["L0_attn_ln_w"],
           "ln_b": inputs["L0_attn_ln_b"]} for ci in cores]
    r1 = run_bass_kernel_spmd(ncs[1], m1, core_ids=cores)
    m2 = [{"Xin": r1.results[ci]["X1"],
           "W_fc": inputs["L0_ffn_W_fc"],
           "ln_w": inputs["L0_ffn_ln_w"],
           "ln_b": inputs["L0_ffn_ln_b"],
           "W_proj": inputs["L0_ffn_W_proj"]} for ci in cores]
    r2 = run_bass_kernel_spmd(ncs[2], m2, core_ids=cores)
    m3 = [{"Xin": r2.results[ci]["Xout"],
           "Wqk": inputs["L1_attn_Wqk"],
           "Wv": inputs["L1_attn_Wv"],
           "Wo": inputs["L1_attn_Wo"],
           "tau": inputs["L1_attn_tau"]} for ci in cores]
    r3 = run_bass_kernel_spmd(ncs[3], m3, core_ids=cores)
    m4 = [{"Xin": r3.results[ci]["Xout"],
           "gate_w": inputs["L1_ffn_gate_w"],
           "gate_b": inputs["L1_ffn_gate_b"],
           "rms_w": inputs["L1_ffn_rms_w"],
           "w1": inputs["L1_ffn_w1"],
           "b1": inputs["L1_ffn_b1"],
           "w2": inputs["L1_ffn_w2"],
           "b2": inputs["L1_ffn_b2"]} for ci in cores]
    r4 = run_bass_kernel_spmd(ncs[4], m4, core_ids=cores)
    return np.concatenate([r4.results[ci]["y_out"] for ci in cores], axis=0)


# version 10 (final)
# speedup vs baseline: 1.6911x; 1.0449x over previous
import contextlib
import math
import numpy as np
import concourse.bass as bass
import concourse.mybir as mybir
from concourse.bass_utils import run_bass_kernel_spmd

F32 = mybir.dt.float32
BF16 = mybir.dt.bfloat16
I32 = mybir.dt.int32
AF = mybir.ActivationFunctionType
ALU = mybir.AluOpType
AX = mybir.AxisListType

NSEQ = 2
T = 2048
D = 1024
DCH = D // 128
TT = 512
NTT = T // TT

D_INNER = 2048
D_STATE = 128
NHEADS = 32
HEADDIM = 64
CONV_DIM = D_INNER + 2 * D_STATE
D_IN_PROJ = 2 * D_INNER + 2 * D_STATE + NHEADS
TC = 128
H_FFN = 4096
N_EXP = 8
H_MOE = 2048

LN_EPS = 1e-5


class Sem:
    def __init__(self, h, name):
        self.h = h
        self.name = name
        self.n = 0

    def bump(self, k):
        self.n += k
        return self.n


class Prog:
    ENGINES = ("tensor", "vector", "scalar", "gpsimd", "sync")
    AUTO = ("vector", "scalar", "gpsimd")

    def __init__(self, nc):
        self.nc = nc
        self.stack = contextlib.ExitStack()
        self.block = self.stack.enter_context(nc.Block())
        self.sbuf_bytes = 0
        self.pending = {e: [] for e in self.ENGINES}
        self.ev = self.sem("auto_v")
        self.ea = self.sem("auto_a")
        self.eg = self.sem("auto_g")
        self._auto = {"vector": self.ev, "scalar": self.ea, "gpsimd": self.eg}

    def flush(self):
        for e in self.ENGINES:
            fns = self.pending[e]
            if not fns:
                continue
            self.pending[e] = []
            getattr(self.block, e)(lambda eng, fns=fns: [f(eng) for f in fns])

    def sem(self, name):
        return Sem(self.stack.enter_context(self.nc.semaphore(name)), name)

    def sbuf(self, name, shape, dtype):
        self.sbuf_bytes += shape[-1] * mybir.dt.size(dtype)
        assert self.sbuf_bytes < 205 * 1024, f"SBUF over budget at {name}: {self.sbuf_bytes}"
        st = self._scope if self._scope is not None else self.stack
        return st.enter_context(self.nc.sbuf_tensor(name, shape, dtype))

    def psum(self, name, shape, dtype=F32):
        st = self._scope if self._scope is not None else self.stack
        return st.enter_context(self.nc.psum_tensor(name, shape, dtype))

    _scope = None

    @contextlib.contextmanager
    def scope(self):
        prev, prev_bytes = self._scope, self.sbuf_bytes
        self._scope = contextlib.ExitStack()
        try:
            yield
        finally:
            self.flush()
            self._scope.close()
            self._scope = prev
            self.sbuf_bytes = prev_bytes

    def op(self, eng, fn, waits=(), inc=None):
        if eng in self.AUTO:
            assert inc is None, "use auto sem counts for cross-engine signaling"
            es = self._auto[eng]
            waits = list(waits) + [(es, es.n)]
            inc = (es, 1)
        if inc is not None:
            inc[0].bump(inc[1])

        def run(e, fn=fn, waits=tuple(waits), inc=inc):
            for (s, v) in waits:
                if v > 0:
                    e.wait_ge(s.h, v)
            r = fn(e)
            if inc is not None:
                r.then_inc(inc[0].h, inc[1])
        self.pending[eng].append(run)

    def sdma(self, fn, waits=(), inc=None):
        if inc is not None:
            inc[0].bump(inc[1])

        def run(e, fn=fn, waits=tuple(waits), inc=inc):
            for (sm, v) in waits:
                if v > 0:
                    e.wait_ge(sm.h, v)
            r = fn(e)
            if inc is not None:
                r.then_inc(inc[0].h, inc[1])
        self.pending["scalar"].append(run)

    def gdma(self, fn, waits=(), inc=None):
        if inc is not None:
            inc[0].bump(inc[1])

        def run(e, fn=fn, waits=tuple(waits), inc=inc):
            for (s, v) in waits:
                if v > 0:
                    e.wait_ge(s.h, v)
            r = fn(e)
            if inc is not None:
                r.then_inc(inc[0].h, inc[1])
        self.pending["gpsimd"].append(run)

    def t(self, fn, **kw): self.op("tensor", fn, **kw)
    def v(self, fn, **kw): self.op("vector", fn, **kw)
    def s(self, fn, **kw): self.op("scalar", fn, **kw)
    def g(self, fn, **kw): self.op("gpsimd", fn, **kw)
    def y(self, fn, **kw): self.op("sync", fn, **kw)

    def barrier(self, waits):
        full = list(waits) + [(self.ev, self.ev.n), (self.ea, self.ea.n),
                              (self.eg, self.eg.n)]
        for e in self.ENGINES:
            if e in self.AUTO:
                self.op(e, lambda x: x.nop(),
                        waits=[(s, v) for (s, v) in full if s is not self._auto[e]])
            else:
                self.op(e, lambda x: x.nop(), waits=full)

    def finalize(self):
        self.flush()
        self.stack.close()


def make_consts(p):
    c = {}
    c["ident_f32"] = p.sbuf("ident_f32", [128, 128], F32)
    c["ident_bf16"] = p.sbuf("ident_bf16", [128, 128], BF16)
    iota_tmp = p.sbuf("iota_tmp", [128, 128], I32)
    c["ones_f32_col"] = p.sbuf("ones_f32_col", [128, 1], F32)
    c["ones_f32_row"] = p.sbuf("ones_f32_row", [1, 128], F32)
    c["ones_bf16_col"] = p.sbuf("ones_bf16_col", [128, 1], BF16)
    c["eps_col"] = p.sbuf("eps_col", [128, 1], F32)
    c["eps6_col"] = p.sbuf("eps6_col", [128, 1], F32)
    c["epsf_col"] = p.sbuf("epsf_col", [128, 1], F32)
    p.g(lambda e: e.memset(c["eps_col"][:, :], LN_EPS))
    p.g(lambda e: e.memset(c["eps6_col"][:, :], 1e-6))
    p.g(lambda e: e.memset(c["epsf_col"][:, :], float(np.finfo(np.float32).eps)))
    p.g(lambda e: e.iota(iota_tmp[:, :], [[1, 128]], base=0, channel_multiplier=-1))
    iota_n = p.eg.n
    p.g(lambda e: e.memset(c["ones_f32_col"][:, :], 1.0))
    p.g(lambda e: e.memset(c["ones_f32_row"][:, :], 1.0))
    p.g(lambda e: e.memset(c["ones_bf16_col"][:, :], 1.0))
    p.v(lambda e: e.tensor_single_scalar(c["ident_f32"][:, :], iota_tmp[:, :], 0,
                                         ALU.is_equal), waits=[(p.eg, iota_n)])
    p.v(lambda e: e.tensor_single_scalar(c["ident_bf16"][:, :], iota_tmp[:, :], 0,
                                         ALU.is_equal))
    c["cmt"] = p.sbuf("cmt", [128, 128], F32)
    p.v(lambda e: e.tensor_single_scalar(c["cmt"][:, :], iota_tmp[:, :], 0,
                                         ALU.is_lt), waits=[(p.eg, iota_n)])
    p.v(lambda e: e.tensor_scalar(c["cmt"][:, :], c["cmt"][:, :], 30000.0, None,
                                  ALU.mult))
    c["waits"] = [(p.eg, p.eg.n), (p.ev, p.ev.n)]
    return c


def build_entry(p, c, x_in, X):
    xt = [p.sbuf(f"en_xt{i}", [128, D], F32) for i in range(2)]
    pt = [p.psum(f"en_ps{i}", [128, 1024], F32) for i in range(2)]
    stg = [p.sbuf(f"en_stg{i}", [128, DCH * TT], F32) for i in range(2)]
    s_ld = [p.sem(f"en_ld{i}") for i in range(2)]
    s_tp = p.sem("en_tp")
    s_st = [p.sem(f"en_st{i}") for i in range(2)]
    tp0 = s_tp.n
    evac_n = {}
    ntile = NSEQ * T // 128
    gpt = TT // 128
    for i in range(ntile):
        s, ti = divmod(i, T // 128)
        g = i // gpt
        b = i % 2
        gb = g % 2
        p.y(lambda e, s=s, ti=ti, b=b: e.dma_start(
                out=xt[b][:, :], in_=x_in[s, ti * 128:(ti + 1) * 128, :]),
            waits=[(s_tp, tp0 + i - 1)],
            inc=(s_ld[b], 16))
        for j in range(DCH):
            p.t(lambda e, b=b, j=j: e.transpose(
                    pt[b][:, j * 128:(j + 1) * 128],
                    xt[b][:, j * 128:(j + 1) * 128], c["ident_f32"][:, :]),
                waits=(c["waits"] + [(s_ld[b], 16 * (i // 2 + 1)),
                                     (p.ev, evac_n.get(i - 2, 0))]
                       if j == 0 else []),
                inc=((s_tp, 1) if j == DCH - 1 else None))
        p.v(lambda e, b=b, gb=gb, off=(i % gpt) * 128: e.tensor_copy(
                stg[gb].ap().rearrange("p (dk tt) -> p dk tt", dk=DCH)[:, :, off:off + 128],
                pt[b].ap().rearrange("p (dk cc) -> p dk cc", dk=DCH)[:, :, :]),
            waits=[(s_tp, tp0 + i + 1), (s_st[gb], 8 * 16 * (g // 2))])
        evac_n[i] = p.ev.n
        if i % gpt == gpt - 1:
            tg = (i % (T // 128)) // gpt
            for dk in range(DCH):
                p.y(lambda e, s=s, dk=dk, tg=tg, gb=gb: e.dma_start(
                        out=X[s, dk, :, tg * TT:(tg + 1) * TT],
                        in_=stg[gb][:, dk * TT:(dk + 1) * TT]),
                    waits=([(p.ev, evac_n[i])] if dk == 0 else []),
                    inc=(s_st[gb], 16))
    return [(s_st[0], s_st[0].n), (s_st[1], s_st[1].n)]


def build_exit(p, c, Xf, y_out, pre_waits=()):
    stg = [p.sbuf(f"ex_stg{i}", [128, DCH * TT], F32) for i in range(2)]
    pt = [p.psum(f"ex_ps{i}", [128, 1024], F32) for i in range(2)]
    ot = [p.sbuf(f"ex_ot{i}", [128, D], F32) for i in range(2)]
    s_ld = [p.sem(f"ex_ld{i}") for i in range(2)]
    s_tp = p.sem("ex_tp")
    s_st = [p.sem(f"ex_st{i}") for i in range(2)]
    tp0 = s_tp.n
    evac_n = {}
    ngrp = NSEQ * T // TT
    for g in range(ngrp):
        s, tg = divmod(g, NTT)
        gb = g % 2
        for dk in range(DCH):
            p.y(lambda e, s=s, tg=tg, dk=dk, gb=gb: e.dma_start(
                    out=stg[gb][:, dk * TT:(dk + 1) * TT],
                    in_=Xf[s, dk, :, tg * TT:(tg + 1) * TT]),
                waits=(list(pre_waits) + [(s_tp, tp0 + 4 * (g - 1))]
                       if dk == 0 else []),
                inc=(s_ld[gb], 16))
        for q in range(TT // 128):
            i = g * 4 + q
            b = i % 2
            for dk in range(DCH):
                p.t(lambda e, gb=gb, dk=dk, q=q, b=b: e.transpose(
                        pt[b][:, dk * 128:(dk + 1) * 128],
                        stg[gb][:, dk * TT + q * 128:dk * TT + q * 128 + 128],
                        c["ident_f32"][:, :]),
                    waits=(c["waits"] + [(s_ld[gb], 16 * 8 * (g // 2 + 1)),
                                         (p.ev, evac_n.get(i - 2, 0))]
                           if dk == 0 else []),
                    inc=((s_tp, 1) if dk == DCH - 1 else None))
            p.v(lambda e, b=b: e.tensor_copy(ot[b][:, :], pt[b][:, :]),
                waits=[(s_tp, tp0 + i + 1), (s_st[b], 16 * (i // 2))])
            evac_n[i] = p.ev.n
            p.y(lambda e, s=s, tg=tg, q=q, b=b: e.dma_start(
                    out=y_out[s, tg * TT + q * 128:tg * TT + (q + 1) * 128, :],
                    in_=ot[b][:, :]),
                waits=[(p.ev, evac_n[i])],
                inc=(s_st[b], 16))
    return [(s_st[0], s_st[0].n), (s_st[1], s_st[1].n)]



LNR = dict(mu=slice(0, TT), mu2=slice(TT, 2 * TT), var=slice(TT, 2 * TT),
           sd=slice(TT, 2 * TT), rs=slice(2 * TT, 3 * TT),
           nmr=slice(3 * TT, 4 * TT))


def make_lnctx(p, c, prefix, st1_ps, st2_ps, rs_ps, nm_ps, s_t):
    return dict(st1=st1_ps, st2=st2_ps, rs=rs_ps, nm=nm_ps, s_t=s_t,
                rowbuf=p.sbuf(prefix + "_rows", [1, 4 * TT], F32),
                rsbc=p.sbuf(prefix + "_rsbc", [128, TT], F32),
                nmrbc=p.sbuf(prefix + "_nmrbc", [128, TT], F32))


def ln_rows_and_bc(p, c, L, n_feat, wait_pairs):
    rowbuf, s_t = L["rowbuf"], L["s_t"]
    p.v(lambda e: e.tensor_scalar(rowbuf[0:1, LNR["mu"]], L["st1"][0:1, :],
                                  1.0 / n_feat, None, ALU.mult),
        waits=wait_pairs)
    p.v(lambda e: e.tensor_mul(rowbuf[0:1, LNR["mu2"]], rowbuf[0:1, LNR["mu"]],
                               rowbuf[0:1, LNR["mu"]]))
    p.v(lambda e: e.scalar_tensor_tensor(
            rowbuf[0:1, LNR["var"]], L["st2"][0:1, :], 1.0 / n_feat,
            rowbuf[0:1, LNR["mu2"]], ALU.mult, ALU.subtract))
    vn = p.ev.n
    p.s(lambda e: e.activation(rowbuf[0:1, LNR["sd"]], rowbuf[0:1, LNR["var"]],
                               AF.Sqrt, bias=c["eps_col"][0:1, :]),
        waits=[(p.ev, vn)])
    an = p.ea.n
    p.v(lambda e: e.reciprocal(rowbuf[0:1, LNR["rs"]], rowbuf[0:1, LNR["sd"]]),
        waits=[(p.ea, an)])
    p.v(lambda e: e.scalar_tensor_tensor(
            rowbuf[0:1, LNR["nmr"]], rowbuf[0:1, LNR["mu"]], -1.0,
            rowbuf[0:1, LNR["rs"]], ALU.mult, ALU.mult))
    vn = p.ev.n
    p.t(lambda e: e.matmul(L["rs"][:, :], c["ones_f32_row"][:, :],
                           rowbuf[0:1, LNR["rs"]], start=True, stop=True),
        waits=[(p.ev, vn)], inc=(s_t, 1))
    p.t(lambda e: e.matmul(L["nm"][:, :], c["ones_f32_row"][:, :],
                           rowbuf[0:1, LNR["nmr"]], start=True, stop=True),
        inc=(s_t, 1))
    tn = s_t.n
    p.v(lambda e: e.tensor_copy(L["rsbc"][:, :], L["rs"][:, :]),
        waits=[(s_t, tn), (p.eg, p.eg.n)])
    p.v(lambda e: e.tensor_copy(L["nmrbc"][:, :], L["nm"][:, :]))
    return p.ev.n


def emit_boundary(p, c, L, xbuf, stg_a, stg_b, t_scr, t_scr_g, Xout, s, tsl,
                  ready, s_xs):
    s_t = L["s_t"]
    st_mm_n = {}
    for dk in range(DCH):
        p.g(lambda e, dk=dk: e.tensor_copy(stg_a[dk % 2][:, :],
                                           xbuf[:, dk * TT:(dk + 1) * TT]),
            waits=(list(ready) if dk == 0 else []) +
                  [(s_t, st_mm_n.get(dk - 2, s_t.n if dk == 0 else 0))])
        p.g(lambda e, dk=dk: e.tensor_mul(stg_b[dk % 2][:, :],
                                          xbuf[:, dk * TT:(dk + 1) * TT],
                                          xbuf[:, dk * TT:(dk + 1) * TT]))
        gn = p.eg.n
        p.t(lambda e, dk=dk: e.matmul(
                L["st1"][0:1, :], c["ones_bf16_col"][:, :], stg_a[dk % 2][:, :],
                start=(dk == 0), stop=(dk == DCH - 1)),
            waits=[(p.eg, gn)] + ([(p.ev, p.ev.n)] if dk == 0 else []),
            inc=(s_t, 1))
        p.t(lambda e, dk=dk: e.matmul(
                L["st2"][0:1, :], c["ones_bf16_col"][:, :], stg_b[dk % 2][:, :],
                start=(dk == 0), stop=(dk == DCH - 1)),
            inc=(s_t, 1))
        st_mm_n[dk] = s_t.n
    ln_rows_and_bc(p, c, L, D, [(s_t, s_t.n)])
    for dk in range(DCH):
        if dk % 2 == 0:
            p.v(lambda e, dk=dk: e.tensor_mul(t_scr[:, :],
                                              xbuf[:, dk * TT:(dk + 1) * TT],
                                              L["rsbc"][:, :]))
            p.v(lambda e, dk=dk: e.tensor_add(xbuf[:, dk * TT:(dk + 1) * TT],
                                              t_scr[:, :], L["nmrbc"][:, :]))
            en, sem = p.ev.n, p.ev
        else:
            p.g(lambda e, dk=dk: e.tensor_mul(t_scr_g[:, :],
                                              xbuf[:, dk * TT:(dk + 1) * TT],
                                              L["rsbc"][:, :]),
                waits=[(p.ev, p.ev.n)])
            p.g(lambda e, dk=dk: e.tensor_add(xbuf[:, dk * TT:(dk + 1) * TT],
                                              t_scr_g[:, :], L["nmrbc"][:, :]))
            en, sem = p.eg.n, p.eg
        p.y(lambda e, s=s, dk=dk, tsl=tsl: e.dma_start(
                out=Xout[s, dk, :, tsl], in_=xbuf[:, dk * TT:(dk + 1) * TT]),
            waits=[(sem, en)], inc=(s_xs, 16))



def build_ffn_prologue(p, c, W_fc, W_proj, ln_w, ln_b, scr_fcT, scr_projT,
                       persist, pre_waits=()):
    s_ld = p.sem("fp_ld")
    s_la = [p.sem(f"fp_la{i}") for i in range(4)]
    s_lb = [p.sem(f"fp_lb{i}") for i in range(4)]
    s_t = p.sem("fp_t")
    s_st = p.sem("fp_st")
    with p.scope():
        wt = [p.sbuf(f"fp_wt{i}", [128, 4096], F32) for i in range(4)]
        s1 = [p.sbuf(f"fp_s1{i}", [128, 1024], F32) for i in range(2)]
        s2 = [p.sbuf(f"fp_s2{i}", [128, 1024], F32) for i in range(2)]
        qt = [p.sbuf(f"fp_qt{i}", [128, 4096], BF16) for i in range(2)]
        tstage = p.sbuf("fp_tst", [128, 4096], BF16)
        qraw = p.sbuf("fp_qraw", [128, 4096], F32)
        abscols = p.sbuf("fp_abs", [128, 40], F32)
        gcols = p.sbuf("fp_gcols", [128, 32], F32)
        bcols = p.sbuf("fp_bcols", [128, 32], F32)
        lnv = p.sbuf("fp_lnv", [32, 128], F32)
        rows = p.sbuf("fp_rows", [1, 8], F32)
        ccol = p.sbuf("fp_ccol", [128, 2], F32)
        nccol = p.sbuf("fp_nccol", [128, 2], F32)
        colsum = p.sbuf("fp_colsum", [128, 2], F32)
        tp_bf = [p.psum(f"fp_tpbf{i}", [128, 1024], BF16) for i in range(2)]
        tp_f = p.psum("fp_tpf", [128, 128], F32)
        mi_ps = p.psum("fp_mi", [128, 512], F32)
        st_ps = p.psum("fp_stp", [128, 512], F32)
        cb_ps = p.psum("fp_cb", [128, 512], F32)

        CW = [(W_fc, 32, 1024), (W_proj, 8, 4096)]

        for w, (src, dst) in enumerate([(ln_w, gcols), (ln_b, bcols)]):
            p.y(lambda e, src=src: e.dma_start(
                    out=lnv[:, :], in_=src.rearrange("(a b) -> a b", a=32)),
                waits=list(pre_waits) + [(s_t, s_t.n)], inc=(s_ld, 16))
            ldn = s_ld.n
            p.t(lambda e: e.transpose(tp_f[:, :32], lnv[:32, :],
                                      c["ident_f32"][:32, :32]),
                waits=c["waits"] + [(s_ld, ldn), (p.ev, p.ev.n)], inc=(s_t, 1))
            p.v(lambda e, dst=dst: e.tensor_copy(dst[:, :32], tp_f[:, :32]),
                waits=[(s_t, s_t.n)])

        coli = 0
        red_vn = {}
        la_cnt = [0, 0, 0, 0]
        gi = 0
        for w, (W, ntiles, tw) in enumerate(CW):
            lo = coli
            for i in range(ntiles):
                b = gi % 4
                la_cnt[b] += 1
                p.y(lambda e, W=W, i=i, tw=tw, b=b: e.dma_start(
                        out=wt[b][:, :tw], in_=W[i * 128:(i + 1) * 128, :]),
                    waits=[(p.ev, red_vn.get(gi - 4, 0))], inc=(s_la[b], 16))
                lan = 16 * la_cnt[b]
                p.v(lambda e, ci=coli, tw=tw, b=b: e.reduce_sum(
                        abscols[:, ci:ci + 1], wt[b][:, :tw], AX.X,
                        apply_absolute_value=True),
                    waits=[(s_la[b], lan)])
                red_vn[gi] = p.ev.n
                coli += 1
                gi += 1
            p.v(lambda e, w=w, lo=lo, hi=coli: e.reduce_sum(
                    colsum[:, w:w + 1], abscols[:, lo:hi], AX.X))
            p.t(lambda e, w=w: e.matmul(mi_ps[0:1, w:w + 1], colsum[:, w:w + 1],
                                        c["ones_f32_col"][:, :], start=True, stop=True),
                waits=[(p.ev, p.ev.n)], inc=(s_t, 1))
            p.v(lambda e, w=w: e.tensor_scalar(rows[0:1, 2 + w:3 + w],
                                               mi_ps[0:1, w:w + 1],
                                               1.0 / 4194304.0, 1e-6,
                                               ALU.mult, ALU.add),
                waits=[(s_t, s_t.n)])
            p.v(lambda e, w=w: e.tensor_scalar(rows[0:1, 4 + w:5 + w],
                                               rows[0:1, 2 + w:3 + w],
                                               0.5, None, ALU.mult))
            p.t(lambda e, w=w: e.matmul(mi_ps[:, 4 + w:5 + w], c["ones_f32_row"][:, :],
                                        rows[0:1, 4 + w:5 + w], start=True, stop=True),
                waits=[(p.ev, p.ev.n)], inc=(s_t, 1))
            p.v(lambda e, w=w: e.tensor_copy(ccol[:, w:w + 1], mi_ps[:, 4 + w:5 + w]),
                waits=[(s_t, s_t.n)])
            p.v(lambda e, w=w: e.tensor_scalar(nccol[:, w:w + 1], ccol[:, w:w + 1],
                                               -1.0, None, ALU.mult))

        vnA = p.ev.n
        lb_cnt = [0, 0, 0, 0]
        sign_an = {}
        add_vn = {}
        tp_done = {}
        bank_ev = {0: 0, 1: 0}
        grp = 0
        gi = 0
        ck = 0
        for w, (W, ntiles, tw) in enumerate(CW):
            nsub = tw // 128
            for i in range(ntiles):
                b = gi % 4
                lb_cnt[b] += 1
                p.y(lambda e, W=W, i=i, tw=tw, b=b: e.dma_start(
                        out=wt[b][:, :tw], in_=W[i * 128:(i + 1) * 128, :]),
                    waits=[(p.ea, sign_an.get(gi - 4, 0)),
                           (p.ev, vnA if gi < 4 else 0)],
                    inc=(s_lb[b], 16))
                lbn = 16 * lb_cnt[b]
                bq = gi % 2
                for st in range(tw // 1024):
                    sl = slice(st * 1024, (st + 1) * 1024)
                    cb = ck % 2
                    p.s(lambda e, sl=sl, w=w, b=b, cb=cb: e.sign(
                            s1[cb][:, :], wt[b][:, sl], bias=nccol[:, w:w + 1]),
                        waits=[(s_lb[b], lbn),
                               (p.ev, add_vn.get(ck - 2, vnA if ck < 2 else 0))])
                    p.s(lambda e, sl=sl, w=w, b=b, cb=cb: e.sign(
                            s2[cb][:, :], wt[b][:, sl], bias=ccol[:, w:w + 1]))
                    an = p.ea.n
                    p.v(lambda e, cb=cb: e.tensor_add(
                            s1[cb][:, :], s1[cb][:, :], s2[cb][:, :]),
                        waits=[(p.ea, an)])
                    p.v(lambda e, sl=sl, bq=bq, cb=cb: e.tensor_scalar(
                            qt[bq][:, sl], s1[cb][:, :], 0.5, None, ALU.mult),
                        waits=([(s_t, tp_done.get(gi - 2, 0))] if st == 0 else []))
                    add_vn[ck] = p.ev.n
                    ck += 1
                sign_an[gi] = p.ea.n
                qt_vn = p.ev.n
                if w == 0:
                    bank = grp % 2
                    grp += 1
                    for j in range(nsub):
                        p.t(lambda e, j=j, bq=bq, bank=bank: e.transpose(
                                tp_bf[bank][:, j * 128:(j + 1) * 128],
                                qt[bq][:, j * 128:(j + 1) * 128],
                                c["ident_bf16"][:, :]),
                            waits=([(p.ev, qt_vn)] if j == 0 else []),
                            inc=(s_t, 1))
                    tn = s_t.n
                    p.v(lambda e, bank=bank: e.tensor_copy(
                            tstage[:, 0:1024], tp_bf[bank][:, 0:1024]),
                        waits=[(s_t, tn), (s_st, s_st.n)])
                    bank_ev[bank] = p.ev.n
                    vn = p.ev.n
                    for j in range(nsub):
                        p.y(lambda e, j=j, i=i: e.dma_start(
                                out=scr_fcT[j, :, i * 128:(i + 1) * 128],
                                in_=tstage[:, j * 128:(j + 1) * 128]),
                            waits=([(p.ev, vn)] if j == 0 else []),
                            inc=(s_st, 16))
                else:
                    for g in range(nsub // 8):
                        bank = grp % 2
                        grp += 1
                        for jj in range(8):
                            j = g * 8 + jj
                            p.t(lambda e, j=j, jj=jj, bq=bq, bank=bank:
                                e.transpose(
                                    tp_bf[bank][:, jj * 128:(jj + 1) * 128],
                                    qt[bq][:, j * 128:(j + 1) * 128],
                                    c["ident_bf16"][:, :]),
                                waits=([(p.ev, qt_vn if g == 0
                                         else bank_ev[bank])]
                                       if jj == 0 else []),
                                inc=(s_t, 1))
                        tn = s_t.n
                        p.v(lambda e, g=g, bank=bank: e.tensor_copy(
                                qraw[:, g * 1024:(g + 1) * 1024],
                                tp_bf[bank][:, 0:1024]),
                            waits=[(s_t, tn)] +
                                  ([(s_st, s_st.n)] if g == 0 else []))
                        for jj in range(8):
                            j = g * 8 + jj
                            p.v(lambda e, j=j, jj=jj, bank=bank:
                                e.tensor_scalar(
                                    tstage[:, j * 128:(j + 1) * 128],
                                    tp_bf[bank][:, jj * 128:(jj + 1) * 128],
                                    gcols[:, j:j + 1], None, ALU.mult))
                        bank_ev[bank] = p.ev.n
                        vn = p.ev.n
                        for jj in range(8):
                            j = g * 8 + jj
                            p.y(lambda e, j=j, i=i: e.dma_start(
                                    out=scr_projT[j, :, i * 128:(i + 1) * 128],
                                    in_=tstage[:, j * 128:(j + 1) * 128]),
                                waits=([(p.ev, vn)] if jj == 0 else []),
                                inc=(s_st, 16))
                    vn_all = p.ev.n
                    for j in range(nsub):
                        p.t(lambda e, i=i, j=j: e.matmul(
                                cb_ps[:, i:i + 1],
                                qraw[:, j * 128:(j + 1) * 128],
                                bcols[:, j:j + 1],
                                start=(j == 0), stop=(j == nsub - 1)),
                            waits=([(p.ev, vn_all)] if j == 0 else []),
                            inc=(s_t, 1))
                        p.t(lambda e, i=i, j=j: e.matmul(
                                st_ps[:, i:i + 1],
                                tstage[:, j * 128:(j + 1) * 128],
                                c["ones_bf16_col"][:, :],
                                start=(j == 0), stop=(j == nsub - 1)),
                            inc=(s_t, 1))
                tp_done[gi] = s_t.n
                gi += 1
        p.v(lambda e: e.tensor_copy(persist["st_col"][:, :], st_ps[:, 0:8]),
            waits=[(s_t, s_t.n)])
        p.v(lambda e: e.tensor_copy(persist["cb_col"][:, :], cb_ps[:, 0:8]))
        done = [(p.ev, p.ev.n), (s_st, s_st.n), (p.ea, p.ea.n), (s_t, s_t.n),
                (s_ld, s_ld.n)]
        p.barrier(done)
    return done


def build_ffn(p, c, Xin, Xout, scr_fcT, scr_projT, persist, pre_waits=()):
    NHK = H_FFN // 128
    s_w = p.sem("ff_w")
    s_wp = [p.sem(f"ff_wp{i}") for i in range(2)]
    s_xl = p.sem("ff_xl")
    s_xs = p.sem("ff_xs")
    s_t = p.sem("ff_t")
    wfc = [p.sbuf(f"ff_wfc{dk}", [128, H_FFN], BF16) for dk in range(DCH)]
    wprs = [p.sbuf(f"ff_wprs{i}", [128, D], BF16) for i in range(2)]
    xbuf = p.sbuf("ff_x", [128, DCH * TT], F32)
    xb = p.sbuf("ff_xb", [128, DCH * TT], BF16)
    h_sb = p.sbuf("ff_h", [128, (H_FFN // 128) * TT], BF16)
    stg_a = [p.sbuf(f"ff_sa{i}", [128, TT], BF16) for i in range(2)]
    stg_b = [p.sbuf(f"ff_sb{i}", [128, TT], BF16) for i in range(2)]
    t_scr = p.sbuf("ff_t0", [128, TT], F32)
    t_scr_g = p.sbuf("ff_t0g", [128, TT], F32)
    rowbuf = p.sbuf("ff_rows", [1, 4 * TT], F32)
    rsbc = p.sbuf("ff_rsbc", [128, TT], F32)
    nmrbc = p.sbuf("ff_nmrbc", [128, TT], F32)
    pb = [p.psum(f"ff_pb{i}", [128, TT], F32) for i in range(8)]
    h_ps = [pb[0], pb[1]]
    st1_ps, st2_ps = pb[2], pb[3]
    rs_ps, nm_ps = pb[4], pb[5]
    R = dict(mu=slice(0, TT), mu2=slice(TT, 2 * TT), var=slice(TT, 2 * TT),
             sd=slice(TT, 2 * TT), rs=slice(2 * TT, 3 * TT),
             nmr=slice(3 * TT, 4 * TT))

    for dk in range(DCH):
        p.y(lambda e, dk=dk: e.dma_start(out=wfc[dk][:, :], in_=scr_fcT[dk, :, :]),
            waits=(list(pre_waits) if dk == 0 else []), inc=(s_w, 16))
    wn = s_w.n

    gelu_cnt = {}
    t_after_P_prev = 0

    def rows_and_bc(n_feat, wait_pairs):
        p.v(lambda e: e.tensor_scalar(rowbuf[0:1, R["mu"]], st1_ps[0:1, :],
                                      1.0 / n_feat, None, ALU.mult),
            waits=wait_pairs)
        vn = p.ev.n
        p.s(lambda e: e.activation(rowbuf[0:1, R["mu2"]], rowbuf[0:1, R["mu"]],
                                   AF.Square), waits=[(p.ev, vn)])
        an = p.ea.n
        p.v(lambda e: e.scalar_tensor_tensor(
                rowbuf[0:1, R["var"]], st2_ps[0:1, :], 1.0 / n_feat,
                rowbuf[0:1, R["mu2"]], ALU.mult, ALU.subtract),
            waits=[(p.ea, an)])
        vn = p.ev.n
        p.s(lambda e: e.activation(rowbuf[0:1, R["sd"]], rowbuf[0:1, R["var"]],
                                   AF.Sqrt, bias=c["eps_col"][0:1, :]),
            waits=[(p.ev, vn)])
        an = p.ea.n
        p.v(lambda e: e.reciprocal(rowbuf[0:1, R["rs"]], rowbuf[0:1, R["sd"]]),
            waits=[(p.ea, an)])
        p.v(lambda e: e.scalar_tensor_tensor(
                rowbuf[0:1, R["nmr"]], rowbuf[0:1, R["mu"]], -1.0,
                rowbuf[0:1, R["rs"]], ALU.mult, ALU.mult))
        vn = p.ev.n
        p.t(lambda e: e.matmul(rs_ps[:, :], c["ones_f32_row"][:, :],
                               rowbuf[0:1, R["rs"]], start=True, stop=True),
            waits=[(p.ev, vn)], inc=(s_t, 1))
        p.t(lambda e: e.matmul(nm_ps[:, :], c["ones_f32_row"][:, :],
                               rowbuf[0:1, R["nmr"]], start=True, stop=True),
            inc=(s_t, 1))
        tn = s_t.n
        p.v(lambda e: e.tensor_copy(rsbc[:, :], rs_ps[:, :]),
            waits=[(s_t, tn), (p.eg, p.eg.n)])
        p.v(lambda e: e.tensor_copy(nmrbc[:, :], nm_ps[:, :]))
        return p.ev.n

    NHK = H_FFN // 128
    for tt in range(NSEQ * NTT):
        s, tg = divmod(tt, NTT)
        tsl = slice(tg * TT, (tg + 1) * TT)
        for dk in range(DCH):
            p.y(lambda e, s=s, dk=dk, tsl=tsl: e.dma_start(
                    out=xbuf[:, dk * TT:(dk + 1) * TT], in_=Xin[s, dk, :, tsl]),
                waits=([(s_xs, s_xs.n), (p.ev, p.ev.n), (p.eg, p.eg.n)]
                       if dk == 0 else []),
                inc=(s_xl, 16))
        xln = s_xl.n
        for dk in range(DCH):
            p.v(lambda e, dk=dk: e.tensor_copy(xb[:, dk * TT:(dk + 1) * TT],
                                               xbuf[:, dk * TT:(dk + 1) * TT]),
                waits=([(s_xl, xln), (s_t, t_after_P_prev)] if dk == 0 else []))
        cast_vn = p.ev.n
        for hk in range(NHK):
            hsl = slice(hk * TT, (hk + 1) * TT)
            for dk in range(DCH):
                p.t(lambda e, hk=hk, dk=dk: e.matmul(
                        h_ps[hk % 2][:, :],
                        wfc[dk][:, hk * 128:(hk + 1) * 128],
                        xb[:, dk * TT:(dk + 1) * TT],
                        start=(dk == 0), stop=(dk == DCH - 1)),
                    waits=([(s_w, wn), (p.ev, cast_vn),
                            (p.ea, gelu_cnt.get((tt, hk - 2),
                                  gelu_cnt.get((tt - 1, NHK - 2 + hk), 0)))]
                           if dk == 0 else []),
                    inc=((s_t, 1) if dk == DCH - 1 else None))
            tn = s_t.n
            p.s(lambda e, hk=hk, hsl=hsl: e.activation(
                    h_sb[:, hsl], h_ps[hk % 2][:, :], AF.Gelu),
                waits=[(s_t, tn)])
            gelu_cnt[(tt, hk)] = p.ea.n
            p.v(lambda e, hk=hk, hsl=hsl: e.tensor_mul(
                    stg_b[hk % 2][:, :], h_sb[:, hsl], h_sb[:, hsl]),
                waits=[(s_t, s_t.n), (p.ea, gelu_cnt[(tt, hk)])])
            sq_vn = p.ev.n
            p.t(lambda e, hk=hk, hsl=hsl: e.matmul(
                    st1_ps[0:1, :], c["ones_bf16_col"][:, :], h_sb[:, hsl],
                    start=(hk == 0), stop=(hk == NHK - 1)),
                waits=[(p.ea, gelu_cnt[(tt, hk)])] +
                      ([(p.ev, p.ev.n)] if hk == 0 else []),
                inc=(s_t, 1))
            p.t(lambda e, hk=hk: e.matmul(
                    st2_ps[0:1, :], c["ones_bf16_col"][:, :], stg_b[hk % 2][:, :],
                    start=(hk == 0), stop=(hk == NHK - 1)),
                waits=[(p.ev, sq_vn)], inc=(s_t, 1))
        all_gelu = p.ea.n
        bc_vn = rows_and_bc(H_FFN, [(s_t, s_t.n)])
        wpr_mm_n = {}
        for hk in range(NHK):
            b = hk % 2
            p.y(lambda e, hk=hk, b=b: e.dma_start(out=wprs[b][:, :],
                                                  in_=scr_projT[hk, :, :]),
                waits=[(s_t, wpr_mm_n.get(hk - 2, 0))], inc=(s_wp[b], 16))
            wpn = s_wp[b].n
            for dk in range(DCH):
                p.t(lambda e, hk=hk, dk=dk, b=b: e.matmul(
                        pb[dk][:, :],
                        wprs[b][:, dk * 128:(dk + 1) * 128],
                        h_sb[:, hk * TT:(hk + 1) * TT],
                        start=(hk == 0), stop=(hk == NHK - 1)),
                    waits=([(p.ea, all_gelu), (s_wp[b], wpn), (p.ev, bc_vn)]
                           if dk == 0 else []),
                    inc=((s_t, 1) if dk == DCH - 1 else None))
            wpr_mm_n[hk] = s_t.n
        tn_P = s_t.n
        for dk in range(DCH):
            p.v(lambda e, dk=dk: e.tensor_mul(t_scr[:, :], pb[dk][:, :],
                                              rsbc[:, :]),
                waits=[(s_t, tn_P)])
            p.v(lambda e, dk=dk: e.scalar_tensor_tensor(
                    t_scr[:, :], nmrbc[:, :], persist["st_col"][:, dk:dk + 1],
                    t_scr[:, :], ALU.mult, ALU.add))
            p.v(lambda e, dk=dk: e.scalar_tensor_tensor(
                    xbuf[:, dk * TT:(dk + 1) * TT], t_scr[:, :],
                    persist["cb_col"][:, dk:dk + 1],
                    xbuf[:, dk * TT:(dk + 1) * TT], ALU.add, ALU.add))
        epi_vn = p.ev.n
        for dk in range(DCH):
            p.g(lambda e, dk=dk: e.tensor_copy(stg_a[dk % 2][:, :],
                                               xbuf[:, dk * TT:(dk + 1) * TT]),
                waits=[(p.ev, epi_vn), (s_t, s_t.n)])
            p.g(lambda e, dk=dk: e.tensor_mul(stg_b[dk % 2][:, :],
                                              xbuf[:, dk * TT:(dk + 1) * TT],
                                              xbuf[:, dk * TT:(dk + 1) * TT]))
            gn = p.eg.n
            p.t(lambda e, dk=dk: e.matmul(
                    st1_ps[0:1, :], c["ones_bf16_col"][:, :], stg_a[dk % 2][:, :],
                    start=(dk == 0), stop=(dk == DCH - 1)),
                waits=[(p.eg, gn)] + ([(p.ev, p.ev.n)] if dk == 0 else []),
                inc=(s_t, 1))
            p.t(lambda e, dk=dk: e.matmul(
                    st2_ps[0:1, :], c["ones_bf16_col"][:, :], stg_b[dk % 2][:, :],
                    start=(dk == 0), stop=(dk == DCH - 1)),
                inc=(s_t, 1))
        rows_and_bc(D, [(s_t, s_t.n)])
        for dk in range(DCH):
            if dk % 2 == 0:
                p.v(lambda e, dk=dk: e.tensor_mul(t_scr[:, :],
                                                  xbuf[:, dk * TT:(dk + 1) * TT],
                                                  rsbc[:, :]))
                p.v(lambda e, dk=dk: e.tensor_add(xbuf[:, dk * TT:(dk + 1) * TT],
                                                  t_scr[:, :], nmrbc[:, :]))
                en, sem = p.ev.n, p.ev
            else:
                p.g(lambda e, dk=dk: e.tensor_mul(t_scr_g[:, :],
                                                  xbuf[:, dk * TT:(dk + 1) * TT],
                                                  rsbc[:, :]),
                    waits=[(p.ev, p.ev.n)])
                p.g(lambda e, dk=dk: e.tensor_add(xbuf[:, dk * TT:(dk + 1) * TT],
                                                  t_scr_g[:, :], nmrbc[:, :]))
                en, sem = p.eg.n, p.eg
            p.y(lambda e, s=s, dk=dk, tsl=tsl: e.dma_start(
                    out=Xout[s, dk, :, tsl], in_=xbuf[:, dk * TT:(dk + 1) * TT]),
                waits=[(sem, en)], inc=(s_xs, 16))
        t_after_P_prev = s_t.n
    return [(s_xs, s_xs.n)]



MAMBA_PLIM = None


def build_mamba_prologue(p, c, in_proj_w, out_proj_w, scr_inT, scr_outT,
                         pre_waits=()):
    s_ld = [p.sem(f"mp_ld{i}") for i in range(2)]
    s_t = p.sem("mp_t")
    s_st4 = [[p.sem(f"mp_st{i}{j}") for j in range(2)] for i in range(2)]
    with p.scope():
        qt = [p.sbuf(f"mp_qt{i}", [128, 2048], BF16) for i in range(2)]
        tstage = [p.sbuf(f"mp_tst{i}", [128, 2048], BF16) for i in range(2)]
        tp_bf = [p.psum(f"mp_tpbf{i}", [128, 1024], BF16) for i in range(2)]
        CW = [(in_proj_w, 35, 1024, scr_inT, DCH),
              (out_proj_w, 8, 2048, scr_outT, 16)]
        if MAMBA_PLIM is not None:
            CW = [(W, min(nt, MAMBA_PLIM[wi]), tw, scr, ns)
                  for wi, (W, nt, tw, scr, ns) in enumerate(CW)]
        tile_mm = {}
        bank_ev = {0: 0, 1: 0}
        st_by_slot = {0: None, 1: None}
        ti = 0
        for w, (W, ntiles, tw, scr, nsub) in enumerate(CW):
            for i in range(ntiles):
                rows = 128 if (w == 1 or i < 34) else 32
                b = ti % 2
                p.gdma(lambda e, W=W, i=i, tw=tw, rows=rows, b=b: e.dma_start(
                        out=qt[b][:rows, :tw], in_=W[i * 128:i * 128 + rows, :]),
                    waits=list(pre_waits) + [(s_t, tile_mm.get(ti - 2, 0))],
                    inc=(s_ld[b], 16))
                ldn = s_ld[b].n
                nround = nsub // 8
                for r in range(nround):
                    bank = (ti * nround + r) % 2
                    for jj in range(8):
                        j = r * 8 + jj
                        p.t(lambda e, j=j, jj=jj, b=b, bank=bank, rows=rows:
                            e.transpose(
                                tp_bf[bank][:, jj * 128:jj * 128 + rows],
                                qt[b][:rows, j * 128:(j + 1) * 128],
                                c["ident_bf16"][:rows, :rows]),
                            waits=(c["waits"] + [(s_ld[b], ldn),
                                                 (p.ev, bank_ev[bank])]
                                   if jj == 0 else []),
                            inc=((s_t, 1) if jj == 7 else None))
                    tn = s_t.n
                    p.v(lambda e, r=r, b=b, bank=bank, rows=rows: e.tensor_copy(
                            tstage[b].ap().rearrange("pp (a q) -> pp a q", q=128)[:, r * 8:(r + 1) * 8, 0:rows],
                            tp_bf[bank].ap().rearrange("pp (a q) -> pp a q", q=128)[:, :, 0:rows]),
                        waits=[(s_t, tn)] + (st_by_slot[b] or []))
                    bank_ev[bank] = p.ev.n
                vn = p.ev.n
                tile_mm[ti] = s_t.n
                hs = nsub if nsub <= 8 else (nsub + 1) // 2
                for sh in range(2 if nsub > 8 else 1):
                    lo = sh * hs
                    hi = min(nsub, lo + hs)
                    sem_st = s_st4[b][sh]
                    p.y(lambda e, scr=scr, i=i, rows=rows, lo=lo, hi=hi, b=b: e.dma_start(
                            out=scr.rearrange("a pp ee -> pp a ee")[:, lo:hi, i * 128:i * 128 + rows],
                            in_=tstage[b].ap().rearrange("pp (a q) -> pp a q", q=128)[:, lo:hi, 0:rows]),
                        waits=[(p.ev, vn)], inc=(sem_st, 16))
                st_by_slot[b] = [(s_st4[b][0], s_st4[b][0].n),
                                 (s_st4[b][1], s_st4[b][1].n)]
                ti += 1
        done = ([(p.ev, p.ev.n), (s_t, s_t.n),
                 (s_ld[0], s_ld[0].n), (s_ld[1], s_ld[1].n)] +
                [(s_st4[i][j], s_st4[i][j].n) for i in range(2) for j in range(2)])
        p.barrier(done)
    return done


MAMBA_LIMIT = None


def build_mamba(p, c, Xin, Xout, scr_inT, scr_outT, conv_w, conv_b, dt_bias,
                A_log, Dp, norm_w, ln_w, ln_b, pre_waits=(), dbg=None):
    lim = MAMBA_LIMIT or (NSEQ, NTT, TT // TC, NHEADS)

    def dump(name, tile):
        if dbg is None or name not in dbg:
            return
        p.barrier([])
        p.y(lambda e: e.dma_start(out=dbg[name], in_=tile), inc=(s_xl, 16))
        p.barrier([(s_xl, s_xl.n)])
    """x1 = LN(x + MambaBlock(x)). MambaBlock = postLN(gamma,beta) of
    (out_proj(gated-RMS(SSD(conv(in_proj x)))) + x)."""
    NCC = CONV_DIM // 128
    NZC = D_INNER // 128
    NEC = D_IN_PROJ // 128
    s_w = p.sem("mm_w")
    s_wa = [p.sem(f"mm_wa{i}") for i in range(2)]
    s_xl = p.sem("mm_xl")
    s_xs = p.sem("mm_xs")
    s_t = p.sem("mm_t")
    s_fl = p.sem("mm_fl")
    s_rb = p.sem("mm_rb")
    s_rb2 = [p.sem(f"mm_rb2{i}") for i in range(2)]
    lam_flat_dram = p.nc.dram_tensor("lam_flat_dram", [1, 4 * NHEADS * TC], F32,
                                     kind="Internal").ap()
    rbc_cons = {0: 0, 1: 0}
    wA = [p.sbuf(f"mm_wA{i}", [128, DCH * 128], BF16) for i in range(2)]
    z_sil = p.sbuf("mm_zsil", [128, NZC * TT], BF16)
    xbc = p.sbuf("mm_xbc", [128, NCC * (TT + 3)], BF16)
    xssm = p.sbuf("mm_xssm", [128, NZC * TT], BF16)
    bcb = p.sbuf("mm_bcb", [128, 2 * TT], BF16)
    y_fm = p.sbuf("mm_yfm", [128, NZC * TT], BF16)
    xe = p.sbuf("mm_xe", [128, DCH * TT], F32)
    xb1 = p.sbuf("mm_xb1", [128, DCH * TT], F32)
    X_tm = p.sbuf("mm_xtm", [128, D_INNER], BF16)
    Y_tm = p.sbuf("mm_ytm", [128, D_INNER], BF16)
    xt_all = p.sbuf("mm_xtall", [128, D_INNER], BF16)
    H_f = p.sbuf("mm_hf", [128, D_INNER], F32)
    H_b = p.sbuf("mm_hb", [128, D_INNER], BF16)
    B_tm = p.sbuf("mm_btm", [128, 128], BF16)
    mt_half = p.sbuf("mm_mthalf", [128, 2048], BF16)
    lam_rbc = [p.sbuf(f"mm_lamrbc{i}", [128, 2048], F32) for i in range(2)]
    ytmp4 = p.sbuf("mm_ytmp4", [128, 256], F32)
    dt_tm_bf = p.sbuf("mm_dttmb", [128, 128], BF16)
    dtr = p.sbuf("mm_dtr", [32, TT], F32)
    dt_fm = p.sbuf("mm_dtfm", [32, TT], F32)
    r_scr = p.sbuf("mm_rscr", [32, TT], F32)
    a_scr = p.sbuf("mm_ascr", [32, TT], F32)
    lam_fm = p.sbuf("mm_lam", [32, TT], F32)
    lam_loc_all = p.sbuf("mm_lamloc", [32, TT], F32)
    zeros32 = p.sbuf("mm_z32", [32, TT], F32)
    carry = p.sbuf("mm_carry", [32, 2], F32)
    lam_tm_all = p.sbuf("mm_lamtm", [128, 128], F32)
    dt_tm_all = p.sbuf("mm_dttm", [128, 128], F32)
    expl_all = p.sbuf("mm_expltm", [128, 128], F32)
    dlbc_all = p.sbuf("mm_dlbc", [128, 128], F32)
    decs_all = p.sbuf("mm_decs", [128, 128], F32)
    w_all4 = p.sbuf("mm_wall", [128, 128], F32)
    wdt_all4 = p.sbuf("mm_wdt", [128, 128], BF16)
    dlrow4 = p.sbuf("mm_dlr", [1, 128], F32)
    dl_cols = p.sbuf("mm_dlcols", [32, 4], F32)
    cwcols = p.sbuf("mm_cw", [128, NCC * 4], F32)
    cbcols = p.sbuf("mm_cb", [128, NCC], F32)
    dtb_col = p.sbuf("mm_dtb", [32, 1], F32)
    nega_col = p.sbuf("mm_negA", [32, 1], F32)
    d_bc = p.sbuf("mm_dbc", [128, 32], BF16)
    nw_cols = p.sbuf("mm_nw", [128, NZC], F32)
    lnw_cols = p.sbuf("mm_lnw", [128, DCH], F32)
    lnb_cols = p.sbuf("mm_lnb", [128, DCH], F32)
    rn_bc = p.sbuf("mm_rnbc", [128, TT], BF16)
    stg_a = [p.sbuf(f"mm_sa{i}", [128, TT], BF16) for i in range(2)]
    stg_b = [p.sbuf(f"mm_sb{i}", [128, TT], BF16) for i in range(2)]
    t_scr = p.sbuf("mm_t0", [128, TT], F32)
    t_scr_g = p.sbuf("mm_t0g", [128, TT], F32)
    tmp_g = p.sbuf("mm_tmpg", [128, TT], F32)
    xbb = p.sbuf("mm_xbb", [128, DCH * TT], BF16)
    pbS = [p.psum(f"mm_pb{i}", [128, TT], F32) for i in range(7)]
    G_ps = pbS[0]
    R_ps = [pbS[1], pbS[2]]
    A_ps = [pbS[3], pbS[5]]
    B_ps = [pbS[4], pbS[6]]
    TP_ps = pbS[5]
    tp_bf = p.psum("mm_tpb", [128, 1024], BF16)
    L = make_lnctx(p, c, "mm", pbS[2], pbS[3], pbS[4], pbS[5], s_t)

    first_w = list(pre_waits)
    for a in range(NCC):
        p.y(lambda e, a=a: e.dma_start(out=cwcols[:, a * 4:(a + 1) * 4],
                                       in_=conv_w[a * 128:(a + 1) * 128, :]),
            waits=(first_w if a == 0 else []), inc=(s_w, 16))
        p.y(lambda e, a=a: e.dma_start(
                out=cbcols[:, a:a + 1],
                in_=conv_b[a * 128:(a + 1) * 128].rearrange("(x b) -> x b", b=1)),
            inc=(s_w, 16))
    p.y(lambda e: e.dma_start(out=dtb_col[:, :],
                              in_=dt_bias.rearrange("(x b) -> x b", b=1)),
        inc=(s_w, 16))
    p.y(lambda e: e.dma_start(out=zeros32[0:1, 0:32],
                              in_=A_log.rearrange("(x b) -> b x", b=1)),
        inc=(s_w, 16))
    p.y(lambda e: e.dma_start(out=zeros32[0:1, 32:64],
                              in_=Dp.rearrange("(x b) -> b x", b=1)),
        inc=(s_w, 16))
    for src, dst, n in [(norm_w, nw_cols, NZC), (ln_w, lnw_cols, DCH),
                        (ln_b, lnb_cols, DCH)]:
        p.y(lambda e, src=src, n=n: e.dma_start(
                out=zeros32[0:n, 64:192],
                in_=src.rearrange("(a b) -> a b", a=n)),
            waits=[(s_t, s_t.n)], inc=(s_w, 16))
        wn0 = s_w.n
        p.t(lambda e, n=n: e.transpose(TP_ps[:, 0:n], zeros32[0:n, 64:192],
                                       c["ident_f32"][:n, :n]),
            waits=c["waits"] + [(s_w, wn0), (p.ev, p.ev.n)], inc=(s_t, 1))
        p.v(lambda e, dst=dst, n=n: e.tensor_copy(dst[:, 0:n], TP_ps[:, 0:n]),
            waits=[(s_t, s_t.n)])
    wn0 = s_w.n
    p.s(lambda e: e.activation(zeros32[0:1, 192:224], zeros32[0:1, 0:32], AF.Exp),
        waits=[(s_w, wn0)])
    an = p.ea.n
    p.t(lambda e: e.transpose(TP_ps[0:32, 0:1], zeros32[0:1, 192:224],
                              c["ident_f32"][:1, :1]),
        waits=[(p.ea, an), (p.ev, p.ev.n)], inc=(s_t, 1))
    p.v(lambda e: e.tensor_scalar(nega_col[:, :], TP_ps[0:32, 0:1], -1.0, None,
                                  ALU.mult),
        waits=[(s_t, s_t.n)])
    p.t(lambda e: e.matmul(TP_ps[:, 64:96], c["ones_f32_row"][:, :],
                           zeros32[0:1, 32:64], start=True, stop=True),
        waits=[(s_w, wn0), (p.ev, p.ev.n)], inc=(s_t, 1))
    p.v(lambda e: e.tensor_copy(d_bc[:, :], TP_ps[:, 64:96]),
        waits=[(s_t, s_t.n)])
    p.v(lambda e: e.memset(zeros32[:, :], 0.0))
    zero_vn = p.ev.n

    for s in range(lim[0]):
        p.g(lambda e: e.memset(H_f[:, :], 0.0),
            waits=[(s_t, s_t.n), (p.ev, p.ev.n), (p.ea, p.ea.n)])
        p.g(lambda e: e.memset(H_b[:, :], 0.0))
        p.g(lambda e: e.memset(carry[:, :], 0.0))
        p.g(lambda e: e.memset(
                xbc.ap().rearrange("p (a w) -> p a w", a=NCC)[:, :, 0:3], 0.0))
        init_gn = p.eg.n
        for tt in range(lim[1]):
            tsl = slice(tt * TT, (tt + 1) * TT)
            for dk in range(DCH):
                p.y(lambda e, s=s, dk=dk, tsl=tsl: e.dma_start(
                        out=xe[:, dk * TT:(dk + 1) * TT], in_=Xin[s, dk, :, tsl]),
                    waits=([(s_xs, s_xs.n), (p.ev, p.ev.n), (p.eg, p.eg.n),
                            (p.ea, p.ea.n), (s_t, s_t.n)] if dk == 0 else []),
                    inc=(s_xl, 16))
            xln = s_xl.n
            for dk in range(DCH):
                p.v(lambda e, dk=dk: e.tensor_copy(xbb[:, dk * TT:(dk + 1) * TT],
                                                   xe[:, dk * TT:(dk + 1) * TT]),
                    waits=([(s_xl, xln), (s_t, s_t.n), (p.eg, init_gn)]
                           if dk == 0 else []))
            cast_vn = p.ev.n
            evac_cnt = {}
            mm_cnt = {}
            for ech in range(NEC + 1):
                ecols = 128 if ech < NEC else D_IN_PROJ - NEC * 128
                b = ech % 2
                p.y(lambda e, ech=ech, ecols=ecols, b=b: e.dma_start(
                        out=wA[b].ap().rearrange("p (a ee) -> p a ee", a=DCH)[:, :, 0:ecols],
                        in_=scr_inT.rearrange("a pp ee -> pp a ee")[:, :, ech * 128:ech * 128 + ecols]),
                    waits=[(s_t, mm_cnt.get(ech - 2, 0))], inc=(s_wa[b], 16))
                wan = s_wa[b].n
                for dk in range(DCH):
                    p.t(lambda e, ech=ech, ecols=ecols, b=b, dk=dk: e.matmul(
                            pbS[b][0:ecols, :],
                            wA[b][:, dk * 128:dk * 128 + ecols],
                            xbb[:, dk * TT:(dk + 1) * TT],
                            start=(dk == 0), stop=(dk == DCH - 1)),
                        waits=([(s_wa[b], wan), (p.ev, cast_vn)] +
                               ([evac_cnt[ech - 2]] if ech - 2 in evac_cnt else [])
                               if dk == 0 else []),
                        inc=((s_t, 1) if dk == DCH - 1 else None))
                tn = s_t.n
                mm_cnt[ech] = tn
                if ech < NZC:
                    p.s(lambda e, ech=ech: e.activation(
                            z_sil[:, ech * TT:(ech + 1) * TT], pbS[ech % 2][:, :],
                            AF.Silu),
                        waits=[(s_t, tn)])
                    evac_cnt[ech] = (p.ea, p.ea.n)
                elif ech < NZC + NCC:
                    a = ech - NZC
                    p.s(lambda e, a=a, ech=ech: e.activation(
                            xbc[:, a * (TT + 3) + 3:(a + 1) * (TT + 3)],
                            pbS[ech % 2][:, :], AF.Copy),
                        waits=[(s_t, tn)])
                    evac_cnt[ech] = (p.ea, p.ea.n)
                else:
                    p.v(lambda e, ech=ech: e.tensor_scalar(
                            dtr[:, :], pbS[ech % 2][0:32, :], dtb_col[:, 0:1],
                            None, ALU.add),
                        waits=[(s_t, tn)])
                    evac_cnt[ech] = (p.ev, p.ev.n)
            vn0 = p.ev.n
            p.s(lambda e: e.activation(r_scr[:, :], dtr[:, :], AF.Relu),
                waits=[(p.ev, vn0)])
            p.s(lambda e: e.activation(a_scr[:, :], dtr[:, :], AF.Abs))
            p.s(lambda e: e.activation(a_scr[:, :], a_scr[:, :], AF.Exp,
                                       scale=-1.0))
            p.s(lambda e: e.activation(a_scr[:, :], a_scr[:, :], AF.Ln,
                                       bias=c["ones_f32_col"][0:32, 0:1]))
            an = p.ea.n
            p.v(lambda e: e.tensor_add(dt_fm[:, :], r_scr[:, :], a_scr[:, :]),
                waits=[(p.ea, an)])
            p.v(lambda e: e.tensor_scalar(r_scr[:, :], dt_fm[:, :],
                                          nega_col[:, 0:1], None, ALU.mult))
            p.v(lambda e: e.tensor_copy(carry[:, 1:2], carry[:, 0:1]))
            p.v(lambda e: e.tensor_tensor_scan(lam_fm[:, :], r_scr[:, :],
                                               zeros32[:, :], carry[:, 0:1],
                                               ALU.add, ALU.add),
                waits=[(p.ev, zero_vn)])
            p.v(lambda e: e.tensor_copy(carry[:, 0:1], lam_fm[:, TT - 1:TT]))
            dt_vn = p.ev.n
            conv_first = True
            conv_sil = {}
            for a in range(NCC):
                base = a * (TT + 3)
                use_v = True
                acc = t_scr if (a % 2 == 0) else t_scr_g
                em = p.v
                if conv_first:
                    w0 = [(s_w, wn0), (s_t, s_t.n), (p.ea, p.ea.n)]
                    conv_first = False
                else:
                    w0 = [(p.ea, conv_sil.get(a - 2, 0))]
                em(lambda e, base=base, a=a, acc=acc: e.tensor_scalar(
                        acc[:, :], xbc[:, base:base + TT],
                        cwcols[:, 4 * a:4 * a + 1], None, ALU.mult),
                   waits=w0 + [evac_cnt[NZC + a]])
                for k2 in range(1, 4):
                    if use_v:
                        em(lambda e, base=base, a=a, k2=k2, acc=acc:
                           e.scalar_tensor_tensor(
                                acc[:, :], xbc[:, base + k2:base + k2 + TT],
                                cwcols[:, 4 * a + k2:4 * a + k2 + 1], acc[:, :],
                                ALU.mult, ALU.add))
                    else:
                        em(lambda e, base=base, a=a, k2=k2: e.tensor_scalar(
                                tmp_g[:, :], xbc[:, base + k2:base + k2 + TT],
                                cwcols[:, 4 * a + k2:4 * a + k2 + 1], None,
                                ALU.mult))
                        em(lambda e, acc=acc: e.tensor_add(acc[:, :], acc[:, :],
                                                           tmp_g[:, :]))
                ecnt = (p.ev, p.ev.n) if use_v else (p.eg, p.eg.n)
                if a < NZC:
                    p.s(lambda e, a=a, acc=acc: e.activation(
                            xssm[:, a * TT:(a + 1) * TT], acc[:, :], AF.Silu,
                            bias=cbcols[:, a:a + 1]),
                        waits=[ecnt, (s_t, s_t.n)])
                else:
                    p.s(lambda e, a=a, acc=acc: e.activation(
                            bcb[:, (a - NZC) * TT:(a - NZC + 1) * TT], acc[:, :],
                            AF.Silu, bias=cbcols[:, a:a + 1]),
                        waits=[ecnt, (s_t, s_t.n)])
                conv_sil[a] = p.ea.n
            conv_an = p.ea.n
            p.v(lambda e: e.tensor_copy(
                    xbc.ap().rearrange("p (a w) -> p a w", a=NCC)[:, :, 0:3],
                    xbc.ap().rearrange("p (a w) -> p a w", a=NCC)[:, :, TT:TT + 3]),
                waits=[(p.ea, conv_an), (p.eg, p.eg.n)])
            if s == 0 and tt == 0:
                dump("z_sil", z_sil[:, :])
                dump("xssm", xssm[:, :])
                dump("bcb", bcb[:, :])
                dump("dt_fm", dt_fm[:, :])
                dump("lam_fm", lam_fm[:, :])
            for q in range(4):
                if q == 0:
                    p.v(lambda e: e.tensor_scalar(
                            lam_loc_all[:, 0:TC], lam_fm[:, 0:TC],
                            carry[:, 1:2], None, ALU.subtract),
                        waits=[(s_t, s_t.n)])
                else:
                    p.v(lambda e, q=q: e.tensor_scalar(
                            lam_loc_all[:, q * TC:(q + 1) * TC],
                            lam_fm[:, q * TC:(q + 1) * TC],
                            lam_fm[:, q * TC - 1:q * TC], None, ALU.subtract))
                p.v(lambda e, q=q: e.tensor_copy(dl_cols[:, q:q + 1],
                                                 lam_loc_all[:, (q + 1) * TC - 1:(q + 1) * TC]))
            lamall_vn = p.ev.n
            for q in range(4):
                p.y(lambda e, q=q: e.dma_start(
                        out=lam_flat_dram.rearrange("o (qq a b) -> o qq a b", qq=4, a=32)[:, q, :, :],
                        in_=lam_loc_all[:, q * TC:(q + 1) * TC]),
                    waits=([(p.ev, lamall_vn), (s_rb2[0], s_rb2[0].n),
                            (s_rb2[1], s_rb2[1].n)] if q == 0 else []),
                    inc=(s_fl, 16))
            fln = s_fl.n
            for q in range(4):
                p.t(lambda e, q=q: e.transpose(pbS[5][:, q * 32:(q + 1) * 32],
                                               lam_loc_all[:, q * TC:(q + 1) * TC],
                                               c["ident_f32"][:32, :32]),
                    waits=([(p.ev, lamall_vn)] if q == 0 else []),
                    inc=((s_t, 1) if q == 3 else None))
            tn = s_t.n
            p.v(lambda e: e.tensor_copy(lam_tm_all[:, :], pbS[5][:, 0:128]),
                waits=[(s_t, tn)])
            for q in range(4):
                p.t(lambda e, q=q: e.transpose(pbS[5][:, 128 + q * 32:128 + (q + 1) * 32],
                                               dt_fm[:, q * TC:(q + 1) * TC],
                                               c["ident_f32"][:32, :32]),
                    waits=([(p.ev, p.ev.n)] if q == 0 else []),
                    inc=((s_t, 1) if q == 3 else None))
            tn = s_t.n
            p.v(lambda e: e.tensor_copy(dt_tm_all[:, :], pbS[5][:, 128:256]),
                waits=[(s_t, tn)])
            p.v(lambda e: e.tensor_copy(dt_tm_bf[:, :], dt_tm_all[:, :]))
            for q in range(4):
                p.t(lambda e, q=q: e.transpose(pbS[5][0:1, 256 + q * 32:256 + (q + 1) * 32],
                                               dl_cols[:, q:q + 1],
                                               c["ident_f32"][:32, :32]),
                    waits=([(p.ev, p.ev.n)] if q == 0 else []),
                    inc=((s_t, 1) if q == 3 else None))
            tn = s_t.n
            p.v(lambda e: e.tensor_copy(dlrow4[0:1, :], pbS[5][0:1, 256:384]),
                waits=[(s_t, tn)])
            vn = p.ev.n
            for q in range(4):
                p.t(lambda e, q=q: e.matmul(
                        pbS[5][:, (q % 2) * 32:(q % 2) * 32 + 32],
                        c["ones_f32_row"][:, :], dlrow4[0:1, q * 32:(q + 1) * 32],
                        start=True, stop=True),
                    waits=[(p.ev, vn if q == 0 else p.ev.n)],
                    inc=(s_t, 1))
                tnq = s_t.n
                p.v(lambda e, q=q: e.tensor_copy(dlbc_all[:, q * 32:(q + 1) * 32],
                                                 pbS[5][:, (q % 2) * 32:(q % 2) * 32 + 32]),
                    waits=[(s_t, tnq)])
            vn = p.ev.n
            p.s(lambda e: e.activation(expl_all[:, :], lam_tm_all[:, :], AF.Exp),
                waits=[(p.ev, vn)])
            p.s(lambda e: e.activation(decs_all[:, :], dlbc_all[:, :], AF.Exp))
            p.v(lambda e: e.tensor_sub(w_all4[:, :], dlbc_all[:, :],
                                       lam_tm_all[:, :]),
                waits=[(p.ea, p.ea.n)])
            vn = p.ev.n
            p.s(lambda e: e.activation(w_all4[:, :], w_all4[:, :], AF.Exp),
                waits=[(p.ev, vn)])
            an = p.ea.n
            p.v(lambda e: e.tensor_mul(wdt_all4[:, :], w_all4[:, :],
                                       dt_tm_all[:, :]),
                waits=[(p.ea, an)])
            prep_vn = p.ev.n
            prep_an = p.ea.n
            rb_n = {}
            for q in range(lim[2]):
                qof = q * NHEADS * TC
                ch = slice(q * TC, (q + 1) * TC)
                for r in range(2):
                    for j in range(8):
                        k = r * 8 + j
                        p.t(lambda e, k=k, j=j, q=q: e.transpose(
                                tp_bf[:, j * 128:(j + 1) * 128],
                                xssm[:, k * TT + q * TC:k * TT + (q + 1) * TC],
                                c["ident_bf16"][:, :]),
                            waits=([(p.ev, p.ev.n), (p.ea, p.ea.n)] if j == 0 else []),
                            inc=((s_t, 1) if j == 7 else None))
                    tn = s_t.n
                    p.v(lambda e, r=r: e.tensor_copy(
                            X_tm[:, r * 1024:(r + 1) * 1024], tp_bf[:, :]),
                        waits=[(s_t, tn)])
                p.t(lambda e, q=q: e.transpose(
                        tp_bf[:, 0:128], bcb[:, q * TC:(q + 1) * TC],
                        c["ident_bf16"][:, :]),
                    waits=[(p.ev, p.ev.n)], inc=(s_t, 1))
                tn = s_t.n
                p.v(lambda e: e.tensor_copy(B_tm[:, :], tp_bf[:, 0:128]),
                    waits=[(s_t, tn)])
                xtm_vn = p.ev.n
                p.t(lambda e, q=q: e.matmul(
                        G_ps[:, 0:128], bcb[:, q * TC:(q + 1) * TC],
                        bcb[:, TT + q * TC:TT + (q + 1) * TC],
                        start=True, stop=True),
                    waits=[(p.ev, xtm_vn)], inc=(s_t, 1))
                g_tn = s_t.n
                p.v(lambda e, q=q: e.tensor_tensor(
                        xt_all[:, :], X_tm[:, :],
                        bass.AP(dt_tm_bf.ap().tensor, q * 32,
                                [[128, 128], [1, 32], [0, 64]]),
                        ALU.mult))
                xt3_vn = p.ev.n
                ev4 = {}
                for half in range(2):
                    hb2 = (2 * q + half) % 2
                    if (q, half) not in rb_n:
                        for ah in range(2 * q + half, min(8, 2 * q + half + 2)):
                            aq, ahf = divmod(ah, 2)
                            ab = ah % 2
                            p.y(lambda e, aq=aq, ahf=ahf, ab=ab: e.dma_start(
                                    out=lam_rbc[ab][:, :],
                                    in_=bass.AP(lam_flat_dram.tensor,
                                                aq * NHEADS * TC + ahf * 2048,
                                                [[0, 128], [1, 2048]])),
                                waits=[(s_fl, fln),
                                       (p.ev, rbc_cons[ab])],
                                inc=(s_rb2[ab], 16))
                            rb_n[(aq, ahf)] = s_rb2[ab].n
                    import os as _os
                    _dqh = _os.environ.get("MAMBA_DQH")
                    _hit = (_dqh is not None and dbg is not None and s == 0
                            and tt == 0 and (q, half) ==
                            tuple(int(z) for z in _dqh.split(",")))
                    if _hit:
                        p.barrier([(s_rb2[hb2], rb_n[(q, half)])])
                        p.y(lambda e, hb2=hb2: e.dma_start(
                                out=dbg["rbc_raw"], in_=lam_rbc[hb2][:, :]),
                            inc=(s_xl, 16))
                        p.barrier([(s_xl, s_xl.n)])
                    p.v(lambda e, q=q, half=half, hb2=hb2: e.tensor_tensor(
                            lam_rbc[hb2][:, :], lam_rbc[hb2][:, :],
                            bass.AP(lam_tm_all.ap().tensor, q * 32 + half * 16,
                                    [[128, 128], [1, 16], [0, 128]]),
                            ALU.subtract),
                        waits=[(s_rb2[hb2], rb_n[(q, half)])])
                    if _hit:
                        dump("rbc_t1", lam_rbc[hb2][:, :])
                    p.v(lambda e, hb2=hb2: e.tensor_tensor(
                            lam_rbc[hb2][:, :], lam_rbc[hb2][:, :],
                            bass.AP(c["cmt"].ap().tensor, 0,
                                    [[128, 128], [0, 16], [1, 128]]),
                            ALU.subtract))
                    vn2 = p.ev.n
                    if _hit:
                        dump("rbc_t2", lam_rbc[hb2][:, :])
                        dump("lam_tm_all", lam_tm_all[:, :])
                        dump("lam_loc_all", lam_loc_all[:, :])
                    p.s(lambda e, hb2=hb2: e.activation(
                            lam_rbc[hb2][:, :], lam_rbc[hb2][:, :], AF.Exp),
                        waits=[(p.ev, vn2)])
                    an = p.ea.n
                    p.v(lambda e, hb2=hb2: e.tensor_tensor(
                            mt_half[:, :], lam_rbc[hb2][:, :],
                            bass.AP(G_ps.ap().tensor, 0,
                                    [[TT, 128], [0, 16], [1, 128]]),
                            ALU.mult),
                        waits=[(p.ea, an), (s_t, s_t.n)])
                    rbc_cons[hb2] = p.ev.n
                    mt_vn = p.ev.n
                    for grp in range(4):
                        gb2 = grp % 2
                        for hh in range(4):
                            h = half * 16 + grp * 4 + hh
                            hl = grp * 4 + hh
                            p.t(lambda e, h=h, hl=hl, gb2=gb2, hh=hh: e.matmul(
                                    A_ps[gb2][:, hh * 64:(hh + 1) * 64],
                                    mt_half[:, hl * 128:(hl + 1) * 128],
                                    xt_all[:, h * 64:(h + 1) * 64],
                                    start=True, stop=True),
                                waits=([(p.ev, mt_vn), (s_t, s_t.n)]
                                       if hh == 0 and grp < 2 else
                                       ([(p.ev, ev4.get((half, grp - 2), 0))]
                                        if hh == 0 else [])),
                                inc=None)
                            p.t(lambda e, h=h, gb2=gb2, hh=hh, q=q: e.matmul(
                                    B_ps[gb2][:, hh * 64:(hh + 1) * 64],
                                    bcb[:, TT + q * TC:TT + (q + 1) * TC],
                                    H_b[:, h * 64:(h + 1) * 64],
                                    start=True, stop=True),
                                inc=((s_t, 1) if hh == 3 else None))
                        tn = s_t.n
                        h0 = half * 16 + grp * 4
                        p.v(lambda e, h0=h0, gb2=gb2, q=q: e.tensor_tensor(
                                ytmp4[:, :], B_ps[gb2][:, 0:256],
                                bass.AP(expl_all.ap().tensor, q * 32 + h0 - (h0 // 32) * 32 if False else q * 32 + (h0 % 32),
                                        [[128, 128], [1, 4], [0, 64]]),
                                ALU.mult),
                            waits=[(s_t, tn)])
                        p.v(lambda e, h0=h0, gb2=gb2: e.tensor_add(
                                Y_tm[:, h0 * 64:(h0 + 4) * 64],
                                A_ps[gb2][:, 0:256], ytmp4[:, :]))
                        ev4[(half, grp)] = p.ev.n
                skip_add_gn = {}
                for part in range(4):
                    dview = bass.AP(d_bc.ap().tensor, part * 8,
                                    [[32, 128], [1, 8], [0, 64]])
                    p.v(lambda e, part=part, dview=dview: e.tensor_tensor(
                            stg_b[1][:, :TT] if part % 2 == 0 else stg_a[1][:, :TT],
                            X_tm[:, part * 512:part * 512 + 512],
                            dview, ALU.mult),
                        waits=[(p.eg, skip_add_gn.get(part - 2, p.eg.n if part < 2 else 0))])
                    vnp = p.ev.n
                    p.g(lambda e, part=part: e.tensor_add(
                            Y_tm[:, part * 512:part * 512 + 512],
                            Y_tm[:, part * 512:part * 512 + 512],
                            stg_b[1][:, :TT] if part % 2 == 0 else stg_a[1][:, :TT]),
                        waits=[(p.ev, vnp)])
                    skip_add_gn[part] = p.eg.n
                skip_gn = p.eg.n
                p.v(lambda e, q=q: e.tensor_tensor(
                        xt_all[:, :], X_tm[:, :],
                        bass.AP(wdt_all4.ap().tensor, q * 32,
                                [[128, 128], [1, 32], [0, 64]]),
                        ALU.mult),
                    waits=[(s_t, s_t.n)])
                xt_vn = p.ev.n
                for part in range(4):
                    pb = A_ps[part % 2]
                    p.t(lambda e, part=part, pb=pb: e.matmul(
                            pb[:, :], B_tm[:, :],
                            xt_all[:, part * 512:(part + 1) * 512],
                            start=True, stop=True),
                        waits=[(p.ev, xt_vn if part < 2 else p.ev.n)],
                        inc=(s_t, 1))
                    tn = s_t.n
                    dcv = bass.AP(decs_all.ap().tensor, q * 32 + part * 8,
                                  [[128, 128], [1, 8], [0, 64]])
                    p.v(lambda e, part=part, dcv=dcv: e.tensor_tensor(
                            t_scr[:, :], H_f[:, part * 512:(part + 1) * 512],
                            dcv, ALU.mult),
                        waits=[(s_t, tn)])
                    p.v(lambda e, part=part, pb=pb: e.tensor_add(
                            H_f[:, part * 512:(part + 1) * 512], t_scr[:, :],
                            pb[:, :]))
                p.v(lambda e: e.tensor_copy(H_b[:, :], H_f[:, :]))
                for r in range(2):
                    vn = p.ev.n
                    for j in range(8):
                        k = r * 8 + j
                        p.t(lambda e, k=k, j=j: e.transpose(
                                tp_bf[:, j * 128:(j + 1) * 128],
                                Y_tm[:, k * 128:(k + 1) * 128],
                                c["ident_bf16"][:, :]),
                            waits=([(p.ev, vn), (p.eg, skip_gn)] if j == 0 else []),
                            inc=((s_t, 1) if j == 7 else None))
                    tn = s_t.n
                    p.v(lambda e, r=r, q=q: e.tensor_copy(
                            y_fm.ap().rearrange("p (k t) -> p k t", k=NZC)
                                [:, r * 8:(r + 1) * 8, q * TC:(q + 1) * TC],
                            tp_bf.ap().rearrange("p (j t) -> p j t", j=8)[:, :, :]),
                        waits=[(s_t, tn)])
            gst_vn = p.ev.n
            for k in range(NZC):
                p.v(lambda e, k=k: e.tensor_mul(
                        y_fm[:, k * TT:(k + 1) * TT], y_fm[:, k * TT:(k + 1) * TT],
                        z_sil[:, k * TT:(k + 1) * TT]),
                    waits=([(p.ea, p.ea.n)] if k == 0 else []))
            for k in range(NZC):
                p.s(lambda e, k=k: e.activation(stg_b[k % 2][:, :],
                                                y_fm[:, k * TT:(k + 1) * TT],
                                                AF.Square),
                    waits=[(p.ev, p.ev.n), (s_t, s_t.n)] if k == 0 else
                          [(s_t, s_t.n)])
                an = p.ea.n
                p.t(lambda e, k=k: e.matmul(
                        pbS[6][0:1, :], c["ones_bf16_col"][:, :],
                        stg_b[k % 2][:, :],
                        start=(k == 0), stop=(k == NZC - 1)),
                    waits=[(p.ea, an)] + ([(p.ev, p.ev.n)] if k == 0 else []),
                    inc=(s_t, 1))
            tn = s_t.n
            p.v(lambda e: e.tensor_scalar(L["rowbuf"][0:1, LNR["mu"]],
                                          pbS[6][0:1, :], 1.0 / D_INNER, None,
                                          ALU.mult),
                waits=[(s_t, tn)])
            vn = p.ev.n
            p.s(lambda e: e.activation(L["rowbuf"][0:1, LNR["sd"]],
                                       L["rowbuf"][0:1, LNR["mu"]], AF.Sqrt,
                                       bias=c["eps_col"][0:1, :]),
                waits=[(p.ev, vn)])
            an = p.ea.n
            p.v(lambda e: e.reciprocal(L["rowbuf"][0:1, LNR["rs"]],
                                       L["rowbuf"][0:1, LNR["sd"]]),
                waits=[(p.ea, an)])
            vn = p.ev.n
            p.t(lambda e: e.matmul(pbS[4][:, :], c["ones_f32_row"][:, :],
                                   L["rowbuf"][0:1, LNR["rs"]],
                                   start=True, stop=True),
                waits=[(p.ev, vn)], inc=(s_t, 1))
            tn = s_t.n
            p.v(lambda e: e.tensor_copy(rn_bc[:, :], pbS[4][:, :]),
                waits=[(s_t, tn)])
            for k in range(NZC):
                p.v(lambda e, k=k: e.scalar_tensor_tensor(
                        y_fm[:, k * TT:(k + 1) * TT],
                        y_fm[:, k * TT:(k + 1) * TT], nw_cols[:, k:k + 1],
                        rn_bc[:, :], ALU.mult, ALU.mult))
            gfin_vn = p.ev.n
            if s == 0 and tt == 0:
                dump("y_fm", y_fm[:, :])
            wo_mm = {}
            pass_ready_vn = gfin_vn
            for pa in range(2):
                for hk in range(NZC):
                    b = hk % 2
                    p.y(lambda e, hk=hk, b=b: e.dma_start(
                            out=wA[b][:, :],
                            in_=scr_outT[hk, :, :]),
                        waits=[(s_t, wo_mm.get((pa, hk - 2),
                                wo_mm.get((pa - 1, NZC - 2 + hk), mm_cnt.get(NEC, 0))))],
                        inc=(s_wa[b], 16))
                    wan = s_wa[b].n
                    for dd in range(4):
                        dk = pa * 4 + dd
                        p.t(lambda e, hk=hk, b=b, dd=dd, dk=dk: e.matmul(
                                pbS[dd][:, :],
                                wA[b][:, dk * 128:(dk + 1) * 128],
                                y_fm[:, hk * TT:(hk + 1) * TT],
                                start=(hk == 0), stop=(hk == NZC - 1)),
                            waits=([(s_wa[b], wan), (p.ev, pass_ready_vn)]
                                   if dd == 0 and hk == 0 else
                                   ([(s_wa[b], wan)] if dd == 0 else [])),
                            inc=((s_t, 1) if dd == 3 else None))
                    wo_mm[(pa, hk)] = s_t.n
                tn = s_t.n
                for dd in range(4):
                    dk = pa * 4 + dd
                    p.v(lambda e, dd=dd, dk=dk: e.tensor_add(
                            xb1[:, dk * TT:(dk + 1) * TT], pbS[dd][:, :],
                            xe[:, dk * TT:(dk + 1) * TT]),
                        waits=([(s_t, tn), (s_xl, xln), (p.eg, p.eg.n),
                                (s_xs, s_xs.n)] if dd == 0 else []))
                pass_ready_vn = p.ev.n
            pre1_vn = p.ev.n
            st_mm_n = {}
            for dk in range(DCH):
                p.g(lambda e, dk=dk: e.tensor_copy(stg_a[dk % 2][:, :],
                                                   xb1[:, dk * TT:(dk + 1) * TT]),
                    waits=([(p.ev, pre1_vn)] if dk == 0 else []) +
                          [(s_t, st_mm_n.get(dk - 2, s_t.n if dk == 0 else 0))])
                p.g(lambda e, dk=dk: e.tensor_mul(stg_b[dk % 2][:, :],
                                                  xb1[:, dk * TT:(dk + 1) * TT],
                                                  xb1[:, dk * TT:(dk + 1) * TT]))
                gn = p.eg.n
                p.t(lambda e, dk=dk: e.matmul(
                        L["st1"][0:1, :], c["ones_bf16_col"][:, :],
                        stg_a[dk % 2][:, :],
                        start=(dk == 0), stop=(dk == DCH - 1)),
                    waits=[(p.eg, gn)] + ([(p.ev, p.ev.n)] if dk == 0 else []),
                    inc=(s_t, 1))
                p.t(lambda e, dk=dk: e.matmul(
                        L["st2"][0:1, :], c["ones_bf16_col"][:, :],
                        stg_b[dk % 2][:, :],
                        start=(dk == 0), stop=(dk == DCH - 1)),
                    inc=(s_t, 1))
                st_mm_n[dk] = s_t.n
            ln_rows_and_bc(p, c, L, D, [(s_t, s_t.n)])
            for dk in range(DCH):
                p.v(lambda e, dk=dk: e.tensor_mul(t_scr[:, :],
                                                  xb1[:, dk * TT:(dk + 1) * TT],
                                                  L["rsbc"][:, :]))
                p.v(lambda e, dk=dk: e.tensor_add(t_scr[:, :], t_scr[:, :],
                                                  L["nmrbc"][:, :]))
                lnbv = bass.AP(lnb_cols.ap().tensor, dk, [[DCH, 128], [0, TT]])
                p.v(lambda e, dk=dk, lnbv=lnbv: e.scalar_tensor_tensor(
                        t_scr[:, :], t_scr[:, :], lnw_cols[:, dk:dk + 1],
                        lnbv, ALU.mult, ALU.add))
                p.v(lambda e, dk=dk: e.tensor_add(
                        xb1[:, dk * TT:(dk + 1) * TT], t_scr[:, :],
                        xe[:, dk * TT:(dk + 1) * TT]))
            pre2_vn = p.ev.n
            emit_boundary(p, c, L, xb1, stg_a, stg_b, t_scr, t_scr_g, Xout, s,
                          tsl, [(p.ev, pre2_vn)], s_xs)
    return [(s_xs, s_xs.n)]



def build_attn(p, c, Xin, Xout, Wqk, Wv, Wo, tau, pre_waits=()):
    s_wA = p.sem("at_wA")
    s_wv = p.sem("at_wv")
    s_xl = p.sem("at_xl")
    s_xs = p.sem("at_xs")
    s_t = p.sem("at_t")
    wA = p.sbuf("at_wA", [128, DCH * 1024], BF16)
    wv = p.sbuf("at_wv", [128, DCH * 1024], BF16)
    xw = p.sbuf("at_xw", [128, DCH * T], BF16)
    qk_tm = p.sbuf("at_qk", [128, (T // 128) * 1024], BF16)
    v_fm = p.sbuf("at_v", [128, DCH * T], BF16)
    xe = p.sbuf("at_xe", [128, DCH * TT], F32)
    stg_q = [p.sbuf(f"at_sq{i}", [128, 1024], BF16) for i in range(2)]
    P_all = p.sbuf("at_P", [128, 512], BF16)
    lg_sb = p.sbuf("at_lg", [128, 512], F32)
    q2row = p.sbuf("at_q2r", [1, 1024], F32)
    q2rbc = p.sbuf("at_q2rbc", [128, 1024], F32)
    q2cols = p.sbuf("at_q2c", [128, 8], F32)
    tauS = p.sbuf("at_tau", [128, 16], F32)
    tau_row = p.sbuf("at_taur", [1, 16], F32)
    gnm_row = p.sbuf("at_gnm", [1, 16], F32)
    mxc = p.sbuf("at_mxc", [128, 1], F32)
    ngm_cols = p.sbuf("at_ngmc", [128, 16], F32)
    s_colsb = p.sbuf("at_scols", [128, 8], F32)
    r_cols = p.sbuf("at_rcols", [128, 8], F32)
    stg_a = [p.sbuf(f"at_sa{i}", [128, TT], BF16) for i in range(2)]
    stg_b = [p.sbuf(f"at_sb{i}", [128, TT], BF16) for i in range(2)]
    t_scr = p.sbuf("at_t0", [128, TT], F32)
    t_scr_g = p.sbuf("at_t0g", [128, TT], F32)
    pbA = [p.psum(f"at_pb{i}", [128, TT], F32) for i in range(8)]
    L = make_lnctx(p, c, "at", pbA[2], pbA[3], pbA[4], pbA[5], s_t)

    p.y(lambda e: e.dma_start(out=tau_row[:, :],
                              in_=tau.rearrange("a b c -> b (a c)")),
        waits=list(pre_waits), inc=(s_xl, 16))
    xln0 = s_xl.n
    p.v(lambda e: e.tensor_scalar(tau_row[:, :], tau_row[:, :],
                                  1.0 / math.sqrt(float(T)), None, ALU.mult),
        waits=[(s_xl, xln0)])
    vn = p.ev.n
    p.t(lambda e: e.matmul(pbA[6][:, 0:16], c["ones_f32_row"][:, :],
                           tau_row[0:1, :], start=True, stop=True),
        waits=c["waits"] + [(p.ev, vn)], inc=(s_t, 1))
    p.v(lambda e: e.tensor_copy(tauS[:, :], pbA[6][:, 0:16]),
        waits=[(s_t, s_t.n)])

    for s in range(NSEQ):
        wA_gen = s * 2
        for dk in range(DCH):
            p.gdma(lambda e, dk=dk: e.dma_start(
                    out=wA[:, dk * 1024:(dk + 1) * 1024],
                    in_=Wqk[dk * 128:(dk + 1) * 128, :]),
                waits=([(s_t, s_t.n)] if dk == 0 else []), inc=(s_wA, 16))
        wqk_n = s_wA.n
        for dk in range(DCH):
            p.gdma(lambda e, dk=dk: e.dma_start(
                    out=wv[:, dk * 1024:(dk + 1) * 1024],
                    in_=Wv[dk * 128:(dk + 1) * 128, :]),
                waits=([(s_t, s_t.n)] if dk == 0 else []), inc=(s_wv, 16))
        wv_n = s_wv.n
        for tt in range(NTT):
            for dk in range(DCH):
                p.y(lambda e, s=s, dk=dk, tt=tt: e.dma_start(
                        out=xe[:, dk * TT:(dk + 1) * TT],
                        in_=Xin[s, dk, :, tt * TT:(tt + 1) * TT]),
                    waits=([(s_xs, s_xs.n), (p.ev, p.ev.n), (p.eg, p.eg.n),
                            (s_t, s_t.n)] if dk == 0 else []),
                    inc=(s_xl, 16))
            xln = s_xl.n
            for dk in range(DCH):
                p.v(lambda e, dk=dk, tt=tt: e.tensor_copy(
                        xw[:, dk * T + tt * TT:dk * T + (tt + 1) * TT],
                        xe[:, dk * TT:(dk + 1) * TT]),
                    waits=([(s_xl, xln), (s_t, s_t.n)] if dk == 0 else []))
        cast_vn = p.ev.n
        evac_an = {}
        for i in range(T // 128):
            for eh in range(2):
                k = 2 * i + eh
                for dk in range(DCH):
                    p.t(lambda e, i=i, eh=eh, dk=dk, k=k: e.matmul(
                            pbA[k % 2][:, :],
                            xw[:, dk * T + i * 128:dk * T + (i + 1) * 128],
                            wA[:, dk * 1024 + eh * 512:dk * 1024 + (eh + 1) * 512],
                            start=(dk == 0), stop=(dk == DCH - 1)),
                        waits=([(s_wA, wqk_n), (p.ev, cast_vn),
                                (p.ea, evac_an.get(k - 2, 0))]
                               if dk == 0 else []),
                        inc=((s_t, 1) if dk == DCH - 1 else None))
                tn = s_t.n
                p.s(lambda e, i=i, eh=eh, k=k: e.activation(
                        qk_tm[:, i * 1024 + eh * 512:i * 1024 + (eh + 1) * 512],
                        pbA[k % 2][:, :], AF.Copy),
                    waits=[(s_t, tn)])
                evac_an[k] = p.ea.n
        qk_an = p.ea.n
        qk_tn = s_t.n
        for dk in range(DCH):
            p.gdma(lambda e, dk=dk: e.dma_start(
                    out=wA[:, dk * 1024:(dk + 1) * 1024],
                    in_=Wo[dk * 128:(dk + 1) * 128, :]),
                waits=([(s_t, qk_tn)] if dk == 0 else []), inc=(s_wA, 16))
        wo_n = s_wA.n
        evac_cn = {-2: qk_an, -1: qk_an}
        for tt in range(NTT):
            for ek in range(DCH):
                k = tt * DCH + ek
                for dk in range(DCH):
                    p.t(lambda e, tt=tt, ek=ek, dk=dk, k=k: e.matmul(
                            pbA[k % 2][:, :],
                            wv[:, dk * 1024 + ek * 128:dk * 1024 + (ek + 1) * 128],
                            xw[:, dk * T + tt * TT:dk * T + (tt + 1) * TT],
                            start=(dk == 0), stop=(dk == DCH - 1)),
                        waits=([(s_wv, wv_n), (p.ea, evac_cn[k - 2])]
                               if dk == 0 else []),
                        inc=((s_t, 1) if dk == DCH - 1 else None))
                tn = s_t.n
                p.s(lambda e, tt=tt, ek=ek, k=k: e.activation(
                        v_fm[:, ek * T + tt * TT:ek * T + (tt + 1) * TT],
                        pbA[k % 2][:, :], AF.Copy),
                    waits=[(s_t, tn)])
                evac_cn[k] = p.ea.n
        v_an = p.ea.n
        for i in range(T // 128):
            p.v(lambda e, i=i: e.tensor_mul(stg_q[i % 2][:, :],
                                            qk_tm[:, i * 1024:(i + 1) * 1024],
                                            qk_tm[:, i * 1024:(i + 1) * 1024]),
                waits=[(s_t, s_t.n), (p.ea, p.ea.n)])
            an = p.ev.n
            p.t(lambda e, i=i: e.matmul(
                    pbA[2][0:1, :], c["ones_bf16_col"][:, :],
                    stg_q[i % 2][:, 0:512],
                    start=(i == 0), stop=(i == T // 128 - 1)),
                waits=[(p.ev, an)],
                inc=(s_t, 1))
            p.t(lambda e, i=i: e.matmul(
                    pbA[3][0:1, :], c["ones_bf16_col"][:, :],
                    stg_q[i % 2][:, 512:1024],
                    start=(i == 0), stop=(i == T // 128 - 1)),
                inc=(s_t, 1))
        tn = s_t.n
        p.v(lambda e: e.tensor_copy(q2row[0:1, 0:512], pbA[2][0:1, :]),
            waits=[(s_t, tn)])
        p.v(lambda e: e.tensor_copy(q2row[0:1, 512:1024], pbA[3][0:1, :]))
        vn = p.ev.n
        for hh in range(2):
            p.t(lambda e, hh=hh: e.matmul(pbA[6][:, :], c["ones_f32_row"][:, :],
                                          q2row[0:1, hh * 512:(hh + 1) * 512],
                                          start=True, stop=True),
                waits=[(p.ev, p.ev.n)], inc=(s_t, 1))
            tn = s_t.n
            p.v(lambda e, hh=hh: e.tensor_copy(q2rbc[:, hh * 512:(hh + 1) * 512],
                                               pbA[6][:, :]),
                waits=[(s_t, tn)])
        for j in range(DCH):
            p.t(lambda e, j=j: e.transpose(pbA[7][0:128, j:j + 1],
                                           q2row[0:1, j * 128:(j + 1) * 128],
                                           c["ident_f32"][:1, :1]),
                waits=([(p.ev, p.ev.n)] if j == 0 else []), inc=(s_t, 1))
        tn = s_t.n
        p.v(lambda e: e.tensor_copy(q2cols[:, :], pbA[7][0:128, 0:8]),
            waits=[(s_t, tn)])
        for h in range(16):
            r0 = 64 * (h % 2)
            cp = h // 2
            for i in range(T // 128):
                p.t(lambda e, h=h, i=i, r0=r0, cp=cp: e.matmul(
                        pbA[2][r0:r0 + 64, cp * 64:(cp + 1) * 64],
                        qk_tm[:, i * 1024 + h * 64:i * 1024 + (h + 1) * 64],
                        qk_tm[:, i * 1024 + h * 64:i * 1024 + (h + 1) * 64],
                        start=(i == 0), stop=(i == T // 128 - 1)),
                    waits=([(p.ev, p.ev.n), (p.ea, v_an)]
                           if h == 0 and i == 0 else []),
                    inc=((s_t, 1) if i == T // 128 - 1 else None))
        dot_tn = s_t.n
        for h in range(16):
            r0 = 64 * (h % 2)
            cp = h // 2
            sl = (slice(r0, r0 + 64), slice(cp * 64, (cp + 1) * 64))
            p.v(lambda e, sl=sl: e.tensor_scalar(
                    lg_sb[sl[0], sl[1]], pbA[2][sl[0], sl[1]],
                    2.0, None, ALU.mult),
                waits=[(s_t, dot_tn)])
            p.v(lambda e, sl=sl, r0=r0, cp=cp: e.tensor_scalar(
                    lg_sb[sl[0], sl[1]], lg_sb[sl[0], sl[1]],
                    q2cols[r0:r0 + 64, cp:cp + 1], None, ALU.subtract))
            p.v(lambda e, sl=sl, h=h, r0=r0: e.tensor_sub(
                    lg_sb[sl[0], sl[1]], lg_sb[sl[0], sl[1]],
                    q2rbc[r0:r0 + 64, h * 64:(h + 1) * 64]))
            p.v(lambda e, sl=sl, r0=r0, h=h: e.tensor_scalar(
                    lg_sb[sl[0], sl[1]], lg_sb[sl[0], sl[1]],
                    tauS[r0:r0 + 64, h:h + 1], None, ALU.mult))
            p.v(lambda e, sl=sl, r0=r0: e.reduce_max(
                    mxc[r0:r0 + 64, 0:1], lg_sb[sl[0], sl[1]], AX.X))
            vn = p.ev.n
            p.t(lambda e, r0=r0: e.transpose(pbA[6][0:1, 0:64],
                                             mxc[r0:r0 + 64, 0:1],
                                             c["ident_f32"][r0:r0 + 64, r0:r0 + 64]),
                waits=[(p.ev, vn)], inc=(s_t, 1))
            tn = s_t.n
            p.v(lambda e, h=h: e.reduce_max(gnm_row[0:1, h:h + 1],
                                            pbA[6][0:1, 0:64], AX.X, negate=True),
                waits=[(s_t, tn)])
            vn = p.ev.n
            p.t(lambda e, h=h: e.matmul(
                    pbA[7][:, 0:1], c["ones_f32_row"][:, :],
                    gnm_row[0:1, h:h + 1], start=True, stop=True),
                waits=[(p.ev, vn)], inc=(s_t, 1))
            tn = s_t.n
            p.v(lambda e, h=h: e.tensor_copy(
                    ngm_cols[:, h:h + 1], pbA[7][:, 0:1]),
                waits=[(s_t, tn)])
            vn = p.ev.n
            p.s(lambda e, sl=sl, r0=r0, h=h: e.activation(
                    P_all[sl[0], sl[1]], lg_sb[sl[0], sl[1]], AF.Exp,
                    bias=ngm_cols[r0:r0 + 64, h:h + 1]),
                waits=[(p.ev, vn)])
            an = p.ea.n
            p.v(lambda e, sl=sl, r0=r0, cp=cp: e.reduce_sum(
                    s_colsb[r0:r0 + 64, cp:cp + 1], P_all[sl[0], sl[1]], AX.X),
                waits=[(p.ea, an)])
            p.v(lambda e, r0=r0, cp=cp: e.reciprocal(
                    r_cols[r0:r0 + 64, cp:cp + 1], s_colsb[r0:r0 + 64, cp:cp + 1]))
        soft_vn = p.ev.n
        soft_an = p.ea.n
        for tt in range(NTT):
            for pr in range(8):
                b = (tt * 8 + pr) % 2
                for hh in range(2):
                    h = 2 * pr + hh
                    r0 = 64 * hh
                    p.t(lambda e, tt=tt, pr=pr, hh=hh, h=h, r0=r0, b=b: e.matmul(
                            pbA[5 + b][r0:r0 + 64, :],
                            P_all[r0:r0 + 64, pr * 64:(pr + 1) * 64],
                            v_fm[r0:r0 + 64, pr * T + tt * TT:pr * T + (tt + 1) * TT],
                            start=True, stop=True),
                        waits=([(p.ea, soft_an), (p.ev, soft_vn)]
                               if tt == 0 and pr == 0 and hh == 0 else
                               ([(p.ev, p.ev.n)] if hh == 0 else [])),
                        inc=((s_t, 1) if hh == 1 else None))
                tn = s_t.n
                p.v(lambda e, tt=tt, pr=pr, b=b: e.tensor_scalar(
                        xw[:, pr * T + tt * TT:pr * T + (tt + 1) * TT],
                        pbA[5 + b][:, :], r_cols[:, pr:pr + 1], None, ALU.mult),
                    waits=[(s_t, tn)])
        w_vn = p.ev.n
        for tt in range(NTT):
            tsl = slice(tt * TT, (tt + 1) * TT)
            for dk in range(DCH):
                p.y(lambda e, s=s, dk=dk, tsl=tsl: e.dma_start(
                        out=xe[:, dk * TT:(dk + 1) * TT], in_=Xin[s, dk, :, tsl]),
                    waits=([(s_xs, s_xs.n), (p.ev, p.ev.n), (p.eg, p.eg.n)]
                           if dk == 0 else []),
                    inc=(s_xl, 16))
            xln = s_xl.n
            add_vn = {}
            for ek in range(DCH):
                b = ek % 2
                for dk in range(DCH):
                    p.t(lambda e, tt=tt, ek=ek, dk=dk, b=b: e.matmul(
                            pbA[b][:, :],
                            wA[:, dk * 1024 + ek * 128:dk * 1024 + (ek + 1) * 128],
                            xw[:, dk * T + tt * TT:dk * T + (tt + 1) * TT],
                            start=(dk == 0), stop=(dk == DCH - 1)),
                        waits=(([(s_wA, wo_n)] if ek == 0 else []) +
                               [(p.ev, (w_vn if tt == 0 else p.ev.n) if ek < 2
                                 else add_vn[ek - 2])]
                               if dk == 0 else []),
                        inc=((s_t, 1) if dk == DCH - 1 else None))
                tn = s_t.n
                p.v(lambda e, ek=ek, b=b: e.tensor_add(
                        xe[:, ek * TT:(ek + 1) * TT], pbA[b][:, :],
                        xe[:, ek * TT:(ek + 1) * TT]),
                    waits=[(s_t, tn), (s_xl, xln)])
                add_vn[ek] = p.ev.n
            epi_vn = p.ev.n
            emit_boundary(p, c, L, xe, stg_a, stg_b, t_scr, t_scr_g, Xout, s,
                          tsl, [(p.ev, epi_vn)], s_xs)
    return [(s_xs, s_xs.n)]



def build_moe(p, c, Xin, Xout, gate_w, gate_b, rms_w, W1, B1, W2, B2, Yacc,
              pre_waits=()):
    NHK2 = H_MOE // 128
    s_w1 = p.sem("mo_w1")
    s_w2 = p.sem("mo_w2")
    s_xl = p.sem("mo_xl")
    s_xs = p.sem("mo_xs")
    s_t = p.sem("mo_t")
    s_acc = [p.sem(f"mo_acc{i}") for i in range(NTT)]
    s_sm = p.sem("mo_sm")
    xb = p.sbuf("mo_xb", [128, DCH * T], BF16)
    w1b = p.sbuf("mo_w1b", [128, DCH * H_MOE], BF16)
    w2b = p.sbuf("mo_w2b", [128, NHK2 * D], BF16)
    h_e = p.sbuf("mo_h", [128, NHK2 * TT], BF16)
    yt = [p.sbuf(f"mo_yt{i}", [128, DCH * TT], F32) for i in range(2)]
    xe = p.sbuf("mo_xe", [128, DCH * TT], F32)
    ngb = p.sbuf("mo_ngb", [128, NTT * TT], F32)
    gwT = p.sbuf("mo_gwT", [128, DCH * N_EXP], BF16)
    gb_col = p.sbuf("mo_gb", [N_EXP, 1], F32)
    rms2bc = p.sbuf("mo_rms2", [128, N_EXP], F32)
    b1cols = p.sbuf("mo_b1c", [128, NHK2 * N_EXP], F32)
    b2s = p.sbuf("mo_b2s", [128, DCH], F32)
    smv = p.sbuf("mo_smv", [N_EXP, H_MOE], F32)
    lgsb = p.sbuf("mo_lgsb", [N_EXP, TT], F32)
    ltm = p.sbuf("mo_ltm", [128, 32], F32)
    grow = p.sbuf("mo_grow", [1, 8], F32)
    gcolsb = p.sbuf("mo_gcol", [128, 16], F32)
    ngrow = p.sbuf("mo_ngrow", [1, TT], F32)
    stg_a = [p.sbuf(f"mo_sa{i}", [128, TT], BF16) for i in range(2)]
    stg_b = [p.sbuf(f"mo_sb{i}", [128, TT], BF16) for i in range(2)]
    t_scr = p.sbuf("mo_t0", [128, TT], F32)
    t_scr_g = p.sbuf("mo_t0g", [128, TT], F32)
    pbM = [p.psum(f"mo_pb{i}", [128, TT], F32) for i in range(8)]
    L = make_lnctx(p, c, "mo", pbM[4], pbM[5], pbM[6], pbM[7], s_t)

    p.y(lambda e: e.dma_start(out=smv[:N_EXP, :D], in_=gate_w[:, :]),
        waits=list(pre_waits), inc=(s_sm, 16))
    smn = s_sm.n
    for dk in range(DCH):
        p.t(lambda e, dk=dk: e.transpose(
                pbM[0][:, dk * 8:(dk + 1) * 8],
                smv[:N_EXP, dk * 128:(dk + 1) * 128], c["ident_f32"][:8, :8]),
            waits=(c["waits"] + [(s_sm, smn)] if dk == 0 else []),
            inc=((s_t, 1) if dk == DCH - 1 else None))
    p.v(lambda e: e.tensor_copy(gwT[:, :], pbM[0][:, 0:DCH * N_EXP]),
        waits=[(s_t, s_t.n)])
    p.y(lambda e: e.dma_start(out=smv[0:1, 0:8],
                              in_=gate_b.rearrange("(a b) -> a b", a=1)),
        waits=[(p.ev, p.ev.n)], inc=(s_sm, 16))
    smn = s_sm.n
    p.t(lambda e: e.transpose(pbM[1][0:8, 0:1], smv[0:1, 0:8],
                              c["ident_f32"][:1, :1]),
        waits=[(s_sm, smn)], inc=(s_t, 1))
    p.v(lambda e: e.tensor_copy(gb_col[:, :], pbM[1][0:8, 0:1]),
        waits=[(s_t, s_t.n)])
    p.y(lambda e: e.dma_start(out=smv[0:1, 8:16],
                              in_=rms_w.rearrange("(a b) -> a b", a=1)),
        waits=[(p.ev, p.ev.n)], inc=(s_sm, 16))
    smn = s_sm.n
    p.t(lambda e: e.matmul(pbM[2][:, 0:8], c["ones_f32_row"][:, :],
                           smv[0:1, 8:16], start=True, stop=True),
        waits=[(s_sm, smn)], inc=(s_t, 1))
    p.v(lambda e: e.tensor_scalar(rms2bc[:, :], pbM[2][:, 0:8], 2.0, None,
                                  ALU.mult),
        waits=[(s_t, s_t.n)])
    p.y(lambda e: e.dma_start(out=smv[:N_EXP, :H_MOE], in_=B1[:, :]),
        waits=[(s_t, s_t.n), (p.ev, p.ev.n)], inc=(s_sm, 16))
    smn = s_sm.n
    for hk in range(NHK2):
        p.t(lambda e, hk=hk: e.transpose(
                pbM[3][:, hk * 8:(hk + 1) * 8],
                smv[:N_EXP, hk * 128:(hk + 1) * 128], c["ident_f32"][:8, :8]),
            waits=([(s_sm, smn)] if hk == 0 else []),
            inc=((s_t, 1) if hk == NHK2 - 1 else None))
    p.v(lambda e: e.tensor_copy(b1cols[:, :], pbM[3][:, 0:NHK2 * N_EXP]),
        waits=[(s_t, s_t.n)])
    p.y(lambda e: e.dma_start(out=smv[:N_EXP, :D], in_=B2[:, :]),
        waits=[(s_t, s_t.n), (p.ev, p.ev.n)], inc=(s_sm, 16))
    smn = s_sm.n
    for dk in range(DCH):
        p.t(lambda e, dk=dk: e.transpose(
                pbM[0][:, 64 + dk * 8:64 + (dk + 1) * 8],
                smv[:N_EXP, dk * 128:(dk + 1) * 128], c["ident_f32"][:8, :8]),
            waits=([(s_sm, smn), (p.ev, p.ev.n)] if dk == 0 else []),
            inc=((s_t, 1) if dk == DCH - 1 else None))
    p.v(lambda e: e.reduce_sum(
            b2s.ap().rearrange("p (a b) -> p a b", a=DCH)[:, :, :],
            pbM[0].ap().rearrange("p (cc a b) -> p cc a b", cc=8, a=DCH)[:, 1, :, :],
            AX.X),
        waits=[(s_t, s_t.n)])

    for s in range(NSEQ):
        seq_start_tn = s_t.n
        for tt in range(NTT):
            tsl = slice(tt * TT, (tt + 1) * TT)
            for dk in range(DCH):
                p.y(lambda e, s=s, dk=dk, tsl=tsl: e.dma_start(
                        out=xe[:, dk * TT:(dk + 1) * TT], in_=Xin[s, dk, :, tsl]),
                    waits=([(s_xs, s_xs.n), (p.ev, p.ev.n), (p.eg, p.eg.n),
                            (s_t, s_t.n)] if dk == 0 else []),
                    inc=(s_xl, 16))
            xln = s_xl.n
            for dk in range(DCH):
                p.v(lambda e, dk=dk, tt=tt: e.tensor_copy(
                        xb[:, dk * T + tt * TT:dk * T + (tt + 1) * TT],
                        xe[:, dk * TT:(dk + 1) * TT]),
                    waits=([(s_xl, xln), (s_t, s_t.n)] if dk == 0 else []))
            cast_vn = p.ev.n
            for dk in range(DCH):
                p.t(lambda e, dk=dk, tt=tt: e.matmul(
                        pbM[2][0:N_EXP, :], gwT[:, dk * 8:(dk + 1) * 8],
                        xb[:, dk * T + tt * TT:dk * T + (tt + 1) * TT],
                        start=(dk == 0), stop=(dk == DCH - 1)),
                    waits=([(p.ev, cast_vn)] if dk == 0 else []),
                    inc=((s_t, 1) if dk == DCH - 1 else None))
            tn = s_t.n
            p.v(lambda e: e.tensor_scalar(lgsb[:, :], pbM[2][0:N_EXP, :],
                                          gb_col[:, 0:1], None, ALU.add),
                waits=[(s_t, tn)])
            vn = p.ev.n
            for q in range(4):
                p.t(lambda e, q=q: e.transpose(
                        pbM[3][:, q * 8:(q + 1) * 8],
                        lgsb[:N_EXP, q * 128:(q + 1) * 128],
                        c["ident_f32"][:8, :8]),
                    waits=([(p.ev, vn)] if q == 0 else []),
                    inc=((s_t, 1) if q == 3 else None))
            p.v(lambda e: e.tensor_copy(ltm[:, :], pbM[3][:, 0:32]),
                waits=[(s_t, s_t.n)])
            p.v(lambda e: e.tensor_mul(t_scr[:, 0:32], ltm[:, :], ltm[:, :]))
            p.v(lambda e: e.reduce_sum(
                    gcolsb[:, 0:4],
                    t_scr.ap().rearrange("p (cc a b) -> p cc a b", cc=16, a=4)[:, 0, :, :],
                    AX.X))
            p.s(lambda e: e.activation(gcolsb[:, 4:8], gcolsb[:, 0:4], AF.Sqrt,
                                       scale=1.0 / 8.0, bias=c["epsf_col"][:, 0:1]),
                waits=[(p.ev, p.ev.n)])
            p.v(lambda e: e.reciprocal(gcolsb[:, 8:12], gcolsb[:, 4:8]),
                waits=[(p.ea, p.ea.n)])
            for q in range(4):
                p.v(lambda e, q=q: e.tensor_scalar(
                        ltm[:, q * 8:(q + 1) * 8], ltm[:, q * 8:(q + 1) * 8],
                        gcolsb[:, 8 + q:9 + q], None, ALU.mult))
                p.v(lambda e, q=q: e.tensor_mul(
                        ltm[:, q * 8:(q + 1) * 8], ltm[:, q * 8:(q + 1) * 8],
                        rms2bc[:, :]))
            p.v(lambda e: e.reduce_max(
                    gcolsb.ap().rearrange("p (a b) -> p a b", a=4)[:, 0:4, 0:1],
                    ltm.ap().rearrange("p (a b) -> p a b", a=4)[:, :, :],
                    AX.X, negate=True))
            vn = p.ev.n
            for q in range(4):
                p.s(lambda e, q=q: e.activation(
                        ltm[:, q * 8:(q + 1) * 8], ltm[:, q * 8:(q + 1) * 8],
                        AF.Exp, bias=gcolsb[:, q:q + 1]),
                    waits=([(p.ev, vn)] if q == 0 else []))
            an = p.ea.n
            p.v(lambda e: e.reduce_sum(
                    gcolsb[:, 4:8],
                    ltm.ap().rearrange("p (a b) -> p a b", a=4)[:, :, :], AX.X),
                waits=[(p.ea, an)])
            p.v(lambda e: e.reduce_max(
                    gcolsb[:, 8:12],
                    ltm.ap().rearrange("p (a b) -> p a b", a=4)[:, :, :], AX.X))
            p.v(lambda e: e.reciprocal(gcolsb[:, 12:16], gcolsb[:, 4:8]))
            p.v(lambda e: e.tensor_mul(gcolsb[:, 8:12], gcolsb[:, 8:12],
                                       gcolsb[:, 12:16]))
            p.v(lambda e: e.tensor_scalar(gcolsb[:, 12:16], gcolsb[:, 8:12],
                                          1e-6, None, ALU.add))
            p.v(lambda e: e.reciprocal(gcolsb[:, 12:16], gcolsb[:, 12:16]))
            p.v(lambda e: e.tensor_mul(gcolsb[:, 8:12], gcolsb[:, 8:12],
                                       gcolsb[:, 12:16]))
            vn = p.ev.n
            for q in range(4):
                p.t(lambda e, q=q: e.transpose(
                        pbM[3][0:1, q * 128:(q + 1) * 128],
                        gcolsb[:, 8 + q:9 + q], c["ident_f32"][:, :]),
                    waits=([(p.ev, vn)] if q == 0 else []),
                    inc=((s_t, 1) if q == 3 else None))
            p.v(lambda e: e.tensor_copy(ngrow[0:1, :], pbM[3][0:1, 0:TT]),
                waits=[(s_t, s_t.n)])
            vn = p.ev.n
            p.t(lambda e: e.matmul(pbM[3][:, :], c["ones_f32_row"][:, :],
                                   ngrow[0:1, :], start=True, stop=True),
                waits=[(p.ev, vn)], inc=(s_t, 1))
            p.v(lambda e, tt=tt: e.tensor_copy(ngb[:, tt * TT:(tt + 1) * TT],
                                               pbM[3][:, :]),
                waits=[(s_t, s_t.n)])
        gate_done_vn = p.ev.n
        gate_done_tn = s_t.n
        gelu_an = {}
        for ex in range(N_EXP):
            w1h = []
            for hf in range(2):
                p.gdma(lambda e, ex=ex, hf=hf: e.dma_start(
                        out=w1b.ap().rearrange("p (a h) -> p a h", a=DCH)[:, hf * 4:(hf + 1) * 4, :],
                        in_=W1[ex, :, :].rearrange("(a pp) h -> pp a h", pp=128)[:, hf * 4:(hf + 1) * 4, :]),
                    waits=([(s_t, seq_start_tn if ex == 0 else s_t.n)]
                           if hf == 0 else []),
                    inc=(s_w1, 16))
                w1h.append(s_w1.n)
            w1n = s_w1.n
            w2h = []
            for hf in range(2):
                p.gdma(lambda e, ex=ex, hf=hf: e.dma_start(
                        out=w2b.ap().rearrange("p (a dd) -> p a dd", a=NHK2)[:, hf * 8:(hf + 1) * 8, :],
                        in_=W2[ex, :, :].rearrange("(a pp) dd -> pp a dd", pp=128)[:, hf * 8:(hf + 1) * 8, :]),
                    waits=([(s_t, s_t.n)] if hf == 0 else []),
                    inc=(s_w2, 16))
                w2h.append(s_w2.n)
            w2n = s_w2.n
            for tt in range(NTT):
                for hk in range(NHK2):
                    for dk in range(DCH):
                        p.t(lambda e, hk=hk, dk=dk, tt=tt: e.matmul(
                                pbM[hk % 2][:, :],
                                w1b[:, dk * H_MOE + hk * 128:dk * H_MOE + (hk + 1) * 128],
                                xb[:, dk * T + tt * TT:dk * T + (tt + 1) * TT],
                                start=(dk == 0), stop=(dk == DCH - 1)),
                            waits=(([(s_w1, w1h[0]), (p.ev, gate_done_vn),
                                     (p.ea, gelu_an.get(hk - 2,
                                           gelu_an.get(NHK2 + hk - 2, 0)))]
                                    if dk == 0 else
                                    ([(s_w1, w1h[1])] if dk == 4 else []))
                                   if tt == 0 else
                                   ([(p.ea, gelu_an.get(hk - 2,
                                         gelu_an.get(NHK2 + hk - 2, 0)))]
                                    if dk == 0 else [])),
                            inc=((s_t, 1) if dk == DCH - 1 else None))
                    tn = s_t.n
                    p.s(lambda e, hk=hk, ex=ex: e.activation(
                            h_e[:, hk * TT:(hk + 1) * TT], pbM[hk % 2][:, :],
                            AF.Gelu, bias=b1cols[:, hk * 8 + ex:hk * 8 + ex + 1]),
                        waits=[(s_t, tn)])
                    gelu_an[hk] = p.ea.n
                all_g = p.ea.n
                for dk in range(DCH):
                    for hk in range(NHK2):
                        p.t(lambda e, dk=dk, hk=hk: e.matmul(
                                pbM[2 + dk % 2][:, :],
                                w2b[:, hk * D + dk * 128:hk * D + (dk + 1) * 128],
                                h_e[:, hk * TT:(hk + 1) * TT],
                                start=(hk == 0), stop=(hk == NHK2 - 1)),
                            waits=(([(s_w2, w2h[0]), (p.ea, all_g), (p.ev, p.ev.n)]
                                    if hk == 0 else
                                    ([(s_w2, w2h[1])] if hk == 8 and tt == 0 else []))
                                   if dk == 0 else
                                   ([(p.ev, p.ev.n)] if hk == 0 else [])),
                            inc=((s_t, 1) if hk == NHK2 - 1 else None))
                    tn = s_t.n
                    p.v(lambda e, dk=dk, tt=tt: e.tensor_copy(
                            yt[tt % 2][:, dk * TT:(dk + 1) * TT],
                            pbM[2 + dk % 2][:, :]),
                        waits=[(s_t, tn),
                               (s_acc[tt], 16 * (N_EXP * s + ex))] +
                              (([(s_acc[tt - 2], 16 * (N_EXP * s + ex + 1))]
                                if tt >= 2 else
                                [(s_acc[tt + 2], 16 * (N_EXP * s + ex))])
                               if dk == 0 else []))
                ev_y = p.ev.n
                p.gdma(lambda e, s=s, tt=tt, ex=ex: e.dma_start(
                        out=Yacc[s, tt, :, :], in_=yt[tt % 2][:, :],
                        accum_op=(ALU.bypass if ex == 0 else ALU.add),
                        max_dma_last_dim=2048),
                    waits=[(p.ev, ev_y)], inc=(s_acc[tt], 16))
        for tt in range(NTT):
            tsl = slice(tt * TT, (tt + 1) * TT)
            p.y(lambda e, s=s, tt=tt: e.dma_start(
                    out=yt[tt % 2][:, :], in_=Yacc[s, tt, :, :]),
                waits=[(s_acc[tt], 16 * N_EXP * (s + 1)), (p.ev, p.ev.n)] +
                      ([(s_acc[tt + 2], 16 * N_EXP * (s + 1))] if tt < 2 else []),
                inc=(s_xl, 16))
            for dk in range(DCH):
                p.y(lambda e, s=s, dk=dk, tsl=tsl: e.dma_start(
                        out=xe[:, dk * TT:(dk + 1) * TT], in_=Xin[s, dk, :, tsl]),
                    waits=([(s_xs, s_xs.n), (p.eg, p.eg.n), (p.ev, p.ev.n)]
                           if dk == 0 else []),
                    inc=(s_xl, 16))
            xln = s_xl.n
            for dk in range(DCH):
                p.v(lambda e, dk=dk, tt=tt: e.scalar_tensor_tensor(
                        t_scr[:, :], yt[tt % 2][:, dk * TT:(dk + 1) * TT],
                        b2s[:, dk:dk + 1], ngb[:, tt * TT:(tt + 1) * TT],
                        ALU.add, ALU.mult),
                    waits=([(s_xl, xln)] if dk == 0 else []))
                p.v(lambda e, dk=dk: e.tensor_add(
                        xe[:, dk * TT:(dk + 1) * TT], t_scr[:, :],
                        xe[:, dk * TT:(dk + 1) * TT]))
            epi_vn = p.ev.n
            emit_boundary(p, c, L, xe, stg_a, stg_b, t_scr, t_scr_g, Xout, s,
                          tsl, [(p.ev, epi_vn)], s_xs)
    return [(s_xs, s_xs.n)]


def _run_mamba():
    nc = bass.Bass("TRN2", target_bir_lowering=False)
    Xin = nc.dram_tensor("Xin", [NSEQ, DCH, 128, T], F32, kind="ExternalInput").ap()
    in_proj_w = nc.dram_tensor("in_proj_w", [D_IN_PROJ, D], F32, kind="ExternalInput").ap()
    conv_w = nc.dram_tensor("conv_w", [CONV_DIM, 4], F32, kind="ExternalInput").ap()
    conv_b = nc.dram_tensor("conv_b", [CONV_DIM], F32, kind="ExternalInput").ap()
    dt_bias = nc.dram_tensor("dt_bias", [NHEADS], F32, kind="ExternalInput").ap()
    A_log = nc.dram_tensor("A_log", [NHEADS], F32, kind="ExternalInput").ap()
    Dp = nc.dram_tensor("Dp", [NHEADS], F32, kind="ExternalInput").ap()
    norm_w = nc.dram_tensor("norm_w", [D_INNER], F32, kind="ExternalInput").ap()
    out_proj_w = nc.dram_tensor("out_proj_w", [D, D_INNER], F32, kind="ExternalInput").ap()
    ln_w = nc.dram_tensor("ln_w", [D], F32, kind="ExternalInput").ap()
    ln_b = nc.dram_tensor("ln_b", [D], F32, kind="ExternalInput").ap()
    Xout = nc.dram_tensor("Xout", [NSEQ, DCH, 128, T], F32, kind="ExternalOutput").ap()
    scr_inT = nc.dram_tensor("scr_inT", [DCH, 128, D_IN_PROJ], BF16, kind="Internal").ap()
    scr_outT = nc.dram_tensor("scr_outT", [D_INNER // 128, 128, D], BF16, kind="Internal").ap()
    import os
    dbg = None
    if os.environ.get("MAMBA_DBG"):
        shapes = {"rbc_raw": [128, 2048], "rbc_t1": [128, 2048],
                  "rbc_t2": [128, 2048], "lam_tm_all": [128, 128],
                  "lam_loc_all": [32, TT],
                  "z_sil": [128, 16 * TT], "xssm": [128, 16 * TT],
                  "bcb": [128, 2 * TT], "dt_fm": [32, TT], "lam_fm": [32, TT],
                  "X_tm": [128, D_INNER], "Y_tm": [128, D_INNER],
                  "H_f": [128, D_INNER], "expl_tm": [128, 32],
                  "wdt_all": [128, 32], "decs_bc": [128, 32],
                  "lam_tm": [128, 32], "e_sb": [128, 128],
                  "y_fm": [128, 16 * TT]}
        dts = {"vread": BF16, "z_sil": BF16, "xssm": BF16, "bcb": BF16, "X_tm": BF16,
               "Y_tm": BF16, "wdt_all": BF16, "y_fm": BF16}
        dbg = {k: nc.dram_tensor("dbg_" + k, v, dts.get(k, F32),
                                 kind="ExternalOutput").ap()
               for k, v in shapes.items()}
    p = Prog(nc)
    c = make_consts(p)
    dw = build_mamba_prologue(p, c, in_proj_w, out_proj_w, scr_inT, scr_outT)
    done = build_mamba(p, c, Xin, Xout, scr_inT, scr_outT, conv_w, conv_b,
                       dt_bias, A_log, Dp, norm_w, ln_w, ln_b, pre_waits=dw,
                       dbg=dbg)
    p.y(lambda e: e.nop(), waits=done)
    p.finalize()
    return nc


def _run_moe():
    nc = bass.Bass("TRN2", target_bir_lowering=False)
    Xin = nc.dram_tensor("Xin", [NSEQ, DCH, 128, T], F32, kind="ExternalInput").ap()
    gate_w = nc.dram_tensor("gate_w", [N_EXP, D], F32, kind="ExternalInput").ap()
    gate_b = nc.dram_tensor("gate_b", [N_EXP], F32, kind="ExternalInput").ap()
    rms_w = nc.dram_tensor("rms_w", [N_EXP], F32, kind="ExternalInput").ap()
    W1 = nc.dram_tensor("w1", [N_EXP, D, H_MOE], F32, kind="ExternalInput").ap()
    B1 = nc.dram_tensor("b1", [N_EXP, H_MOE], F32, kind="ExternalInput").ap()
    W2 = nc.dram_tensor("w2", [N_EXP, H_MOE, D], F32, kind="ExternalInput").ap()
    B2 = nc.dram_tensor("b2", [N_EXP, D], F32, kind="ExternalInput").ap()
    Xout = nc.dram_tensor("Xout", [NSEQ, DCH, 128, T], F32, kind="ExternalOutput").ap()
    Yacc = nc.dram_tensor("Yacc", [NSEQ, NTT, 128, DCH * TT], F32, kind="Internal").ap()
    p = Prog(nc)
    c = make_consts(p)
    done = build_moe(p, c, Xin, Xout, gate_w, gate_b, rms_w, W1, B1, W2, B2, Yacc)
    p.y(lambda e: e.nop(), waits=done)
    p.finalize()
    return nc



def _run_identity(x8=None):
    nc = bass.Bass("TRN2", target_bir_lowering=False)
    x_in = nc.dram_tensor("x_in", [NSEQ, T, D], F32, kind="ExternalInput").ap()
    y_out = nc.dram_tensor("y_out", [NSEQ, T, D], F32, kind="ExternalOutput").ap()
    X = nc.dram_tensor("Xs", [NSEQ, DCH, 128, T], F32, kind="Internal").ap()
    p = Prog(nc)
    c = make_consts(p)
    done = build_entry(p, c, x_in, X)
    done2 = build_exit(p, c, X, y_out, pre_waits=done)
    p.y(lambda e: e.nop(), waits=done2)
    p.finalize()
    return nc


def _run_ffn():
    nc = bass.Bass("TRN2", target_bir_lowering=False)
    Xin = nc.dram_tensor("Xin", [NSEQ, DCH, 128, T], F32, kind="ExternalInput").ap()
    W_fc = nc.dram_tensor("W_fc", [H_FFN, D], F32, kind="ExternalInput").ap()
    ln_w = nc.dram_tensor("ln_w", [H_FFN], F32, kind="ExternalInput").ap()
    ln_b = nc.dram_tensor("ln_b", [H_FFN], F32, kind="ExternalInput").ap()
    W_proj = nc.dram_tensor("W_proj", [D, H_FFN], F32, kind="ExternalInput").ap()
    Xout = nc.dram_tensor("Xout", [NSEQ, DCH, 128, T], F32, kind="ExternalOutput").ap()
    scr_fcT = nc.dram_tensor("scr_fcT", [DCH, 128, H_FFN], BF16, kind="Internal").ap()
    scr_projT = nc.dram_tensor("scr_projT", [H_FFN // 128, 128, D], BF16,
                               kind="Internal").ap()
    p = Prog(nc)
    c = make_consts(p)
    persist = {"st_col": p.sbuf("ff_stcol", [128, DCH], F32),
               "cb_col": p.sbuf("ff_cbcol", [128, DCH], F32)}
    dw = build_ffn_prologue(p, c, W_fc, W_proj, ln_w, ln_b, scr_fcT, scr_projT,
                            persist)
    done = build_ffn(p, c, Xin, Xout, scr_fcT, scr_projT, persist, pre_waits=dw)
    p.y(lambda e: e.nop(), waits=done)
    p.finalize()
    return nc


def _run_attn():
    nc = bass.Bass("TRN2", target_bir_lowering=False)
    Xin = nc.dram_tensor("Xin", [NSEQ, DCH, 128, T], F32, kind="ExternalInput").ap()
    Wqk = nc.dram_tensor("Wqk", [D, D], F32, kind="ExternalInput").ap()
    Wv = nc.dram_tensor("Wv", [D, D], F32, kind="ExternalInput").ap()
    Wo = nc.dram_tensor("Wo", [D, D], F32, kind="ExternalInput").ap()
    tau = nc.dram_tensor("tau", [16, 1, 1], F32, kind="ExternalInput").ap()
    Xout = nc.dram_tensor("Xout", [NSEQ, DCH, 128, T], F32, kind="ExternalOutput").ap()
    p = Prog(nc)
    c = make_consts(p)
    done = build_attn(p, c, Xin, Xout, Wqk, Wv, Wo, tau)
    p.y(lambda e: e.nop(), waits=done)
    p.finalize()
    return nc


def _build_launch1():
    nc = bass.Bass("TRN2", target_bir_lowering=False)
    x_in = nc.dram_tensor("x_in", [NSEQ, T, D], F32, kind="ExternalInput").ap()
    in_proj_w = nc.dram_tensor("in_proj_w", [D_IN_PROJ, D], F32, kind="ExternalInput").ap()
    conv_w = nc.dram_tensor("conv_w", [CONV_DIM, 4], F32, kind="ExternalInput").ap()
    conv_b = nc.dram_tensor("conv_b", [CONV_DIM], F32, kind="ExternalInput").ap()
    dt_bias = nc.dram_tensor("dt_bias", [NHEADS], F32, kind="ExternalInput").ap()
    A_log = nc.dram_tensor("A_log", [NHEADS], F32, kind="ExternalInput").ap()
    Dp = nc.dram_tensor("Dp", [NHEADS], F32, kind="ExternalInput").ap()
    norm_w = nc.dram_tensor("norm_w", [D_INNER], F32, kind="ExternalInput").ap()
    out_proj_w = nc.dram_tensor("out_proj_w", [D, D_INNER], F32, kind="ExternalInput").ap()
    ln_w = nc.dram_tensor("ln_w", [D], F32, kind="ExternalInput").ap()
    ln_b = nc.dram_tensor("ln_b", [D], F32, kind="ExternalInput").ap()
    X1 = nc.dram_tensor("X1", [NSEQ, DCH, 128, T], F32, kind="ExternalOutput").ap()
    X0 = nc.dram_tensor("X0", [NSEQ, DCH, 128, T], F32, kind="Internal").ap()
    scr_inT = nc.dram_tensor("scr_inT", [DCH, 128, D_IN_PROJ], BF16, kind="Internal").ap()
    scr_outT = nc.dram_tensor("scr_outT", [D_INNER // 128, 128, D], BF16, kind="Internal").ap()
    p = Prog(nc)
    c = make_consts(p)
    dw = build_mamba_prologue(p, c, in_proj_w, out_proj_w, scr_inT, scr_outT)
    with p.scope():
        done = build_entry(p, c, x_in, X0)
        p.barrier(done)
    done = build_mamba(p, c, X0, X1, scr_inT, scr_outT, conv_w, conv_b,
                       dt_bias, A_log, Dp, norm_w, ln_w, ln_b, pre_waits=dw)
    p.y(lambda e: e.nop(), waits=done)
    p.finalize()
    return nc


def _build_launch2():
    nc = bass.Bass("TRN2", target_bir_lowering=False)
    Xin = nc.dram_tensor("Xin", [NSEQ, DCH, 128, T], F32, kind="ExternalInput").ap()
    W_fc = nc.dram_tensor("W_fc", [H_FFN, D], F32, kind="ExternalInput").ap()
    ln_w = nc.dram_tensor("ln_w", [H_FFN], F32, kind="ExternalInput").ap()
    ln_b = nc.dram_tensor("ln_b", [H_FFN], F32, kind="ExternalInput").ap()
    W_proj = nc.dram_tensor("W_proj", [D, H_FFN], F32, kind="ExternalInput").ap()
    Xout = nc.dram_tensor("Xout", [NSEQ, DCH, 128, T], F32, kind="ExternalOutput").ap()
    scr_fcT = nc.dram_tensor("scr_fcT", [DCH, 128, H_FFN], BF16, kind="Internal").ap()
    scr_projT = nc.dram_tensor("scr_projT", [H_FFN // 128, 128, D], BF16,
                               kind="Internal").ap()
    p = Prog(nc)
    c = make_consts(p)
    persist = {"st_col": p.sbuf("ff_stcol", [128, DCH], F32),
               "cb_col": p.sbuf("ff_cbcol", [128, DCH], F32)}
    dw = build_ffn_prologue(p, c, W_fc, W_proj, ln_w, ln_b, scr_fcT, scr_projT,
                            persist)
    done = build_ffn(p, c, Xin, Xout, scr_fcT, scr_projT, persist, pre_waits=dw)
    p.y(lambda e: e.nop(), waits=done)
    p.finalize()
    return nc


def _build_launch3():
    nc = bass.Bass("TRN2", target_bir_lowering=False)
    Xin = nc.dram_tensor("Xin", [NSEQ, DCH, 128, T], F32, kind="ExternalInput").ap()
    Wqk = nc.dram_tensor("Wqk", [D, D], F32, kind="ExternalInput").ap()
    Wv = nc.dram_tensor("Wv", [D, D], F32, kind="ExternalInput").ap()
    Wo = nc.dram_tensor("Wo", [D, D], F32, kind="ExternalInput").ap()
    tau = nc.dram_tensor("tau", [16, 1, 1], F32, kind="ExternalInput").ap()
    Xout = nc.dram_tensor("Xout", [NSEQ, DCH, 128, T], F32, kind="ExternalOutput").ap()
    p = Prog(nc)
    c = make_consts(p)
    done = build_attn(p, c, Xin, Xout, Wqk, Wv, Wo, tau)
    p.y(lambda e: e.nop(), waits=done)
    p.finalize()
    return nc


def _build_launch4():
    nc = bass.Bass("TRN2", target_bir_lowering=False)
    Xin = nc.dram_tensor("Xin", [NSEQ, DCH, 128, T], F32, kind="ExternalInput").ap()
    gate_w = nc.dram_tensor("gate_w", [N_EXP, D], F32, kind="ExternalInput").ap()
    gate_b = nc.dram_tensor("gate_b", [N_EXP], F32, kind="ExternalInput").ap()
    rms_w = nc.dram_tensor("rms_w", [N_EXP], F32, kind="ExternalInput").ap()
    W1 = nc.dram_tensor("w1", [N_EXP, D, H_MOE], F32, kind="ExternalInput").ap()
    B1 = nc.dram_tensor("b1", [N_EXP, H_MOE], F32, kind="ExternalInput").ap()
    W2 = nc.dram_tensor("w2", [N_EXP, H_MOE, D], F32, kind="ExternalInput").ap()
    B2 = nc.dram_tensor("b2", [N_EXP, D], F32, kind="ExternalInput").ap()
    y_out = nc.dram_tensor("y_out", [NSEQ, T, D], F32, kind="ExternalOutput").ap()
    X4 = nc.dram_tensor("X4", [NSEQ, DCH, 128, T], F32, kind="Internal").ap()
    Yacc = nc.dram_tensor("Yacc", [NSEQ, NTT, 128, DCH * TT], F32, kind="Internal").ap()
    p = Prog(nc)
    c = make_consts(p)
    with p.scope():
        done = build_moe(p, c, Xin, X4, gate_w, gate_b, rms_w, W1, B1, W2, B2,
                         Yacc)
        p.barrier(done)
    done = build_exit(p, c, X4, y_out)
    p.y(lambda e: e.nop(), waits=done)
    p.finalize()
    return nc


def _build_fused():
    nc = bass.Bass("TRN2", target_bir_lowering=False)
    EI, EO, IN = "ExternalInput", "ExternalOutput", "Internal"
    x_in = nc.dram_tensor("x_in", [NSEQ, T, D], F32, kind=EI).ap()
    in_proj_w = nc.dram_tensor("in_proj_w", [D_IN_PROJ, D], F32, kind=EI).ap()
    conv_w = nc.dram_tensor("conv_w", [CONV_DIM, 4], F32, kind=EI).ap()
    conv_b = nc.dram_tensor("conv_b", [CONV_DIM], F32, kind=EI).ap()
    dt_bias = nc.dram_tensor("dt_bias", [NHEADS], F32, kind=EI).ap()
    A_log = nc.dram_tensor("A_log", [NHEADS], F32, kind=EI).ap()
    Dp = nc.dram_tensor("Dp", [NHEADS], F32, kind=EI).ap()
    norm_w = nc.dram_tensor("norm_w", [D_INNER], F32, kind=EI).ap()
    out_proj_w = nc.dram_tensor("out_proj_w", [D, D_INNER], F32, kind=EI).ap()
    mln_w = nc.dram_tensor("mln_w", [D], F32, kind=EI).ap()
    mln_b = nc.dram_tensor("mln_b", [D], F32, kind=EI).ap()
    W_fc = nc.dram_tensor("W_fc", [H_FFN, D], F32, kind=EI).ap()
    fln_w = nc.dram_tensor("fln_w", [H_FFN], F32, kind=EI).ap()
    fln_b = nc.dram_tensor("fln_b", [H_FFN], F32, kind=EI).ap()
    W_proj = nc.dram_tensor("W_proj", [D, H_FFN], F32, kind=EI).ap()
    Wqk = nc.dram_tensor("Wqk", [D, D], F32, kind=EI).ap()
    Wv = nc.dram_tensor("Wv", [D, D], F32, kind=EI).ap()
    Wo = nc.dram_tensor("Wo", [D, D], F32, kind=EI).ap()
    tau = nc.dram_tensor("tau", [16, 1, 1], F32, kind=EI).ap()
    gate_w = nc.dram_tensor("gate_w", [N_EXP, D], F32, kind=EI).ap()
    gate_b = nc.dram_tensor("gate_b", [N_EXP], F32, kind=EI).ap()
    rms_w = nc.dram_tensor("rms_w", [N_EXP], F32, kind=EI).ap()
    W1 = nc.dram_tensor("w1", [N_EXP, D, H_MOE], F32, kind=EI).ap()
    B1 = nc.dram_tensor("b1", [N_EXP, H_MOE], F32, kind=EI).ap()
    W2 = nc.dram_tensor("w2", [N_EXP, H_MOE, D], F32, kind=EI).ap()
    B2 = nc.dram_tensor("b2", [N_EXP, D], F32, kind=EI).ap()
    y_out = nc.dram_tensor("y_out", [NSEQ, T, D], F32, kind=EO).ap()
    X0 = nc.dram_tensor("X0", [NSEQ, DCH, 128, T], F32, kind=IN).ap()
    X1 = nc.dram_tensor("X1", [NSEQ, DCH, 128, T], F32, kind=IN).ap()
    X2 = nc.dram_tensor("X2", [NSEQ, DCH, 128, T], F32, kind=IN).ap()
    X3 = nc.dram_tensor("X3", [NSEQ, DCH, 128, T], F32, kind=IN).ap()
    X4 = nc.dram_tensor("X4", [NSEQ, DCH, 128, T], F32, kind=IN).ap()
    scr_inT = nc.dram_tensor("scr_inT", [DCH, 128, D_IN_PROJ], BF16, kind=IN).ap()
    scr_outT = nc.dram_tensor("scr_outT", [D_INNER // 128, 128, D], BF16, kind=IN).ap()
    scr_fcT = nc.dram_tensor("scr_fcT", [DCH, 128, H_FFN], BF16, kind=IN).ap()
    scr_projT = nc.dram_tensor("scr_projT", [H_FFN // 128, 128, D], BF16, kind=IN).ap()
    Yacc = nc.dram_tensor("Yacc", [NSEQ, NTT, 128, DCH * TT], F32, kind=IN).ap()

    p = Prog(nc)
    c = make_consts(p)
    persist = {"st_col": p.sbuf("ff_stcol", [128, DCH], F32),
               "cb_col": p.sbuf("ff_cbcol", [128, DCH], F32)}
    dw_m = build_mamba_prologue(p, c, in_proj_w, out_proj_w, scr_inT, scr_outT)
    dw_f = build_ffn_prologue(p, c, W_fc, W_proj, fln_w, fln_b, scr_fcT,
                              scr_projT, persist, pre_waits=dw_m)
    with p.scope():
        done = build_entry(p, c, x_in, X0)
        p.barrier(done)
    with p.scope():
        done = build_mamba(p, c, X0, X1, scr_inT, scr_outT, conv_w, conv_b,
                           dt_bias, A_log, Dp, norm_w, mln_w, mln_b,
                           pre_waits=dw_f)
        p.barrier(done)
    with p.scope():
        done = build_ffn(p, c, X1, X2, scr_fcT, scr_projT, persist)
        p.barrier(done)
    with p.scope():
        done = build_attn(p, c, X2, X3, Wqk, Wv, Wo, tau)
        p.barrier(done)
    with p.scope():
        done = build_moe(p, c, X3, X4, gate_w, gate_b, rms_w, W1, B1, W2, B2,
                         Yacc)
        p.barrier(done)
    done = build_exit(p, c, X4, y_out)
    p.y(lambda e: e.nop(), waits=done)
    p.finalize()
    return nc


def _kernel_in_map(inputs, shard):
    return {"x_in": shard,
            "in_proj_w": inputs["L0_attn_in_proj_w"],
            "conv_w": inputs["L0_attn_conv_w"],
            "conv_b": inputs["L0_attn_conv_b"],
            "dt_bias": inputs["L0_attn_dt_bias"],
            "A_log": inputs["L0_attn_A_log"],
            "Dp": inputs["L0_attn_D"],
            "norm_w": inputs["L0_attn_norm_w"],
            "out_proj_w": inputs["L0_attn_out_proj_w"],
            "mln_w": inputs["L0_attn_ln_w"],
            "mln_b": inputs["L0_attn_ln_b"],
            "W_fc": inputs["L0_ffn_W_fc"],
            "fln_w": inputs["L0_ffn_ln_w"],
            "fln_b": inputs["L0_ffn_ln_b"],
            "W_proj": inputs["L0_ffn_W_proj"],
            "Wqk": inputs["L1_attn_Wqk"],
            "Wv": inputs["L1_attn_Wv"],
            "Wo": inputs["L1_attn_Wo"],
            "tau": inputs["L1_attn_tau"],
            "gate_w": inputs["L1_ffn_gate_w"],
            "gate_b": inputs["L1_ffn_gate_b"],
            "rms_w": inputs["L1_ffn_rms_w"],
            "w1": inputs["L1_ffn_w1"],
            "b1": inputs["L1_ffn_b1"],
            "w2": inputs["L1_ffn_w2"],
            "b2": inputs["L1_ffn_b2"]}


_NC_CACHE = {}


def kernel_fused(**inputs):
    x = np.ascontiguousarray(inputs["L0_attn_x"])
    if "fused" not in _NC_CACHE:
        _NC_CACHE["fused"] = _build_fused()
    cores = list(range(8))
    maps = [_kernel_in_map(inputs, np.ascontiguousarray(x[2 * i:2 * i + 2]))
            for i in range(8)]
    res = run_bass_kernel_spmd(_NC_CACHE["fused"], maps, core_ids=cores)
    return np.concatenate([res.results[ci]["y_out"] for ci in cores], axis=0)


def kernel(**inputs):
    return kernel_fused(**inputs)


def _kernel_4launch(**inputs):
    x = np.ascontiguousarray(inputs["L0_attn_x"])
    ncs = _NC_CACHE
    if not ncs:
        ncs[1] = _build_launch1()
        ncs[2] = _build_launch2()
        ncs[3] = _build_launch3()
        ncs[4] = _build_launch4()
    shards = [np.ascontiguousarray(x[2 * i:2 * i + 2]) for i in range(8)]
    cores = list(range(8))
    m1 = [{"x_in": shards[ci],
           "in_proj_w": inputs["L0_attn_in_proj_w"],
           "conv_w": inputs["L0_attn_conv_w"],
           "conv_b": inputs["L0_attn_conv_b"],
           "dt_bias": inputs["L0_attn_dt_bias"],
           "A_log": inputs["L0_attn_A_log"],
           "Dp": inputs["L0_attn_D"],
           "norm_w": inputs["L0_attn_norm_w"],
           "out_proj_w": inputs["L0_attn_out_proj_w"],
           "ln_w": inputs["L0_attn_ln_w"],
           "ln_b": inputs["L0_attn_ln_b"]} for ci in cores]
    r1 = run_bass_kernel_spmd(ncs[1], m1, core_ids=cores)
    m2 = [{"Xin": r1.results[ci]["X1"],
           "W_fc": inputs["L0_ffn_W_fc"],
           "ln_w": inputs["L0_ffn_ln_w"],
           "ln_b": inputs["L0_ffn_ln_b"],
           "W_proj": inputs["L0_ffn_W_proj"]} for ci in cores]
    r2 = run_bass_kernel_spmd(ncs[2], m2, core_ids=cores)
    m3 = [{"Xin": r2.results[ci]["Xout"],
           "Wqk": inputs["L1_attn_Wqk"],
           "Wv": inputs["L1_attn_Wv"],
           "Wo": inputs["L1_attn_Wo"],
           "tau": inputs["L1_attn_tau"]} for ci in cores]
    r3 = run_bass_kernel_spmd(ncs[3], m3, core_ids=cores)
    m4 = [{"Xin": r3.results[ci]["Xout"],
           "gate_w": inputs["L1_ffn_gate_w"],
           "gate_b": inputs["L1_ffn_gate_b"],
           "rms_w": inputs["L1_ffn_rms_w"],
           "w1": inputs["L1_ffn_w1"],
           "b1": inputs["L1_ffn_b1"],
           "w2": inputs["L1_ffn_w2"],
           "b2": inputs["L1_ffn_b2"]} for ci in cores]
    r4 = run_bass_kernel_spmd(ncs[4], m4, core_ids=cores)
    return np.concatenate([r4.results[ci]["y_out"] for ci in cores], axis=0)
